# version 10
# baseline (speedup 1.0000x reference)
import contextlib
import numpy as np
import ml_dtypes
import concourse.bass as bass
import concourse.mybir as mybir
from concourse.bass_utils import run_bass_kernel_spmd

F32 = mybir.dt.float32
BF16 = mybir.dt.bfloat16
I16 = mybir.dt.int16
U16 = mybir.dt.uint16
AF = mybir.ActivationFunctionType
ALU = mybir.AluOpType
AX = mybir.AxisListType

MAGIC = 12582912.0
B, S, D = 4, 4096, 1024
TOK = 2048
NH, DK = 16, 64
FI = 3072
EPS = 1e-5

bf16 = ml_dtypes.bfloat16

ENGINES = ("pe", "act", "dve", "pool", "sp")
_ENG_ATTR = {"pe": "tensor", "act": "scalar", "dve": "vector", "pool": "gpsimd", "sp": "sync"}
NDMASEM = 6


class Node:
    __slots__ = ("eng", "fn", "deps", "is_dma", "idx", "flagged", "special")

    def __init__(self, eng, fn, deps, is_dma, special=False):
        self.eng, self.fn, self.deps, self.is_dma = eng, fn, deps, is_dma
        self.idx = -1
        self.flagged = False
        self.special = special


class Prog:

    def __init__(self, name):
        self.nc = bass.Bass("TRN2", target_bir_lowering=False)
        self.name = name
        self.nodes = []
        self._stack = contextlib.ExitStack()

    def sbuf(self, name, shape, dtype):
        return self._stack.enter_context(self.nc.sbuf_tensor(name, shape, dtype))

    def psum(self, name, shape, dtype=F32):
        return self._stack.enter_context(self.nc.psum_tensor(name, shape, dtype))

    def add(self, eng, fn, deps=(), special=False):
        n = Node(eng, fn, [d for d in deps if d is not None], False, special)
        self.nodes.append(n)
        return n

    def dma(self, queue, out, in_, deps=()):
        def fn(e):
            return e.dma_start(out=out, in_=in_)
        n = Node(queue, fn, [d for d in deps if d is not None], True)
        self.nodes.append(n)
        return n

    def build(self):
        nc = self.nc
        for n in self.nodes:
            if n.is_dma:
                n.flagged = True
            for d in n.deps:
                if d.eng != n.eng or d.is_dma or d.special or n.special:
                    d.flagged = True
        comp_count = {e: 0 for e in ENGINES}
        dma_count = {e: 0 for e in ENGINES}
        for n in self.nodes:
            if not n.flagged:
                continue
            if n.is_dma:
                n.idx = dma_count[n.eng]
                dma_count[n.eng] += 1
            else:
                comp_count[n.eng] += 1
                n.idx = comp_count[n.eng]
        self._sems = {}
        for e in ENGINES:
            if comp_count[e]:
                self._sems[("c", e)] = nc.semaphore(f"S_{e}").__enter__()
            if dma_count[e]:
                self._sems[("d", e)] = [nc.semaphore(f"D_{e}{i}").__enter__()
                                        for i in range(NDMASEM)]
        block = nc.Block().__enter__()
        per_eng = {e: [n for n in self.nodes if n.eng == e] for e in ENGINES}

        def dep_req(d):
            if d.is_dma:
                sems = self._sems[("d", d.eng)]
                return (sems[d.idx % NDMASEM], 16 * (d.idx // NDMASEM + 1))
            return (self._sems[("c", d.eng)], d.idx)

        for e in ENGINES:
            nodes = per_eng[e]
            if not nodes:
                continue

            def make_body(e=e, nodes=nodes):
                def body(eng):
                    watermark = {}
                    for n in nodes:
                        for d in n.deps:
                            if (d.eng == e and not d.is_dma and not d.special
                                    and not n.special):
                                continue
                            sem, val = dep_req(d)
                            key = id(sem)
                            if watermark.get(key, -1) >= val:
                                continue
                            eng.wait_ge(sem, val)
                            watermark[key] = val
                        if n.is_dma and n.flagged:
                            sems = self._sems[("d", e)]
                            slot, rnd = n.idx % NDMASEM, n.idx // NDMASEM
                            if rnd > 0:
                                key = id(sems[slot])
                                val = 16 * rnd
                                if watermark.get(key, -1) < val:
                                    eng.wait_ge(sems[slot], val)
                                    watermark[key] = val
                        instr = n.fn(eng)
                        if n.flagged:
                            if n.is_dma:
                                instr.then_inc(self._sems[("d", e)][n.idx % NDMASEM], 16)
                            else:
                                instr.then_inc(self._sems[("c", e)], 1)
                return body
            getattr(block, _ENG_ATTR[e])(make_body())
        block.__exit__(None, None, None)
        return nc


def f32c(x):
    return np.ascontiguousarray(x, dtype=np.float32)


def bf16c(x):
    return np.ascontiguousarray(np.asarray(x).astype(bf16))



def mm(p, out_ap, lhsT, rhs, start, stop, deps=()):
    return p.add("pe", lambda e: e.matmul(out_ap, lhsT, rhs, start=start, stop=stop,
                                          skip_group_check=True), deps)


def mm_group(p, out_ap, pairs, deps):
    def fn(eng):
        last = None
        for i, (lhsT, rhs) in enumerate(pairs):
            last = eng.matmul(out_ap, lhsT, rhs, start=(i == 0),
                              stop=(i == len(pairs) - 1), skip_group_check=True)
        return last
    return p.add("pe", fn, deps)


def act(p, out, in_, func, deps, bias=None, scale=None):
    def fn(eng):
        kw = {}
        if bias is not None:
            kw["bias"] = bias
        if scale is not None:
            kw["scale"] = scale
        return eng.activation(out, in_, func, **kw)
    return p.add("act", fn, deps)


def acopy(p, out, in_, deps):
    return p.add("act", lambda e: e.copy(out, in_), deps)


def ts(p, out, in_, s1, s2, op0, op1, deps):
    return p.add("dve", lambda e: e.tensor_scalar(out, in_, s1, s2, op0, op1), deps)


def tt(p, out, a, b, op, deps):
    return p.add("dve", lambda e: e.tensor_tensor(out, a, b, op), deps)


def stt(p, out, in0, scalar, in1, op0, op1, deps):
    return p.add("dve", lambda e: e.scalar_tensor_tensor(out, in0, scalar, in1, op0, op1), deps)


def vred(p, out, in_, op, deps, absval=None):
    return p.add("dve", lambda e: e.tensor_reduce(out, in_, AX.X, op,
                                                  apply_absolute_value=absval),
                 deps, special=True)


def vcopy(p, out, in_, deps):
    return p.add("dve", lambda e: e.tensor_copy(out, in_), deps)


def vscan(p, out, d0, d1, init, deps):
    return p.add("dve", lambda e: e.tensor_tensor_scan(out, d0, d1, init, ALU.mult, ALU.add),
                 deps, special=True)


def vrecip(p, out, in_, deps):
    return p.add("dve", lambda e: e.reciprocal(out, in_), deps, special=True)


def sink(p, deps):
    return p.add("sp", lambda e: e.nop(), deps)


def chunks2d(handle, n, width):
    def sl(c, a=0, b=None):
        b = width if b is None else b
        return handle[:, c * width + a: c * width + b]
    return sl



def emit_ln(p, in_sl, out_sl, T, sc, dep_in):
    ones_col, ones_row = sc["ones_col"], sc["ones_row"]
    sq = sc["sq"]
    nodes_sq = [act(p, sq[:, c * T:(c + 1) * T], in_sl(c), AF.Square, dep_in)
                for c in range(8)]
    n_sum = mm_group(p, sc["stat"][0:1, 0:T],
                     [(ones_col, in_sl(c)) for c in range(8)], dep_in)
    n_ssq = mm_group(p, sc["stat"][32:33, 0:T],
                     [(ones_col, sq[:, c * T:(c + 1) * T]) for c in range(8)], nodes_sq)
    n_mean = ts(p, sc["rmean"], sc["stat"][0:1, 0:T], 1.0 / D, None,
                ALU.mult, ALU.bypass, [n_sum])
    n_msq = tt(p, sc["rvar"], sc["rmean"], sc["rmean"],
               ALU.mult, [n_mean])
    n_var = stt(p, sc["rvar"], sc["stat"][32:33, 0:T], 1.0 / D,
                sc["rvar"], ALU.mult, ALU.subtract, [n_ssq, n_msq])
    n_vpe = ts(p, sc["rvar"], sc["rvar"], EPS, None, ALU.add, ALU.bypass, [n_var])
    n_ln = act(p, sc["rtmp"], sc["rvar"], AF.Ln, [n_vpe])
    n_rstd = act(p, sc["rrstd"], sc["rtmp"], AF.Exp, [n_ln], scale=-0.5)
    n_mr = tt(p, sc["rmr"], sc["rmean"], sc["rrstd"],
              ALU.mult, [n_rstd])
    n_b1 = mm(p, sc["bc1"][:, 0:T], ones_row, sc["rrstd"], True, True, [n_rstd])
    n_b2 = mm(p, sc["bc2"][:, 0:T], ones_row, sc["rmr"], True, True, [n_mr])
    outs = []
    for c in range(8):
        n_a = tt(p, sq[:, c * T:(c + 1) * T], in_sl(c), sc["bc1"][:, 0:T],
                 ALU.mult, [n_b1, n_ssq])
        n_o = tt(p, out_sl(c), sq[:, c * T:(c + 1) * T], sc["bc2"][:, 0:T],
                 ALU.subtract, [n_b2, n_a])
        outs.append(n_o)
    return outs



NDQ = 18


def build_qkv():
    p = Prog("qkv")
    nc = p.nc
    xT = nc.dram_tensor("xT", [D, TOK], F32, kind="ExternalInput").ap()
    wq = nc.dram_tensor("wq", [D, D], F32, kind="ExternalInput").ap()
    wk = nc.dram_tensor("wk", [D, D], F32, kind="ExternalInput").ap()
    wv = nc.dram_tensor("wv", [D, D], F32, kind="ExternalInput").ap()
    sel = nc.dram_tensor("sel", [128, 128], F32, kind="ExternalInput").ap()
    dq_in = nc.dram_tensor("dq_in", [NDQ * 128, 1024], F32, kind="ExternalInput").ap()

    y_q = nc.dram_tensor("y_q", [D, TOK], BF16, kind="ExternalOutput").ap()
    y_k = nc.dram_tensor("y_k", [D, TOK], BF16, kind="ExternalOutput").ap()
    y_v = nc.dram_tensor("y_v", [TOK, D], BF16, kind="ExternalOutput").ap()
    y_qn2 = nc.dram_tensor("y_qn2", [16, 16], F32, kind="ExternalOutput").ap()
    y_kn2 = nc.dram_tensor("y_kn2", [16, 16], F32, kind="ExternalOutput").ap()
    y_dq = nc.dram_tensor("y_dq", [NDQ * 128, 1024], BF16, kind="ExternalOutput").ap()

    xs = p.sbuf("xs", [128, 2 * TOK], F32)
    xb = p.sbuf("xb", [128, 8 * TOK], BF16)
    ws = p.sbuf("ws", [128, 2 * D], F32)
    wqb = p.sbuf("wqb", [128, 8 * D], BF16)
    wkb = p.sbuf("wkb", [128, 8 * D], BF16)
    wvb = p.sbuf("wvb", [128, 8 * D], BF16)
    ob = p.sbuf("ob", [128, 8 * 512], BF16)
    sels = p.sbuf("sels", [128, 128], F32)
    sq2 = p.sbuf("sq2", [128, 2 * 512], F32)
    n2t = p.sbuf("n2t", [16, 32], F32)
    dqw = p.sbuf("dqw", [128, 2 * 1024], F32)
    dqo = p.sbuf("dqo", [128, 2 * 1024], BF16)
    wi = p.sbuf("wi", [128, 2 * 1024], F32)
    bmax = p.sbuf("bmax", [128, 2 * 32], F32)
    bs = p.sbuf("bs", [128, 2 * 32], F32)
    brec = p.sbuf("brec", [128, 2 * 32], F32)
    ps0 = p.psum("ps0", [128, 512])
    ps1 = p.psum("ps1", [128, 512])
    pn = p.psum("pn", [16, 512])
    if True:
        n_sel = p.dma("sp", sels[:], sel)
        casts = {}
        prev_cast = {}
        for name, src_d, dstb, nch, w in (("x", xT, xb, 8, TOK),):
            for c in range(8):
                buf = c % 2
                deps = []
                if c >= 2:
                    deps.append(casts[("x", c - 2)])
                ld = p.dma("sp", xs[:, buf * TOK:(buf + 1) * TOK],
                           xT[c * 128:(c + 1) * 128, :], deps)
                casts[("x", c)] = vcopy(p, xb[:, c * TOK:(c + 1) * TOK],
                                        xs[:, buf * TOK:(buf + 1) * TOK], [ld])
        for nm, wsrc, dstb in (("wq", wq, wqb), ("wk", wk, wkb), ("wv", wv, wvb)):
            for c in range(8):
                buf = c % 2
                deps = []
                key = (nm, c - 2)
                if key in casts:
                    deps.append(casts[key])
                elif c < 2 and nm != "wq":
                    pk = {"wk": "wq", "wv": "wk"}[nm]
                    deps.append(casts[(pk, 6 + c)])
                ld = p.dma("pool", ws[:, buf * D:(buf + 1) * D],
                           wsrc[c * 128:(c + 1) * 128, :], deps)
                casts[(nm, c)] = vcopy(p, dstb[:, c * D:(c + 1) * D],
                                       ws[:, buf * D:(buf + 1) * D], [ld])
        cx = [casts[("x", c)] for c in range(8)]
        cwq = [casts[("wq", c)] for c in range(8)]
        cwk = [casts[("wk", c)] for c in range(8)]
        cwv = [casts[("wv", c)] for c in range(8)]

        ps = [ps0, ps1]
        use = 0
        prev_evac = {}
        prev_store = {}
        red_nodes = []
        sel_prev = None
        all_selmm = []
        for which, wsb, ydst, cw in ((0, wqb, y_q, cwq), (1, wkb, y_k, cwk)):
            for tc in range(4):
                sq_list = []
                for oc in range(8):
                    pt = ps[use % 2]
                    deps = cx + cw if use == 0 else [cx[-1], cw[-1]]
                    deps = list(deps)
                    if use >= 2:
                        deps.append(prev_evac[use - 2])
                    g = mm_group(p, pt[:, :],
                                 [(wsb[:, kc * D + oc * 128:kc * D + (oc + 1) * 128],
                                   xb[:, kc * TOK + tc * 512:kc * TOK + (tc + 1) * 512])
                                  for kc in range(8)], deps)
                    ring = use % 8
                    edeps = [g]
                    if use >= 8:
                        edeps.append(prev_store[use - 8])
                    ev = acopy(p, ob[:, ring * 512:(ring + 1) * 512], pt[:, :], edeps)
                    st = p.dma("sp", ydst[oc * 128:(oc + 1) * 128, tc * 512:(tc + 1) * 512],
                               ob[:, ring * 512:(ring + 1) * 512], [ev])
                    prev_evac[use] = ev
                    prev_store[use] = st
                    sqb = oc % 2
                    sdeps = [ev]
                    if len(all_selmm) >= 2:
                        sdeps.append(all_selmm[-2])
                    nsq = act(p, sq2[:, sqb * 512:(sqb + 1) * 512],
                              ob[:, ring * 512:(ring + 1) * 512], AF.Square, sdeps)
                    smdeps = [nsq, n_sel]
                    if sel_prev is not None:
                        smdeps.append(sel_prev)
                    sm = mm(p, pn[:, :], sels[:, oc * 16:(oc + 1) * 16],
                            sq2[:, sqb * 512:(sqb + 1) * 512], oc == 0, oc == 7, smdeps)
                    all_selmm.append(sm)
                    sel_prev = sm
                    sq_list.append(sm)
                    use += 1
                rd = vred(p, n2t[0:16, which * 16 + tc * 4: which * 16 + (tc + 1) * 4],
                          pn[0:16, :].rearrange("p (a b) -> p a b", b=128), ALU.max,
                          [sq_list[-1]] + red_nodes[-1:])
                sel_prev = rd
                red_nodes.append(rd)
        for tki in range(16):
            for vc in range(2):
                pt = ps[use % 2]
                deps = [cx[-1], cwv[-1]]
                if use >= 2:
                    deps.append(prev_evac[use - 2])
                g = mm_group(p, pt[:, :],
                             [(xb[:, kc * TOK + tki * 128: kc * TOK + (tki + 1) * 128],
                               wvb[:, kc * D + vc * 512: kc * D + (vc + 1) * 512])
                              for kc in range(8)], deps)
                ring = use % 8
                edeps = [g]
                if use >= 8:
                    edeps.append(prev_store[use - 8])
                ev = acopy(p, ob[:, ring * 512:(ring + 1) * 512], pt[:, :], edeps)
                st = p.dma("sp", y_v[tki * 128:(tki + 1) * 128, vc * 512:(vc + 1) * 512],
                           ob[:, ring * 512:(ring + 1) * 512], [ev])
                prev_evac[use] = ev
                prev_store[use] = st
                use += 1

        dq_stores = {}
        for ch in range(NDQ):
            bq = ch % 2
            o = bq * 1024
            obk = bq * 32
            ldeps = []
            if ch >= 2:
                ldeps.append(dq_stores[ch - 2])
            ld = p.dma("pool", dqw[:, o:o + 1024], dq_in[ch * 128:(ch + 1) * 128, :], ldeps)
            w3 = dqw[:, o:o + 1024].rearrange("p (b l) -> p b l", l=32)
            wi3 = wi[:, o:o + 1024].rearrange("p (b l) -> p b l", l=32)
            dqo3 = dqo[:, o:o + 1024].rearrange("p (b l) -> p b l", l=32)
            r1 = vred(p, bmax[:, obk:obk + 32], w3, ALU.max, [ld] + ldeps, absval=True)
            r2 = ts(p, bs[:, obk:obk + 32], bmax[:, obk:obk + 32], 1.0 / 7.0, 1e-12,
                    ALU.mult, ALU.max, [r1])
            r3 = vrecip(p, brec[:, obk:obk + 32], bs[:, obk:obk + 32], [r2])
            r4 = tt(p, wi3, w3,
                    brec[:, obk:obk + 32].rearrange("p (b o) -> p b o", o=1).broadcast_to([128, 32, 32]),
                    ALU.mult, [r3])
            r5 = ts(p, wi[:, o:o + 1024], wi[:, o:o + 1024], MAGIC, MAGIC, ALU.add, ALU.subtract, [r4])
            r6 = tt(p, dqo3, wi3,
                    bs[:, obk:obk + 32].rearrange("p (b o) -> p b o", o=1).broadcast_to([128, 32, 32]),
                    ALU.mult, [r5])
            dq_stores[ch] = p.dma("pool", y_dq[ch * 128:(ch + 1) * 128, :], dqo[:, o:o + 1024], [r6])

        s4 = p.dma("sp", y_qn2, n2t[0:16, 0:16], [red_nodes[3]])
        s5 = p.dma("sp", y_kn2, n2t[0:16, 16:32], [red_nodes[7]])
        sink(p, [s4, s5] + list(prev_store.values()) + list(dq_stores.values()))
        return p.build()



def build_attn():
    p = Prog("attn")
    nc = p.nc
    qT = nc.dram_tensor("qT", [D, TOK], BF16, kind="ExternalInput").ap()
    kT = nc.dram_tensor("kT", [D, S], BF16, kind="ExternalInput").ap()
    vo = nc.dram_tensor("vo", [S, NH * 65], BF16, kind="ExternalInput").ap()
    xT = nc.dram_tensor("xT", [D, TOK], F32, kind="ExternalInput").ap()
    wo = nc.dram_tensor("wo", [D, D], F32, kind="ExternalInput").ap()
    n2in = nc.dram_tensor("n2in", [16, 48], F32, kind="ExternalInput").ap()
    msk = nc.dram_tensor("msk", [128, 8 * 512], BF16, kind="ExternalInput").ap()
    onc = nc.dram_tensor("onc", [128, 1], F32, kind="ExternalInput").ap()
    onr = nc.dram_tensor("onr", [1, 128], F32, kind="ExternalInput").ap()
    idn = nc.dram_tensor("idn", [128, 128], F32, kind="ExternalInput").ap()
    scr = nc.dram_tensor("scr16", [16, 1], F32)

    y_x1 = nc.dram_tensor("y_x1", [D, TOK], F32, kind="ExternalOutput").ap()
    y_mx = nc.dram_tensor("y_mx", [1, 1], F32, kind="ExternalOutput").ap()

    qb = p.sbuf("qb", [128, 8 * TOK], BF16)
    kb = p.sbuf("kb", [128, 2 * S], BF16)
    vb = p.sbuf("vb", [128, 2 * (32 * 130)], BF16)
    att = p.sbuf("att", [128, 8 * TOK], BF16)
    msks = p.sbuf("msks", [128, 8 * 512], BF16)
    pt = p.sbuf("pt", [128, 4 * 512], BF16)
    n2s = p.sbuf("n2s", [16, 48], F32)
    cq = p.sbuf("cq", [16, 4], F32)
    negc = p.sbuf("negc", [1, 16], F32)
    negcb = p.sbuf("negcb", [128, 16], F32)
    rcp = p.sbuf("rcp", [1, 2 * 512], F32)
    tav = p.sbuf("tav", [64, 2 * 512], F32)
    onc_s = p.sbuf("onc_s", [128, 1], F32)
    onr_s = p.sbuf("onr_s", [1, 128], F32)
    idn_s = p.sbuf("idn_s", [128, 128], F32)
    wos = p.sbuf("wos", [128, 2 * D], F32)
    wob = p.sbuf("wob", [128, 8 * D], BF16)
    xrs = p.sbuf("xrs", [128, 2 * 512], F32)
    rsum = p.sbuf("rsum", [128, 8 * 512], F32)
    x1s = p.sbuf("x1s", [128, 8 * 512], F32)
    sq = p.sbuf("sqln", [128, 8 * 512], F32)
    lnA = p.sbuf("lnA", [65, 512], F32)
    rrstd = p.sbuf("rrstd", [1, 512], F32)
    rmr = p.sbuf("rmr", [1, 512], F32)
    mcol = p.sbuf("mcol", [128, 2], F32)
    mrow = p.sbuf("mrow", [1, 128], F32)

    pp0 = p.psum("pp0", [128, 1024])
    pp1 = p.psum("pp1", [128, 1024])
    avA = p.psum("avA", [128, 512])
    avB = p.psum("avB", [128, 512])
    pmx = p.psum("pmx", [128, 512])
    pbc = p.psum("pbc", [128, 16])

    if True:
        n_q = p.dma("sp", qb[:].rearrange("p (c t) -> p c t", t=TOK),
                    qT.rearrange("(c p) t -> p c t", p=128))
        n_m = p.dma("sp", msks[:], msk)
        n_n2 = p.dma("sp", n2s[:], n2in)
        n_onc = p.dma("sp", onc_s[:], onc)
        n_onr = p.dma("sp", onr_s[:], onr)
        n_idn = p.dma("sp", idn_s[:], idn)

        r_qm = vred(p, cq[0:16, 0:1], n2s[0:16, 0:16], ALU.max, [n_n2])
        r_km = vred(p, cq[0:16, 1:2], n2s[0:16, 16:48], ALU.max, [r_qm])
        a_sq = act(p, cq[0:16, 2:3], cq[0:16, 0:1], AF.Sqrt, [r_qm, r_km])
        a_sk = act(p, cq[0:16, 3:4], cq[0:16, 1:2], AF.Sqrt, [r_km])
        t_c = tt(p, cq[0:16, 0:1], cq[0:16, 2:3], cq[0:16, 3:4], ALU.mult, [a_sq, a_sk])
        t_b = ts(p, cq[0:16, 1:2], cq[0:16, 0:1], -0.1275, -0.00125, ALU.mult, ALU.add, [t_c])
        d_out = p.dma("sp", scr[:, :], cq[0:16, 1:2], [t_b])
        d_in = p.dma("sp", negc[:], scr.rearrange("a b -> b a"), [d_out])
        bc_mm = mm(p, pbc[:, 0:16], onr_s[0:1, :], negc[0:1, :], True, True, [d_in, n_onr])
        n_bias = acopy(p, negcb[:], pbc[:, 0:16], [bc_mm])

        exp_hist = {}
        av_hist = {}
        attn_done = []
        prev_postA = {}
        prev_postB = {}
        last_exp_on = {}
        last_av_on = {}
        gp = [0]
        for hp in range(8):
            dbl = hp % 2
            ldeps = []
            if hp >= 2:
                ldeps.append(exp_hist[(hp - 2, 3, "B")])
                ldeps.append(av_hist[(hp - 2, 3, "B")])
            lk = p.dma("pool", kb[:, dbl * S:(dbl + 1) * S], kT[hp * 128:(hp + 1) * 128, :], ldeps)
            lv = p.dma("pool", vb[:, dbl * 4160:(dbl + 1) * 4160]
                       .rearrange("p (c w) -> p c w", w=130),
                       vo[:, hp * 130:(hp + 1) * 130].rearrange("(c p) w -> p c w", p=128), ldeps)
            for qg in range(4):
                E = 8 * qg + 8
                post_queue = []
                for head, avp in (("A", avA), ("B", avB)):
                    hh = 2 * hp + (0 if head == "A" else 1)
                    base = 0 if head == "A" else 64
                    pmask = {}
                    pt_of = {}

                    def emit_av(kt, head=head, avp=avp, lv=lv, dbl=dbl,
                                hp=hp, qg=qg, E=E):
                        avdeps = [pmask[kt]]
                        if kt == 0:
                            avdeps.append(lv)
                            pp_ = prev_postA if head == "A" else prev_postB
                            if pp_:
                                avdeps.append(pp_["node"])
                        av = mm(p, avp[0:65, :],
                                vb[:, dbl * 4160 + kt * 130 + (0 if head == "A" else 65):
                                   dbl * 4160 + kt * 130 + (65 if head == "A" else 130)],
                                pt_of[kt], kt == 0, kt == E - 1, avdeps)
                        av_hist[(hp, qg, head, kt)] = av
                        return av

                    for kp in range(E // 2):
                        par = gp[0] % 2
                        ring = (pp0, pp1)[par]
                        kt0, kt1 = 2 * kp, 2 * kp + 1
                        sdeps = [lk, n_q]
                        if par in last_exp_on:
                            sdeps.append(last_exp_on[par])
                        smm0 = mm(p, ring[:, 0:512],
                                  kb[base:base + 64, dbl * S + kt0 * 128: dbl * S + (kt0 + 1) * 128],
                                  qb[base:base + 64, hp * TOK + qg * 512: hp * TOK + (qg + 1) * 512],
                                  True, True, sdeps)
                        smm1 = mm(p, ring[:, 512:1024],
                                  kb[base:base + 64, dbl * S + kt1 * 128: dbl * S + (kt1 + 1) * 128],
                                  qb[base:base + 64, hp * TOK + qg * 512: hp * TOK + (qg + 1) * 512],
                                  True, True, sdeps)
                        edeps = [smm0, smm1, n_bias]
                        if par in last_av_on:
                            edeps.append(last_av_on[par])
                        ex = act(p, pt[:, par * 1024:(par + 1) * 1024],
                                 ring[:, 0:1024], AF.Exp, edeps,
                                 bias=negcb[:, hh:hh + 1], scale=0.125)
                        exp_hist[(hp, qg, head)] = ex
                        last_exp_on[par] = ex
                        for kt in (kt0, kt1):
                            half = pt[:, par * 1024 + (kt & 1) * 512: par * 1024 + (kt & 1) * 512 + 512]
                            pt_of[kt] = half
                            lastn = ex
                            u = kt - (E - 8)
                            if u >= 0:
                                lastn = tt(p, half, half,
                                           msks[:, u * 512:(u + 1) * 512], ALU.mult, [ex, n_m])
                            pmask[kt] = lastn
                        if kp >= 1:
                            a0 = emit_av(kt0 - 2)
                            a1 = emit_av(kt1 - 2)
                            last_av_on[(gp[0] - 1) % 2] = a1
                        gp[0] += 1
                    a0 = emit_av(E - 2)
                    a1 = emit_av(E - 1)
                    last_av_on[(gp[0] - 1) % 2] = a1
                    av_hist[(hp, qg, head)] = a1
                    post_queue.append((head, avp))
                for head, avp in post_queue:
                    hh = 2 * hp + (0 if head == "A" else 1)
                    base = 0 if head == "A" else 64
                    fin = av_hist[(hp, qg, head)]
                    half = 0 if head == "A" else 1
                    rc = vrecip(p, rcp[0:1, half * 512:(half + 1) * 512],
                                avp[64:65, :], [fin])
                    bcdep = [rc]
                    if prev_postA:
                        bcdep.append(prev_postA["node"])
                    if prev_postB:
                        bcdep.append(prev_postB["node"])
                    bcm = mm(p, pmx[0 if head == "A" else 64:64 if head == "A" else 128, :],
                             onr_s[0:1, 0:64],
                             rcp[0:1, half * 512:(half + 1) * 512], True, True, bcdep)
                    ev = acopy(p, tav[0:64, half * 512:(half + 1) * 512], avp[0:64, :], [fin, rc])
                    ntt = tt(p, att[base:base + 64, hp * TOK + qg * 512: hp * TOK + (qg + 1) * 512],
                             tav[0:64, half * 512:(half + 1) * 512],
                             pmx[0 if head == "A" else 64:64 if head == "A" else 128, :],
                             ALU.mult, [bcm, ev])
                    if head == "A":
                        prev_postA = {"node": ntt}
                    else:
                        prev_postB = {"node": ntt}
                    attn_done.append(ntt)

        wo_casts = []
        for c in range(8):
            buf = c % 2
            deps = [wo_casts[c - 2]] if c >= 2 else []
            ld = p.dma("pool", wos[:, buf * D:(buf + 1) * D], wo[c * 128:(c + 1) * 128, :], deps)
            wo_casts.append(vcopy(p, wob[:, c * D:(c + 1) * D], wos[:, buf * D:(buf + 1) * D], [ld]))

        sc = {"ones_col": onc_s[:, 0:1], "ones_row": onr_s[0:1, :], "sq": sq,
              "stat": avA, "bc1": pp0[:, 512:1024], "bc2": pp1[:, 512:1024],
              "rmean": lnA[0:1, :], "rvar": lnA[32:33, :], "rtmp": lnA[64:65, :],
              "rrstd": rrstd[0:1, :], "rmr": rmr[0:1, :]}
        n_init = p.add("dve", lambda e: e.memset(mcol[:, 0:1], 0.0), [])
        stores = []
        prev_red = n_init
        ln_last = attn_done[-4:]
        for tc in range(4):
            res_nodes = []
            for oc in range(8):
                xsl = xrs[:, (oc % 2) * 512:(oc % 2 + 1) * 512]
                xwar = res_nodes[-2:-1] if len(res_nodes) >= 2 else (
                    ln_last if tc == 0 else [ln_last[-1]])
                xl = p.dma("pool", xsl,
                           xT[oc * 128:(oc + 1) * 128, tc * 512:(tc + 1) * 512], xwar)
                g = mm_group(p, pp0[:, 0:512] if oc % 2 == 0 else pp1[:, 0:512],
                             [(wob[:, kc * D + oc * 128: kc * D + (oc + 1) * 128],
                               att[:, kc * TOK + tc * 512: kc * TOK + (tc + 1) * 512])
                              for kc in range(8)],
                             attn_done + wo_casts + (res_nodes[-2:-1] if len(res_nodes) >= 2 else []))
                r = stt(p, rsum[:, oc * 512:(oc + 1) * 512], xsl, 2.0,
                        pp0[:, 0:512] if oc % 2 == 0 else pp1[:, 0:512],
                        ALU.mult, ALU.add, [g, xl])
                res_nodes.append(r)
            outs = emit_ln(p, lambda c: rsum[:, c * 512:(c + 1) * 512],
                           lambda c: x1s[:, c * 512:(c + 1) * 512], 512, sc, res_nodes)
            ln_last = outs
            for oc in range(8):
                stores.append(p.dma("sp", y_x1[oc * 128:(oc + 1) * 128, tc * 512:(tc + 1) * 512],
                                    x1s[:, oc * 512:(oc + 1) * 512], [outs[oc]]))
                rd = vred(p, mcol[:, 1:2], x1s[:, oc * 512:(oc + 1) * 512], ALU.max,
                          [outs[oc], prev_red], absval=True)
                prev_red = tt(p, mcol[:, 0:1], mcol[:, 0:1], mcol[:, 1:2], ALU.max, [rd])
        n_tp = p.add("pe", lambda e: e.transpose(pmx[0:1, 0:128], mcol[:, 0:1], idn_s[:, :]),
                     [prev_red, n_idn])
        n_ev = acopy(p, mrow[0:1, :], pmx[0:1, 0:128], [n_tp])
        n_fm = vred(p, mcol[0:1, 0:1], mrow[0:1, :], ALU.max, [n_ev])
        stores.append(p.dma("sp", y_mx, mcol[0:1, 0:1], [n_fm]))
        sink(p, stores)
    return p.build()



def tiles_of(h):
    return [8 * j + h + 2 * m for j in range(4) for m in range(4)]


def make_sel_qkv():
    sel = np.zeros((128, 128), np.float32)
    for oc in range(8):
        for r in range(128):
            head = (oc * 128 + r) // 64
            sel[r, oc * 16 + head] = 1.0
    return sel


def make_masks(h):
    m = np.zeros((128, 8, 4, 128), np.float32)
    tri = (np.arange(128)[:, None] <= np.arange(128)[None, :])
    for u in range(8):
        for sm in range(4):
            d = h + 2 * sm
            if u < d:
                m[:, u, sm, :] = 1.0
            elif u == d:
                m[:, u, sm, :] = tri
    return bf16c(m.reshape(128, 8 * 512))


def dq_pack(c, Wup0, Wg0, Wup1, Wg1, Wdn0, Wdn1):
    rows = []
    for W in (Wup0, Wg0, Wup1, Wg1):
        rows.append(W[c * 384:(c + 1) * 384, :])
    for W in (Wdn0, Wdn1):
        sl = W[c * 128:(c + 1) * 128, :]
        rows.append(sl.reshape(128, 3, 1024).transpose(1, 0, 2).reshape(384, 1024))
    return f32c(np.concatenate(rows, 0))


def dq_unpack(outs):
    ups0, gs0, ups1, gs1, dn0, dn1 = [], [], [], [], [], []
    for c in range(8):
        o = outs[c]
        ups0.append(o[0:384]); gs0.append(o[384:768])
        ups1.append(o[768:1152]); gs1.append(o[1152:1536])
        dn0.append(o[1536:1920].reshape(3, 128, 1024).transpose(1, 0, 2).reshape(128, 3072))
        dn1.append(o[1920:2304].reshape(3, 128, 1024).transpose(1, 0, 2).reshape(128, 3072))
    return (np.concatenate(ups0, 0), np.concatenate(gs0, 0),
            np.concatenate(ups1, 0), np.concatenate(gs1, 0),
            np.concatenate(dn0, 0), np.concatenate(dn1, 0))


ONES_COL = np.ones((128, 1), np.float32)
ONES_ROW = np.ones((1, 128), np.float32)
IDENT = np.eye(128, dtype=np.float32)


def run_qkv(x, Wq, Wk, Wv, dq_packs):
    nc = qkv_prog()
    in_maps = []
    sel = make_sel_qkv()
    for c in range(8):
        b, h = c // 2, c % 2
        in_maps.append({
            "xT": f32c(x[b, h * 2048:(h + 1) * 2048, :].T),
            "wq": f32c(Wq), "wk": f32c(Wk), "wv": f32c(Wv),
            "sel": sel, "dq_in": dq_packs[c],
        })
    return run_bass_kernel_spmd(nc, in_maps, core_ids=list(range(8))).results


def run_attn(x, qkv_res, Wo):
    nc = attn_prog()
    in_maps = []
    for c in range(8):
        b, h = c // 2, c % 2
        ca, cb = 2 * b, 2 * b + 1
        qf = np.concatenate([qkv_res[ca]["y_q"], qkv_res[cb]["y_q"]], axis=1)
        kf = np.concatenate([qkv_res[ca]["y_k"], qkv_res[cb]["y_k"]], axis=1)
        vf = np.concatenate([qkv_res[ca]["y_v"], qkv_res[cb]["y_v"]], axis=0)
        tl = tiles_of(h)
        cols = np.concatenate([np.arange(g * 128, (g + 1) * 128) for g in tl])
        vo = np.ones((S, NH * 65), bf16)
        for hh in range(NH):
            vo[:, hh * 65:hh * 65 + 64] = vf[:, hh * 64:(hh + 1) * 64]
        qn2f = np.concatenate([qkv_res[ca]["y_qn2"], qkv_res[cb]["y_qn2"]], axis=1)
        kn2f = np.concatenate([qkv_res[ca]["y_kn2"], qkv_res[cb]["y_kn2"]], axis=1)
        n2in = np.concatenate([qn2f[:, tl], kn2f], axis=1)
        in_maps.append({
            "qT": np.ascontiguousarray(qf[:, cols]),
            "kT": np.ascontiguousarray(kf),
            "vo": np.ascontiguousarray(vo),
            "xT": f32c(x[b].T[:, cols]),
            "wo": f32c(Wo),
            "n2in": f32c(n2in),
            "msk": make_masks(h),
            "onc": ONES_COL, "onr": ONES_ROW, "idn": IDENT,
        })
    return run_bass_kernel_spmd(nc, in_maps, core_ids=list(range(8))).results


def gather_interleaved(res, key):
    out = np.empty((B, S, D), np.float32)
    for c in range(8):
        b, h = c // 2, c % 2
        tl = tiles_of(h)
        cols = np.concatenate([np.arange(g * 128, (g + 1) * 128) for g in tl])
        out[b, cols, :] = res[c][key].T
    return out


_PROGS = {}


def qkv_prog():
    if "qkv" not in _PROGS:
        _PROGS["qkv"] = build_qkv()
    return _PROGS["qkv"]


def attn_prog():
    if "attn" not in _PROGS:
        _PROGS["attn"] = build_attn()
    return _PROGS["attn"]



def _emit_scales(p, nc, mx_ap, n_mx, srow, sxb, onr_ap, n_onr, pbc, specs):
    raise NotImplementedError


def build_ffn1():
    p = Prog("ffn1")
    nc = p.nc
    xT = nc.dram_tensor("xT", [D, TOK], F32, kind="ExternalInput").ap()
    mxs = nc.dram_tensor("mxs", [1, 8], F32, kind="ExternalInput").ap()
    wupT = nc.dram_tensor("wupT", [D, FI], BF16, kind="ExternalInput").ap()
    wgT = nc.dram_tensor("wgT", [D, FI], BF16, kind="ExternalInput").ap()
    onr = nc.dram_tensor("onr", [1, 128], F32, kind="ExternalInput").ap()
    idn = nc.dram_tensor("idn", [128, 128], F32, kind="ExternalInput").ap()
    y_up = nc.dram_tensor("y_up", [FI, TOK], F32, kind="ExternalOutput").ap()
    y_g = nc.dram_tensor("y_g", [FI, TOK], F32, kind="ExternalOutput").ap()
    y_m = nc.dram_tensor("y_m", [1, 2], F32, kind="ExternalOutput").ap()

    xs = p.sbuf("xs", [128, 2 * TOK], F32)
    xq = p.sbuf("xq", [128, 8 * TOK], BF16)
    tqs = p.sbuf("tqs", [128, TOK], F32)
    wtb = p.sbuf("wtb", [128, 2 * 1024], BF16)
    srow = p.sbuf("srow", [1, 2], F32)
    sxb = p.sbuf("sxb", [128, 2], F32)
    ring = p.sbuf("ring", [128, 4 * 512], F32)
    mc = p.sbuf("mc", [128, 4], F32)
    mrow = p.sbuf("mrow", [1, 128], F32)
    onr_s = p.sbuf("onr_s", [1, 128], F32)
    idn_s = p.sbuf("idn_s", [128, 128], F32)
    pq0 = p.psum("pq0", [128, 512])
    pq1 = p.psum("pq1", [128, 512])
    pbc = p.psum("pbc", [128, 2])
    pmx = p.psum("pmx", [1, 128])

    n_onr = p.dma("sp", onr_s[:], onr)
    n_idn = p.dma("sp", idn_s[:], idn)
    n_mx = p.dma("sp", mrow[0:1, 0:8], mxs)
    r_m = vred(p, srow[0:1, 0:1], mrow[0:1, 0:8], ALU.max, [n_mx])
    r_s = ts(p, srow[0:1, 0:1], srow[0:1, 0:1], 1.0 / 32767.0, 1e-12, ALU.mult, ALU.max, [r_m])
    r_r = vrecip(p, srow[0:1, 1:2], srow[0:1, 0:1], [r_s])
    n_bc = mm(p, pbc[:, 0:2], onr_s[0:1, :], srow[0:1, 0:2], True, True, [r_r, n_onr])
    n_sx = acopy(p, sxb[:], pbc[:, 0:2], [n_bc])

    xq_nodes = []
    for c in range(8):
        buf = c % 2
        deps = [xq_nodes[c - 2]] if c >= 2 else []
        ld = p.dma("sp", xs[:, buf * TOK:(buf + 1) * TOK], xT[c * 128:(c + 1) * 128, :], deps)
        t1 = p.add("dve", (lambda e, a=xs[:, buf * TOK:(buf + 1) * TOK], o=tqs[:]:
                           e.tensor_scalar(o, a, sxb[:, 1:2], MAGIC, ALU.mult, ALU.add)),
                   [ld, n_sx] + deps)
        t2 = p.add("dve", (lambda e, o=xq[:, c * TOK:(c + 1) * TOK], a=tqs[:]:
                           e.tensor_scalar(o, a, MAGIC, sxb[:, 0:1], ALU.subtract, ALU.mult)),
                   [t1])
        xq_nodes.append(t2)

    stores = []
    use = 0
    prev_ev = {}
    prev_rd = {}
    prev_red = None
    mzero = p.add("dve", lambda e: e.memset(mc[:, 0:2], 0.0), [])
    wcons = {}
    for mi, (wsrc, ydst) in enumerate(((wupT, y_up), (wgT, y_g))):
        for ic in range(24):
            wbuf = (mi * 24 + ic) % 2
            wdeps = []
            if (mi * 24 + ic) >= 2:
                wdeps.append(wcons[mi * 24 + ic - 2])
            wl = p.dma("pool", wtb[:, wbuf * 1024:(wbuf + 1) * 1024]
                       .rearrange("p (c w) -> p c w", w=128),
                       wsrc[:, ic * 128:(ic + 1) * 128].rearrange("(c p) w -> p c w", p=128),
                       wdeps)
            for tc in range(4):
                pt = (pq0, pq1)[use % 2]
                deps = [wl, xq_nodes[-1]]
                if use >= 2:
                    deps.append(prev_ev[use - 2])
                    deps.append(prev_rd[use - 2])
                g = mm_group(p, pt[:, :],
                             [(wtb[:, wbuf * 1024 + kc * 128:wbuf * 1024 + (kc + 1) * 128],
                               xq[:, kc * TOK + tc * 512:kc * TOK + (tc + 1) * 512])
                              for kc in range(8)], deps)
                rdeps = [g]
                if prev_red is not None:
                    rdeps.append(prev_red)
                rd = vred(p, mc[:, 2 + mi:3 + mi], pt[:, :], ALU.max, rdeps, absval=True)
                prev_rd[use] = rd
                mx = tt(p, mc[:, mi:mi + 1], mc[:, mi:mi + 1], mc[:, 2 + mi:3 + mi],
                        ALU.max, [rd, mzero])
                prev_red = mx
                rs = use % 4
                edeps = [g, rd]
                if use >= 4:
                    edeps.append(stores[use - 4])
                ev = acopy(p, ring[:, rs * 512:(rs + 1) * 512], pt[:, :], edeps)
                st = p.dma("sp", ydst[ic * 128:(ic + 1) * 128, tc * 512:(tc + 1) * 512],
                           ring[:, rs * 512:(rs + 1) * 512], [ev])
                prev_ev[use] = ev
                stores.append(st)
                use += 1
            wcons[mi * 24 + ic] = g
    n_tp = p.add("pe", lambda e: e.transpose(pmx[0:1, :], mc[:, 0:1], idn_s[:, :]),
                 [prev_red, n_idn])
    n_e1 = acopy(p, mrow[0:1, :], pmx[0:1, :], [n_tp])
    f1 = vred(p, mc[0:1, 2:3], mrow[0:1, :], ALU.max, [n_e1, prev_red])
    n_tp2 = p.add("pe", lambda e: e.transpose(pmx[0:1, :], mc[:, 1:2], idn_s[:, :]),
                  [f1, n_tp])
    n_e2 = acopy(p, mrow[0:1, :], pmx[0:1, :], [n_tp2, f1])
    f2 = vred(p, mc[0:1, 3:4], mrow[0:1, :], ALU.max, [n_e2, f1])
    st_m = p.dma("sp", y_m, mc[0:1, 2:4], [f2])
    sink(p, stores + [st_m])
    return p.build()


def build_ffn2():
    p = Prog("ffn2")
    nc = p.nc
    gT = nc.dram_tensor("gT", [FI, TOK], F32, kind="ExternalInput").ap()
    m16 = nc.dram_tensor("m16", [1, 16], F32, kind="ExternalInput").ap()
    onr = nc.dram_tensor("onr", [1, 128], F32, kind="ExternalInput").ap()
    idn = nc.dram_tensor("idn", [128, 128], F32, kind="ExternalInput").ap()
    y_gi = nc.dram_tensor("y_gi", [FI, TOK], I16, kind="ExternalOutput").ap()
    y_sr = nc.dram_tensor("y_sr", [FI, TOK], U16, kind="ExternalOutput").ap()
    y_m = nc.dram_tensor("y_m", [1, 1], F32, kind="ExternalOutput").ap()

    mr = p.sbuf("mr", [1, 16], F32)
    srow = p.sbuf("srow", [1, 4], F32)
    sgb = p.sbuf("sgb", [128, 4], F32)
    gld = p.sbuf("gld", [128, 4 * 512], F32)
    tf = p.sbuf("tf", [128, 2 * 512], F32)
    tf2 = p.sbuf("tf2", [128, 512], F32)
    tf3 = p.sbuf("tf3", [128, 512], F32)
    sg = p.sbuf("sg", [128, 2 * 512], F32)
    gi = p.sbuf("gi", [128, 4 * 512], I16)
    sr = p.sbuf("sr", [128, 4 * 512], U16)
    mc = p.sbuf("mc", [128, 2], F32)
    mrow = p.sbuf("mrow", [1, 128], F32)
    onr_s = p.sbuf("onr_s", [1, 128], F32)
    idn_s = p.sbuf("idn_s", [128, 128], F32)
    pbc = p.psum("pbc", [128, 4])
    pmx = p.psum("pmx", [1, 128])

    n_onr = p.dma("sp", onr_s[:], onr)
    n_idn = p.dma("sp", idn_s[:], idn)
    n_mx = p.dma("sp", mr[:], m16)
    r_m = vred(p, srow[0:1, 0:2], mr[0:1, :].rearrange("a (b two) -> a two b", two=2),
               ALU.max, [n_mx])
    r_s = ts(p, srow[0:1, 0:1], srow[0:1, 1:2], 1.0 / 32767.0, 1e-12, ALU.mult, ALU.max, [r_m])
    r_r = vrecip(p, srow[0:1, 1:2], srow[0:1, 0:1], [r_s])
    r_2 = ts(p, srow[0:1, 2:3], srow[0:1, 0:1], 1.0 / 65536.0, None, ALU.mult, ALU.bypass, [r_r])
    n_bc = mm(p, pbc[:, 0:4], onr_s[0:1, :], srow[0:1, 0:4], True, True, [r_2, n_onr])
    n_sx = acopy(p, sgb[:], pbc[:, 0:4], [n_bc])

    stores = []
    prev = {}
    prev_red = None
    mzero = p.add("dve", lambda e: e.memset(mc[:, 0:1], 0.0), [])
    st1s, st2s = {}, {}
    stage1 = {}

    def emit_s1(i):
        ic, tc = i // 4, i % 4
        bq = i % 4
        pb = i % 2
        ldeps = [prev[i - 4]] if i >= 4 else []
        ld = p.dma("sp", gld[:, bq * 512:(bq + 1) * 512],
                   gT[ic * 128:(ic + 1) * 128, tc * 512:(tc + 1) * 512], ldeps)
        t1deps = [ld, n_sx]
        if i >= 2:
            t1deps.append(stage1[i - 2]["t2"])
        t1 = p.add("dve", (lambda e, a=gld[:, bq * 512:(bq + 1) * 512], pb=pb:
                           e.tensor_scalar(tf[:, pb * 512:(pb + 1) * 512], a,
                                           sgb[:, 1:2], MAGIC, ALU.mult, ALU.add)),
                   t1deps)
        t2 = p.add("dve", (lambda e, o=gi[:, bq * 512:(bq + 1) * 512], pb=pb:
                           e.tensor_scalar(o, tf[:, pb * 512:(pb + 1) * 512], MAGIC, None,
                                           ALU.subtract, ALU.bypass)),
                   [t1] + ([st1s[i - 4]] if i >= 4 else []))
        st1 = p.dma("pool", y_gi[ic * 128:(ic + 1) * 128, tc * 512:(tc + 1) * 512],
                    gi[:, bq * 512:(bq + 1) * 512], [t2])
        sdep = [t2]
        if i >= 2:
            sdep.append(stage1[i - 2]["t3"])
        a_s = act(p, sg[:, pb * 512:(pb + 1) * 512], gi[:, bq * 512:(bq + 1) * 512],
                  AF.Sigmoid, sdep, scale=sgb[:, 0:1])
        st1s[i] = st1
        stage1[i] = {"t2": t2, "a_s": a_s, "st1": st1, "t3": None}

    def emit_s2(i, prev_red):
        ic, tc = i // 4, i % 4
        bq = i % 4
        pb = i % 2
        a_s = stage1[i]["a_s"]
        t3 = ts(p, tf2[:], sg[:, pb * 512:(pb + 1) * 512], 65536.0, MAGIC,
                ALU.mult, ALU.add, [a_s])
        stage1[i]["t3"] = t3
        t4 = p.add("dve", (lambda e, o=sr[:, bq * 512:(bq + 1) * 512]:
                           e.tensor_scalar(o, tf2[:], MAGIC, 65535.0, ALU.subtract, ALU.min)),
                   [t3] + ([st2s[i - 4]] if i >= 4 else []))
        st2 = p.dma("pool", y_sr[ic * 128:(ic + 1) * 128, tc * 512:(tc + 1) * 512],
                    sr[:, bq * 512:(bq + 1) * 512], [t4])
        g2 = p.add("dve", (lambda e, a=sr[:, bq * 512:(bq + 1) * 512],
                           b=gi[:, bq * 512:(bq + 1) * 512]:
                           e.scalar_tensor_tensor(tf3[:], a, sgb[:, 2:3], b, ALU.mult, ALU.mult)),
                   [t4])
        rdeps = [g2]
        if prev_red is not None:
            rdeps.append(prev_red)
        rd = vred(p, mc[:, 1:2], tf3[:], ALU.max, rdeps, absval=True)
        nred = tt(p, mc[:, 0:1], mc[:, 0:1], mc[:, 1:2], ALU.max, [rd, mzero])
        prev[i] = g2
        st2s[i] = st2
        stores.extend([stage1[i]["st1"], st2])
        return nred

    emit_s1(0)
    for i in range(96):
        if i + 1 < 96:
            emit_s1(i + 1)
        prev_red = emit_s2(i, prev_red)
    n_tp = p.add("pe", lambda e: e.transpose(pmx[0:1, :], mc[:, 0:1], idn_s[:, :]),
                 [prev_red, n_idn])
    n_e1 = acopy(p, mrow[0:1, :], pmx[0:1, :], [n_tp])
    f1 = vred(p, mc[0:1, 1:2], mrow[0:1, :], ALU.max, [n_e1, prev_red])
    st_m = p.dma("sp", y_m, mc[0:1, 1:2], [f1])
    sink(p, stores + [st_m])
    return p.build()


def build_ffn3():
    p = Prog("ffn3")
    nc = p.nc
    uT = nc.dram_tensor("uT", [FI, TOK], F32, kind="ExternalInput").ap()
    giT = nc.dram_tensor("giT", [FI, TOK], I16, kind="ExternalInput").ap()
    srT = nc.dram_tensor("srT", [FI, TOK], U16, kind="ExternalInput").ap()
    m24 = nc.dram_tensor("m24", [1, 24], F32, kind="ExternalInput").ap()
    onr = nc.dram_tensor("onr", [1, 128], F32, kind="ExternalInput").ap()
    idn = nc.dram_tensor("idn", [128, 128], F32, kind="ExternalInput").ap()
    y_ui = nc.dram_tensor("y_ui", [FI, TOK], I16, kind="ExternalOutput").ap()
    y_g2i = nc.dram_tensor("y_g2i", [FI, TOK], I16, kind="ExternalOutput").ap()
    y_m = nc.dram_tensor("y_m", [1, 1], F32, kind="ExternalOutput").ap()

    mr = p.sbuf("mr", [1, 24], F32)
    srow = p.sbuf("srow", [1, 8], F32)
    sgb = p.sbuf("sgb", [128, 8], F32)
    uld = p.sbuf("uld", [128, 4 * 512], F32)
    gild = p.sbuf("gild", [128, 4 * 512], I16)
    srld = p.sbuf("srld", [128, 4 * 512], U16)
    tf = p.sbuf("tf", [128, 512], F32)
    tf2 = p.sbuf("tf2", [128, 512], F32)
    tf3 = p.sbuf("tf3", [128, 512], F32)
    ui = p.sbuf("ui", [128, 4 * 512], I16)
    g2i = p.sbuf("g2i", [128, 4 * 512], I16)
    mc = p.sbuf("mc", [128, 2], F32)
    mrow = p.sbuf("mrow", [1, 128], F32)
    onr_s = p.sbuf("onr_s", [1, 128], F32)
    idn_s = p.sbuf("idn_s", [128, 128], F32)
    pbc = p.psum("pbc", [128, 8])
    pmx = p.psum("pmx", [1, 128])

    n_onr = p.dma("sp", onr_s[:], onr)
    n_idn = p.dma("sp", idn_s[:], idn)
    n_mx = p.dma("sp", mr[:], m24)
    r_m = vred(p, srow[0:1, 0:3], mr[0:1, :].rearrange("a (b three) -> a three b", three=3),
               ALU.max, [n_mx])
    r1 = ts(p, srow[0:1, 3:4], srow[0:1, 0:1], 1.0 / 32767.0, 1e-12, ALU.mult, ALU.max, [r_m])
    r2 = ts(p, srow[0:1, 5:6], srow[0:1, 1:2], 1.0 / 32767.0, 1e-12, ALU.mult, ALU.max, [r1])
    r3 = ts(p, srow[0:1, 6:7], srow[0:1, 2:3], 1.0 / 32767.0, 1e-12, ALU.mult, ALU.max, [r2])
    r4 = vrecip(p, srow[0:1, 4:5], srow[0:1, 3:4], [r3])
    r5 = vrecip(p, srow[0:1, 7:8], srow[0:1, 6:7], [r4])
    r6 = ts(p, srow[0:1, 0:1], srow[0:1, 5:6], 1.0 / 65536.0, None, ALU.mult, ALU.bypass, [r5])
    r7 = tt(p, srow[0:1, 1:2], srow[0:1, 6:7], srow[0:1, 3:4], ALU.mult, [r6])
    n_bc = mm(p, pbc[:, 0:8], onr_s[0:1, :], srow[0:1, 0:8], True, True, [r7, n_onr])
    n_sx = acopy(p, sgb[:], pbc[:, 0:8], [n_bc])

    stores = []
    prev = {}
    prev_red = None
    mzero = p.add("dve", lambda e: e.memset(mc[:, 0:1], 0.0), [])
    s1s, s2s = {}, {}
    for i in range(96):
        ic, tc = i // 4, i % 4
        bq = i % 4
        sl = (bq * 512, (bq + 1) * 512)
        ldeps = [prev[i - 4]] if i >= 4 else []
        l1 = p.dma("pool", uld[:, sl[0]:sl[1]],
                   uT[ic * 128:(ic + 1) * 128, tc * 512:(tc + 1) * 512], ldeps)
        l2 = p.dma("pool", gild[:, sl[0]:sl[1]],
                   giT[ic * 128:(ic + 1) * 128, tc * 512:(tc + 1) * 512], ldeps)
        l3 = p.dma("pool", srld[:, sl[0]:sl[1]],
                   srT[ic * 128:(ic + 1) * 128, tc * 512:(tc + 1) * 512], ldeps)
        t1 = p.add("dve", (lambda e, a=uld[:, sl[0]:sl[1]]:
                           e.tensor_scalar(tf[:], a, sgb[:, 4:5], MAGIC, ALU.mult, ALU.add)),
                   [l1, n_sx])
        t2 = p.add("dve", (lambda e, o=ui[:, sl[0]:sl[1]]:
                           e.tensor_scalar(o, tf[:], MAGIC, None, ALU.subtract, ALU.bypass)),
                   [t1] + ([s1s[i - 4]] if i >= 4 else []))
        s1 = p.dma("sp", y_ui[ic * 128:(ic + 1) * 128, tc * 512:(tc + 1) * 512],
                   ui[:, sl[0]:sl[1]], [t2])
        g2 = p.add("dve", (lambda e, a=srld[:, sl[0]:sl[1]], b=gild[:, sl[0]:sl[1]]:
                           e.scalar_tensor_tensor(tf2[:], a, sgb[:, 0:1], b, ALU.mult, ALU.mult)),
                   [l2, l3, n_sx])
        t3 = p.add("dve", (lambda e:
                           e.tensor_scalar(tf[:], tf2[:], sgb[:, 7:8], MAGIC, ALU.mult, ALU.add)),
                   [g2, t2])
        t4 = p.add("dve", (lambda e, o=g2i[:, sl[0]:sl[1]]:
                           e.tensor_scalar(o, tf[:], MAGIC, None, ALU.subtract, ALU.bypass)),
                   [t3] + ([s2s[i - 4]] if i >= 4 else []))
        s2 = p.dma("sp", y_g2i[ic * 128:(ic + 1) * 128, tc * 512:(tc + 1) * 512],
                   g2i[:, sl[0]:sl[1]], [t4])
        oo = p.add("dve", (lambda e, a=g2i[:, sl[0]:sl[1]], b=ui[:, sl[0]:sl[1]]:
                           e.scalar_tensor_tensor(tf3[:], a, sgb[:, 1:2], b, ALU.mult, ALU.mult)),
                   [t4])
        rdeps = [oo]
        if prev_red is not None:
            rdeps.append(prev_red)
        rd = vred(p, mc[:, 1:2], tf3[:], ALU.max, rdeps, absval=True)
        prev_red = tt(p, mc[:, 0:1], mc[:, 0:1], mc[:, 1:2], ALU.max, [rd, mzero])
        prev[i] = oo
        s1s[i], s2s[i] = s1, s2
        stores += [s1, s2]
    n_tp = p.add("pe", lambda e: e.transpose(pmx[0:1, :], mc[:, 0:1], idn_s[:, :]),
                 [prev_red, n_idn])
    n_e1 = acopy(p, mrow[0:1, :], pmx[0:1, :], [n_tp])
    f1 = vred(p, mc[0:1, 1:2], mrow[0:1, :], ALU.max, [n_e1, prev_red])
    st_m = p.dma("sp", y_m, mc[0:1, 1:2], [f1])
    sink(p, stores + [st_m])
    return p.build()


def build_ffn4():
    p = Prog("ffn4")
    nc = p.nc
    giT = nc.dram_tensor("giT", [FI, TOK], I16, kind="ExternalInput").ap()
    uiT = nc.dram_tensor("uiT", [FI, TOK], I16, kind="ExternalInput").ap()
    m32 = nc.dram_tensor("m32", [1, 32], F32, kind="ExternalInput").ap()
    wdT = nc.dram_tensor("wdT", [FI, D], BF16, kind="ExternalInput").ap()
    xT = nc.dram_tensor("xT", [D, TOK], F32, kind="ExternalInput").ap()
    onc = nc.dram_tensor("onc", [128, 1], F32, kind="ExternalInput").ap()
    onr = nc.dram_tensor("onr", [1, 128], F32, kind="ExternalInput").ap()
    idn = nc.dram_tensor("idn", [128, 128], F32, kind="ExternalInput").ap()
    y_x2 = nc.dram_tensor("y_x2", [D, TOK], F32, kind="ExternalOutput").ap()
    y_mx = nc.dram_tensor("y_mx", [1, 1], F32, kind="ExternalOutput").ap()

    mr = p.sbuf("mr", [1, 32], F32)
    srow = p.sbuf("srow", [1, 8], F32)
    sgb = p.sbuf("sgb", [128, 8], F32)
    gild = p.sbuf("gild", [128, 4 * 512], I16)
    uild = p.sbuf("uild", [128, 4 * 512], I16)
    tf = p.sbuf("tf", [128, 512], F32)
    tf2 = p.sbuf("tf2", [128, 512], F32)
    obf = p.sbuf("obf", [128, 24 * 512], BF16)
    wdb = p.sbuf("wdb", [128, 2 * 24 * 128], BF16)
    xrs = p.sbuf("xrs", [128, 2 * 512], F32)
    rsum = p.sbuf("rsum", [128, 8 * 512], F32)
    x1s = p.sbuf("x1s", [128, 8 * 512], F32)
    sq = p.sbuf("sqln", [128, 8 * 512], F32)
    lnA = p.sbuf("lnA", [65, 512], F32)
    rrstd = p.sbuf("rrstd", [1, 512], F32)
    rmr = p.sbuf("rmr", [1, 512], F32)
    mc = p.sbuf("mc", [128, 2], F32)
    mrow = p.sbuf("mrow", [1, 128], F32)
    onc_s = p.sbuf("onc_s", [128, 1], F32)
    onr_s = p.sbuf("onr_s", [1, 128], F32)
    idn_s = p.sbuf("idn_s", [128, 128], F32)
    pq0 = p.psum("pq0", [128, 512])
    pq1 = p.psum("pq1", [128, 512])
    pst = p.psum("pst", [64, 512])
    pb1 = p.psum("pb1", [128, 512])
    pb2 = p.psum("pb2", [128, 512])
    pbc = p.psum("pbc", [128, 8])
    pmx = p.psum("pmx", [1, 128])

    n_onc = p.dma("sp", onc_s[:], onc)
    n_onr = p.dma("sp", onr_s[:], onr)
    n_idn = p.dma("sp", idn_s[:], idn)
    n_mx = p.dma("sp", mr[:], m32)
    r_m = vred(p, srow[0:1, 0:4], mr[0:1, :].rearrange("a (b four) -> a four b", four=4),
               ALU.max, [n_mx])
    r1 = ts(p, srow[0:1, 4:5], srow[0:1, 0:1], 1.0 / 32767.0, 1e-12, ALU.mult, ALU.max, [r_m])
    r2 = ts(p, srow[0:1, 5:6], srow[0:1, 2:3], 1.0 / 32767.0, 1e-12, ALU.mult, ALU.max, [r1])
    r3 = ts(p, srow[0:1, 6:7], srow[0:1, 3:4], 1.0 / 32767.0, 1e-12, ALU.mult, ALU.max, [r2])
    r4 = tt(p, srow[0:1, 0:1], srow[0:1, 5:6], srow[0:1, 4:5], ALU.mult, [r3])
    r5 = vrecip(p, srow[0:1, 1:2], srow[0:1, 6:7], [r4])
    n_bc = mm(p, pbc[:, 0:8], onr_s[0:1, :], srow[0:1, 0:8], True, True, [r5, n_onr])
    n_sx = acopy(p, sgb[:], pbc[:, 0:8], [n_bc])

    sc = {"ones_col": onc_s[:, 0:1], "ones_row": onr_s[0:1, :], "sq": sq,
          "stat": pst, "bc1": pb1, "bc2": pb2,
          "rmean": lnA[0:1, :], "rvar": lnA[32:33, :], "rtmp": lnA[64:65, :],
          "rrstd": rrstd[0:1, :], "rmr": rmr[0:1, :]}
    stores = []
    prev_o = {}
    prev_red = None
    mzero = p.add("dve", lambda e: e.memset(mc[:, 0:1], 0.0), [])
    ln_prev = None
    wl_prev = {}
    res_prev = {}
    for tc in range(4):
        o_nodes = []
        for ic in range(24):
            i = tc * 24 + ic
            bq = i % 4
            sl = (bq * 512, (bq + 1) * 512)
            ldeps = [prev_o[i - 4]] if i >= 4 else []
            l1 = p.dma("pool", gild[:, sl[0]:sl[1]],
                       giT[ic * 128:(ic + 1) * 128, tc * 512:(tc + 1) * 512], ldeps)
            l2 = p.dma("pool", uild[:, sl[0]:sl[1]],
                       uiT[ic * 128:(ic + 1) * 128, tc * 512:(tc + 1) * 512], ldeps)
            oo = p.add("dve", (lambda e, a=gild[:, sl[0]:sl[1]], b=uild[:, sl[0]:sl[1]]:
                               e.scalar_tensor_tensor(tf[:], a, sgb[:, 0:1], b, ALU.mult, ALU.mult)),
                       [l1, l2, n_sx])
            t1 = p.add("dve", (lambda e:
                               e.tensor_scalar(tf2[:], tf[:], sgb[:, 1:2], MAGIC, ALU.mult, ALU.add)),
                       [oo])
            odeps = [t1]
            if tc > 0:
                odeps.append(res_prev[(tc - 1, 7)])
            t2 = p.add("dve", (lambda e, o=obf[:, ic * 512:(ic + 1) * 512]:
                               e.tensor_scalar(o, tf2[:], MAGIC, sgb[:, 6:7], ALU.subtract, ALU.mult)),
                       odeps)
            prev_o[i] = t2
            o_nodes.append(t2)
        res_nodes = []
        for oc in range(8):
            j = tc * 8 + oc
            wbuf = j % 2
            wdeps = [wl_prev[j - 2]] if j >= 2 else []
            wl = p.dma("pool", wdb[:, wbuf * 3072:(wbuf + 1) * 3072]
                       .rearrange("p (c w) -> p c w", w=128),
                       wdT[:, oc * 128:(oc + 1) * 128].rearrange("(c p) w -> p c w", p=128),
                       wdeps)
            pt = (pq0, pq1)[oc % 2]
            gdeps = o_nodes + [wl]
            if len(res_nodes) >= 2:
                gdeps.append(res_nodes[-2])
            elif tc > 0:
                gdeps.append(res_prev[(tc - 1, 6 + oc)] if (6 + oc) <= 7 else None)
            g = mm_group(p, pt[:, :],
                         [(wdb[:, wbuf * 3072 + kc * 128:wbuf * 3072 + (kc + 1) * 128],
                           obf[:, kc * 512:(kc + 1) * 512])
                          for kc in range(24)], [dd for dd in gdeps if dd])
            xsl = xrs[:, (oc % 2) * 512:(oc % 2 + 1) * 512]
            xwar = res_nodes[-2:-1] if len(res_nodes) >= 2 else (
                [ln_prev] if ln_prev is not None else [])
            xl = p.dma("pool", xsl, xT[oc * 128:(oc + 1) * 128, tc * 512:(tc + 1) * 512], xwar)
            r = tt(p, rsum[:, oc * 512:(oc + 1) * 512], pt[:, :], xsl, ALU.add, [g, xl])
            res_nodes.append(r)
            res_prev[(tc, oc)] = r
            wl_prev[j] = g
        outs = emit_ln(p, lambda c: rsum[:, c * 512:(c + 1) * 512],
                       lambda c: x1s[:, c * 512:(c + 1) * 512], 512, sc, res_nodes)
        ln_prev = outs[-1]
        for oc in range(8):
            stores.append(p.dma("sp", y_x2[oc * 128:(oc + 1) * 128, tc * 512:(tc + 1) * 512],
                                x1s[:, oc * 512:(oc + 1) * 512], [outs[oc]]))
            rdeps = [outs[oc], mzero]
            if prev_red is not None:
                rdeps.append(prev_red)
            rd = vred(p, mc[:, 1:2], x1s[:, oc * 512:(oc + 1) * 512], ALU.max, rdeps, absval=True)
            prev_red = tt(p, mc[:, 0:1], mc[:, 0:1], mc[:, 1:2], ALU.max, [rd])
    n_tp = p.add("pe", lambda e: e.transpose(pmx[0:1, :], mc[:, 0:1], idn_s[:, :]),
                 [prev_red, n_idn])
    n_e1 = acopy(p, mrow[0:1, :], pmx[0:1, :], [n_tp])
    f1 = vred(p, mc[0:1, 1:2], mrow[0:1, :], ALU.max, [n_e1, prev_red])
    st_m = p.dma("sp", y_mx, mc[0:1, 1:2], [f1])
    sink(p, stores + [st_m])
    return p.build()


def run_ffn(x1, mx_list, wupT_bf, wgT_bf, wdnT_bf):
    def xmaps():
        ms = []
        for c in range(8):
            b, h = c // 2, c % 2
            ms.append(f32c(x1[b, h * 2048:(h + 1) * 2048, :].T))
        return ms

    xTs = xmaps()
    mxs = np.concatenate([np.asarray(m).reshape(1, 1) for m in mx_list], axis=1)
    p1 = run_bass_kernel_spmd(ffn1_prog(), [
        {"xT": xTs[c], "mxs": f32c(mxs), "wupT": wupT_bf, "wgT": wgT_bf,
         "onr": ONES_ROW, "idn": IDENT} for c in range(8)],
        core_ids=list(range(8))).results
    m16 = np.concatenate([p1[c]["y_m"] for c in range(8)], axis=1)
    p2 = run_bass_kernel_spmd(ffn2_prog(), [
        {"gT": p1[c]["y_g"], "m16": f32c(m16), "onr": ONES_ROW, "idn": IDENT}
        for c in range(8)], core_ids=list(range(8))).results
    m24 = np.concatenate([np.concatenate([p1[c]["y_m"], p2[c]["y_m"]], axis=1)
                          for c in range(8)], axis=1)
    p3 = run_bass_kernel_spmd(ffn3_prog(), [
        {"uT": p1[c]["y_up"], "giT": p2[c]["y_gi"], "srT": p2[c]["y_sr"],
         "m24": f32c(m24), "onr": ONES_ROW, "idn": IDENT}
        for c in range(8)], core_ids=list(range(8))).results
    m32 = np.concatenate([np.concatenate([p1[c]["y_m"], p2[c]["y_m"], p3[c]["y_m"]], axis=1)
                          for c in range(8)], axis=1)
    p4 = run_bass_kernel_spmd(ffn4_prog(), [
        {"giT": p3[c]["y_g2i"], "uiT": p3[c]["y_ui"], "m32": f32c(m32),
         "wdT": wdnT_bf, "xT": xTs[c],
         "onc": ONES_COL, "onr": ONES_ROW, "idn": IDENT}
        for c in range(8)], core_ids=list(range(8))).results
    x2 = np.empty((B, S, D), np.float32)
    for c in range(8):
        b, h = c // 2, c % 2
        x2[b, h * 2048:(h + 1) * 2048, :] = p4[c]["y_x2"].T
    return x2, [p4[c]["y_mx"] for c in range(8)]


def ffn1_prog():
    if "ffn1" not in _PROGS:
        _PROGS["ffn1"] = build_ffn1()
    return _PROGS["ffn1"]


def ffn2_prog():
    if "ffn2" not in _PROGS:
        _PROGS["ffn2"] = build_ffn2()
    return _PROGS["ffn2"]


def ffn3_prog():
    if "ffn3" not in _PROGS:
        _PROGS["ffn3"] = build_ffn3()
    return _PROGS["ffn3"]


def ffn4_prog():
    if "ffn4" not in _PROGS:
        _PROGS["ffn4"] = build_ffn4()
    return _PROGS["ffn4"]


XPD = 4224
DI = 1024
DM = 512


def build_mamba(debug=False):
    p = Prog("mamba")
    nc = p.nc
    xdT = nc.dram_tensor("xdT", [DM, XPD], F32, kind="ExternalInput").ap()
    inwT = nc.dram_tensor("inwT", [DM, 2 * DI], F32, kind="ExternalInput").ap()
    xpjT = nc.dram_tensor("xpjT", [DI, 64], F32, kind="ExternalInput").ap()
    dtwT = nc.dram_tensor("dtwT", [32, DI], F32, kind="ExternalInput").ap()
    outwT = nc.dram_tensor("outwT", [DI, DM], F32, kind="ExternalInput").ap()
    convW = nc.dram_tensor("convW", [128, 32], F32, kind="ExternalInput").ap()
    convB = nc.dram_tensor("convB", [128, 8], F32, kind="ExternalInput").ap()
    dtB = nc.dram_tensor("dtB", [128, 8], F32, kind="ExternalInput").ap()
    DPc = nc.dram_tensor("DPc", [128, 8], F32, kind="ExternalInput").ap()
    AL = nc.dram_tensor("AL", [128, 128], F32, kind="ExternalInput").ap()
    esel = nc.dram_tensor("esel", [80, 2048], F32, kind="ExternalInput").ap()
    idnb = nc.dram_tensor("idnb", [128, 128], BF16, kind="ExternalInput").ap()
    y_h = nc.dram_tensor("y_h", [DM, 4096], F32, kind="ExternalOutput").ap()
    kind = dict(kind="ExternalOutput") if debug else {}
    xcbuf = nc.dram_tensor("xcbuf", [DI, 4096], F32, **kind)
    zbuf = nc.dram_tensor("zbuf", [DI, XPD], F32, **kind)
    if debug:
        y_dbc = nc.dram_tensor("y_dbc", [80, 4096], F32, kind="ExternalOutput").ap()
        y_del = nc.dram_tensor("y_del", [128, 8 * 512], F32, kind="ExternalOutput").ap()
        y_B = nc.dram_tensor("y_B", [128, 16 * 512], BF16, kind="ExternalOutput").ap()
        y_yf = nc.dram_tensor("y_yf", [128, 8 * 512], BF16, kind="ExternalOutput").ap()
        y_st = nc.dram_tensor("y_st", [128, 128], F32, kind="ExternalOutput").ap()

    xdb = p.sbuf("xdb", [128, 4 * XPD], BF16)
    inwb = p.sbuf("inwb", [128, 4 * 2048], BF16)
    xpjb = p.sbuf("xpjb", [128, 8 * 64], BF16)
    dtwb = p.sbuf("dtwb", [32, 1024], BF16)
    outwb = p.sbuf("outwb", [128, 8 * 512], BF16)
    cW = p.sbuf("cW", [128, 32], F32)
    cB = p.sbuf("cB", [128, 8], F32)
    dB = p.sbuf("dB", [128, 8], F32)
    dP = p.sbuf("dP", [128, 8], F32)
    Acol = p.sbuf("Acol", [128, 128], F32)
    esl = p.sbuf("esl", [80, 2048], F32)
    idb = p.sbuf("idb", [128, 128], BF16)
    xh = p.sbuf("xh", [128, XPD], F32)
    xcc = p.sbuf("xcc", [128, 4096], F32)
    zev = p.sbuf("zev", [128, 512], F32)
    xcw = p.sbuf("xcw", [128, 8 * 512], F32)
    xcwb = p.sbuf("xcwb", [128, 2 * 512], BF16)
    dtb16 = p.sbuf("dtb16", [32, 512], BF16)
    dbcs = p.sbuf("dbcs", [80, 512], F32)
    dels = p.sbuf("dels", [128, 8 * 512], F32)
    us = p.sbuf("us", [128, 8 * 512], BF16)
    ta = p.sbuf("ta", [128, 512], F32)
    tb = p.sbuf("tb", [128, 512], F32)
    Gt = p.sbuf("Gt", [128, 4 * 512], F32)
    vt = p.sbuf("vt", [128, 4 * 512], BF16)
    ht = p.sbuf("ht", [128, 2 * 512], BF16)
    hct = p.sbuf("hct", [128, 2 * 512], BF16)
    stcols = p.sbuf("stcols", [128, 128], F32)
    sz = p.sbuf("sz", [128, 512], F32)
    yf = p.sbuf("yf", [128, 8 * 512], BF16)
    hor = p.sbuf("hor", [128, 2 * 512], F32)
    Bbc = xh[:].bitcast(BF16)
    Cbc = xcc[:].bitcast(BF16)
    pin0 = p.psum("pin0", [128, 512])
    pin1 = p.psum("pin1", [128, 512])
    pdbc = p.psum("pdbc", [128, 512])
    pdel = p.psum("pdel", [128, 512])
    pBC = p.psum("pBC", [128, 512])
    py0 = p.psum("py0", [128, 512])
    py1 = p.psum("py1", [128, 512])
    pho = p.psum("pho", [128, 512])

    l_cw = p.dma("sp", cW[:], convW)
    l_cb = p.dma("sp", cB[:], convB)
    l_db = p.dma("sp", dB[:], dtB)
    l_dp = p.dma("sp", dP[:], DPc)
    l_al = p.dma("sp", Acol[:], AL)
    l_es = p.dma("sp", esl[:], esel)
    l_id = p.dma("sp", idb[:], idnb)
    n_A1 = act(p, Acol[:], Acol[:], AF.Exp, [l_al])
    n_A = ts(p, Acol[:], Acol[:], -1.0, None, ALU.mult, ALU.bypass, [n_A1])
    casts = []

    def staged(dst, src_ap, width):
        i = len(casts)
        use_xh = (width > 4096) or (i % 2 == 0)
        sbuf_st = xh[:, 0:width] if use_xh else xcc[:, 0:width]
        dep = [c for c in casts[-2:]]
        ld = p.dma("sp" if i % 2 == 0 else "pool", sbuf_st, src_ap, dep)
        casts.append(vcopy(p, dst, sbuf_st, [ld]))
    for c in range(4):
        staged(xdb[:, c * XPD:(c + 1) * XPD], xdT[c * 128:(c + 1) * 128, :], XPD)
    for c in range(4):
        staged(inwb[:, c * 2048:(c + 1) * 2048], inwT[c * 128:(c + 1) * 128, :], 2048)
    xd_done = casts[3]
    inw_done = casts[7]
    phase1_cast_done = casts[-1]
    wcasts = []

    def staged2(dst, src_ap, width, slot):
        dep = [wcasts[-2]] if len(wcasts) >= 2 else [casts[-1]]
        st_ap = xcw[:, slot * 512: slot * 512 + width]
        ld = p.dma("pool", st_ap, src_ap, dep)
        wcasts.append(vcopy(p, dst, st_ap, [ld]))
    for c in range(8):
        staged2(xpjb[:, c * 64:(c + 1) * 64], xpjT[c * 128:(c + 1) * 128, :], 64, c % 2)
    ldw = p.dma("pool", xcw[0:32, 1024:2048], dtwT, [wcasts[-2]])
    wcasts.append(vcopy(p, dtwb[:], xcw[0:32, 1024:2048], [ldw]))
    for c in range(8):
        staged2(outwb[:, c * 512:(c + 1) * 512], outwT[c * 128:(c + 1) * 128, :], 512, 4 + c % 2)
    cast_done = wcasts[-1]

    pins = (pin0, pin1)
    use = 0
    prev_ev = {}
    conv_last = None
    xc_stores = []
    z_stores = []
    for ec in range(16):
        is_xh = ec < 8
        evs = []
        for t9 in range(9):
            w9 = 512 if t9 < 8 else XPD - 8 * 512
            pt = pins[use % 2]
            deps = [inw_done, xd_done]
            if use >= 2:
                deps.append(prev_ev[use - 2])
            g = mm_group(p, pt[:, 0:w9],
                         [(inwb[:, kc * 2048 + ec * 128:kc * 2048 + (ec + 1) * 128],
                           xdb[:, kc * XPD + t9 * 512:kc * XPD + t9 * 512 + w9])
                          for kc in range(4)], deps)
            if is_xh:
                ev = acopy(p, xh[:, t9 * 512:t9 * 512 + w9], pt[:, 0:w9],
                           [g, conv_last, phase1_cast_done])
            else:
                edeps = [g]
                if z_stores:
                    edeps.append(z_stores[-1])
                ev = acopy(p, zev[:, 0:w9], pt[:, 0:w9], edeps)
                z_stores.append(p.dma("pool", zbuf[(ec - 8) * 128:(ec - 7) * 128, t9 * 512:t9 * 512 + w9],
                                      zev[:, 0:w9], [ev]))
            prev_ev[use] = ev
            evs.append(ev)
            use += 1
        if is_xh:
            c0 = p.add("dve", (lambda e, c=ec: e.tensor_scalar(
                xcc[:], xh[:, 0:4096], cW[:, c * 4:c * 4 + 1], None, ALU.mult, ALU.bypass)),
                evs + [phase1_cast_done] + ([conv_last] if conv_last else []))
            c1 = p.add("dve", (lambda e, c=ec: e.scalar_tensor_tensor(
                xcc[:], xh[:, 1:4097], cW[:, c * 4 + 1:c * 4 + 2], xcc[:], ALU.mult, ALU.add)), [c0])
            c2 = p.add("dve", (lambda e, c=ec: e.scalar_tensor_tensor(
                xcc[:], xh[:, 2:4098], cW[:, c * 4 + 2:c * 4 + 3], xcc[:], ALU.mult, ALU.add)), [c1])
            c3 = p.add("dve", (lambda e, c=ec: e.scalar_tensor_tensor(
                xcc[:], xh[:, 3:4099], cW[:, c * 4 + 3:c * 4 + 4], xcc[:], ALU.mult, ALU.add)), [c2])
            c4 = act(p, xcc[:], xcc[:], AF.Silu, [c3, l_cb], bias=cB[:, ec:ec + 1])
            st = p.dma("pool", xcbuf[ec * 128:(ec + 1) * 128, :], xcc[:], [c4])
            xc_stores.append(st)
            conv_last = st

    n_z0 = p.add("dve", lambda e: e.memset(stcols[:], 0.0), [])
    sdel_done = {}
    prev_out = []
    prev_gate = {}
    prev_hos = {}
    prev_xcl = {}
    prev_dtev = None
    prev_scan = {}
    prev_hc = {}
    prev_ymm = {}
    prev_rep = {}
    h_stores = []
    for tw in range(8):
        tcols = (3 + tw * 512, 3 + (tw + 1) * 512)
        xcl = []
        for c in range(8):
            dep = [prev_gate[(tw - 1, c)]] if tw > 0 else [xc_stores[c], cast_done]
            if tw > 0:
                dep.append(xc_stores[c])
            xcl.append(p.dma("sp", xcw[:, c * 512:(c + 1) * 512],
                             xcbuf[c * 128:(c + 1) * 128, tw * 512:(tw + 1) * 512], dep))
        dbc_mms = []
        for c in range(8):
            cb = c % 2
            ccdep = [xcl[c]]
            if len(dbc_mms) >= 2:
                ccdep.append(dbc_mms[-2])
            elif prev_dtev is not None:
                ccdep.append(prev_dtev)
            cc = vcopy(p, xcwb[:, cb * 512:(cb + 1) * 512], xcw[:, c * 512:(c + 1) * 512], ccdep)
            g1 = mm(p, pdbc[0:48, :], xpjb[:, c * 64:c * 64 + 48],
                    xcwb[:, cb * 512:(cb + 1) * 512], c == 0, c == 7, [cc, cast_done])
            g2 = mm(p, pdbc[64:80, :], xpjb[:, c * 64 + 48:c * 64 + 64],
                    xcwb[:, cb * 512:(cb + 1) * 512], c == 0, c == 7, [cc])
            dbc_mms.append(g2)
        dbc_ev = acopy(p, dbcs[0:80, 0:512], pdbc[0:80, :], dbc_mms)
        dtev = vcopy(p, dtb16[:], dbcs[0:32, 0:512], [dbc_ev])
        prev_dtev = dtev
        del_nodes = []
        gds = {}
        t2s = {}

        def emit_gd(c):
            pdl = (pin0, pin1)[c % 2]
            gdeps = [dtev]
            if c - 2 in t2s:
                gdeps.append(t2s[c - 2])
            elif tw == 0 and c < 2:
                gdeps.extend(z_stores[-2:])
            gds[c] = mm(p, pdl[:, :], dtwb[0:32, c * 128:(c + 1) * 128], dtb16[:],
                        True, True, [d for d in gdeps if d])

        emit_gd(0)
        emit_gd(1)
        for c in range(8):
            pdl = (pin0, pin1)[c % 2]
            tah = ta[:, 0:512] if False else ta[:]
            a1 = act(p, tah, pdl[:, :], AF.Abs,
                     [gds[c], l_db] + ([del_nodes[-1]] if del_nodes else []),
                     bias=dB[:, c:c + 1])
            a2 = act(p, tah, tah, AF.Exp, [a1], scale=-1.0)
            t1 = ts(p, tah, tah, 1.0, None, ALU.add, ALU.bypass, [a2])
            a3 = act(p, tah, tah, AF.Ln, [t1])
            t2 = p.add("dve", (lambda e, c=c, pdl=pdl: e.tensor_scalar(
                tb[:], pdl[:, :], dB[:, c:c + 1], 0.0, ALU.add, ALU.max)), [a3])
            t2s[c] = t2
            if c + 2 < 8:
                emit_gd(c + 2)
            dl = tt(p, dels[:, c * 512:(c + 1) * 512], tah, tb[:], ALU.add, [t2, a3])
            uu = tt(p, us[:, c * 512:(c + 1) * 512], dels[:, c * 512:(c + 1) * 512],
                    xcw[:, c * 512:(c + 1) * 512], ALU.mult, [dl])
            del_nodes.append(uu)
        if debug:
            h_stores.append(p.dma("pool", y_dbc[:, tw * 512:(tw + 1) * 512],
                                  dbcs[0:80, tw * 512:(tw + 1) * 512], [dbc_ev]))
            if tw == 0:
                h_stores.append(p.dma("pool", y_del, dels[:], [del_nodes[-1]]))
        rep_nodes = {}
        last_eB = [None]
        last_eC = [None]

        def emit_rep(s, extra):
            dep = [dbc_ev, l_es] + extra
            if (tw - 1, "v", s) in prev_rep:
                dep.append(prev_rep[(tw - 1, "v", s)])
            if (tw - 1, "h", s) in prev_rep:
                dep.append(prev_rep[(tw - 1, "h", s)])
            if last_eB[0] is not None:
                dep.append(last_eB[0])
            gB = mm(p, pBC[:, :], esl[32:48, s * 128:(s + 1) * 128],
                    dbcs[32:48, 0:512], True, True, dep)
            eB = acopy(p, Bbc[:, s * 512:(s + 1) * 512], pBC[:, :], [gB])
            dep2 = [dbc_ev]
            if last_eC[0] is not None:
                dep2.append(last_eC[0])
            gC = mm(p, pdbc[0:128, :], esl[64:80, s * 128:(s + 1) * 128],
                    dbcs[64:80, 0:512], True, True, dep2 + [eB])
            eC = acopy(p, Cbc[:, s * 512:(s + 1) * 512], pdbc[0:128, :], [gC])
            last_eB[0] = eB
            last_eC[0] = eC
            rep_nodes[s] = eC
        for s in range(3):
            emit_rep(s, [])
        if debug and tw == 0:
            h_stores.append(p.dma("pool", y_B, Bbc[:, 0:16 * 512], [rep_nodes[2]]))
        def lin(dic, tw_, c_, s_, d):
            g = (tw_ * 8 + c_) * 16 + s_ - d
            if g < 0:
                return None
            tw2, r_ = divmod(g, 128)
            c2, s2 = divmod(r_, 16)
            return dic.get((tw2, c2, s2))

        for c in range(8):
            pyt = (py0, py1)[c % 2]

            def ring_war(j, c=c, tw=tw):
                return lin(prev_scan, tw, c, j, 4)

            def emit_gx(s, c=c, tw=tw):
                sb = s % 4
                gdep = [del_nodes[c], n_A, ring_war(s)]
                return act(p, Gt[:, sb * 512:(sb + 1) * 512],
                           dels[:, c * 512:(c + 1) * 512], AF.Exp, gdep,
                           scale=Acol[:, c * 16 + s:c * 16 + s + 1])

            def emit_vv(s, c=c, tw=tw):
                sb = s % 4
                vdeps = [rep_nodes[s], del_nodes[c], ring_war(s)]
                return tt(p, vt[:, sb * 512:(sb + 1) * 512], us[:, c * 512:(c + 1) * 512],
                          Bbc[:, s * 512:(s + 1) * 512], ALU.mult, vdeps)

            gxs = {0: emit_gx(0), 1: emit_gx(1)}
            vvs = {0: emit_vv(0), 1: emit_vv(1)}
            for s in range(16):
                sb = s % 4
                hb = s % 2
                col = c * 16 + s
                sdeps = [gxs[s], vvs[s], n_z0, lin(prev_hc, tw, c, s, 2)]
                if (tw - 1, c, s) in prev_scan:
                    sdeps.append(prev_scan[(tw - 1, c, s)])
                sc_ = p.add("dve", (lambda e, sb=sb, hb=hb, col=col:
                                    e.tensor_tensor_scan(ht[:, hb * 512:(hb + 1) * 512],
                                                         Gt[:, sb * 512:(sb + 1) * 512],
                                                         vt[:, sb * 512:(sb + 1) * 512],
                                                         stcols[:, col:col + 1],
                                                         ALU.mult, ALU.add)),
                            sdeps, special=True)
                if c == 0 and s + 3 < 16:
                    emit_rep(s + 3, [])
                if s + 2 < 16:
                    gxs[s + 2] = emit_gx(s + 2)
                    vvs[s + 2] = emit_vv(s + 2)
                st_cp = p.add("dve", (lambda e, hb=hb, col=col:
                                      e.tensor_copy(stcols[:, col:col + 1],
                                                    ht[:, hb * 512 + 511: hb * 512 + 512])),
                              [sc_])
                prev_scan[(tw, c, s)] = st_cp
                hcdep = [sc_, rep_nodes[s], lin(prev_ymm, tw, c, s, 2)]
                hc = tt(p, hct[:, hb * 512:(hb + 1) * 512], ht[:, hb * 512:(hb + 1) * 512],
                        Cbc[:, s * 512:(s + 1) * 512], ALU.mult, hcdep)
                prev_hc[(tw, c, s)] = hc
                if c == 7:
                    prev_rep[(tw, "v", s)] = vvs[s]
                    prev_rep[(tw, "h", s)] = hc
                ydep = [hc, l_id]
                if s == 0 and (tw, c - 2, "gate") in prev_gate:
                    ydep.append(prev_gate[(tw, c - 2, "gate")])
                ym = mm(p, pyt[:, :], idb[:], hct[:, hb * 512:(hb + 1) * 512],
                        s == 0, s == 15, ydep)
                prev_ymm[(tw, c, s)] = ym
            zldep = []
            if prev_gate.get((tw, c - 1)) is not None:
                zldep.append(prev_gate[(tw, c - 1)])
            elif tw > 0:
                zldep.append(prev_gate[(tw - 1, 7)])
            elif c == 0:
                zldep.append(z_stores[-1])
            zl = p.dma("pool", sz[:], zbuf[c * 128:(c + 1) * 128, tcols[0]:tcols[1]], zldep)
            slu = act(p, sz[:], sz[:], AF.Silu, [zl])
            g1 = p.add("dve", (lambda e, c=c: e.scalar_tensor_tensor(
                ta[:], xcw[:, c * 512:(c + 1) * 512], dP[:, c:c + 1],
                (py0, py1)[c % 2][:, :], ALU.mult, ALU.add)),
                [prev_ymm[(tw, c, 15)], l_dp, slu])
            g2 = tt(p, yf[:, c * 512:(c + 1) * 512], ta[:], sz[:], ALU.mult,
                    [g1] + ([prev_out[-1]] if prev_out else []))
            prev_gate[(tw, c)] = g2
            prev_gate[(tw, c, "gate")] = g1
        if debug and tw == 0:
            h_stores.append(p.dma("pool", y_yf, yf[:],
                                  [prev_gate[(tw, cc)] for cc in range(8)]))
        for oc in range(4):
            hdeps = [prev_gate[(tw, cc)] for cc in range(8)] + [cast_done]
            if (tw - 1, oc) in prev_hos:
                hdeps.append(prev_hos[(tw - 1, oc)])
            g = mm_group(p, pho[:, :],
                         [(outwb[:, kc * 512 + oc * 128:kc * 512 + (oc + 1) * 128],
                           yf[:, kc * 512:(kc + 1) * 512]) for kc in range(8)],
                         hdeps + ([prev_hos[(tw, oc - 1)]] if (tw, oc - 1) in prev_hos else []))
            hb = oc % 2
            ev = acopy(p, hor[:, hb * 512:(hb + 1) * 512], pho[:, :],
                       [g] + ([h_stores[-2]] if len(h_stores) >= 2 else []))
            prev_hos[(tw, oc)] = ev
            h_stores.append(p.dma("sp", y_h[oc * 128:(oc + 1) * 128, tw * 512:(tw + 1) * 512],
                                  hor[:, hb * 512:(hb + 1) * 512], [ev]))
        prev_out.append(prev_hos[(tw, 3)])
    if debug:
        h_stores.append(p.dma("pool", y_st, stcols[:],
                              [prev_scan[(7, c, 15)] for c in range(8)]))
    sink(p, h_stores + z_stores + xc_stores)
    return p.build()


def mamba_prog():
    if "mamba" not in _PROGS:
        _PROGS["mamba"] = build_mamba()
    return _PROGS["mamba"]


def build_lnmax():
    p = Prog("lnmax")
    nc = p.nc
    hidT = nc.dram_tensor("hidT", [D, TOK], F32, kind="ExternalInput").ap()
    xT = nc.dram_tensor("xT", [D, TOK], F32, kind="ExternalInput").ap()
    onc = nc.dram_tensor("onc", [128, 1], F32, kind="ExternalInput").ap()
    onr = nc.dram_tensor("onr", [1, 128], F32, kind="ExternalInput").ap()
    idn = nc.dram_tensor("idn", [128, 128], F32, kind="ExternalInput").ap()
    y_x3 = nc.dram_tensor("y_x3", [D, TOK], F32, kind="ExternalOutput").ap()
    y_mx = nc.dram_tensor("y_mx", [1, 1], F32, kind="ExternalOutput").ap()

    ha = p.sbuf("ha", [128, 4 * 512], F32)
    xa = p.sbuf("xa", [128, 4 * 512], F32)
    rsum = p.sbuf("rsum", [128, 8 * 512], F32)
    x1s = p.sbuf("x1s", [128, 8 * 512], F32)
    sq = p.sbuf("sqln", [128, 8 * 512], F32)
    lnA = p.sbuf("lnA", [65, 512], F32)
    rrstd = p.sbuf("rrstd", [1, 512], F32)
    rmr = p.sbuf("rmr", [1, 512], F32)
    mc = p.sbuf("mc", [128, 2], F32)
    mrow = p.sbuf("mrow", [1, 128], F32)
    onc_s = p.sbuf("onc_s", [128, 1], F32)
    onr_s = p.sbuf("onr_s", [1, 128], F32)
    idn_s = p.sbuf("idn_s", [128, 128], F32)
    pst = p.psum("pst", [64, 512])
    pb1 = p.psum("pb1", [128, 512])
    pb2 = p.psum("pb2", [128, 512])
    pmx = p.psum("pmx", [1, 128])

    n_onc = p.dma("sp", onc_s[:], onc)
    n_onr = p.dma("sp", onr_s[:], onr)
    n_idn = p.dma("sp", idn_s[:], idn)
    sc = {"ones_col": onc_s[:, 0:1], "ones_row": onr_s[0:1, :], "sq": sq,
          "stat": pst, "bc1": pb1, "bc2": pb2,
          "rmean": lnA[0:1, :], "rvar": lnA[32:33, :], "rtmp": lnA[64:65, :],
          "rrstd": rrstd[0:1, :], "rmr": rmr[0:1, :]}
    stores = []
    prev_red = None
    mzero = p.add("dve", lambda e: e.memset(mc[:, 0:1], 0.0), [])
    ln_last = None
    res_hist = []
    for tc in range(4):
        res_nodes = []
        for oc in range(8):
            b = oc % 4
            war = [res_hist[-4]] if len(res_hist) >= 4 else []
            l1 = p.dma("pool", ha[:, b * 512:(b + 1) * 512],
                       hidT[oc * 128:(oc + 1) * 128, tc * 512:(tc + 1) * 512], war)
            l2 = p.dma("pool", xa[:, b * 512:(b + 1) * 512],
                       xT[oc * 128:(oc + 1) * 128, tc * 512:(tc + 1) * 512], war)
            r = tt(p, rsum[:, oc * 512:(oc + 1) * 512], ha[:, b * 512:(b + 1) * 512],
                   xa[:, b * 512:(b + 1) * 512], ALU.add,
                   [l1, l2] + ([ln_last] if tc > 0 and oc == 0 else []))
            res_nodes.append(r)
            res_hist.append(r)
        outs = emit_ln(p, lambda c: rsum[:, c * 512:(c + 1) * 512],
                       lambda c: x1s[:, c * 512:(c + 1) * 512], 512, sc, res_nodes)
        ln_last = outs[-1]
        for oc in range(8):
            stores.append(p.dma("sp", y_x3[oc * 128:(oc + 1) * 128, tc * 512:(tc + 1) * 512],
                                x1s[:, oc * 512:(oc + 1) * 512], [outs[oc]]))
            rdeps = [outs[oc], mzero]
            if prev_red is not None:
                rdeps.append(prev_red)
            rd = vred(p, mc[:, 1:2], x1s[:, oc * 512:(oc + 1) * 512], ALU.max, rdeps, absval=True)
            prev_red = tt(p, mc[:, 0:1], mc[:, 0:1], mc[:, 1:2], ALU.max, [rd])
    n_tp = p.add("pe", lambda e: e.transpose(pmx[0:1, :], mc[:, 0:1], idn_s[:, :]),
                 [prev_red, n_idn])
    n_ev = acopy(p, mrow[0:1, :], pmx[0:1, :], [n_tp])
    n_fm = vred(p, mc[0:1, 0:1], mrow[0:1, :], ALU.max, [n_ev])
    stores.append(p.dma("sp", y_mx, mc[0:1, 0:1], [n_fm]))
    sink(p, stores)
    return p.build()


def lnmax_prog():
    if "lnmax" not in _PROGS:
        _PROGS["lnmax"] = build_lnmax()
    return _PROGS["lnmax"]


def _conv_layout(w):
    return f32c(w.reshape(8, 128, 4).transpose(1, 0, 2).reshape(128, 32))


def _col_layout(v):
    return f32c(v.reshape(8, 128).T)


def _al_layout(a):
    return f32c(a.reshape(8, 128, 16).transpose(1, 0, 2).reshape(128, 128))


def make_esel():
    e = np.zeros((80, 2048), np.float32)
    for k in range(16):
        for s in range(16):
            if k == s:
                e[32 + k, s * 128:(s + 1) * 128] = 1.0
                e[64 + k, s * 128:(s + 1) * 128] = 1.0
    return e


IDENT_BF = np.eye(128, dtype=np.float32).astype(bf16)
ESEL = make_esel()


def mamba_inmap(xd, prm, tag):
    xpad = np.zeros((DM, XPD), np.float32)
    xpad[:, 3:3 + 4096] = xd.T
    return {
        "xdT": xpad,
        "inwT": f32c(prm[f"in_w{tag}"].T),
        "xpjT": f32c(prm[f"xproj_w{tag}"].T),
        "dtwT": f32c(prm[f"dt_w{tag}"].T),
        "outwT": f32c(prm[f"out_w{tag}"].T),
        "convW": _conv_layout(prm[f"conv_w{tag}"]),
        "convB": _col_layout(prm[f"conv_b{tag}"]),
        "dtB": _col_layout(prm[f"dt_b{tag}"]),
        "DPc": _col_layout(prm[f"D{tag}"]),
        "AL": _al_layout(prm[f"A_log{tag}"]),
        "esel": ESEL, "idnb": IDENT_BF,
    }


def run_mamba(x2, prm):
    in_maps = []
    for c in range(8):
        b, dirb = c // 2, c % 2
        if dirb == 0:
            xd = x2[b, :, 0:512]
            in_maps.append(mamba_inmap(xd, prm, "1"))
        else:
            xd = x2[b, ::-1, 512:1024]
            in_maps.append(mamba_inmap(xd, prm, "2"))
    return run_bass_kernel_spmd(mamba_prog(), in_maps, core_ids=list(range(8))).results


def run_lnmax(x2, mres):
    in_maps = []
    for c in range(8):
        b, h = c // 2, c % 2
        sl = slice(h * 2048, (h + 1) * 2048)
        hid = np.concatenate([mres[2 * b]["y_h"][:, sl], mres[2 * b + 1]["y_h"][:, sl]], axis=0)
        in_maps.append({"hidT": f32c(hid), "xT": f32c(x2[b, sl, :].T),
                        "onc": ONES_COL, "onr": ONES_ROW, "idn": IDENT})
    res = run_bass_kernel_spmd(lnmax_prog(), in_maps, core_ids=list(range(8))).results
    x3 = np.empty((B, S, D), np.float32)
    for c in range(8):
        b, h = c // 2, c % 2
        x3[b, h * 2048:(h + 1) * 2048, :] = res[c]["y_x3"].T
    return x3, [res[c]["y_mx"] for c in range(8)]



def kernel(L0_attn_x, L0_attn_Wq, L0_attn_Wk, L0_attn_Wv, L0_attn_Wo,
           L0_ffn_w_gate, L0_ffn_w_up, L0_ffn_w_down,
           L1_attn_in_w1, L1_attn_conv_w1, L1_attn_conv_b1, L1_attn_xproj_w1,
           L1_attn_dt_w1, L1_attn_dt_b1, L1_attn_A_log1, L1_attn_D1, L1_attn_out_w1,
           L1_attn_in_w2, L1_attn_conv_w2, L1_attn_conv_b2, L1_attn_xproj_w2,
           L1_attn_dt_w2, L1_attn_dt_b2, L1_attn_A_log2, L1_attn_D2, L1_attn_out_w2,
           L1_ffn_w_gate, L1_ffn_w_up, L1_ffn_w_down):
    x = np.asarray(L0_attn_x, np.float32)
    dq_packs = [dq_pack(c, np.asarray(L0_ffn_w_up), np.asarray(L0_ffn_w_gate),
                        np.asarray(L1_ffn_w_up), np.asarray(L1_ffn_w_gate),
                        np.asarray(L0_ffn_w_down), np.asarray(L1_ffn_w_down))
                for c in range(8)]
    qkv_res = run_qkv(x, L0_attn_Wq, L0_attn_Wk, L0_attn_Wv, dq_packs)
    attn_res = run_attn(x, qkv_res, L0_attn_Wo)
    x1 = gather_interleaved(attn_res, "y_x1")
    mx1 = [attn_res[c]["y_mx"] for c in range(8)]
    up0, g0, up1, g1, dn0, dn1 = dq_unpack([qkv_res[c]["y_dq"] for c in range(8)])
    x2, mx2 = run_ffn(x1, mx1, np.ascontiguousarray(up0.T), np.ascontiguousarray(g0.T),
                      np.ascontiguousarray(dn0.T))
    prm = {
        "in_w1": L1_attn_in_w1, "conv_w1": L1_attn_conv_w1, "conv_b1": L1_attn_conv_b1,
        "xproj_w1": L1_attn_xproj_w1, "dt_w1": L1_attn_dt_w1, "dt_b1": L1_attn_dt_b1,
        "A_log1": L1_attn_A_log1, "D1": L1_attn_D1, "out_w1": L1_attn_out_w1,
        "in_w2": L1_attn_in_w2, "conv_w2": L1_attn_conv_w2, "conv_b2": L1_attn_conv_b2,
        "xproj_w2": L1_attn_xproj_w2, "dt_w2": L1_attn_dt_w2, "dt_b2": L1_attn_dt_b2,
        "A_log2": L1_attn_A_log2, "D2": L1_attn_D2, "out_w2": L1_attn_out_w2,
    }
    prm = {k: np.asarray(v, np.float32) for k, v in prm.items()}
    mres = run_mamba(x2, prm)
    x3, mx3 = run_lnmax(x2, mres)
    x4, _ = run_ffn(x3, mx3, np.ascontiguousarray(up1.T), np.ascontiguousarray(g1.T),
                    np.ascontiguousarray(dn1.T))
    return x4


# version 11
# speedup vs baseline: 1.0112x; 1.0112x over previous
import contextlib
import numpy as np
import ml_dtypes
import concourse.bass as bass
import concourse.mybir as mybir
from concourse.bass_utils import run_bass_kernel_spmd

F32 = mybir.dt.float32
BF16 = mybir.dt.bfloat16
I16 = mybir.dt.int16
U16 = mybir.dt.uint16
AF = mybir.ActivationFunctionType
ALU = mybir.AluOpType
AX = mybir.AxisListType

MAGIC = 12582912.0
B, S, D = 4, 4096, 1024
TOK = 2048
NH, DK = 16, 64
FI = 3072
EPS = 1e-5

bf16 = ml_dtypes.bfloat16

ENGINES = ("pe", "act", "dve", "pool", "sp")
_ENG_ATTR = {"pe": "tensor", "act": "scalar", "dve": "vector", "pool": "gpsimd", "sp": "sync"}
NDMASEM = 6


class Node:
    __slots__ = ("eng", "fn", "deps", "is_dma", "idx", "flagged", "special")

    def __init__(self, eng, fn, deps, is_dma, special=False):
        self.eng, self.fn, self.deps, self.is_dma = eng, fn, deps, is_dma
        self.idx = -1
        self.flagged = False
        self.special = special


class Prog:

    def __init__(self, name):
        self.nc = bass.Bass("TRN2", target_bir_lowering=False)
        self.name = name
        self.nodes = []
        self._stack = contextlib.ExitStack()

    def sbuf(self, name, shape, dtype):
        return self._stack.enter_context(self.nc.sbuf_tensor(name, shape, dtype))

    def psum(self, name, shape, dtype=F32):
        return self._stack.enter_context(self.nc.psum_tensor(name, shape, dtype))

    def add(self, eng, fn, deps=(), special=False):
        n = Node(eng, fn, [d for d in deps if d is not None], False, special)
        self.nodes.append(n)
        return n

    def dma(self, queue, out, in_, deps=()):
        def fn(e):
            return e.dma_start(out=out, in_=in_)
        n = Node(queue, fn, [d for d in deps if d is not None], True)
        self.nodes.append(n)
        return n

    def build(self):
        nc = self.nc
        for n in self.nodes:
            if n.is_dma:
                n.flagged = True
            for d in n.deps:
                if d.eng != n.eng or d.is_dma or d.special or n.special:
                    d.flagged = True
        comp_count = {e: 0 for e in ENGINES}
        dma_count = {e: 0 for e in ENGINES}
        for n in self.nodes:
            if not n.flagged:
                continue
            if n.is_dma:
                n.idx = dma_count[n.eng]
                dma_count[n.eng] += 1
            else:
                comp_count[n.eng] += 1
                n.idx = comp_count[n.eng]
        self._sems = {}
        for e in ENGINES:
            if comp_count[e]:
                self._sems[("c", e)] = nc.semaphore(f"S_{e}").__enter__()
            if dma_count[e]:
                self._sems[("d", e)] = [nc.semaphore(f"D_{e}{i}").__enter__()
                                        for i in range(NDMASEM)]
        block = nc.Block().__enter__()
        per_eng = {e: [n for n in self.nodes if n.eng == e] for e in ENGINES}

        def dep_req(d):
            if d.is_dma:
                sems = self._sems[("d", d.eng)]
                return (sems[d.idx % NDMASEM], 16 * (d.idx // NDMASEM + 1))
            return (self._sems[("c", d.eng)], d.idx)

        for e in ENGINES:
            nodes = per_eng[e]
            if not nodes:
                continue

            def make_body(e=e, nodes=nodes):
                def body(eng):
                    watermark = {}
                    for n in nodes:
                        for d in n.deps:
                            if (d.eng == e and not d.is_dma and not d.special
                                    and not n.special):
                                continue
                            sem, val = dep_req(d)
                            key = id(sem)
                            if watermark.get(key, -1) >= val:
                                continue
                            eng.wait_ge(sem, val)
                            watermark[key] = val
                        if n.is_dma and n.flagged:
                            sems = self._sems[("d", e)]
                            slot, rnd = n.idx % NDMASEM, n.idx // NDMASEM
                            if rnd > 0:
                                key = id(sems[slot])
                                val = 16 * rnd
                                if watermark.get(key, -1) < val:
                                    eng.wait_ge(sems[slot], val)
                                    watermark[key] = val
                        instr = n.fn(eng)
                        if n.flagged:
                            if n.is_dma:
                                instr.then_inc(self._sems[("d", e)][n.idx % NDMASEM], 16)
                            else:
                                instr.then_inc(self._sems[("c", e)], 1)
                return body
            getattr(block, _ENG_ATTR[e])(make_body())
        block.__exit__(None, None, None)
        return nc


def f32c(x):
    return np.ascontiguousarray(x, dtype=np.float32)


def bf16c(x):
    return np.ascontiguousarray(np.asarray(x).astype(bf16))



def mm(p, out_ap, lhsT, rhs, start, stop, deps=()):
    return p.add("pe", lambda e: e.matmul(out_ap, lhsT, rhs, start=start, stop=stop,
                                          skip_group_check=True), deps)


def mm_group(p, out_ap, pairs, deps):
    def fn(eng):
        last = None
        for i, (lhsT, rhs) in enumerate(pairs):
            last = eng.matmul(out_ap, lhsT, rhs, start=(i == 0),
                              stop=(i == len(pairs) - 1), skip_group_check=True)
        return last
    return p.add("pe", fn, deps)


def act(p, out, in_, func, deps, bias=None, scale=None):
    def fn(eng):
        kw = {}
        if bias is not None:
            kw["bias"] = bias
        if scale is not None:
            kw["scale"] = scale
        return eng.activation(out, in_, func, **kw)
    return p.add("act", fn, deps)


def acopy(p, out, in_, deps):
    return p.add("act", lambda e: e.copy(out, in_), deps)


def ts(p, out, in_, s1, s2, op0, op1, deps):
    return p.add("dve", lambda e: e.tensor_scalar(out, in_, s1, s2, op0, op1), deps)


def tt(p, out, a, b, op, deps):
    return p.add("dve", lambda e: e.tensor_tensor(out, a, b, op), deps)


def stt(p, out, in0, scalar, in1, op0, op1, deps):
    return p.add("dve", lambda e: e.scalar_tensor_tensor(out, in0, scalar, in1, op0, op1), deps)


def vred(p, out, in_, op, deps, absval=None):
    return p.add("dve", lambda e: e.tensor_reduce(out, in_, AX.X, op,
                                                  apply_absolute_value=absval),
                 deps, special=True)


def vcopy(p, out, in_, deps):
    return p.add("dve", lambda e: e.tensor_copy(out, in_), deps)


def vscan(p, out, d0, d1, init, deps):
    return p.add("dve", lambda e: e.tensor_tensor_scan(out, d0, d1, init, ALU.mult, ALU.add),
                 deps, special=True)


def vrecip(p, out, in_, deps):
    return p.add("dve", lambda e: e.reciprocal(out, in_), deps, special=True)


def sink(p, deps):
    return p.add("sp", lambda e: e.nop(), deps)


def chunks2d(handle, n, width):
    def sl(c, a=0, b=None):
        b = width if b is None else b
        return handle[:, c * width + a: c * width + b]
    return sl



def emit_ln(p, in_sl, out_sl, T, sc, dep_in):
    ones_col, ones_row = sc["ones_col"], sc["ones_row"]
    sq = sc["sq"]
    nodes_sq = [act(p, sq[:, c * T:(c + 1) * T], in_sl(c), AF.Square, dep_in)
                for c in range(8)]
    n_sum = mm_group(p, sc["stat"][0:1, 0:T],
                     [(ones_col, in_sl(c)) for c in range(8)], dep_in)
    n_ssq = mm_group(p, sc["stat"][32:33, 0:T],
                     [(ones_col, sq[:, c * T:(c + 1) * T]) for c in range(8)], nodes_sq)
    n_mean = ts(p, sc["rmean"], sc["stat"][0:1, 0:T], 1.0 / D, None,
                ALU.mult, ALU.bypass, [n_sum])
    n_msq = tt(p, sc["rvar"], sc["rmean"], sc["rmean"],
               ALU.mult, [n_mean])
    n_var = stt(p, sc["rvar"], sc["stat"][32:33, 0:T], 1.0 / D,
                sc["rvar"], ALU.mult, ALU.subtract, [n_ssq, n_msq])
    n_vpe = ts(p, sc["rvar"], sc["rvar"], EPS, None, ALU.add, ALU.bypass, [n_var])
    n_ln = act(p, sc["rtmp"], sc["rvar"], AF.Ln, [n_vpe])
    n_rstd = act(p, sc["rrstd"], sc["rtmp"], AF.Exp, [n_ln], scale=-0.5)
    n_mr = tt(p, sc["rmr"], sc["rmean"], sc["rrstd"],
              ALU.mult, [n_rstd])
    n_b1 = mm(p, sc["bc1"][:, 0:T], ones_row, sc["rrstd"], True, True, [n_rstd])
    n_b2 = mm(p, sc["bc2"][:, 0:T], ones_row, sc["rmr"], True, True, [n_mr])
    outs = []
    for c in range(8):
        n_a = tt(p, sq[:, c * T:(c + 1) * T], in_sl(c), sc["bc1"][:, 0:T],
                 ALU.mult, [n_b1, n_ssq])
        n_o = tt(p, out_sl(c), sq[:, c * T:(c + 1) * T], sc["bc2"][:, 0:T],
                 ALU.subtract, [n_b2, n_a])
        outs.append(n_o)
    return outs



NDQ = 18


def build_qkv():
    p = Prog("qkv")
    nc = p.nc
    xT = nc.dram_tensor("xT", [D, TOK], F32, kind="ExternalInput").ap()
    wq = nc.dram_tensor("wq", [D, D], F32, kind="ExternalInput").ap()
    wk = nc.dram_tensor("wk", [D, D], F32, kind="ExternalInput").ap()
    wv = nc.dram_tensor("wv", [D, D], F32, kind="ExternalInput").ap()
    sel = nc.dram_tensor("sel", [128, 128], F32, kind="ExternalInput").ap()
    dq_in = nc.dram_tensor("dq_in", [NDQ * 128, 1024], F32, kind="ExternalInput").ap()

    y_q = nc.dram_tensor("y_q", [D, TOK], BF16, kind="ExternalOutput").ap()
    y_k = nc.dram_tensor("y_k", [D, TOK], BF16, kind="ExternalOutput").ap()
    y_v = nc.dram_tensor("y_v", [TOK, D], BF16, kind="ExternalOutput").ap()
    y_qn2 = nc.dram_tensor("y_qn2", [16, 16], F32, kind="ExternalOutput").ap()
    y_kn2 = nc.dram_tensor("y_kn2", [16, 16], F32, kind="ExternalOutput").ap()
    y_dq = nc.dram_tensor("y_dq", [NDQ * 128, 1024], BF16, kind="ExternalOutput").ap()

    xs = p.sbuf("xs", [128, 2 * TOK], F32)
    xb = p.sbuf("xb", [128, 8 * TOK], BF16)
    ws = p.sbuf("ws", [128, 2 * D], F32)
    wqb = p.sbuf("wqb", [128, 8 * D], BF16)
    wkb = p.sbuf("wkb", [128, 8 * D], BF16)
    wvb = p.sbuf("wvb", [128, 8 * D], BF16)
    ob = p.sbuf("ob", [128, 8 * 512], BF16)
    sels = p.sbuf("sels", [128, 128], F32)
    sq2 = p.sbuf("sq2", [128, 2 * 512], F32)
    n2t = p.sbuf("n2t", [16, 32], F32)
    dqw = p.sbuf("dqw", [128, 2 * 1024], F32)
    dqo = p.sbuf("dqo", [128, 2 * 1024], BF16)
    wi = p.sbuf("wi", [128, 2 * 1024], F32)
    bmax = p.sbuf("bmax", [128, 2 * 32], F32)
    bs = p.sbuf("bs", [128, 2 * 32], F32)
    brec = p.sbuf("brec", [128, 2 * 32], F32)
    ps0 = p.psum("ps0", [128, 512])
    ps1 = p.psum("ps1", [128, 512])
    pn = p.psum("pn", [16, 512])
    if True:
        n_sel = p.dma("sp", sels[:], sel)
        casts = {}
        prev_cast = {}
        for name, src_d, dstb, nch, w in (("x", xT, xb, 8, TOK),):
            for c in range(8):
                buf = c % 2
                deps = []
                if c >= 2:
                    deps.append(casts[("x", c - 2)])
                ld = p.dma("sp", xs[:, buf * TOK:(buf + 1) * TOK],
                           xT[c * 128:(c + 1) * 128, :], deps)
                casts[("x", c)] = vcopy(p, xb[:, c * TOK:(c + 1) * TOK],
                                        xs[:, buf * TOK:(buf + 1) * TOK], [ld])
        for nm, wsrc, dstb in (("wq", wq, wqb), ("wk", wk, wkb), ("wv", wv, wvb)):
            for c in range(8):
                buf = c % 2
                deps = []
                key = (nm, c - 2)
                if key in casts:
                    deps.append(casts[key])
                elif c < 2 and nm != "wq":
                    pk = {"wk": "wq", "wv": "wk"}[nm]
                    deps.append(casts[(pk, 6 + c)])
                ld = p.dma("pool", ws[:, buf * D:(buf + 1) * D],
                           wsrc[c * 128:(c + 1) * 128, :], deps)
                casts[(nm, c)] = vcopy(p, dstb[:, c * D:(c + 1) * D],
                                       ws[:, buf * D:(buf + 1) * D], [ld])
        cx = [casts[("x", c)] for c in range(8)]
        cwq = [casts[("wq", c)] for c in range(8)]
        cwk = [casts[("wk", c)] for c in range(8)]
        cwv = [casts[("wv", c)] for c in range(8)]

        ps = [ps0, ps1]
        use = 0
        prev_evac = {}
        prev_store = {}
        red_nodes = []
        sel_prev = None
        all_selmm = []
        for which, wsb, ydst, cw in ((0, wqb, y_q, cwq), (1, wkb, y_k, cwk)):
            for tc in range(4):
                sq_list = []
                for oc in range(8):
                    pt = ps[use % 2]
                    deps = cx + cw if use == 0 else [cx[-1], cw[-1]]
                    deps = list(deps)
                    if use >= 2:
                        deps.append(prev_evac[use - 2])
                    g = mm_group(p, pt[:, :],
                                 [(wsb[:, kc * D + oc * 128:kc * D + (oc + 1) * 128],
                                   xb[:, kc * TOK + tc * 512:kc * TOK + (tc + 1) * 512])
                                  for kc in range(8)], deps)
                    ring = use % 8
                    edeps = [g]
                    if use >= 8:
                        edeps.append(prev_store[use - 8])
                    ev = acopy(p, ob[:, ring * 512:(ring + 1) * 512], pt[:, :], edeps)
                    st = p.dma("sp", ydst[oc * 128:(oc + 1) * 128, tc * 512:(tc + 1) * 512],
                               ob[:, ring * 512:(ring + 1) * 512], [ev])
                    prev_evac[use] = ev
                    prev_store[use] = st
                    sqb = oc % 2
                    sdeps = [ev]
                    if len(all_selmm) >= 2:
                        sdeps.append(all_selmm[-2])
                    nsq = act(p, sq2[:, sqb * 512:(sqb + 1) * 512],
                              ob[:, ring * 512:(ring + 1) * 512], AF.Square, sdeps)
                    smdeps = [nsq, n_sel]
                    if sel_prev is not None:
                        smdeps.append(sel_prev)
                    sm = mm(p, pn[:, :], sels[:, oc * 16:(oc + 1) * 16],
                            sq2[:, sqb * 512:(sqb + 1) * 512], oc == 0, oc == 7, smdeps)
                    all_selmm.append(sm)
                    sel_prev = sm
                    sq_list.append(sm)
                    use += 1
                rd = vred(p, n2t[0:16, which * 16 + tc * 4: which * 16 + (tc + 1) * 4],
                          pn[0:16, :].rearrange("p (a b) -> p a b", b=128), ALU.max,
                          [sq_list[-1]] + red_nodes[-1:])
                sel_prev = rd
                red_nodes.append(rd)
        for tki in range(16):
            for vc in range(2):
                pt = ps[use % 2]
                deps = [cx[-1], cwv[-1]]
                if use >= 2:
                    deps.append(prev_evac[use - 2])
                g = mm_group(p, pt[:, :],
                             [(xb[:, kc * TOK + tki * 128: kc * TOK + (tki + 1) * 128],
                               wvb[:, kc * D + vc * 512: kc * D + (vc + 1) * 512])
                              for kc in range(8)], deps)
                ring = use % 8
                edeps = [g]
                if use >= 8:
                    edeps.append(prev_store[use - 8])
                ev = acopy(p, ob[:, ring * 512:(ring + 1) * 512], pt[:, :], edeps)
                st = p.dma("sp", y_v[tki * 128:(tki + 1) * 128, vc * 512:(vc + 1) * 512],
                           ob[:, ring * 512:(ring + 1) * 512], [ev])
                prev_evac[use] = ev
                prev_store[use] = st
                use += 1

        dq_stores = {}
        for ch in range(NDQ):
            bq = ch % 2
            o = bq * 1024
            obk = bq * 32
            ldeps = []
            if ch >= 2:
                ldeps.append(dq_stores[ch - 2])
            ld = p.dma("pool", dqw[:, o:o + 1024], dq_in[ch * 128:(ch + 1) * 128, :], ldeps)
            w3 = dqw[:, o:o + 1024].rearrange("p (b l) -> p b l", l=32)
            wi3 = wi[:, o:o + 1024].rearrange("p (b l) -> p b l", l=32)
            dqo3 = dqo[:, o:o + 1024].rearrange("p (b l) -> p b l", l=32)
            r1 = vred(p, bmax[:, obk:obk + 32], w3, ALU.max, [ld] + ldeps, absval=True)
            r2 = ts(p, bs[:, obk:obk + 32], bmax[:, obk:obk + 32], 1.0 / 7.0, 1e-12,
                    ALU.mult, ALU.max, [r1])
            r3 = vrecip(p, brec[:, obk:obk + 32], bs[:, obk:obk + 32], [r2])
            r4 = tt(p, wi3, w3,
                    brec[:, obk:obk + 32].rearrange("p (b o) -> p b o", o=1).broadcast_to([128, 32, 32]),
                    ALU.mult, [r3])
            r5 = ts(p, wi[:, o:o + 1024], wi[:, o:o + 1024], MAGIC, MAGIC, ALU.add, ALU.subtract, [r4])
            r6 = tt(p, dqo3, wi3,
                    bs[:, obk:obk + 32].rearrange("p (b o) -> p b o", o=1).broadcast_to([128, 32, 32]),
                    ALU.mult, [r5])
            dq_stores[ch] = p.dma("pool", y_dq[ch * 128:(ch + 1) * 128, :], dqo[:, o:o + 1024], [r6])

        s4 = p.dma("sp", y_qn2, n2t[0:16, 0:16], [red_nodes[3]])
        s5 = p.dma("sp", y_kn2, n2t[0:16, 16:32], [red_nodes[7]])
        sink(p, [s4, s5] + list(prev_store.values()) + list(dq_stores.values()))
        return p.build()



def build_attn():
    p = Prog("attn")
    nc = p.nc
    qT = nc.dram_tensor("qT", [D, TOK], BF16, kind="ExternalInput").ap()
    kT = nc.dram_tensor("kT", [D, S], BF16, kind="ExternalInput").ap()
    vo = nc.dram_tensor("vo", [S, NH * 65], BF16, kind="ExternalInput").ap()
    xT = nc.dram_tensor("xT", [D, TOK], F32, kind="ExternalInput").ap()
    wo = nc.dram_tensor("wo", [D, D], F32, kind="ExternalInput").ap()
    n2in = nc.dram_tensor("n2in", [16, 48], F32, kind="ExternalInput").ap()
    msk = nc.dram_tensor("msk", [128, 8 * 512], BF16, kind="ExternalInput").ap()
    onc = nc.dram_tensor("onc", [128, 1], F32, kind="ExternalInput").ap()
    onr = nc.dram_tensor("onr", [1, 128], F32, kind="ExternalInput").ap()
    idn = nc.dram_tensor("idn", [128, 128], F32, kind="ExternalInput").ap()
    scr = nc.dram_tensor("scr16", [16, 1], F32)

    y_x1 = nc.dram_tensor("y_x1", [D, TOK], F32, kind="ExternalOutput").ap()
    y_mx = nc.dram_tensor("y_mx", [1, 1], F32, kind="ExternalOutput").ap()

    qb = p.sbuf("qb", [128, 8 * TOK], BF16)
    kb = p.sbuf("kb", [128, 2 * S], BF16)
    vb = p.sbuf("vb", [128, 2 * (32 * 130)], BF16)
    att = p.sbuf("att", [128, 8 * TOK], BF16)
    msks = p.sbuf("msks", [128, 8 * 512], BF16)
    pt = p.sbuf("pt", [128, 4 * 512], BF16)
    n2s = p.sbuf("n2s", [16, 48], F32)
    cq = p.sbuf("cq", [16, 4], F32)
    negc = p.sbuf("negc", [1, 16], F32)
    negcb = p.sbuf("negcb", [128, 16], F32)
    rcp = p.sbuf("rcp", [1, 2 * 512], F32)
    tav = p.sbuf("tav", [64, 2 * 512], F32)
    onc_s = p.sbuf("onc_s", [128, 1], F32)
    onr_s = p.sbuf("onr_s", [1, 128], F32)
    idn_s = p.sbuf("idn_s", [128, 128], F32)
    wos = p.sbuf("wos", [128, 2 * D], F32)
    wob = p.sbuf("wob", [128, 8 * D], BF16)
    xrs = p.sbuf("xrs", [128, 2 * 512], F32)
    rsum = p.sbuf("rsum", [128, 8 * 512], F32)
    x1s = p.sbuf("x1s", [128, 8 * 512], F32)
    sq = p.sbuf("sqln", [128, 8 * 512], F32)
    lnA = p.sbuf("lnA", [65, 512], F32)
    rrstd = p.sbuf("rrstd", [1, 512], F32)
    rmr = p.sbuf("rmr", [1, 512], F32)
    mcol = p.sbuf("mcol", [128, 2], F32)
    mrow = p.sbuf("mrow", [1, 128], F32)

    pp0 = p.psum("pp0", [128, 1024])
    pp1 = p.psum("pp1", [128, 1024])
    avA = p.psum("avA", [128, 512])
    avB = p.psum("avB", [128, 512])
    pmx = p.psum("pmx", [128, 512])
    pbc = p.psum("pbc", [128, 16])

    if True:
        n_q = p.dma("sp", qb[:].rearrange("p (c t) -> p c t", t=TOK),
                    qT.rearrange("(c p) t -> p c t", p=128))
        n_m = p.dma("sp", msks[:], msk)
        n_n2 = p.dma("sp", n2s[:], n2in)
        n_onc = p.dma("sp", onc_s[:], onc)
        n_onr = p.dma("sp", onr_s[:], onr)
        n_idn = p.dma("sp", idn_s[:], idn)

        r_qm = vred(p, cq[0:16, 0:1], n2s[0:16, 0:16], ALU.max, [n_n2])
        r_km = vred(p, cq[0:16, 1:2], n2s[0:16, 16:48], ALU.max, [r_qm])
        a_sq = act(p, cq[0:16, 2:3], cq[0:16, 0:1], AF.Sqrt, [r_qm, r_km])
        a_sk = act(p, cq[0:16, 3:4], cq[0:16, 1:2], AF.Sqrt, [r_km])
        t_c = tt(p, cq[0:16, 0:1], cq[0:16, 2:3], cq[0:16, 3:4], ALU.mult, [a_sq, a_sk])
        t_b = ts(p, cq[0:16, 1:2], cq[0:16, 0:1], -0.1275, -0.00125, ALU.mult, ALU.add, [t_c])
        d_out = p.dma("sp", scr[:, :], cq[0:16, 1:2], [t_b])
        d_in = p.dma("sp", negc[:], scr.rearrange("a b -> b a"), [d_out])
        bc_mm = mm(p, pbc[:, 0:16], onr_s[0:1, :], negc[0:1, :], True, True, [d_in, n_onr])
        n_bias = acopy(p, negcb[:], pbc[:, 0:16], [bc_mm])

        exp_hist = {}
        av_hist = {}
        attn_done = []
        prev_postA = {}
        prev_postB = {}
        last_exp_on = {}
        last_av_on = {}
        gp = [0]
        for hp in range(8):
            dbl = hp % 2
            ldeps = []
            if hp >= 2:
                ldeps.append(exp_hist[(hp - 2, 3, "B")])
                ldeps.append(av_hist[(hp - 2, 3, "B")])
            lk = p.dma("pool", kb[:, dbl * S:(dbl + 1) * S], kT[hp * 128:(hp + 1) * 128, :], ldeps)
            lv = p.dma("pool", vb[:, dbl * 4160:(dbl + 1) * 4160]
                       .rearrange("p (c w) -> p c w", w=130),
                       vo[:, hp * 130:(hp + 1) * 130].rearrange("(c p) w -> p c w", p=128), ldeps)
            for qg in range(4):
                E = 8 * qg + 8
                post_queue = []
                for head, avp in (("A", avA), ("B", avB)):
                    hh = 2 * hp + (0 if head == "A" else 1)
                    base = 0 if head == "A" else 64
                    pmask = {}
                    pt_of = {}

                    def emit_av(kt, head=head, avp=avp, lv=lv, dbl=dbl,
                                hp=hp, qg=qg, E=E):
                        avdeps = [pmask[kt]]
                        if kt == 0:
                            avdeps.append(lv)
                            pp_ = prev_postA if head == "A" else prev_postB
                            if pp_:
                                avdeps.append(pp_["node"])
                        av = mm(p, avp[0:65, :],
                                vb[:, dbl * 4160 + kt * 130 + (0 if head == "A" else 65):
                                   dbl * 4160 + kt * 130 + (65 if head == "A" else 130)],
                                pt_of[kt], kt == 0, kt == E - 1, avdeps)
                        av_hist[(hp, qg, head, kt)] = av
                        return av

                    for kp in range(E // 2):
                        par = gp[0] % 2
                        ring = (pp0, pp1)[par]
                        kt0, kt1 = 2 * kp, 2 * kp + 1
                        sdeps = [lk, n_q]
                        if par in last_exp_on:
                            sdeps.append(last_exp_on[par])
                        smm0 = mm(p, ring[:, 0:512],
                                  kb[base:base + 64, dbl * S + kt0 * 128: dbl * S + (kt0 + 1) * 128],
                                  qb[base:base + 64, hp * TOK + qg * 512: hp * TOK + (qg + 1) * 512],
                                  True, True, sdeps)
                        smm1 = mm(p, ring[:, 512:1024],
                                  kb[base:base + 64, dbl * S + kt1 * 128: dbl * S + (kt1 + 1) * 128],
                                  qb[base:base + 64, hp * TOK + qg * 512: hp * TOK + (qg + 1) * 512],
                                  True, True, sdeps)
                        edeps = [smm0, smm1, n_bias]
                        if par in last_av_on:
                            edeps.append(last_av_on[par])
                        ex = act(p, pt[:, par * 1024:(par + 1) * 1024],
                                 ring[:, 0:1024], AF.Exp, edeps,
                                 bias=negcb[:, hh:hh + 1], scale=0.125)
                        exp_hist[(hp, qg, head)] = ex
                        last_exp_on[par] = ex
                        for kt in (kt0, kt1):
                            half = pt[:, par * 1024 + (kt & 1) * 512: par * 1024 + (kt & 1) * 512 + 512]
                            pt_of[kt] = half
                            lastn = ex
                            u = kt - (E - 8)
                            if u >= 0:
                                lastn = tt(p, half, half,
                                           msks[:, u * 512:(u + 1) * 512], ALU.mult, [ex, n_m])
                            pmask[kt] = lastn
                        if kp >= 1:
                            a0 = emit_av(kt0 - 2)
                            a1 = emit_av(kt1 - 2)
                            last_av_on[(gp[0] - 1) % 2] = a1
                        gp[0] += 1
                    a0 = emit_av(E - 2)
                    a1 = emit_av(E - 1)
                    last_av_on[(gp[0] - 1) % 2] = a1
                    av_hist[(hp, qg, head)] = a1
                    post_queue.append((head, avp))
                for head, avp in post_queue:
                    hh = 2 * hp + (0 if head == "A" else 1)
                    base = 0 if head == "A" else 64
                    fin = av_hist[(hp, qg, head)]
                    half = 0 if head == "A" else 1
                    rc = vrecip(p, rcp[0:1, half * 512:(half + 1) * 512],
                                avp[64:65, :], [fin])
                    bcdep = [rc]
                    if prev_postA:
                        bcdep.append(prev_postA["node"])
                    if prev_postB:
                        bcdep.append(prev_postB["node"])
                    bcm = mm(p, pmx[0 if head == "A" else 64:64 if head == "A" else 128, :],
                             onr_s[0:1, 0:64],
                             rcp[0:1, half * 512:(half + 1) * 512], True, True, bcdep)
                    ev = acopy(p, tav[0:64, half * 512:(half + 1) * 512], avp[0:64, :], [fin, rc])
                    ntt = tt(p, att[base:base + 64, hp * TOK + qg * 512: hp * TOK + (qg + 1) * 512],
                             tav[0:64, half * 512:(half + 1) * 512],
                             pmx[0 if head == "A" else 64:64 if head == "A" else 128, :],
                             ALU.mult, [bcm, ev])
                    if head == "A":
                        prev_postA = {"node": ntt}
                    else:
                        prev_postB = {"node": ntt}
                    attn_done.append(ntt)

        wo_casts = []
        for c in range(8):
            buf = c % 2
            deps = [wo_casts[c - 2]] if c >= 2 else []
            ld = p.dma("pool", wos[:, buf * D:(buf + 1) * D], wo[c * 128:(c + 1) * 128, :], deps)
            wo_casts.append(vcopy(p, wob[:, c * D:(c + 1) * D], wos[:, buf * D:(buf + 1) * D], [ld]))

        sc = {"ones_col": onc_s[:, 0:1], "ones_row": onr_s[0:1, :], "sq": sq,
              "stat": avA, "bc1": pp0[:, 512:1024], "bc2": pp1[:, 512:1024],
              "rmean": lnA[0:1, :], "rvar": lnA[32:33, :], "rtmp": lnA[64:65, :],
              "rrstd": rrstd[0:1, :], "rmr": rmr[0:1, :]}
        n_init = p.add("dve", lambda e: e.memset(mcol[:, 0:1], 0.0), [])
        stores = []
        prev_red = n_init
        ln_last = attn_done[-4:]
        for tc in range(4):
            res_nodes = []
            for oc in range(8):
                xsl = xrs[:, (oc % 2) * 512:(oc % 2 + 1) * 512]
                xwar = res_nodes[-2:-1] if len(res_nodes) >= 2 else (
                    ln_last if tc == 0 else [ln_last[-1]])
                xl = p.dma("pool", xsl,
                           xT[oc * 128:(oc + 1) * 128, tc * 512:(tc + 1) * 512], xwar)
                g = mm_group(p, pp0[:, 0:512] if oc % 2 == 0 else pp1[:, 0:512],
                             [(wob[:, kc * D + oc * 128: kc * D + (oc + 1) * 128],
                               att[:, kc * TOK + tc * 512: kc * TOK + (tc + 1) * 512])
                              for kc in range(8)],
                             attn_done + wo_casts + (res_nodes[-2:-1] if len(res_nodes) >= 2 else []))
                r = stt(p, rsum[:, oc * 512:(oc + 1) * 512], xsl, 2.0,
                        pp0[:, 0:512] if oc % 2 == 0 else pp1[:, 0:512],
                        ALU.mult, ALU.add, [g, xl])
                res_nodes.append(r)
            outs = emit_ln(p, lambda c: rsum[:, c * 512:(c + 1) * 512],
                           lambda c: x1s[:, c * 512:(c + 1) * 512], 512, sc, res_nodes)
            ln_last = outs
            for oc in range(8):
                stores.append(p.dma("sp", y_x1[oc * 128:(oc + 1) * 128, tc * 512:(tc + 1) * 512],
                                    x1s[:, oc * 512:(oc + 1) * 512], [outs[oc]]))
                rd = vred(p, mcol[:, 1:2], x1s[:, oc * 512:(oc + 1) * 512], ALU.max,
                          [outs[oc], prev_red], absval=True)
                prev_red = tt(p, mcol[:, 0:1], mcol[:, 0:1], mcol[:, 1:2], ALU.max, [rd])
        n_tp = p.add("pe", lambda e: e.transpose(pmx[0:1, 0:128], mcol[:, 0:1], idn_s[:, :]),
                     [prev_red, n_idn])
        n_ev = acopy(p, mrow[0:1, :], pmx[0:1, 0:128], [n_tp])
        n_fm = vred(p, mcol[0:1, 0:1], mrow[0:1, :], ALU.max, [n_ev])
        stores.append(p.dma("sp", y_mx, mcol[0:1, 0:1], [n_fm]))
        sink(p, stores)
    return p.build()



def tiles_of(h):
    return [8 * j + h + 2 * m for j in range(4) for m in range(4)]


def make_sel_qkv():
    sel = np.zeros((128, 128), np.float32)
    for oc in range(8):
        for r in range(128):
            head = (oc * 128 + r) // 64
            sel[r, oc * 16 + head] = 1.0
    return sel


def make_masks(h):
    m = np.zeros((128, 8, 4, 128), np.float32)
    tri = (np.arange(128)[:, None] <= np.arange(128)[None, :])
    for u in range(8):
        for sm in range(4):
            d = h + 2 * sm
            if u < d:
                m[:, u, sm, :] = 1.0
            elif u == d:
                m[:, u, sm, :] = tri
    return bf16c(m.reshape(128, 8 * 512))


def dq_pack(c, Wup0, Wg0, Wup1, Wg1, Wdn0, Wdn1):
    rows = []
    for W in (Wup0, Wg0, Wup1, Wg1):
        rows.append(W[c * 384:(c + 1) * 384, :])
    for W in (Wdn0, Wdn1):
        sl = W[c * 128:(c + 1) * 128, :]
        rows.append(sl.reshape(128, 3, 1024).transpose(1, 0, 2).reshape(384, 1024))
    return f32c(np.concatenate(rows, 0))


def dq_unpack(outs):
    ups0, gs0, ups1, gs1, dn0, dn1 = [], [], [], [], [], []
    for c in range(8):
        o = outs[c]
        ups0.append(o[0:384]); gs0.append(o[384:768])
        ups1.append(o[768:1152]); gs1.append(o[1152:1536])
        dn0.append(o[1536:1920].reshape(3, 128, 1024).transpose(1, 0, 2).reshape(128, 3072))
        dn1.append(o[1920:2304].reshape(3, 128, 1024).transpose(1, 0, 2).reshape(128, 3072))
    return (np.concatenate(ups0, 0), np.concatenate(gs0, 0),
            np.concatenate(ups1, 0), np.concatenate(gs1, 0),
            np.concatenate(dn0, 0), np.concatenate(dn1, 0))


ONES_COL = np.ones((128, 1), np.float32)
ONES_ROW = np.ones((1, 128), np.float32)
IDENT = np.eye(128, dtype=np.float32)


def run_qkv(x, Wq, Wk, Wv, dq_packs):
    nc = qkv_prog()
    in_maps = []
    sel = make_sel_qkv()
    for c in range(8):
        b, h = c // 2, c % 2
        in_maps.append({
            "xT": f32c(x[b, h * 2048:(h + 1) * 2048, :].T),
            "wq": f32c(Wq), "wk": f32c(Wk), "wv": f32c(Wv),
            "sel": sel, "dq_in": dq_packs[c],
        })
    return run_bass_kernel_spmd(nc, in_maps, core_ids=list(range(8))).results


def run_attn(x, qkv_res, Wo):
    nc = attn_prog()
    in_maps = []
    for c in range(8):
        b, h = c // 2, c % 2
        ca, cb = 2 * b, 2 * b + 1
        qf = np.concatenate([qkv_res[ca]["y_q"], qkv_res[cb]["y_q"]], axis=1)
        kf = np.concatenate([qkv_res[ca]["y_k"], qkv_res[cb]["y_k"]], axis=1)
        vf = np.concatenate([qkv_res[ca]["y_v"], qkv_res[cb]["y_v"]], axis=0)
        tl = tiles_of(h)
        cols = np.concatenate([np.arange(g * 128, (g + 1) * 128) for g in tl])
        vo = np.ones((S, NH * 65), bf16)
        for hh in range(NH):
            vo[:, hh * 65:hh * 65 + 64] = vf[:, hh * 64:(hh + 1) * 64]
        qn2f = np.concatenate([qkv_res[ca]["y_qn2"], qkv_res[cb]["y_qn2"]], axis=1)
        kn2f = np.concatenate([qkv_res[ca]["y_kn2"], qkv_res[cb]["y_kn2"]], axis=1)
        n2in = np.concatenate([qn2f[:, tl], kn2f], axis=1)
        in_maps.append({
            "qT": np.ascontiguousarray(qf[:, cols]),
            "kT": np.ascontiguousarray(kf),
            "vo": np.ascontiguousarray(vo),
            "xT": f32c(x[b].T[:, cols]),
            "wo": f32c(Wo),
            "n2in": f32c(n2in),
            "msk": make_masks(h),
            "onc": ONES_COL, "onr": ONES_ROW, "idn": IDENT,
        })
    return run_bass_kernel_spmd(nc, in_maps, core_ids=list(range(8))).results


def gather_interleaved(res, key):
    out = np.empty((B, S, D), np.float32)
    for c in range(8):
        b, h = c // 2, c % 2
        tl = tiles_of(h)
        cols = np.concatenate([np.arange(g * 128, (g + 1) * 128) for g in tl])
        out[b, cols, :] = res[c][key].T
    return out


_PROGS = {}


def qkv_prog():
    if "qkv" not in _PROGS:
        _PROGS["qkv"] = build_qkv()
    return _PROGS["qkv"]


def attn_prog():
    if "attn" not in _PROGS:
        _PROGS["attn"] = build_attn()
    return _PROGS["attn"]



def _emit_scales(p, nc, mx_ap, n_mx, srow, sxb, onr_ap, n_onr, pbc, specs):
    raise NotImplementedError


def build_ffn1():
    p = Prog("ffn1")
    nc = p.nc
    xT = nc.dram_tensor("xT", [D, TOK], F32, kind="ExternalInput").ap()
    mxs = nc.dram_tensor("mxs", [1, 8], F32, kind="ExternalInput").ap()
    wupT = nc.dram_tensor("wupT", [D, FI], BF16, kind="ExternalInput").ap()
    wgT = nc.dram_tensor("wgT", [D, FI], BF16, kind="ExternalInput").ap()
    onr = nc.dram_tensor("onr", [1, 128], F32, kind="ExternalInput").ap()
    idn = nc.dram_tensor("idn", [128, 128], F32, kind="ExternalInput").ap()
    y_up = nc.dram_tensor("y_up", [FI, TOK], F32, kind="ExternalOutput").ap()
    y_g = nc.dram_tensor("y_g", [FI, TOK], F32, kind="ExternalOutput").ap()
    y_m = nc.dram_tensor("y_m", [1, 2], F32, kind="ExternalOutput").ap()

    xs = p.sbuf("xs", [128, 2 * TOK], F32)
    xq = p.sbuf("xq", [128, 8 * TOK], BF16)
    tqs = p.sbuf("tqs", [128, TOK], F32)
    wtb = p.sbuf("wtb", [128, 2 * 1024], BF16)
    srow = p.sbuf("srow", [1, 2], F32)
    sxb = p.sbuf("sxb", [128, 2], F32)
    ring = p.sbuf("ring", [128, 4 * 512], F32)
    mc = p.sbuf("mc", [128, 4], F32)
    mrow = p.sbuf("mrow", [1, 128], F32)
    onr_s = p.sbuf("onr_s", [1, 128], F32)
    idn_s = p.sbuf("idn_s", [128, 128], F32)
    pq0 = p.psum("pq0", [128, 512])
    pq1 = p.psum("pq1", [128, 512])
    pq2 = p.psum("pq2", [128, 512])
    pq3 = p.psum("pq3", [128, 512])
    pbc = p.psum("pbc", [128, 2])
    pmx = p.psum("pmx", [1, 128])

    n_onr = p.dma("sp", onr_s[:], onr)
    n_idn = p.dma("sp", idn_s[:], idn)
    n_mx = p.dma("sp", mrow[0:1, 0:8], mxs)
    r_m = vred(p, srow[0:1, 0:1], mrow[0:1, 0:8], ALU.max, [n_mx])
    r_s = ts(p, srow[0:1, 0:1], srow[0:1, 0:1], 1.0 / 32767.0, 1e-12, ALU.mult, ALU.max, [r_m])
    r_r = vrecip(p, srow[0:1, 1:2], srow[0:1, 0:1], [r_s])
    n_bc = mm(p, pbc[:, 0:2], onr_s[0:1, :], srow[0:1, 0:2], True, True, [r_r, n_onr])
    n_sx = acopy(p, sxb[:], pbc[:, 0:2], [n_bc])

    xq_nodes = []
    for c in range(8):
        buf = c % 2
        deps = [xq_nodes[c - 2]] if c >= 2 else []
        ld = p.dma("sp", xs[:, buf * TOK:(buf + 1) * TOK], xT[c * 128:(c + 1) * 128, :], deps)
        t1 = p.add("dve", (lambda e, a=xs[:, buf * TOK:(buf + 1) * TOK], o=tqs[:]:
                           e.tensor_scalar(o, a, sxb[:, 1:2], MAGIC, ALU.mult, ALU.add)),
                   [ld, n_sx] + deps)
        t2 = p.add("dve", (lambda e, o=xq[:, c * TOK:(c + 1) * TOK], a=tqs[:]:
                           e.tensor_scalar(o, a, MAGIC, sxb[:, 0:1], ALU.subtract, ALU.mult)),
                   [t1])
        xq_nodes.append(t2)

    stores = []
    use = 0
    prev_ev = {}
    prev_rd = {}
    prev_red = None
    mzero = p.add("dve", lambda e: e.memset(mc[:, 0:2], 0.0), [])
    wcons = {}
    for mi, (wsrc, ydst) in enumerate(((wupT, y_up), (wgT, y_g))):
        for ic in range(24):
            wbuf = (mi * 24 + ic) % 2
            wdeps = []
            if (mi * 24 + ic) >= 2:
                wdeps.append(wcons[mi * 24 + ic - 2])
            wl = p.dma("pool", wtb[:, wbuf * 1024:(wbuf + 1) * 1024]
                       .rearrange("p (c w) -> p c w", w=128),
                       wsrc[:, ic * 128:(ic + 1) * 128].rearrange("(c p) w -> p c w", p=128),
                       wdeps)
            for tc in range(4):
                pt = (pq0, pq1, pq2, pq3)[use % 4]
                deps = [wl, xq_nodes[-1]]
                if use >= 4:
                    deps.append(prev_ev[use - 4])
                    deps.append(prev_rd[use - 4])
                g = mm_group(p, pt[:, :],
                             [(wtb[:, wbuf * 1024 + kc * 128:wbuf * 1024 + (kc + 1) * 128],
                               xq[:, kc * TOK + tc * 512:kc * TOK + (tc + 1) * 512])
                              for kc in range(8)], deps)
                rdeps = [g]
                if prev_red is not None:
                    rdeps.append(prev_red)
                rd = vred(p, mc[:, 2 + mi:3 + mi], pt[:, :], ALU.max, rdeps, absval=True)
                prev_rd[use] = rd
                mx = tt(p, mc[:, mi:mi + 1], mc[:, mi:mi + 1], mc[:, 2 + mi:3 + mi],
                        ALU.max, [rd, mzero])
                prev_red = mx
                rs = use % 4
                edeps = [g, rd]
                if use >= 4:
                    edeps.append(stores[use - 4])
                ev = acopy(p, ring[:, rs * 512:(rs + 1) * 512], pt[:, :], edeps)
                st = p.dma("sp", ydst[ic * 128:(ic + 1) * 128, tc * 512:(tc + 1) * 512],
                           ring[:, rs * 512:(rs + 1) * 512], [ev])
                prev_ev[use] = ev
                stores.append(st)
                use += 1
            wcons[mi * 24 + ic] = g
    n_tp = p.add("pe", lambda e: e.transpose(pmx[0:1, :], mc[:, 0:1], idn_s[:, :]),
                 [prev_red, n_idn])
    n_e1 = acopy(p, mrow[0:1, :], pmx[0:1, :], [n_tp])
    f1 = vred(p, mc[0:1, 2:3], mrow[0:1, :], ALU.max, [n_e1, prev_red])
    n_tp2 = p.add("pe", lambda e: e.transpose(pmx[0:1, :], mc[:, 1:2], idn_s[:, :]),
                  [f1, n_tp])
    n_e2 = acopy(p, mrow[0:1, :], pmx[0:1, :], [n_tp2, f1])
    f2 = vred(p, mc[0:1, 3:4], mrow[0:1, :], ALU.max, [n_e2, f1])
    st_m = p.dma("sp", y_m, mc[0:1, 2:4], [f2])
    sink(p, stores + [st_m])
    return p.build()


def build_ffn2():
    p = Prog("ffn2")
    nc = p.nc
    gT = nc.dram_tensor("gT", [FI, TOK], F32, kind="ExternalInput").ap()
    m16 = nc.dram_tensor("m16", [1, 16], F32, kind="ExternalInput").ap()
    onr = nc.dram_tensor("onr", [1, 128], F32, kind="ExternalInput").ap()
    idn = nc.dram_tensor("idn", [128, 128], F32, kind="ExternalInput").ap()
    y_gi = nc.dram_tensor("y_gi", [FI, TOK], I16, kind="ExternalOutput").ap()
    y_sr = nc.dram_tensor("y_sr", [FI, TOK], U16, kind="ExternalOutput").ap()
    y_m = nc.dram_tensor("y_m", [1, 1], F32, kind="ExternalOutput").ap()

    mr = p.sbuf("mr", [1, 16], F32)
    srow = p.sbuf("srow", [1, 4], F32)
    sgb = p.sbuf("sgb", [128, 4], F32)
    gld = p.sbuf("gld", [128, 4 * 512], F32)
    tf = p.sbuf("tf", [128, 2 * 512], F32)
    tf2 = p.sbuf("tf2", [128, 512], F32)
    tf3 = p.sbuf("tf3", [128, 512], F32)
    sg = p.sbuf("sg", [128, 2 * 512], F32)
    gi = p.sbuf("gi", [128, 4 * 512], I16)
    sr = p.sbuf("sr", [128, 4 * 512], U16)
    mc = p.sbuf("mc", [128, 2], F32)
    mrow = p.sbuf("mrow", [1, 128], F32)
    onr_s = p.sbuf("onr_s", [1, 128], F32)
    idn_s = p.sbuf("idn_s", [128, 128], F32)
    pbc = p.psum("pbc", [128, 4])
    pmx = p.psum("pmx", [1, 128])

    n_onr = p.dma("sp", onr_s[:], onr)
    n_idn = p.dma("sp", idn_s[:], idn)
    n_mx = p.dma("sp", mr[:], m16)
    r_m = vred(p, srow[0:1, 0:2], mr[0:1, :].rearrange("a (b two) -> a two b", two=2),
               ALU.max, [n_mx])
    r_s = ts(p, srow[0:1, 0:1], srow[0:1, 1:2], 1.0 / 32767.0, 1e-12, ALU.mult, ALU.max, [r_m])
    r_r = vrecip(p, srow[0:1, 1:2], srow[0:1, 0:1], [r_s])
    r_2 = ts(p, srow[0:1, 2:3], srow[0:1, 0:1], 1.0 / 65536.0, None, ALU.mult, ALU.bypass, [r_r])
    n_bc = mm(p, pbc[:, 0:4], onr_s[0:1, :], srow[0:1, 0:4], True, True, [r_2, n_onr])
    n_sx = acopy(p, sgb[:], pbc[:, 0:4], [n_bc])

    stores = []
    prev = {}
    prev_red = None
    mzero = p.add("dve", lambda e: e.memset(mc[:, 0:1], 0.0), [])
    st1s, st2s = {}, {}
    stage1 = {}

    def emit_s1(i):
        ic, tc = i // 4, i % 4
        bq = i % 4
        pb = i % 2
        ldeps = [prev[i - 4]] if i >= 4 else []
        ld = p.dma("sp", gld[:, bq * 512:(bq + 1) * 512],
                   gT[ic * 128:(ic + 1) * 128, tc * 512:(tc + 1) * 512], ldeps)
        t1deps = [ld, n_sx]
        if i >= 2:
            t1deps.append(stage1[i - 2]["t2"])
        t1 = p.add("dve", (lambda e, a=gld[:, bq * 512:(bq + 1) * 512], pb=pb:
                           e.tensor_scalar(tf[:, pb * 512:(pb + 1) * 512], a,
                                           sgb[:, 1:2], MAGIC, ALU.mult, ALU.add)),
                   t1deps)
        t2 = p.add("dve", (lambda e, o=gi[:, bq * 512:(bq + 1) * 512], pb=pb:
                           e.tensor_scalar(o, tf[:, pb * 512:(pb + 1) * 512], MAGIC, None,
                                           ALU.subtract, ALU.bypass)),
                   [t1] + ([st1s[i - 4]] if i >= 4 else []))
        st1 = p.dma("pool", y_gi[ic * 128:(ic + 1) * 128, tc * 512:(tc + 1) * 512],
                    gi[:, bq * 512:(bq + 1) * 512], [t2])
        sdep = [t2]
        if i >= 2:
            sdep.append(stage1[i - 2]["t3"])
        a_s = act(p, sg[:, pb * 512:(pb + 1) * 512], gi[:, bq * 512:(bq + 1) * 512],
                  AF.Sigmoid, sdep, scale=sgb[:, 0:1])
        st1s[i] = st1
        stage1[i] = {"t2": t2, "a_s": a_s, "st1": st1, "t3": None}

    def emit_s2(i, prev_red):
        ic, tc = i // 4, i % 4
        bq = i % 4
        pb = i % 2
        a_s = stage1[i]["a_s"]
        t3 = ts(p, tf2[:], sg[:, pb * 512:(pb + 1) * 512], 65536.0, MAGIC,
                ALU.mult, ALU.add, [a_s])
        stage1[i]["t3"] = t3
        t4 = p.add("dve", (lambda e, o=sr[:, bq * 512:(bq + 1) * 512]:
                           e.tensor_scalar(o, tf2[:], MAGIC, 65535.0, ALU.subtract, ALU.min)),
                   [t3] + ([st2s[i - 4]] if i >= 4 else []))
        st2 = p.dma("pool", y_sr[ic * 128:(ic + 1) * 128, tc * 512:(tc + 1) * 512],
                    sr[:, bq * 512:(bq + 1) * 512], [t4])
        g2 = p.add("dve", (lambda e, a=sr[:, bq * 512:(bq + 1) * 512],
                           b=gi[:, bq * 512:(bq + 1) * 512]:
                           e.scalar_tensor_tensor(tf3[:], a, sgb[:, 2:3], b, ALU.mult, ALU.mult)),
                   [t4])
        rdeps = [g2]
        if prev_red is not None:
            rdeps.append(prev_red)
        rd = vred(p, mc[:, 1:2], tf3[:], ALU.max, rdeps, absval=True)
        nred = tt(p, mc[:, 0:1], mc[:, 0:1], mc[:, 1:2], ALU.max, [rd, mzero])
        prev[i] = g2
        st2s[i] = st2
        stores.extend([stage1[i]["st1"], st2])
        return nred

    emit_s1(0)
    for i in range(96):
        if i + 1 < 96:
            emit_s1(i + 1)
        prev_red = emit_s2(i, prev_red)
    n_tp = p.add("pe", lambda e: e.transpose(pmx[0:1, :], mc[:, 0:1], idn_s[:, :]),
                 [prev_red, n_idn])
    n_e1 = acopy(p, mrow[0:1, :], pmx[0:1, :], [n_tp])
    f1 = vred(p, mc[0:1, 1:2], mrow[0:1, :], ALU.max, [n_e1, prev_red])
    st_m = p.dma("sp", y_m, mc[0:1, 1:2], [f1])
    sink(p, stores + [st_m])
    return p.build()


def build_ffn3():
    p = Prog("ffn3")
    nc = p.nc
    uT = nc.dram_tensor("uT", [FI, TOK], F32, kind="ExternalInput").ap()
    giT = nc.dram_tensor("giT", [FI, TOK], I16, kind="ExternalInput").ap()
    srT = nc.dram_tensor("srT", [FI, TOK], U16, kind="ExternalInput").ap()
    m24 = nc.dram_tensor("m24", [1, 24], F32, kind="ExternalInput").ap()
    onr = nc.dram_tensor("onr", [1, 128], F32, kind="ExternalInput").ap()
    idn = nc.dram_tensor("idn", [128, 128], F32, kind="ExternalInput").ap()
    y_ui = nc.dram_tensor("y_ui", [FI, TOK], I16, kind="ExternalOutput").ap()
    y_g2i = nc.dram_tensor("y_g2i", [FI, TOK], I16, kind="ExternalOutput").ap()
    y_m = nc.dram_tensor("y_m", [1, 1], F32, kind="ExternalOutput").ap()

    mr = p.sbuf("mr", [1, 24], F32)
    srow = p.sbuf("srow", [1, 8], F32)
    sgb = p.sbuf("sgb", [128, 8], F32)
    uld = p.sbuf("uld", [128, 4 * 512], F32)
    gild = p.sbuf("gild", [128, 4 * 512], I16)
    srld = p.sbuf("srld", [128, 4 * 512], U16)
    tf = p.sbuf("tf", [128, 512], F32)
    tf2 = p.sbuf("tf2", [128, 512], F32)
    tf3 = p.sbuf("tf3", [128, 512], F32)
    ui = p.sbuf("ui", [128, 4 * 512], I16)
    g2i = p.sbuf("g2i", [128, 4 * 512], I16)
    mc = p.sbuf("mc", [128, 2], F32)
    mrow = p.sbuf("mrow", [1, 128], F32)
    onr_s = p.sbuf("onr_s", [1, 128], F32)
    idn_s = p.sbuf("idn_s", [128, 128], F32)
    pbc = p.psum("pbc", [128, 8])
    pmx = p.psum("pmx", [1, 128])

    n_onr = p.dma("sp", onr_s[:], onr)
    n_idn = p.dma("sp", idn_s[:], idn)
    n_mx = p.dma("sp", mr[:], m24)
    r_m = vred(p, srow[0:1, 0:3], mr[0:1, :].rearrange("a (b three) -> a three b", three=3),
               ALU.max, [n_mx])
    r1 = ts(p, srow[0:1, 3:4], srow[0:1, 0:1], 1.0 / 32767.0, 1e-12, ALU.mult, ALU.max, [r_m])
    r2 = ts(p, srow[0:1, 5:6], srow[0:1, 1:2], 1.0 / 32767.0, 1e-12, ALU.mult, ALU.max, [r1])
    r3 = ts(p, srow[0:1, 6:7], srow[0:1, 2:3], 1.0 / 32767.0, 1e-12, ALU.mult, ALU.max, [r2])
    r4 = vrecip(p, srow[0:1, 4:5], srow[0:1, 3:4], [r3])
    r5 = vrecip(p, srow[0:1, 7:8], srow[0:1, 6:7], [r4])
    r6 = ts(p, srow[0:1, 0:1], srow[0:1, 5:6], 1.0 / 65536.0, None, ALU.mult, ALU.bypass, [r5])
    r7 = tt(p, srow[0:1, 1:2], srow[0:1, 6:7], srow[0:1, 3:4], ALU.mult, [r6])
    n_bc = mm(p, pbc[:, 0:8], onr_s[0:1, :], srow[0:1, 0:8], True, True, [r7, n_onr])
    n_sx = acopy(p, sgb[:], pbc[:, 0:8], [n_bc])

    stores = []
    prev = {}
    prev_red = None
    mzero = p.add("dve", lambda e: e.memset(mc[:, 0:1], 0.0), [])
    s1s, s2s = {}, {}
    for i in range(96):
        ic, tc = i // 4, i % 4
        bq = i % 4
        sl = (bq * 512, (bq + 1) * 512)
        ldeps = [prev[i - 4]] if i >= 4 else []
        l1 = p.dma("pool", uld[:, sl[0]:sl[1]],
                   uT[ic * 128:(ic + 1) * 128, tc * 512:(tc + 1) * 512], ldeps)
        l2 = p.dma("pool", gild[:, sl[0]:sl[1]],
                   giT[ic * 128:(ic + 1) * 128, tc * 512:(tc + 1) * 512], ldeps)
        l3 = p.dma("pool", srld[:, sl[0]:sl[1]],
                   srT[ic * 128:(ic + 1) * 128, tc * 512:(tc + 1) * 512], ldeps)
        t1 = p.add("dve", (lambda e, a=uld[:, sl[0]:sl[1]]:
                           e.tensor_scalar(tf[:], a, sgb[:, 4:5], MAGIC, ALU.mult, ALU.add)),
                   [l1, n_sx])
        t2 = p.add("dve", (lambda e, o=ui[:, sl[0]:sl[1]]:
                           e.tensor_scalar(o, tf[:], MAGIC, None, ALU.subtract, ALU.bypass)),
                   [t1] + ([s1s[i - 4]] if i >= 4 else []))
        s1 = p.dma("sp", y_ui[ic * 128:(ic + 1) * 128, tc * 512:(tc + 1) * 512],
                   ui[:, sl[0]:sl[1]], [t2])
        g2 = p.add("dve", (lambda e, a=srld[:, sl[0]:sl[1]], b=gild[:, sl[0]:sl[1]]:
                           e.scalar_tensor_tensor(tf2[:], a, sgb[:, 0:1], b, ALU.mult, ALU.mult)),
                   [l2, l3, n_sx])
        t3 = p.add("dve", (lambda e:
                           e.tensor_scalar(tf[:], tf2[:], sgb[:, 7:8], MAGIC, ALU.mult, ALU.add)),
                   [g2, t2])
        t4 = p.add("dve", (lambda e, o=g2i[:, sl[0]:sl[1]]:
                           e.tensor_scalar(o, tf[:], MAGIC, None, ALU.subtract, ALU.bypass)),
                   [t3] + ([s2s[i - 4]] if i >= 4 else []))
        s2 = p.dma("sp", y_g2i[ic * 128:(ic + 1) * 128, tc * 512:(tc + 1) * 512],
                   g2i[:, sl[0]:sl[1]], [t4])
        oo = p.add("dve", (lambda e, a=g2i[:, sl[0]:sl[1]], b=ui[:, sl[0]:sl[1]]:
                           e.scalar_tensor_tensor(tf3[:], a, sgb[:, 1:2], b, ALU.mult, ALU.mult)),
                   [t4])
        rdeps = [oo]
        if prev_red is not None:
            rdeps.append(prev_red)
        rd = vred(p, mc[:, 1:2], tf3[:], ALU.max, rdeps, absval=True)
        prev_red = tt(p, mc[:, 0:1], mc[:, 0:1], mc[:, 1:2], ALU.max, [rd, mzero])
        prev[i] = oo
        s1s[i], s2s[i] = s1, s2
        stores += [s1, s2]
    n_tp = p.add("pe", lambda e: e.transpose(pmx[0:1, :], mc[:, 0:1], idn_s[:, :]),
                 [prev_red, n_idn])
    n_e1 = acopy(p, mrow[0:1, :], pmx[0:1, :], [n_tp])
    f1 = vred(p, mc[0:1, 1:2], mrow[0:1, :], ALU.max, [n_e1, prev_red])
    st_m = p.dma("sp", y_m, mc[0:1, 1:2], [f1])
    sink(p, stores + [st_m])
    return p.build()


def build_ffn4():
    p = Prog("ffn4")
    nc = p.nc
    giT = nc.dram_tensor("giT", [FI, TOK], I16, kind="ExternalInput").ap()
    uiT = nc.dram_tensor("uiT", [FI, TOK], I16, kind="ExternalInput").ap()
    m32 = nc.dram_tensor("m32", [1, 32], F32, kind="ExternalInput").ap()
    wdT = nc.dram_tensor("wdT", [FI, D], BF16, kind="ExternalInput").ap()
    xT = nc.dram_tensor("xT", [D, TOK], F32, kind="ExternalInput").ap()
    onc = nc.dram_tensor("onc", [128, 1], F32, kind="ExternalInput").ap()
    onr = nc.dram_tensor("onr", [1, 128], F32, kind="ExternalInput").ap()
    idn = nc.dram_tensor("idn", [128, 128], F32, kind="ExternalInput").ap()
    y_x2 = nc.dram_tensor("y_x2", [D, TOK], F32, kind="ExternalOutput").ap()
    y_mx = nc.dram_tensor("y_mx", [1, 1], F32, kind="ExternalOutput").ap()

    mr = p.sbuf("mr", [1, 32], F32)
    srow = p.sbuf("srow", [1, 8], F32)
    sgb = p.sbuf("sgb", [128, 8], F32)
    gild = p.sbuf("gild", [128, 4 * 512], I16)
    uild = p.sbuf("uild", [128, 4 * 512], I16)
    tf = p.sbuf("tf", [128, 512], F32)
    tf2 = p.sbuf("tf2", [128, 512], F32)
    obf = p.sbuf("obf", [128, 24 * 512], BF16)
    wdb = p.sbuf("wdb", [128, 2 * 24 * 128], BF16)
    xrs = p.sbuf("xrs", [128, 2 * 512], F32)
    rsum = p.sbuf("rsum", [128, 8 * 512], F32)
    x1s = p.sbuf("x1s", [128, 8 * 512], F32)
    sq = p.sbuf("sqln", [128, 8 * 512], F32)
    lnA = p.sbuf("lnA", [65, 512], F32)
    rrstd = p.sbuf("rrstd", [1, 512], F32)
    rmr = p.sbuf("rmr", [1, 512], F32)
    mc = p.sbuf("mc", [128, 2], F32)
    mrow = p.sbuf("mrow", [1, 128], F32)
    onc_s = p.sbuf("onc_s", [128, 1], F32)
    onr_s = p.sbuf("onr_s", [1, 128], F32)
    idn_s = p.sbuf("idn_s", [128, 128], F32)
    pq0 = p.psum("pq0", [128, 512])
    pq1 = p.psum("pq1", [128, 512])
    pst = p.psum("pst", [64, 512])
    pb1 = p.psum("pb1", [128, 512])
    pb2 = p.psum("pb2", [128, 512])
    pbc = p.psum("pbc", [128, 8])
    pmx = p.psum("pmx", [1, 128])

    n_onc = p.dma("sp", onc_s[:], onc)
    n_onr = p.dma("sp", onr_s[:], onr)
    n_idn = p.dma("sp", idn_s[:], idn)
    n_mx = p.dma("sp", mr[:], m32)
    r_m = vred(p, srow[0:1, 0:4], mr[0:1, :].rearrange("a (b four) -> a four b", four=4),
               ALU.max, [n_mx])
    r1 = ts(p, srow[0:1, 4:5], srow[0:1, 0:1], 1.0 / 32767.0, 1e-12, ALU.mult, ALU.max, [r_m])
    r2 = ts(p, srow[0:1, 5:6], srow[0:1, 2:3], 1.0 / 32767.0, 1e-12, ALU.mult, ALU.max, [r1])
    r3 = ts(p, srow[0:1, 6:7], srow[0:1, 3:4], 1.0 / 32767.0, 1e-12, ALU.mult, ALU.max, [r2])
    r4 = tt(p, srow[0:1, 0:1], srow[0:1, 5:6], srow[0:1, 4:5], ALU.mult, [r3])
    r5 = vrecip(p, srow[0:1, 1:2], srow[0:1, 6:7], [r4])
    n_bc = mm(p, pbc[:, 0:8], onr_s[0:1, :], srow[0:1, 0:8], True, True, [r5, n_onr])
    n_sx = acopy(p, sgb[:], pbc[:, 0:8], [n_bc])

    sc = {"ones_col": onc_s[:, 0:1], "ones_row": onr_s[0:1, :], "sq": sq,
          "stat": pst, "bc1": pb1, "bc2": pb2,
          "rmean": lnA[0:1, :], "rvar": lnA[32:33, :], "rtmp": lnA[64:65, :],
          "rrstd": rrstd[0:1, :], "rmr": rmr[0:1, :]}
    stores = []
    prev_o = {}
    prev_red = None
    mzero = p.add("dve", lambda e: e.memset(mc[:, 0:1], 0.0), [])
    ln_prev = None
    wl_prev = {}
    res_prev = {}
    for tc in range(4):
        o_nodes = []
        for ic in range(24):
            i = tc * 24 + ic
            bq = i % 4
            sl = (bq * 512, (bq + 1) * 512)
            ldeps = [prev_o[i - 4]] if i >= 4 else []
            l1 = p.dma("pool", gild[:, sl[0]:sl[1]],
                       giT[ic * 128:(ic + 1) * 128, tc * 512:(tc + 1) * 512], ldeps)
            l2 = p.dma("pool", uild[:, sl[0]:sl[1]],
                       uiT[ic * 128:(ic + 1) * 128, tc * 512:(tc + 1) * 512], ldeps)
            oo = p.add("dve", (lambda e, a=gild[:, sl[0]:sl[1]], b=uild[:, sl[0]:sl[1]]:
                               e.scalar_tensor_tensor(tf[:], a, sgb[:, 0:1], b, ALU.mult, ALU.mult)),
                       [l1, l2, n_sx])
            t1 = p.add("dve", (lambda e:
                               e.tensor_scalar(tf2[:], tf[:], sgb[:, 1:2], MAGIC, ALU.mult, ALU.add)),
                       [oo])
            odeps = [t1]
            if tc > 0:
                odeps.append(res_prev[(tc - 1, 7)])
            t2 = p.add("dve", (lambda e, o=obf[:, ic * 512:(ic + 1) * 512]:
                               e.tensor_scalar(o, tf2[:], MAGIC, sgb[:, 6:7], ALU.subtract, ALU.mult)),
                       odeps)
            prev_o[i] = t2
            o_nodes.append(t2)
        res_nodes = []
        for oc in range(8):
            j = tc * 8 + oc
            wbuf = j % 2
            wdeps = [wl_prev[j - 2]] if j >= 2 else []
            wl = p.dma("pool", wdb[:, wbuf * 3072:(wbuf + 1) * 3072]
                       .rearrange("p (c w) -> p c w", w=128),
                       wdT[:, oc * 128:(oc + 1) * 128].rearrange("(c p) w -> p c w", p=128),
                       wdeps)
            pt = (pq0, pq1)[oc % 2]
            gdeps = o_nodes + [wl]
            if len(res_nodes) >= 2:
                gdeps.append(res_nodes[-2])
            elif tc > 0:
                gdeps.append(res_prev[(tc - 1, 6 + oc)] if (6 + oc) <= 7 else None)
            g = mm_group(p, pt[:, :],
                         [(wdb[:, wbuf * 3072 + kc * 128:wbuf * 3072 + (kc + 1) * 128],
                           obf[:, kc * 512:(kc + 1) * 512])
                          for kc in range(24)], [dd for dd in gdeps if dd])
            xsl = xrs[:, (oc % 2) * 512:(oc % 2 + 1) * 512]
            xwar = res_nodes[-2:-1] if len(res_nodes) >= 2 else (
                [ln_prev] if ln_prev is not None else [])
            xl = p.dma("pool", xsl, xT[oc * 128:(oc + 1) * 128, tc * 512:(tc + 1) * 512], xwar)
            r = tt(p, rsum[:, oc * 512:(oc + 1) * 512], pt[:, :], xsl, ALU.add, [g, xl])
            res_nodes.append(r)
            res_prev[(tc, oc)] = r
            wl_prev[j] = g
        outs = emit_ln(p, lambda c: rsum[:, c * 512:(c + 1) * 512],
                       lambda c: x1s[:, c * 512:(c + 1) * 512], 512, sc, res_nodes)
        ln_prev = outs[-1]
        for oc in range(8):
            stores.append(p.dma("sp", y_x2[oc * 128:(oc + 1) * 128, tc * 512:(tc + 1) * 512],
                                x1s[:, oc * 512:(oc + 1) * 512], [outs[oc]]))
            rdeps = [outs[oc], mzero]
            if prev_red is not None:
                rdeps.append(prev_red)
            rd = vred(p, mc[:, 1:2], x1s[:, oc * 512:(oc + 1) * 512], ALU.max, rdeps, absval=True)
            prev_red = tt(p, mc[:, 0:1], mc[:, 0:1], mc[:, 1:2], ALU.max, [rd])
    n_tp = p.add("pe", lambda e: e.transpose(pmx[0:1, :], mc[:, 0:1], idn_s[:, :]),
                 [prev_red, n_idn])
    n_e1 = acopy(p, mrow[0:1, :], pmx[0:1, :], [n_tp])
    f1 = vred(p, mc[0:1, 1:2], mrow[0:1, :], ALU.max, [n_e1, prev_red])
    st_m = p.dma("sp", y_mx, mc[0:1, 1:2], [f1])
    sink(p, stores + [st_m])
    return p.build()


def run_ffn(x1, mx_list, wupT_bf, wgT_bf, wdnT_bf):
    def xmaps():
        ms = []
        for c in range(8):
            b, h = c // 2, c % 2
            ms.append(f32c(x1[b, h * 2048:(h + 1) * 2048, :].T))
        return ms

    xTs = xmaps()
    mxs = np.concatenate([np.asarray(m).reshape(1, 1) for m in mx_list], axis=1)
    p1 = run_bass_kernel_spmd(ffn1_prog(), [
        {"xT": xTs[c], "mxs": f32c(mxs), "wupT": wupT_bf, "wgT": wgT_bf,
         "onr": ONES_ROW, "idn": IDENT} for c in range(8)],
        core_ids=list(range(8))).results
    m16 = np.concatenate([p1[c]["y_m"] for c in range(8)], axis=1)
    p2 = run_bass_kernel_spmd(ffn2_prog(), [
        {"gT": p1[c]["y_g"], "m16": f32c(m16), "onr": ONES_ROW, "idn": IDENT}
        for c in range(8)], core_ids=list(range(8))).results
    m24 = np.concatenate([np.concatenate([p1[c]["y_m"], p2[c]["y_m"]], axis=1)
                          for c in range(8)], axis=1)
    p3 = run_bass_kernel_spmd(ffn3_prog(), [
        {"uT": p1[c]["y_up"], "giT": p2[c]["y_gi"], "srT": p2[c]["y_sr"],
         "m24": f32c(m24), "onr": ONES_ROW, "idn": IDENT}
        for c in range(8)], core_ids=list(range(8))).results
    m32 = np.concatenate([np.concatenate([p1[c]["y_m"], p2[c]["y_m"], p3[c]["y_m"]], axis=1)
                          for c in range(8)], axis=1)
    p4 = run_bass_kernel_spmd(ffn4_prog(), [
        {"giT": p3[c]["y_g2i"], "uiT": p3[c]["y_ui"], "m32": f32c(m32),
         "wdT": wdnT_bf, "xT": xTs[c],
         "onc": ONES_COL, "onr": ONES_ROW, "idn": IDENT}
        for c in range(8)], core_ids=list(range(8))).results
    x2 = np.empty((B, S, D), np.float32)
    for c in range(8):
        b, h = c // 2, c % 2
        x2[b, h * 2048:(h + 1) * 2048, :] = p4[c]["y_x2"].T
    return x2, [p4[c]["y_mx"] for c in range(8)]


def ffn1_prog():
    if "ffn1" not in _PROGS:
        _PROGS["ffn1"] = build_ffn1()
    return _PROGS["ffn1"]


def ffn2_prog():
    if "ffn2" not in _PROGS:
        _PROGS["ffn2"] = build_ffn2()
    return _PROGS["ffn2"]


def ffn3_prog():
    if "ffn3" not in _PROGS:
        _PROGS["ffn3"] = build_ffn3()
    return _PROGS["ffn3"]


def ffn4_prog():
    if "ffn4" not in _PROGS:
        _PROGS["ffn4"] = build_ffn4()
    return _PROGS["ffn4"]


XPD = 4224
DI = 1024
DM = 512


def build_mamba(debug=False):
    p = Prog("mamba")
    nc = p.nc
    xdT = nc.dram_tensor("xdT", [DM, XPD], F32, kind="ExternalInput").ap()
    inwT = nc.dram_tensor("inwT", [DM, 2 * DI], F32, kind="ExternalInput").ap()
    xpjT = nc.dram_tensor("xpjT", [DI, 64], F32, kind="ExternalInput").ap()
    dtwT = nc.dram_tensor("dtwT", [32, DI], F32, kind="ExternalInput").ap()
    outwT = nc.dram_tensor("outwT", [DI, DM], F32, kind="ExternalInput").ap()
    convW = nc.dram_tensor("convW", [128, 32], F32, kind="ExternalInput").ap()
    convB = nc.dram_tensor("convB", [128, 8], F32, kind="ExternalInput").ap()
    dtB = nc.dram_tensor("dtB", [128, 8], F32, kind="ExternalInput").ap()
    DPc = nc.dram_tensor("DPc", [128, 8], F32, kind="ExternalInput").ap()
    AL = nc.dram_tensor("AL", [128, 128], F32, kind="ExternalInput").ap()
    esel = nc.dram_tensor("esel", [80, 2048], F32, kind="ExternalInput").ap()
    idnb = nc.dram_tensor("idnb", [128, 128], BF16, kind="ExternalInput").ap()
    y_h = nc.dram_tensor("y_h", [DM, 4096], F32, kind="ExternalOutput").ap()
    kind = dict(kind="ExternalOutput") if debug else {}
    xcbuf = nc.dram_tensor("xcbuf", [DI, 4096], F32, **kind)
    zbuf = nc.dram_tensor("zbuf", [DI, XPD], F32, **kind)
    if debug:
        y_dbc = nc.dram_tensor("y_dbc", [80, 4096], F32, kind="ExternalOutput").ap()
        y_del = nc.dram_tensor("y_del", [128, 8 * 512], F32, kind="ExternalOutput").ap()
        y_B = nc.dram_tensor("y_B", [128, 16 * 512], BF16, kind="ExternalOutput").ap()
        y_yf = nc.dram_tensor("y_yf", [128, 8 * 512], BF16, kind="ExternalOutput").ap()
        y_st = nc.dram_tensor("y_st", [128, 128], F32, kind="ExternalOutput").ap()

    xdb = p.sbuf("xdb", [128, 4 * XPD], BF16)
    inwb = p.sbuf("inwb", [128, 4 * 2048], BF16)
    xpjb = p.sbuf("xpjb", [128, 8 * 64], BF16)
    dtwb = p.sbuf("dtwb", [32, 1024], BF16)
    outwb = p.sbuf("outwb", [128, 8 * 512], BF16)
    cW = p.sbuf("cW", [128, 32], F32)
    cB = p.sbuf("cB", [128, 8], F32)
    dB = p.sbuf("dB", [128, 8], F32)
    dP = p.sbuf("dP", [128, 8], F32)
    Acol = p.sbuf("Acol", [128, 128], F32)
    esl = p.sbuf("esl", [80, 2048], F32)
    idb = p.sbuf("idb", [128, 128], BF16)
    xh = p.sbuf("xh", [128, XPD], F32)
    xcc = p.sbuf("xcc", [128, 4096], F32)
    zev = p.sbuf("zev", [128, 512], F32)
    xcw = p.sbuf("xcw", [128, 8 * 512], F32)
    xcwb = p.sbuf("xcwb", [128, 2 * 512], BF16)
    dtb16 = p.sbuf("dtb16", [32, 512], BF16)
    dbcs = p.sbuf("dbcs", [80, 512], F32)
    dels = p.sbuf("dels", [128, 8 * 512], F32)
    us = p.sbuf("us", [128, 8 * 512], BF16)
    ta = p.sbuf("ta", [128, 512], F32)
    tb = p.sbuf("tb", [128, 512], F32)
    Gt = p.sbuf("Gt", [128, 4 * 512], F32)
    vt = p.sbuf("vt", [128, 4 * 512], BF16)
    ht = p.sbuf("ht", [128, 2 * 512], BF16)
    hct = p.sbuf("hct", [128, 2 * 512], BF16)
    stcols = p.sbuf("stcols", [128, 128], F32)
    sz = p.sbuf("sz", [128, 512], F32)
    yf = p.sbuf("yf", [128, 8 * 512], BF16)
    hor = p.sbuf("hor", [128, 2 * 512], F32)
    Bbc = xh[:].bitcast(BF16)
    Cbc = xcc[:].bitcast(BF16)
    pin0 = p.psum("pin0", [128, 512])
    pin1 = p.psum("pin1", [128, 512])
    pdbc = p.psum("pdbc", [128, 512])
    pdel = p.psum("pdel", [128, 512])
    pBC = p.psum("pBC", [128, 512])
    py0 = p.psum("py0", [128, 512])
    py1 = p.psum("py1", [128, 512])
    pho = p.psum("pho", [128, 512])

    l_cw = p.dma("sp", cW[:], convW)
    l_cb = p.dma("sp", cB[:], convB)
    l_db = p.dma("sp", dB[:], dtB)
    l_dp = p.dma("sp", dP[:], DPc)
    l_al = p.dma("sp", Acol[:], AL)
    l_es = p.dma("sp", esl[:], esel)
    l_id = p.dma("sp", idb[:], idnb)
    n_A1 = act(p, Acol[:], Acol[:], AF.Exp, [l_al])
    n_A = ts(p, Acol[:], Acol[:], -1.0, None, ALU.mult, ALU.bypass, [n_A1])
    casts = []

    def staged(dst, src_ap, width):
        i = len(casts)
        use_xh = (width > 4096) or (i % 2 == 0)
        sbuf_st = xh[:, 0:width] if use_xh else xcc[:, 0:width]
        dep = [c for c in casts[-2:]]
        ld = p.dma("sp" if i % 2 == 0 else "pool", sbuf_st, src_ap, dep)
        casts.append(vcopy(p, dst, sbuf_st, [ld]))
    for c in range(4):
        staged(xdb[:, c * XPD:(c + 1) * XPD], xdT[c * 128:(c + 1) * 128, :], XPD)
    for c in range(4):
        staged(inwb[:, c * 2048:(c + 1) * 2048], inwT[c * 128:(c + 1) * 128, :], 2048)
    xd_done = casts[3]
    inw_done = casts[7]
    phase1_cast_done = casts[-1]
    wcasts = []

    def staged2(dst, src_ap, width, slot):
        dep = [wcasts[-2]] if len(wcasts) >= 2 else [casts[-1]]
        st_ap = xcw[:, slot * 512: slot * 512 + width]
        ld = p.dma("pool", st_ap, src_ap, dep)
        wcasts.append(vcopy(p, dst, st_ap, [ld]))
    for c in range(8):
        staged2(xpjb[:, c * 64:(c + 1) * 64], xpjT[c * 128:(c + 1) * 128, :], 64, c % 2)
    ldw = p.dma("pool", xcw[0:32, 1024:2048], dtwT, [wcasts[-2]])
    wcasts.append(vcopy(p, dtwb[:], xcw[0:32, 1024:2048], [ldw]))
    for c in range(8):
        staged2(outwb[:, c * 512:(c + 1) * 512], outwT[c * 128:(c + 1) * 128, :], 512, 4 + c % 2)
    cast_done = wcasts[-1]

    pins = (pin0, pin1)
    use = 0
    prev_ev = {}
    conv_last = None
    xc_stores = []
    z_stores = []
    for ec in range(16):
        is_xh = ec < 8
        evs = []
        for t9 in range(9):
            w9 = 512 if t9 < 8 else XPD - 8 * 512
            pt = pins[use % 2]
            deps = [inw_done, xd_done]
            if use >= 2:
                deps.append(prev_ev[use - 2])
            g = mm_group(p, pt[:, 0:w9],
                         [(inwb[:, kc * 2048 + ec * 128:kc * 2048 + (ec + 1) * 128],
                           xdb[:, kc * XPD + t9 * 512:kc * XPD + t9 * 512 + w9])
                          for kc in range(4)], deps)
            if is_xh:
                ev = acopy(p, xh[:, t9 * 512:t9 * 512 + w9], pt[:, 0:w9],
                           [g, conv_last, phase1_cast_done])
            else:
                edeps = [g]
                if z_stores:
                    edeps.append(z_stores[-1])
                ev = acopy(p, zev[:, 0:w9], pt[:, 0:w9], edeps)
                z_stores.append(p.dma("pool", zbuf[(ec - 8) * 128:(ec - 7) * 128, t9 * 512:t9 * 512 + w9],
                                      zev[:, 0:w9], [ev]))
            prev_ev[use] = ev
            evs.append(ev)
            use += 1
        if is_xh:
            c0 = p.add("dve", (lambda e, c=ec: e.tensor_scalar(
                xcc[:], xh[:, 0:4096], cW[:, c * 4:c * 4 + 1], None, ALU.mult, ALU.bypass)),
                evs + [phase1_cast_done] + ([conv_last] if conv_last else []))
            c1 = p.add("dve", (lambda e, c=ec: e.scalar_tensor_tensor(
                xcc[:], xh[:, 1:4097], cW[:, c * 4 + 1:c * 4 + 2], xcc[:], ALU.mult, ALU.add)), [c0])
            c2 = p.add("dve", (lambda e, c=ec: e.scalar_tensor_tensor(
                xcc[:], xh[:, 2:4098], cW[:, c * 4 + 2:c * 4 + 3], xcc[:], ALU.mult, ALU.add)), [c1])
            c3 = p.add("dve", (lambda e, c=ec: e.scalar_tensor_tensor(
                xcc[:], xh[:, 3:4099], cW[:, c * 4 + 3:c * 4 + 4], xcc[:], ALU.mult, ALU.add)), [c2])
            c4 = act(p, xcc[:], xcc[:], AF.Silu, [c3, l_cb], bias=cB[:, ec:ec + 1])
            st = p.dma("pool", xcbuf[ec * 128:(ec + 1) * 128, :], xcc[:], [c4])
            xc_stores.append(st)
            conv_last = st

    n_z0 = p.add("dve", lambda e: e.memset(stcols[:], 0.0), [])
    sdel_done = {}
    prev_out = []
    prev_gate = {}
    prev_hos = {}
    prev_xcl = {}
    prev_dtev = None
    prev_scan = {}
    prev_hc = {}
    prev_ymm = {}
    prev_rep = {}
    h_stores = []
    for tw in range(8):
        tcols = (3 + tw * 512, 3 + (tw + 1) * 512)
        xcl = []
        for c in range(8):
            dep = [prev_gate[(tw - 1, c)]] if tw > 0 else [xc_stores[c], cast_done]
            if tw > 0:
                dep.append(xc_stores[c])
            xcl.append(p.dma("sp", xcw[:, c * 512:(c + 1) * 512],
                             xcbuf[c * 128:(c + 1) * 128, tw * 512:(tw + 1) * 512], dep))
        dbc_mms = []
        for c in range(8):
            cb = c % 2
            ccdep = [xcl[c]]
            if len(dbc_mms) >= 2:
                ccdep.append(dbc_mms[-2])
            elif prev_dtev is not None:
                ccdep.append(prev_dtev)
            cc = vcopy(p, xcwb[:, cb * 512:(cb + 1) * 512], xcw[:, c * 512:(c + 1) * 512], ccdep)
            g1 = mm(p, pdbc[0:48, :], xpjb[:, c * 64:c * 64 + 48],
                    xcwb[:, cb * 512:(cb + 1) * 512], c == 0, c == 7, [cc, cast_done])
            g2 = mm(p, pdbc[64:80, :], xpjb[:, c * 64 + 48:c * 64 + 64],
                    xcwb[:, cb * 512:(cb + 1) * 512], c == 0, c == 7, [cc])
            dbc_mms.append(g2)
        dbc_ev = acopy(p, dbcs[0:80, 0:512], pdbc[0:80, :], dbc_mms)
        dtev = vcopy(p, dtb16[:], dbcs[0:32, 0:512], [dbc_ev])
        prev_dtev = dtev
        del_nodes = []
        gds = {}
        t2s = {}

        def emit_gd(c):
            pdl = (pin0, pin1)[c % 2]
            gdeps = [dtev]
            if c - 2 in t2s:
                gdeps.append(t2s[c - 2])
            elif tw == 0 and c < 2:
                gdeps.extend(z_stores[-2:])
            gds[c] = mm(p, pdl[:, :], dtwb[0:32, c * 128:(c + 1) * 128], dtb16[:],
                        True, True, [d for d in gdeps if d])

        emit_gd(0)
        emit_gd(1)
        for c in range(8):
            pdl = (pin0, pin1)[c % 2]
            tah = ta[:, 0:512] if False else ta[:]
            a1 = act(p, tah, pdl[:, :], AF.Abs,
                     [gds[c], l_db] + ([del_nodes[-1]] if del_nodes else []),
                     bias=dB[:, c:c + 1])
            a2 = act(p, tah, tah, AF.Exp, [a1], scale=-1.0)
            t1 = ts(p, tah, tah, 1.0, None, ALU.add, ALU.bypass, [a2])
            a3 = act(p, tah, tah, AF.Ln, [t1])
            t2 = p.add("dve", (lambda e, c=c, pdl=pdl: e.tensor_scalar(
                tb[:], pdl[:, :], dB[:, c:c + 1], 0.0, ALU.add, ALU.max)), [a3])
            t2s[c] = t2
            if c + 2 < 8:
                emit_gd(c + 2)
            dl = tt(p, dels[:, c * 512:(c + 1) * 512], tah, tb[:], ALU.add, [t2, a3])
            uu = tt(p, us[:, c * 512:(c + 1) * 512], dels[:, c * 512:(c + 1) * 512],
                    xcw[:, c * 512:(c + 1) * 512], ALU.mult, [dl])
            del_nodes.append(uu)
        if debug:
            h_stores.append(p.dma("pool", y_dbc[:, tw * 512:(tw + 1) * 512],
                                  dbcs[0:80, tw * 512:(tw + 1) * 512], [dbc_ev]))
            if tw == 0:
                h_stores.append(p.dma("pool", y_del, dels[:], [del_nodes[-1]]))
        rep_nodes = {}
        last_eB = [None]
        last_eC = [None]

        def emit_rep(s, extra):
            dep = [dbc_ev, l_es] + extra
            if (tw - 1, "v", s) in prev_rep:
                dep.append(prev_rep[(tw - 1, "v", s)])
            if (tw - 1, "h", s) in prev_rep:
                dep.append(prev_rep[(tw - 1, "h", s)])
            if last_eB[0] is not None:
                dep.append(last_eB[0])
            gB = mm(p, pBC[:, :], esl[32:48, s * 128:(s + 1) * 128],
                    dbcs[32:48, 0:512], True, True, dep)
            eB = acopy(p, Bbc[:, s * 512:(s + 1) * 512], pBC[:, :], [gB])
            dep2 = [dbc_ev]
            if last_eC[0] is not None:
                dep2.append(last_eC[0])
            gC = mm(p, pdbc[0:128, :], esl[64:80, s * 128:(s + 1) * 128],
                    dbcs[64:80, 0:512], True, True, dep2 + [eB])
            eC = acopy(p, Cbc[:, s * 512:(s + 1) * 512], pdbc[0:128, :], [gC])
            last_eB[0] = eB
            last_eC[0] = eC
            rep_nodes[s] = eC
        for s in range(3):
            emit_rep(s, [])
        if debug and tw == 0:
            h_stores.append(p.dma("pool", y_B, Bbc[:, 0:16 * 512], [rep_nodes[2]]))
        def lin(dic, tw_, c_, s_, d):
            g = (tw_ * 8 + c_) * 16 + s_ - d
            if g < 0:
                return None
            tw2, r_ = divmod(g, 128)
            c2, s2 = divmod(r_, 16)
            return dic.get((tw2, c2, s2))

        for c in range(8):
            pyt = (py0, py1)[c % 2]

            def ring_war(j, c=c, tw=tw):
                return lin(prev_scan, tw, c, j, 4)

            def emit_gx(s, c=c, tw=tw):
                sb = s % 4
                gdep = [del_nodes[c], n_A, ring_war(s)]
                return act(p, Gt[:, sb * 512:(sb + 1) * 512],
                           dels[:, c * 512:(c + 1) * 512], AF.Exp, gdep,
                           scale=Acol[:, c * 16 + s:c * 16 + s + 1])

            def emit_vv(s, c=c, tw=tw):
                sb = s % 4
                vdeps = [rep_nodes[s], del_nodes[c], ring_war(s)]
                return tt(p, vt[:, sb * 512:(sb + 1) * 512], us[:, c * 512:(c + 1) * 512],
                          Bbc[:, s * 512:(s + 1) * 512], ALU.mult, vdeps)

            gxs = {0: emit_gx(0), 1: emit_gx(1)}
            vvs = {0: emit_vv(0), 1: emit_vv(1)}
            for s in range(16):
                sb = s % 4
                hb = s % 2
                col = c * 16 + s
                sdeps = [gxs[s], vvs[s], n_z0, lin(prev_hc, tw, c, s, 2)]
                if (tw - 1, c, s) in prev_scan:
                    sdeps.append(prev_scan[(tw - 1, c, s)])
                sc_ = p.add("dve", (lambda e, sb=sb, hb=hb, col=col:
                                    e.tensor_tensor_scan(ht[:, hb * 512:(hb + 1) * 512],
                                                         Gt[:, sb * 512:(sb + 1) * 512],
                                                         vt[:, sb * 512:(sb + 1) * 512],
                                                         stcols[:, col:col + 1],
                                                         ALU.mult, ALU.add)),
                            sdeps, special=True)
                if c == 0 and s + 3 < 16:
                    emit_rep(s + 3, [])
                if s + 2 < 16:
                    gxs[s + 2] = emit_gx(s + 2)
                    vvs[s + 2] = emit_vv(s + 2)
                st_cp = p.add("dve", (lambda e, hb=hb, col=col:
                                      e.tensor_copy(stcols[:, col:col + 1],
                                                    ht[:, hb * 512 + 511: hb * 512 + 512])),
                              [sc_])
                prev_scan[(tw, c, s)] = st_cp
                hcdep = [sc_, rep_nodes[s], lin(prev_ymm, tw, c, s, 2)]
                hc = tt(p, hct[:, hb * 512:(hb + 1) * 512], ht[:, hb * 512:(hb + 1) * 512],
                        Cbc[:, s * 512:(s + 1) * 512], ALU.mult, hcdep)
                prev_hc[(tw, c, s)] = hc
                if c == 7:
                    prev_rep[(tw, "v", s)] = vvs[s]
                    prev_rep[(tw, "h", s)] = hc
                ydep = [hc, l_id]
                if s == 0 and (tw, c - 2, "gate") in prev_gate:
                    ydep.append(prev_gate[(tw, c - 2, "gate")])
                ym = mm(p, pyt[:, :], idb[:], hct[:, hb * 512:(hb + 1) * 512],
                        s == 0, s == 15, ydep)
                prev_ymm[(tw, c, s)] = ym
            zldep = []
            if prev_gate.get((tw, c - 1)) is not None:
                zldep.append(prev_gate[(tw, c - 1)])
            elif tw > 0:
                zldep.append(prev_gate[(tw - 1, 7)])
            elif c == 0:
                zldep.append(z_stores[-1])
            zl = p.dma("pool", sz[:], zbuf[c * 128:(c + 1) * 128, tcols[0]:tcols[1]], zldep)
            slu = act(p, sz[:], sz[:], AF.Silu, [zl])
            g1 = p.add("dve", (lambda e, c=c: e.scalar_tensor_tensor(
                ta[:], xcw[:, c * 512:(c + 1) * 512], dP[:, c:c + 1],
                (py0, py1)[c % 2][:, :], ALU.mult, ALU.add)),
                [prev_ymm[(tw, c, 15)], l_dp, slu])
            g2 = tt(p, yf[:, c * 512:(c + 1) * 512], ta[:], sz[:], ALU.mult,
                    [g1] + ([prev_out[-1]] if prev_out else []))
            prev_gate[(tw, c)] = g2
            prev_gate[(tw, c, "gate")] = g1
        if debug and tw == 0:
            h_stores.append(p.dma("pool", y_yf, yf[:],
                                  [prev_gate[(tw, cc)] for cc in range(8)]))
        for oc in range(4):
            hdeps = [prev_gate[(tw, cc)] for cc in range(8)] + [cast_done]
            if (tw - 1, oc) in prev_hos:
                hdeps.append(prev_hos[(tw - 1, oc)])
            g = mm_group(p, pho[:, :],
                         [(outwb[:, kc * 512 + oc * 128:kc * 512 + (oc + 1) * 128],
                           yf[:, kc * 512:(kc + 1) * 512]) for kc in range(8)],
                         hdeps + ([prev_hos[(tw, oc - 1)]] if (tw, oc - 1) in prev_hos else []))
            hb = oc % 2
            ev = acopy(p, hor[:, hb * 512:(hb + 1) * 512], pho[:, :],
                       [g] + ([h_stores[-2]] if len(h_stores) >= 2 else []))
            prev_hos[(tw, oc)] = ev
            h_stores.append(p.dma("sp", y_h[oc * 128:(oc + 1) * 128, tw * 512:(tw + 1) * 512],
                                  hor[:, hb * 512:(hb + 1) * 512], [ev]))
        prev_out.append(prev_hos[(tw, 3)])
    if debug:
        h_stores.append(p.dma("pool", y_st, stcols[:],
                              [prev_scan[(7, c, 15)] for c in range(8)]))
    sink(p, h_stores + z_stores + xc_stores)
    return p.build()


def mamba_prog():
    if "mamba" not in _PROGS:
        _PROGS["mamba"] = build_mamba()
    return _PROGS["mamba"]


def build_lnmax():
    p = Prog("lnmax")
    nc = p.nc
    hidT = nc.dram_tensor("hidT", [D, TOK], F32, kind="ExternalInput").ap()
    xT = nc.dram_tensor("xT", [D, TOK], F32, kind="ExternalInput").ap()
    onc = nc.dram_tensor("onc", [128, 1], F32, kind="ExternalInput").ap()
    onr = nc.dram_tensor("onr", [1, 128], F32, kind="ExternalInput").ap()
    idn = nc.dram_tensor("idn", [128, 128], F32, kind="ExternalInput").ap()
    y_x3 = nc.dram_tensor("y_x3", [D, TOK], F32, kind="ExternalOutput").ap()
    y_mx = nc.dram_tensor("y_mx", [1, 1], F32, kind="ExternalOutput").ap()

    ha = p.sbuf("ha", [128, 4 * 512], F32)
    xa = p.sbuf("xa", [128, 4 * 512], F32)
    rsum = p.sbuf("rsum", [128, 8 * 512], F32)
    x1s = p.sbuf("x1s", [128, 8 * 512], F32)
    sq = p.sbuf("sqln", [128, 8 * 512], F32)
    lnA = p.sbuf("lnA", [65, 512], F32)
    rrstd = p.sbuf("rrstd", [1, 512], F32)
    rmr = p.sbuf("rmr", [1, 512], F32)
    mc = p.sbuf("mc", [128, 2], F32)
    mrow = p.sbuf("mrow", [1, 128], F32)
    onc_s = p.sbuf("onc_s", [128, 1], F32)
    onr_s = p.sbuf("onr_s", [1, 128], F32)
    idn_s = p.sbuf("idn_s", [128, 128], F32)
    pst = p.psum("pst", [64, 512])
    pb1 = p.psum("pb1", [128, 512])
    pb2 = p.psum("pb2", [128, 512])
    pmx = p.psum("pmx", [1, 128])

    n_onc = p.dma("sp", onc_s[:], onc)
    n_onr = p.dma("sp", onr_s[:], onr)
    n_idn = p.dma("sp", idn_s[:], idn)
    sc = {"ones_col": onc_s[:, 0:1], "ones_row": onr_s[0:1, :], "sq": sq,
          "stat": pst, "bc1": pb1, "bc2": pb2,
          "rmean": lnA[0:1, :], "rvar": lnA[32:33, :], "rtmp": lnA[64:65, :],
          "rrstd": rrstd[0:1, :], "rmr": rmr[0:1, :]}
    stores = []
    prev_red = None
    mzero = p.add("dve", lambda e: e.memset(mc[:, 0:1], 0.0), [])
    ln_last = None
    res_hist = []
    for tc in range(4):
        res_nodes = []
        for oc in range(8):
            b = oc % 4
            war = [res_hist[-4]] if len(res_hist) >= 4 else []
            l1 = p.dma("pool", ha[:, b * 512:(b + 1) * 512],
                       hidT[oc * 128:(oc + 1) * 128, tc * 512:(tc + 1) * 512], war)
            l2 = p.dma("pool", xa[:, b * 512:(b + 1) * 512],
                       xT[oc * 128:(oc + 1) * 128, tc * 512:(tc + 1) * 512], war)
            r = tt(p, rsum[:, oc * 512:(oc + 1) * 512], ha[:, b * 512:(b + 1) * 512],
                   xa[:, b * 512:(b + 1) * 512], ALU.add,
                   [l1, l2] + ([ln_last] if tc > 0 and oc == 0 else []))
            res_nodes.append(r)
            res_hist.append(r)
        outs = emit_ln(p, lambda c: rsum[:, c * 512:(c + 1) * 512],
                       lambda c: x1s[:, c * 512:(c + 1) * 512], 512, sc, res_nodes)
        ln_last = outs[-1]
        for oc in range(8):
            stores.append(p.dma("sp", y_x3[oc * 128:(oc + 1) * 128, tc * 512:(tc + 1) * 512],
                                x1s[:, oc * 512:(oc + 1) * 512], [outs[oc]]))
            rdeps = [outs[oc], mzero]
            if prev_red is not None:
                rdeps.append(prev_red)
            rd = vred(p, mc[:, 1:2], x1s[:, oc * 512:(oc + 1) * 512], ALU.max, rdeps, absval=True)
            prev_red = tt(p, mc[:, 0:1], mc[:, 0:1], mc[:, 1:2], ALU.max, [rd])
    n_tp = p.add("pe", lambda e: e.transpose(pmx[0:1, :], mc[:, 0:1], idn_s[:, :]),
                 [prev_red, n_idn])
    n_ev = acopy(p, mrow[0:1, :], pmx[0:1, :], [n_tp])
    n_fm = vred(p, mc[0:1, 0:1], mrow[0:1, :], ALU.max, [n_ev])
    stores.append(p.dma("sp", y_mx, mc[0:1, 0:1], [n_fm]))
    sink(p, stores)
    return p.build()


def lnmax_prog():
    if "lnmax" not in _PROGS:
        _PROGS["lnmax"] = build_lnmax()
    return _PROGS["lnmax"]


def _conv_layout(w):
    return f32c(w.reshape(8, 128, 4).transpose(1, 0, 2).reshape(128, 32))


def _col_layout(v):
    return f32c(v.reshape(8, 128).T)


def _al_layout(a):
    return f32c(a.reshape(8, 128, 16).transpose(1, 0, 2).reshape(128, 128))


def make_esel():
    e = np.zeros((80, 2048), np.float32)
    for k in range(16):
        for s in range(16):
            if k == s:
                e[32 + k, s * 128:(s + 1) * 128] = 1.0
                e[64 + k, s * 128:(s + 1) * 128] = 1.0
    return e


IDENT_BF = np.eye(128, dtype=np.float32).astype(bf16)
ESEL = make_esel()


def mamba_inmap(xd, prm, tag):
    xpad = np.zeros((DM, XPD), np.float32)
    xpad[:, 3:3 + 4096] = xd.T
    return {
        "xdT": xpad,
        "inwT": f32c(prm[f"in_w{tag}"].T),
        "xpjT": f32c(prm[f"xproj_w{tag}"].T),
        "dtwT": f32c(prm[f"dt_w{tag}"].T),
        "outwT": f32c(prm[f"out_w{tag}"].T),
        "convW": _conv_layout(prm[f"conv_w{tag}"]),
        "convB": _col_layout(prm[f"conv_b{tag}"]),
        "dtB": _col_layout(prm[f"dt_b{tag}"]),
        "DPc": _col_layout(prm[f"D{tag}"]),
        "AL": _al_layout(prm[f"A_log{tag}"]),
        "esel": ESEL, "idnb": IDENT_BF,
    }


def run_mamba(x2, prm):
    in_maps = []
    for c in range(8):
        b, dirb = c // 2, c % 2
        if dirb == 0:
            xd = x2[b, :, 0:512]
            in_maps.append(mamba_inmap(xd, prm, "1"))
        else:
            xd = x2[b, ::-1, 512:1024]
            in_maps.append(mamba_inmap(xd, prm, "2"))
    return run_bass_kernel_spmd(mamba_prog(), in_maps, core_ids=list(range(8))).results


def run_lnmax(x2, mres):
    in_maps = []
    for c in range(8):
        b, h = c // 2, c % 2
        sl = slice(h * 2048, (h + 1) * 2048)
        hid = np.concatenate([mres[2 * b]["y_h"][:, sl], mres[2 * b + 1]["y_h"][:, sl]], axis=0)
        in_maps.append({"hidT": f32c(hid), "xT": f32c(x2[b, sl, :].T),
                        "onc": ONES_COL, "onr": ONES_ROW, "idn": IDENT})
    res = run_bass_kernel_spmd(lnmax_prog(), in_maps, core_ids=list(range(8))).results
    x3 = np.empty((B, S, D), np.float32)
    for c in range(8):
        b, h = c // 2, c % 2
        x3[b, h * 2048:(h + 1) * 2048, :] = res[c]["y_x3"].T
    return x3, [res[c]["y_mx"] for c in range(8)]



def kernel(L0_attn_x, L0_attn_Wq, L0_attn_Wk, L0_attn_Wv, L0_attn_Wo,
           L0_ffn_w_gate, L0_ffn_w_up, L0_ffn_w_down,
           L1_attn_in_w1, L1_attn_conv_w1, L1_attn_conv_b1, L1_attn_xproj_w1,
           L1_attn_dt_w1, L1_attn_dt_b1, L1_attn_A_log1, L1_attn_D1, L1_attn_out_w1,
           L1_attn_in_w2, L1_attn_conv_w2, L1_attn_conv_b2, L1_attn_xproj_w2,
           L1_attn_dt_w2, L1_attn_dt_b2, L1_attn_A_log2, L1_attn_D2, L1_attn_out_w2,
           L1_ffn_w_gate, L1_ffn_w_up, L1_ffn_w_down):
    x = np.asarray(L0_attn_x, np.float32)
    dq_packs = [dq_pack(c, np.asarray(L0_ffn_w_up), np.asarray(L0_ffn_w_gate),
                        np.asarray(L1_ffn_w_up), np.asarray(L1_ffn_w_gate),
                        np.asarray(L0_ffn_w_down), np.asarray(L1_ffn_w_down))
                for c in range(8)]
    qkv_res = run_qkv(x, L0_attn_Wq, L0_attn_Wk, L0_attn_Wv, dq_packs)
    attn_res = run_attn(x, qkv_res, L0_attn_Wo)
    x1 = gather_interleaved(attn_res, "y_x1")
    mx1 = [attn_res[c]["y_mx"] for c in range(8)]
    up0, g0, up1, g1, dn0, dn1 = dq_unpack([qkv_res[c]["y_dq"] for c in range(8)])
    x2, mx2 = run_ffn(x1, mx1, np.ascontiguousarray(up0.T), np.ascontiguousarray(g0.T),
                      np.ascontiguousarray(dn0.T))
    prm = {
        "in_w1": L1_attn_in_w1, "conv_w1": L1_attn_conv_w1, "conv_b1": L1_attn_conv_b1,
        "xproj_w1": L1_attn_xproj_w1, "dt_w1": L1_attn_dt_w1, "dt_b1": L1_attn_dt_b1,
        "A_log1": L1_attn_A_log1, "D1": L1_attn_D1, "out_w1": L1_attn_out_w1,
        "in_w2": L1_attn_in_w2, "conv_w2": L1_attn_conv_w2, "conv_b2": L1_attn_conv_b2,
        "xproj_w2": L1_attn_xproj_w2, "dt_w2": L1_attn_dt_w2, "dt_b2": L1_attn_dt_b2,
        "A_log2": L1_attn_A_log2, "D2": L1_attn_D2, "out_w2": L1_attn_out_w2,
    }
    prm = {k: np.asarray(v, np.float32) for k, v in prm.items()}
    mres = run_mamba(x2, prm)
    x3, mx3 = run_lnmax(x2, mres)
    x4, _ = run_ffn(x3, mx3, np.ascontiguousarray(up1.T), np.ascontiguousarray(g1.T),
                    np.ascontiguousarray(dn1.T))
    return x4


# version 12
# speedup vs baseline: 1.0275x; 1.0161x over previous
import contextlib
import numpy as np
import ml_dtypes
import concourse.bass as bass
import concourse.mybir as mybir
from concourse.bass_utils import run_bass_kernel_spmd

F32 = mybir.dt.float32
BF16 = mybir.dt.bfloat16
I16 = mybir.dt.int16
U16 = mybir.dt.uint16
AF = mybir.ActivationFunctionType
ALU = mybir.AluOpType
AX = mybir.AxisListType

MAGIC = 12582912.0
B, S, D = 4, 4096, 1024
TOK = 2048
NH, DK = 16, 64
FI = 3072
EPS = 1e-5

bf16 = ml_dtypes.bfloat16

ENGINES = ("pe", "act", "dve", "pool", "sp")
_ENG_ATTR = {"pe": "tensor", "act": "scalar", "dve": "vector", "pool": "gpsimd", "sp": "sync"}
NDMASEM = 6


class Node:
    __slots__ = ("eng", "fn", "deps", "is_dma", "idx", "flagged", "special")

    def __init__(self, eng, fn, deps, is_dma, special=False):
        self.eng, self.fn, self.deps, self.is_dma = eng, fn, deps, is_dma
        self.idx = -1
        self.flagged = False
        self.special = special


class Prog:

    def __init__(self, name):
        self.nc = bass.Bass("TRN2", target_bir_lowering=False)
        self.name = name
        self.nodes = []
        self._stack = contextlib.ExitStack()

    def sbuf(self, name, shape, dtype):
        return self._stack.enter_context(self.nc.sbuf_tensor(name, shape, dtype))

    def psum(self, name, shape, dtype=F32):
        return self._stack.enter_context(self.nc.psum_tensor(name, shape, dtype))

    def add(self, eng, fn, deps=(), special=False):
        n = Node(eng, fn, [d for d in deps if d is not None], False, special)
        self.nodes.append(n)
        return n

    def dma(self, queue, out, in_, deps=()):
        def fn(e):
            return e.dma_start(out=out, in_=in_)
        n = Node(queue, fn, [d for d in deps if d is not None], True)
        self.nodes.append(n)
        return n

    def build(self):
        nc = self.nc
        for n in self.nodes:
            if n.is_dma:
                n.flagged = True
            for d in n.deps:
                if d.eng != n.eng or d.is_dma or d.special or n.special:
                    d.flagged = True
        comp_count = {e: 0 for e in ENGINES}
        dma_count = {e: 0 for e in ENGINES}
        for n in self.nodes:
            if not n.flagged:
                continue
            if n.is_dma:
                n.idx = dma_count[n.eng]
                dma_count[n.eng] += 1
            else:
                comp_count[n.eng] += 1
                n.idx = comp_count[n.eng]
        self._sems = {}
        for e in ENGINES:
            if comp_count[e]:
                self._sems[("c", e)] = nc.semaphore(f"S_{e}").__enter__()
            if dma_count[e]:
                self._sems[("d", e)] = [nc.semaphore(f"D_{e}{i}").__enter__()
                                        for i in range(NDMASEM)]
        block = nc.Block().__enter__()
        per_eng = {e: [n for n in self.nodes if n.eng == e] for e in ENGINES}

        def dep_req(d):
            if d.is_dma:
                sems = self._sems[("d", d.eng)]
                return (sems[d.idx % NDMASEM], 16 * (d.idx // NDMASEM + 1))
            return (self._sems[("c", d.eng)], d.idx)

        for e in ENGINES:
            nodes = per_eng[e]
            if not nodes:
                continue

            def make_body(e=e, nodes=nodes):
                def body(eng):
                    watermark = {}
                    for n in nodes:
                        for d in n.deps:
                            if (d.eng == e and not d.is_dma and not d.special
                                    and not n.special):
                                continue
                            sem, val = dep_req(d)
                            key = id(sem)
                            if watermark.get(key, -1) >= val:
                                continue
                            eng.wait_ge(sem, val)
                            watermark[key] = val
                        if n.is_dma and n.flagged:
                            sems = self._sems[("d", e)]
                            slot, rnd = n.idx % NDMASEM, n.idx // NDMASEM
                            if rnd > 0:
                                key = id(sems[slot])
                                val = 16 * rnd
                                if watermark.get(key, -1) < val:
                                    eng.wait_ge(sems[slot], val)
                                    watermark[key] = val
                        instr = n.fn(eng)
                        if n.flagged:
                            if n.is_dma:
                                instr.then_inc(self._sems[("d", e)][n.idx % NDMASEM], 16)
                            else:
                                instr.then_inc(self._sems[("c", e)], 1)
                return body
            getattr(block, _ENG_ATTR[e])(make_body())
        block.__exit__(None, None, None)
        return nc


def f32c(x):
    return np.ascontiguousarray(x, dtype=np.float32)


def bf16c(x):
    return np.ascontiguousarray(np.asarray(x).astype(bf16))



def mm(p, out_ap, lhsT, rhs, start, stop, deps=()):
    return p.add("pe", lambda e: e.matmul(out_ap, lhsT, rhs, start=start, stop=stop,
                                          skip_group_check=True), deps)


def mm_group(p, out_ap, pairs, deps):
    def fn(eng):
        last = None
        for i, (lhsT, rhs) in enumerate(pairs):
            last = eng.matmul(out_ap, lhsT, rhs, start=(i == 0),
                              stop=(i == len(pairs) - 1), skip_group_check=True)
        return last
    return p.add("pe", fn, deps)


def act(p, out, in_, func, deps, bias=None, scale=None):
    def fn(eng):
        kw = {}
        if bias is not None:
            kw["bias"] = bias
        if scale is not None:
            kw["scale"] = scale
        return eng.activation(out, in_, func, **kw)
    return p.add("act", fn, deps)


def acopy(p, out, in_, deps):
    return p.add("act", lambda e: e.copy(out, in_), deps)


def ts(p, out, in_, s1, s2, op0, op1, deps):
    return p.add("dve", lambda e: e.tensor_scalar(out, in_, s1, s2, op0, op1), deps)


def tt(p, out, a, b, op, deps):
    return p.add("dve", lambda e: e.tensor_tensor(out, a, b, op), deps)


def stt(p, out, in0, scalar, in1, op0, op1, deps):
    return p.add("dve", lambda e: e.scalar_tensor_tensor(out, in0, scalar, in1, op0, op1), deps)


def vred(p, out, in_, op, deps, absval=None):
    return p.add("dve", lambda e: e.tensor_reduce(out, in_, AX.X, op,
                                                  apply_absolute_value=absval),
                 deps, special=True)


def vcopy(p, out, in_, deps):
    return p.add("dve", lambda e: e.tensor_copy(out, in_), deps)


def vscan(p, out, d0, d1, init, deps):
    return p.add("dve", lambda e: e.tensor_tensor_scan(out, d0, d1, init, ALU.mult, ALU.add),
                 deps, special=True)


def vrecip(p, out, in_, deps):
    return p.add("dve", lambda e: e.reciprocal(out, in_), deps, special=True)


def sink(p, deps):
    return p.add("sp", lambda e: e.nop(), deps)


def chunks2d(handle, n, width):
    def sl(c, a=0, b=None):
        b = width if b is None else b
        return handle[:, c * width + a: c * width + b]
    return sl



def emit_ln(p, in_sl, out_sl, T, sc, dep_in):
    ones_col, ones_row = sc["ones_col"], sc["ones_row"]
    sq = sc["sq"]
    nodes_sq = [act(p, sq[:, c * T:(c + 1) * T], in_sl(c), AF.Square, dep_in)
                for c in range(8)]
    n_sum = mm_group(p, sc["stat"][0:1, 0:T],
                     [(ones_col, in_sl(c)) for c in range(8)], dep_in)
    n_ssq = mm_group(p, sc["stat"][32:33, 0:T],
                     [(ones_col, sq[:, c * T:(c + 1) * T]) for c in range(8)], nodes_sq)
    n_mean = ts(p, sc["rmean"], sc["stat"][0:1, 0:T], 1.0 / D, None,
                ALU.mult, ALU.bypass, [n_sum])
    n_msq = tt(p, sc["rvar"], sc["rmean"], sc["rmean"],
               ALU.mult, [n_mean])
    n_var = stt(p, sc["rvar"], sc["stat"][32:33, 0:T], 1.0 / D,
                sc["rvar"], ALU.mult, ALU.subtract, [n_ssq, n_msq])
    n_vpe = ts(p, sc["rvar"], sc["rvar"], EPS, None, ALU.add, ALU.bypass, [n_var])
    n_ln = act(p, sc["rtmp"], sc["rvar"], AF.Ln, [n_vpe])
    n_rstd = act(p, sc["rrstd"], sc["rtmp"], AF.Exp, [n_ln], scale=-0.5)
    n_mr = tt(p, sc["rmr"], sc["rmean"], sc["rrstd"],
              ALU.mult, [n_rstd])
    n_b1 = mm(p, sc["bc1"][:, 0:T], ones_row, sc["rrstd"], True, True, [n_rstd])
    n_b2 = mm(p, sc["bc2"][:, 0:T], ones_row, sc["rmr"], True, True, [n_mr])
    outs = []
    for c in range(8):
        n_a = tt(p, sq[:, c * T:(c + 1) * T], in_sl(c), sc["bc1"][:, 0:T],
                 ALU.mult, [n_b1, n_ssq])
        n_o = tt(p, out_sl(c), sq[:, c * T:(c + 1) * T], sc["bc2"][:, 0:T],
                 ALU.subtract, [n_b2, n_a])
        outs.append(n_o)
    return outs



NDQ = 18


def build_qkv():
    p = Prog("qkv")
    nc = p.nc
    xT = nc.dram_tensor("xT", [D, TOK], F32, kind="ExternalInput").ap()
    wq = nc.dram_tensor("wq", [D, D], F32, kind="ExternalInput").ap()
    wk = nc.dram_tensor("wk", [D, D], F32, kind="ExternalInput").ap()
    wv = nc.dram_tensor("wv", [D, D], F32, kind="ExternalInput").ap()
    sel = nc.dram_tensor("sel", [128, 128], F32, kind="ExternalInput").ap()
    dq_in = nc.dram_tensor("dq_in", [NDQ * 128, 1024], F32, kind="ExternalInput").ap()

    y_q = nc.dram_tensor("y_q", [D, TOK], BF16, kind="ExternalOutput").ap()
    y_k = nc.dram_tensor("y_k", [D, TOK], BF16, kind="ExternalOutput").ap()
    y_v = nc.dram_tensor("y_v", [TOK, D], BF16, kind="ExternalOutput").ap()
    y_qn2 = nc.dram_tensor("y_qn2", [16, 16], F32, kind="ExternalOutput").ap()
    y_kn2 = nc.dram_tensor("y_kn2", [16, 16], F32, kind="ExternalOutput").ap()
    y_dq = nc.dram_tensor("y_dq", [NDQ * 128, 1024], BF16, kind="ExternalOutput").ap()

    xs = p.sbuf("xs", [128, 2 * TOK], F32)
    xb = p.sbuf("xb", [128, 8 * TOK], BF16)
    ws = p.sbuf("ws", [128, 2 * D], F32)
    wqb = p.sbuf("wqb", [128, 8 * D], BF16)
    wkb = p.sbuf("wkb", [128, 8 * D], BF16)
    wvb = p.sbuf("wvb", [128, 8 * D], BF16)
    ob = p.sbuf("ob", [128, 8 * 512], BF16)
    sels = p.sbuf("sels", [128, 128], F32)
    sq2 = p.sbuf("sq2", [128, 2 * 512], F32)
    n2t = p.sbuf("n2t", [16, 32], F32)
    dqw = p.sbuf("dqw", [128, 2 * 1024], F32)
    dqo = p.sbuf("dqo", [128, 2 * 1024], BF16)
    wi = p.sbuf("wi", [128, 2 * 1024], F32)
    bmax = p.sbuf("bmax", [128, 2 * 32], F32)
    bs = p.sbuf("bs", [128, 2 * 32], F32)
    brec = p.sbuf("brec", [128, 2 * 32], F32)
    ps0 = p.psum("ps0", [128, 512])
    ps1 = p.psum("ps1", [128, 512])
    pn = p.psum("pn", [16, 512])
    if True:
        n_sel = p.dma("sp", sels[:], sel)
        casts = {}
        prev_cast = {}
        for name, src_d, dstb, nch, w in (("x", xT, xb, 8, TOK),):
            for c in range(8):
                buf = c % 2
                deps = []
                if c >= 2:
                    deps.append(casts[("x", c - 2)])
                ld = p.dma("sp", xs[:, buf * TOK:(buf + 1) * TOK],
                           xT[c * 128:(c + 1) * 128, :], deps)
                casts[("x", c)] = vcopy(p, xb[:, c * TOK:(c + 1) * TOK],
                                        xs[:, buf * TOK:(buf + 1) * TOK], [ld])
        for nm, wsrc, dstb in (("wq", wq, wqb), ("wk", wk, wkb), ("wv", wv, wvb)):
            for c in range(8):
                buf = c % 2
                deps = []
                key = (nm, c - 2)
                if key in casts:
                    deps.append(casts[key])
                elif c < 2 and nm != "wq":
                    pk = {"wk": "wq", "wv": "wk"}[nm]
                    deps.append(casts[(pk, 6 + c)])
                ld = p.dma("pool", ws[:, buf * D:(buf + 1) * D],
                           wsrc[c * 128:(c + 1) * 128, :], deps)
                casts[(nm, c)] = vcopy(p, dstb[:, c * D:(c + 1) * D],
                                       ws[:, buf * D:(buf + 1) * D], [ld])
        cx = [casts[("x", c)] for c in range(8)]
        cwq = [casts[("wq", c)] for c in range(8)]
        cwk = [casts[("wk", c)] for c in range(8)]
        cwv = [casts[("wv", c)] for c in range(8)]

        ps = [ps0, ps1]
        use = 0
        prev_evac = {}
        prev_store = {}
        red_nodes = []
        sel_prev = None
        all_selmm = []
        for which, wsb, ydst, cw in ((0, wqb, y_q, cwq), (1, wkb, y_k, cwk)):
            for tc in range(4):
                sq_list = []
                for oc in range(8):
                    pt = ps[use % 2]
                    deps = cx + cw if use == 0 else [cx[-1], cw[-1]]
                    deps = list(deps)
                    if use >= 2:
                        deps.append(prev_evac[use - 2])
                    g = mm_group(p, pt[:, :],
                                 [(wsb[:, kc * D + oc * 128:kc * D + (oc + 1) * 128],
                                   xb[:, kc * TOK + tc * 512:kc * TOK + (tc + 1) * 512])
                                  for kc in range(8)], deps)
                    ring = use % 8
                    edeps = [g]
                    if use >= 8:
                        edeps.append(prev_store[use - 8])
                    ev = acopy(p, ob[:, ring * 512:(ring + 1) * 512], pt[:, :], edeps)
                    st = p.dma("sp", ydst[oc * 128:(oc + 1) * 128, tc * 512:(tc + 1) * 512],
                               ob[:, ring * 512:(ring + 1) * 512], [ev])
                    prev_evac[use] = ev
                    prev_store[use] = st
                    sqb = oc % 2
                    sdeps = [ev]
                    if len(all_selmm) >= 2:
                        sdeps.append(all_selmm[-2])
                    nsq = act(p, sq2[:, sqb * 512:(sqb + 1) * 512],
                              ob[:, ring * 512:(ring + 1) * 512], AF.Square, sdeps)
                    smdeps = [nsq, n_sel]
                    if sel_prev is not None:
                        smdeps.append(sel_prev)
                    sm = mm(p, pn[:, :], sels[:, oc * 16:(oc + 1) * 16],
                            sq2[:, sqb * 512:(sqb + 1) * 512], oc == 0, oc == 7, smdeps)
                    all_selmm.append(sm)
                    sel_prev = sm
                    sq_list.append(sm)
                    use += 1
                rd = vred(p, n2t[0:16, which * 16 + tc * 4: which * 16 + (tc + 1) * 4],
                          pn[0:16, :].rearrange("p (a b) -> p a b", b=128), ALU.max,
                          [sq_list[-1]] + red_nodes[-1:])
                sel_prev = rd
                red_nodes.append(rd)
        for tki in range(16):
            for vc in range(2):
                pt = ps[use % 2]
                deps = [cx[-1], cwv[-1]]
                if use >= 2:
                    deps.append(prev_evac[use - 2])
                g = mm_group(p, pt[:, :],
                             [(xb[:, kc * TOK + tki * 128: kc * TOK + (tki + 1) * 128],
                               wvb[:, kc * D + vc * 512: kc * D + (vc + 1) * 512])
                              for kc in range(8)], deps)
                ring = use % 8
                edeps = [g]
                if use >= 8:
                    edeps.append(prev_store[use - 8])
                ev = acopy(p, ob[:, ring * 512:(ring + 1) * 512], pt[:, :], edeps)
                st = p.dma("sp", y_v[tki * 128:(tki + 1) * 128, vc * 512:(vc + 1) * 512],
                           ob[:, ring * 512:(ring + 1) * 512], [ev])
                prev_evac[use] = ev
                prev_store[use] = st
                use += 1

        dq_stores = {}
        for ch in range(NDQ):
            bq = ch % 2
            o = bq * 1024
            obk = bq * 32
            ldeps = []
            if ch >= 2:
                ldeps.append(dq_stores[ch - 2])
            ld = p.dma("pool", dqw[:, o:o + 1024], dq_in[ch * 128:(ch + 1) * 128, :], ldeps)
            w3 = dqw[:, o:o + 1024].rearrange("p (b l) -> p b l", l=32)
            wi3 = wi[:, o:o + 1024].rearrange("p (b l) -> p b l", l=32)
            dqo3 = dqo[:, o:o + 1024].rearrange("p (b l) -> p b l", l=32)
            r1 = vred(p, bmax[:, obk:obk + 32], w3, ALU.max, [ld] + ldeps, absval=True)
            r2 = ts(p, bs[:, obk:obk + 32], bmax[:, obk:obk + 32], 1.0 / 7.0, 1e-12,
                    ALU.mult, ALU.max, [r1])
            r3 = vrecip(p, brec[:, obk:obk + 32], bs[:, obk:obk + 32], [r2])
            r4 = tt(p, wi3, w3,
                    brec[:, obk:obk + 32].rearrange("p (b o) -> p b o", o=1).broadcast_to([128, 32, 32]),
                    ALU.mult, [r3])
            r5 = ts(p, wi[:, o:o + 1024], wi[:, o:o + 1024], MAGIC, MAGIC, ALU.add, ALU.subtract, [r4])
            r6 = tt(p, dqo3, wi3,
                    bs[:, obk:obk + 32].rearrange("p (b o) -> p b o", o=1).broadcast_to([128, 32, 32]),
                    ALU.mult, [r5])
            dq_stores[ch] = p.dma("pool", y_dq[ch * 128:(ch + 1) * 128, :], dqo[:, o:o + 1024], [r6])

        s4 = p.dma("sp", y_qn2, n2t[0:16, 0:16], [red_nodes[3]])
        s5 = p.dma("sp", y_kn2, n2t[0:16, 16:32], [red_nodes[7]])
        sink(p, [s4, s5] + list(prev_store.values()) + list(dq_stores.values()))
        return p.build()



def build_attn():
    p = Prog("attn")
    nc = p.nc
    qT = nc.dram_tensor("qT", [D, TOK], BF16, kind="ExternalInput").ap()
    kT = nc.dram_tensor("kT", [D, S], BF16, kind="ExternalInput").ap()
    vo = nc.dram_tensor("vo", [S, NH * 65], BF16, kind="ExternalInput").ap()
    xT = nc.dram_tensor("xT", [D, TOK], F32, kind="ExternalInput").ap()
    wo = nc.dram_tensor("wo", [D, D], F32, kind="ExternalInput").ap()
    n2in = nc.dram_tensor("n2in", [16, 48], F32, kind="ExternalInput").ap()
    msk = nc.dram_tensor("msk", [128, 8 * 512], BF16, kind="ExternalInput").ap()
    onc = nc.dram_tensor("onc", [128, 1], F32, kind="ExternalInput").ap()
    onr = nc.dram_tensor("onr", [1, 128], F32, kind="ExternalInput").ap()
    idn = nc.dram_tensor("idn", [128, 128], F32, kind="ExternalInput").ap()
    scr = nc.dram_tensor("scr16", [16, 1], F32)

    y_x1 = nc.dram_tensor("y_x1", [D, TOK], F32, kind="ExternalOutput").ap()
    y_mx = nc.dram_tensor("y_mx", [1, 1], F32, kind="ExternalOutput").ap()

    qb = p.sbuf("qb", [128, 8 * TOK], BF16)
    kb = p.sbuf("kb", [128, 2 * S], BF16)
    vb = p.sbuf("vb", [128, 2 * (32 * 130)], BF16)
    att = p.sbuf("att", [128, 8 * TOK], BF16)
    msks = p.sbuf("msks", [128, 8 * 512], BF16)
    pt = p.sbuf("pt", [128, 4 * 512], BF16)
    n2s = p.sbuf("n2s", [16, 48], F32)
    cq = p.sbuf("cq", [16, 4], F32)
    negc = p.sbuf("negc", [1, 16], F32)
    negcb = p.sbuf("negcb", [128, 16], F32)
    rcp = p.sbuf("rcp", [1, 2 * 512], F32)
    tav = p.sbuf("tav", [64, 2 * 512], F32)
    onc_s = p.sbuf("onc_s", [128, 1], F32)
    onr_s = p.sbuf("onr_s", [1, 128], F32)
    idn_s = p.sbuf("idn_s", [128, 128], F32)
    wos = p.sbuf("wos", [128, 2 * D], F32)
    wob = p.sbuf("wob", [128, 8 * D], BF16)
    xrs = p.sbuf("xrs", [128, 2 * 512], F32)
    rsum = p.sbuf("rsum", [128, 8 * 512], F32)
    x1s = p.sbuf("x1s", [128, 8 * 512], F32)
    sq = p.sbuf("sqln", [128, 8 * 512], F32)
    lnA = p.sbuf("lnA", [65, 512], F32)
    rrstd = p.sbuf("rrstd", [1, 512], F32)
    rmr = p.sbuf("rmr", [1, 512], F32)
    mcol = p.sbuf("mcol", [128, 2], F32)
    mrow = p.sbuf("mrow", [1, 128], F32)

    pp0 = p.psum("pp0", [128, 1024])
    pp1 = p.psum("pp1", [128, 1024])
    avA = p.psum("avA", [128, 512])
    avB = p.psum("avB", [128, 512])
    pmx = p.psum("pmx", [128, 512])
    pbc = p.psum("pbc", [128, 16])

    if True:
        n_q = p.dma("sp", qb[:].rearrange("p (c t) -> p c t", t=TOK),
                    qT.rearrange("(c p) t -> p c t", p=128))
        n_m = p.dma("sp", msks[:], msk)
        n_n2 = p.dma("sp", n2s[:], n2in)
        n_onc = p.dma("sp", onc_s[:], onc)
        n_onr = p.dma("sp", onr_s[:], onr)
        n_idn = p.dma("sp", idn_s[:], idn)

        r_qm = vred(p, cq[0:16, 0:1], n2s[0:16, 0:16], ALU.max, [n_n2])
        r_km = vred(p, cq[0:16, 1:2], n2s[0:16, 16:48], ALU.max, [r_qm])
        a_sq = act(p, cq[0:16, 2:3], cq[0:16, 0:1], AF.Sqrt, [r_qm, r_km])
        a_sk = act(p, cq[0:16, 3:4], cq[0:16, 1:2], AF.Sqrt, [r_km])
        t_c = tt(p, cq[0:16, 0:1], cq[0:16, 2:3], cq[0:16, 3:4], ALU.mult, [a_sq, a_sk])
        t_b = ts(p, cq[0:16, 1:2], cq[0:16, 0:1], -0.1275, -0.00125, ALU.mult, ALU.add, [t_c])
        d_out = p.dma("sp", scr[:, :], cq[0:16, 1:2], [t_b])
        d_in = p.dma("sp", negc[:], scr.rearrange("a b -> b a"), [d_out])
        bc_mm = mm(p, pbc[:, 0:16], onr_s[0:1, :], negc[0:1, :], True, True, [d_in, n_onr])
        n_bias = acopy(p, negcb[:], pbc[:, 0:16], [bc_mm])

        exp_hist = {}
        av_hist = {}
        attn_done = []
        prev_postA = {}
        prev_postB = {}
        last_exp_on = {}
        last_av_on = {}
        gp = [0]
        for hp in range(8):
            dbl = hp % 2
            ldeps = []
            if hp >= 2:
                ldeps.append(exp_hist[(hp - 2, 3, "B")])
                ldeps.append(av_hist[(hp - 2, 3, "B")])
            lk = p.dma("pool", kb[:, dbl * S:(dbl + 1) * S], kT[hp * 128:(hp + 1) * 128, :], ldeps)
            lv = p.dma("pool", vb[:, dbl * 4160:(dbl + 1) * 4160]
                       .rearrange("p (c w) -> p c w", w=130),
                       vo[:, hp * 130:(hp + 1) * 130].rearrange("(c p) w -> p c w", p=128), ldeps)
            for qg in range(4):
                E = 8 * qg + 8
                post_queue = []
                for head, avp in (("A", avA), ("B", avB)):
                    hh = 2 * hp + (0 if head == "A" else 1)
                    base = 0 if head == "A" else 64
                    pmask = {}
                    pt_of = {}

                    def emit_av(kt, head=head, avp=avp, lv=lv, dbl=dbl,
                                hp=hp, qg=qg, E=E):
                        avdeps = [pmask[kt]]
                        if kt == 0:
                            avdeps.append(lv)
                            pp_ = prev_postA if head == "A" else prev_postB
                            if pp_:
                                avdeps.append(pp_["node"])
                        av = mm(p, avp[0:65, :],
                                vb[:, dbl * 4160 + kt * 130 + (0 if head == "A" else 65):
                                   dbl * 4160 + kt * 130 + (65 if head == "A" else 130)],
                                pt_of[kt], kt == 0, kt == E - 1, avdeps)
                        av_hist[(hp, qg, head, kt)] = av
                        return av

                    for kp in range(E // 2):
                        par = gp[0] % 2
                        ring = (pp0, pp1)[par]
                        kt0, kt1 = 2 * kp, 2 * kp + 1
                        sdeps = [lk, n_q]
                        if par in last_exp_on:
                            sdeps.append(last_exp_on[par])
                        smm0 = mm(p, ring[:, 0:512],
                                  kb[base:base + 64, dbl * S + kt0 * 128: dbl * S + (kt0 + 1) * 128],
                                  qb[base:base + 64, hp * TOK + qg * 512: hp * TOK + (qg + 1) * 512],
                                  True, True, sdeps)
                        smm1 = mm(p, ring[:, 512:1024],
                                  kb[base:base + 64, dbl * S + kt1 * 128: dbl * S + (kt1 + 1) * 128],
                                  qb[base:base + 64, hp * TOK + qg * 512: hp * TOK + (qg + 1) * 512],
                                  True, True, sdeps)
                        edeps = [smm0, smm1, n_bias]
                        if par in last_av_on:
                            edeps.append(last_av_on[par])
                        ex = act(p, pt[:, par * 1024:(par + 1) * 1024],
                                 ring[:, 0:1024], AF.Exp, edeps,
                                 bias=negcb[:, hh:hh + 1], scale=0.125)
                        exp_hist[(hp, qg, head)] = ex
                        last_exp_on[par] = ex
                        for kt in (kt0, kt1):
                            half = pt[:, par * 1024 + (kt & 1) * 512: par * 1024 + (kt & 1) * 512 + 512]
                            pt_of[kt] = half
                            lastn = ex
                            u = kt - (E - 8)
                            if u >= 0:
                                lastn = tt(p, half, half,
                                           msks[:, u * 512:(u + 1) * 512], ALU.mult, [ex, n_m])
                            pmask[kt] = lastn
                        if kp >= 1:
                            a0 = emit_av(kt0 - 2)
                            a1 = emit_av(kt1 - 2)
                            last_av_on[(gp[0] - 1) % 2] = a1
                        gp[0] += 1
                    a0 = emit_av(E - 2)
                    a1 = emit_av(E - 1)
                    last_av_on[(gp[0] - 1) % 2] = a1
                    av_hist[(hp, qg, head)] = a1
                    post_queue.append((head, avp))
                for head, avp in post_queue:
                    hh = 2 * hp + (0 if head == "A" else 1)
                    base = 0 if head == "A" else 64
                    fin = av_hist[(hp, qg, head)]
                    half = 0 if head == "A" else 1
                    rc = vrecip(p, rcp[0:1, half * 512:(half + 1) * 512],
                                avp[64:65, :], [fin])
                    bcdep = [rc]
                    if prev_postA:
                        bcdep.append(prev_postA["node"])
                    if prev_postB:
                        bcdep.append(prev_postB["node"])
                    bcm = mm(p, pmx[0 if head == "A" else 64:64 if head == "A" else 128, :],
                             onr_s[0:1, 0:64],
                             rcp[0:1, half * 512:(half + 1) * 512], True, True, bcdep)
                    ev = acopy(p, tav[0:64, half * 512:(half + 1) * 512], avp[0:64, :], [fin, rc])
                    ntt = tt(p, att[base:base + 64, hp * TOK + qg * 512: hp * TOK + (qg + 1) * 512],
                             tav[0:64, half * 512:(half + 1) * 512],
                             pmx[0 if head == "A" else 64:64 if head == "A" else 128, :],
                             ALU.mult, [bcm, ev])
                    if head == "A":
                        prev_postA = {"node": ntt}
                    else:
                        prev_postB = {"node": ntt}
                    attn_done.append(ntt)

        wo_casts = []
        for c in range(8):
            buf = c % 2
            deps = [wo_casts[c - 2]] if c >= 2 else []
            ld = p.dma("pool", wos[:, buf * D:(buf + 1) * D], wo[c * 128:(c + 1) * 128, :], deps)
            wo_casts.append(vcopy(p, wob[:, c * D:(c + 1) * D], wos[:, buf * D:(buf + 1) * D], [ld]))

        sc = {"ones_col": onc_s[:, 0:1], "ones_row": onr_s[0:1, :], "sq": sq,
              "stat": avA, "bc1": pp0[:, 512:1024], "bc2": pp1[:, 512:1024],
              "rmean": lnA[0:1, :], "rvar": lnA[32:33, :], "rtmp": lnA[64:65, :],
              "rrstd": rrstd[0:1, :], "rmr": rmr[0:1, :]}
        n_init = p.add("dve", lambda e: e.memset(mcol[:, 0:1], 0.0), [])
        stores = []
        prev_red = n_init
        ln_last = attn_done[-4:]
        for tc in range(4):
            res_nodes = []
            for oc in range(8):
                xsl = xrs[:, (oc % 2) * 512:(oc % 2 + 1) * 512]
                xwar = res_nodes[-2:-1] if len(res_nodes) >= 2 else (
                    ln_last if tc == 0 else [ln_last[-1]])
                xl = p.dma("pool", xsl,
                           xT[oc * 128:(oc + 1) * 128, tc * 512:(tc + 1) * 512], xwar)
                g = mm_group(p, pp0[:, 0:512] if oc % 2 == 0 else pp1[:, 0:512],
                             [(wob[:, kc * D + oc * 128: kc * D + (oc + 1) * 128],
                               att[:, kc * TOK + tc * 512: kc * TOK + (tc + 1) * 512])
                              for kc in range(8)],
                             attn_done + wo_casts + (res_nodes[-2:-1] if len(res_nodes) >= 2 else []))
                r = stt(p, rsum[:, oc * 512:(oc + 1) * 512], xsl, 2.0,
                        pp0[:, 0:512] if oc % 2 == 0 else pp1[:, 0:512],
                        ALU.mult, ALU.add, [g, xl])
                res_nodes.append(r)
            outs = emit_ln(p, lambda c: rsum[:, c * 512:(c + 1) * 512],
                           lambda c: x1s[:, c * 512:(c + 1) * 512], 512, sc, res_nodes)
            ln_last = outs
            for oc in range(8):
                stores.append(p.dma("sp", y_x1[oc * 128:(oc + 1) * 128, tc * 512:(tc + 1) * 512],
                                    x1s[:, oc * 512:(oc + 1) * 512], [outs[oc]]))
                rd = vred(p, mcol[:, 1:2], x1s[:, oc * 512:(oc + 1) * 512], ALU.max,
                          [outs[oc], prev_red], absval=True)
                prev_red = tt(p, mcol[:, 0:1], mcol[:, 0:1], mcol[:, 1:2], ALU.max, [rd])
        n_tp = p.add("pe", lambda e: e.transpose(pmx[0:1, 0:128], mcol[:, 0:1], idn_s[:, :]),
                     [prev_red, n_idn])
        n_ev = acopy(p, mrow[0:1, :], pmx[0:1, 0:128], [n_tp])
        n_fm = vred(p, mcol[0:1, 0:1], mrow[0:1, :], ALU.max, [n_ev])
        stores.append(p.dma("sp", y_mx, mcol[0:1, 0:1], [n_fm]))
        sink(p, stores)
    return p.build()



def tiles_of(h):
    return [8 * j + h + 2 * m for j in range(4) for m in range(4)]


def make_sel_qkv():
    sel = np.zeros((128, 128), np.float32)
    for oc in range(8):
        for r in range(128):
            head = (oc * 128 + r) // 64
            sel[r, oc * 16 + head] = 1.0
    return sel


def make_masks(h):
    m = np.zeros((128, 8, 4, 128), np.float32)
    tri = (np.arange(128)[:, None] <= np.arange(128)[None, :])
    for u in range(8):
        for sm in range(4):
            d = h + 2 * sm
            if u < d:
                m[:, u, sm, :] = 1.0
            elif u == d:
                m[:, u, sm, :] = tri
    return bf16c(m.reshape(128, 8 * 512))


def dq_pack(c, Wup0, Wg0, Wup1, Wg1, Wdn0, Wdn1):
    rows = []
    for W in (Wup0, Wg0, Wup1, Wg1):
        rows.append(W[c * 384:(c + 1) * 384, :])
    for W in (Wdn0, Wdn1):
        sl = W[c * 128:(c + 1) * 128, :]
        rows.append(sl.reshape(128, 3, 1024).transpose(1, 0, 2).reshape(384, 1024))
    return f32c(np.concatenate(rows, 0))


def dq_unpack(outs):
    ups0, gs0, ups1, gs1, dn0, dn1 = [], [], [], [], [], []
    for c in range(8):
        o = outs[c]
        ups0.append(o[0:384]); gs0.append(o[384:768])
        ups1.append(o[768:1152]); gs1.append(o[1152:1536])
        dn0.append(o[1536:1920].reshape(3, 128, 1024).transpose(1, 0, 2).reshape(128, 3072))
        dn1.append(o[1920:2304].reshape(3, 128, 1024).transpose(1, 0, 2).reshape(128, 3072))
    return (np.concatenate(ups0, 0), np.concatenate(gs0, 0),
            np.concatenate(ups1, 0), np.concatenate(gs1, 0),
            np.concatenate(dn0, 0), np.concatenate(dn1, 0))


ONES_COL = np.ones((128, 1), np.float32)
ONES_ROW = np.ones((1, 128), np.float32)
IDENT = np.eye(128, dtype=np.float32)


def run_qkv(x, Wq, Wk, Wv, dq_packs):
    nc = qkv_prog()
    in_maps = []
    sel = make_sel_qkv()
    for c in range(8):
        b, h = c // 2, c % 2
        in_maps.append({
            "xT": f32c(x[b, h * 2048:(h + 1) * 2048, :].T),
            "wq": f32c(Wq), "wk": f32c(Wk), "wv": f32c(Wv),
            "sel": sel, "dq_in": dq_packs[c],
        })
    return run_bass_kernel_spmd(nc, in_maps, core_ids=list(range(8))).results


def run_attn(x, qkv_res, Wo):
    nc = attn_prog()
    in_maps = []
    for c in range(8):
        b, h = c // 2, c % 2
        ca, cb = 2 * b, 2 * b + 1
        qf = np.concatenate([qkv_res[ca]["y_q"], qkv_res[cb]["y_q"]], axis=1)
        kf = np.concatenate([qkv_res[ca]["y_k"], qkv_res[cb]["y_k"]], axis=1)
        vf = np.concatenate([qkv_res[ca]["y_v"], qkv_res[cb]["y_v"]], axis=0)
        tl = tiles_of(h)
        cols = np.concatenate([np.arange(g * 128, (g + 1) * 128) for g in tl])
        vo = np.ones((S, NH * 65), bf16)
        for hh in range(NH):
            vo[:, hh * 65:hh * 65 + 64] = vf[:, hh * 64:(hh + 1) * 64]
        qn2f = np.concatenate([qkv_res[ca]["y_qn2"], qkv_res[cb]["y_qn2"]], axis=1)
        kn2f = np.concatenate([qkv_res[ca]["y_kn2"], qkv_res[cb]["y_kn2"]], axis=1)
        n2in = np.concatenate([qn2f[:, tl], kn2f], axis=1)
        in_maps.append({
            "qT": np.ascontiguousarray(qf[:, cols]),
            "kT": np.ascontiguousarray(kf),
            "vo": np.ascontiguousarray(vo),
            "xT": f32c(x[b].T[:, cols]),
            "wo": f32c(Wo),
            "n2in": f32c(n2in),
            "msk": make_masks(h),
            "onc": ONES_COL, "onr": ONES_ROW, "idn": IDENT,
        })
    return run_bass_kernel_spmd(nc, in_maps, core_ids=list(range(8))).results


def gather_interleaved(res, key):
    out = np.empty((B, S, D), np.float32)
    for c in range(8):
        b, h = c // 2, c % 2
        tl = tiles_of(h)
        cols = np.concatenate([np.arange(g * 128, (g + 1) * 128) for g in tl])
        out[b, cols, :] = res[c][key].T
    return out


_PROGS = {}


def qkv_prog():
    if "qkv" not in _PROGS:
        _PROGS["qkv"] = build_qkv()
    return _PROGS["qkv"]


def attn_prog():
    if "attn" not in _PROGS:
        _PROGS["attn"] = build_attn()
    return _PROGS["attn"]



def _emit_scales(p, nc, mx_ap, n_mx, srow, sxb, onr_ap, n_onr, pbc, specs):
    raise NotImplementedError


def build_ffn1():
    p = Prog("ffn1")
    nc = p.nc
    xT = nc.dram_tensor("xT", [D, TOK], F32, kind="ExternalInput").ap()
    mxs = nc.dram_tensor("mxs", [1, 8], F32, kind="ExternalInput").ap()
    wupT = nc.dram_tensor("wupT", [D, FI], BF16, kind="ExternalInput").ap()
    wgT = nc.dram_tensor("wgT", [D, FI], BF16, kind="ExternalInput").ap()
    onr = nc.dram_tensor("onr", [1, 128], F32, kind="ExternalInput").ap()
    idn = nc.dram_tensor("idn", [128, 128], F32, kind="ExternalInput").ap()
    y_up = nc.dram_tensor("y_up", [FI, TOK], F32, kind="ExternalOutput").ap()
    y_g = nc.dram_tensor("y_g", [FI, TOK], F32, kind="ExternalOutput").ap()
    y_m = nc.dram_tensor("y_m", [1, 2], F32, kind="ExternalOutput").ap()

    xs = p.sbuf("xs", [128, 2 * TOK], F32)
    xq = p.sbuf("xq", [128, 8 * TOK], BF16)
    tqs = p.sbuf("tqs", [128, TOK], F32)
    wtb = p.sbuf("wtb", [128, 2 * 1024], BF16)
    srow = p.sbuf("srow", [1, 2], F32)
    sxb = p.sbuf("sxb", [128, 2], F32)
    ring = p.sbuf("ring", [128, 4 * 512], F32)
    mc = p.sbuf("mc", [128, 4], F32)
    mrow = p.sbuf("mrow", [1, 128], F32)
    onr_s = p.sbuf("onr_s", [1, 128], F32)
    idn_s = p.sbuf("idn_s", [128, 128], F32)
    pq0 = p.psum("pq0", [128, 512])
    pq1 = p.psum("pq1", [128, 512])
    pq2 = p.psum("pq2", [128, 512])
    pq3 = p.psum("pq3", [128, 512])
    pbc = p.psum("pbc", [128, 2])
    pmx = p.psum("pmx", [1, 128])

    n_onr = p.dma("sp", onr_s[:], onr)
    n_idn = p.dma("sp", idn_s[:], idn)
    n_mx = p.dma("sp", mrow[0:1, 0:8], mxs)
    r_m = vred(p, srow[0:1, 0:1], mrow[0:1, 0:8], ALU.max, [n_mx])
    r_s = ts(p, srow[0:1, 0:1], srow[0:1, 0:1], 1.0 / 32767.0, 1e-12, ALU.mult, ALU.max, [r_m])
    r_r = vrecip(p, srow[0:1, 1:2], srow[0:1, 0:1], [r_s])
    n_bc = mm(p, pbc[:, 0:2], onr_s[0:1, :], srow[0:1, 0:2], True, True, [r_r, n_onr])
    n_sx = acopy(p, sxb[:], pbc[:, 0:2], [n_bc])

    xq_nodes = []
    for c in range(8):
        buf = c % 2
        deps = [xq_nodes[c - 2]] if c >= 2 else []
        ld = p.dma("sp", xs[:, buf * TOK:(buf + 1) * TOK], xT[c * 128:(c + 1) * 128, :], deps)
        t1 = p.add("dve", (lambda e, a=xs[:, buf * TOK:(buf + 1) * TOK], o=tqs[:]:
                           e.tensor_scalar(o, a, sxb[:, 1:2], MAGIC, ALU.mult, ALU.add)),
                   [ld, n_sx] + deps)
        t2 = p.add("dve", (lambda e, o=xq[:, c * TOK:(c + 1) * TOK], a=tqs[:]:
                           e.tensor_scalar(o, a, MAGIC, sxb[:, 0:1], ALU.subtract, ALU.mult)),
                   [t1])
        xq_nodes.append(t2)

    stores = []
    use = 0
    prev_ev = {}
    prev_rd = {}
    prev_red = None
    mzero = p.add("dve", lambda e: e.memset(mc[:, 0:2], 0.0), [])
    wcons = {}
    for mi, (wsrc, ydst) in enumerate(((wupT, y_up), (wgT, y_g))):
        for ic in range(24):
            wbuf = (mi * 24 + ic) % 2
            wdeps = []
            if (mi * 24 + ic) >= 2:
                wdeps.append(wcons[mi * 24 + ic - 2])
            wl = p.dma("pool", wtb[:, wbuf * 1024:(wbuf + 1) * 1024]
                       .rearrange("p (c w) -> p c w", w=128),
                       wsrc[:, ic * 128:(ic + 1) * 128].rearrange("(c p) w -> p c w", p=128),
                       wdeps)
            for tc in range(4):
                pt = (pq0, pq1, pq2, pq3)[use % 4]
                deps = [wl, xq_nodes[-1]]
                if use >= 4:
                    deps.append(prev_ev[use - 4])
                    deps.append(prev_rd[use - 4])
                g = mm_group(p, pt[:, :],
                             [(wtb[:, wbuf * 1024 + kc * 128:wbuf * 1024 + (kc + 1) * 128],
                               xq[:, kc * TOK + tc * 512:kc * TOK + (tc + 1) * 512])
                              for kc in range(8)], deps)
                rdeps = [g]
                if prev_red is not None:
                    rdeps.append(prev_red)
                rd = vred(p, mc[:, 2 + mi:3 + mi], pt[:, :], ALU.max, rdeps, absval=True)
                prev_rd[use] = rd
                mx = tt(p, mc[:, mi:mi + 1], mc[:, mi:mi + 1], mc[:, 2 + mi:3 + mi],
                        ALU.max, [rd, mzero])
                prev_red = mx
                rs = use % 4
                edeps = [g, rd]
                if use >= 4:
                    edeps.append(stores[use - 4])
                ev = acopy(p, ring[:, rs * 512:(rs + 1) * 512], pt[:, :], edeps)
                st = p.dma("sp", ydst[ic * 128:(ic + 1) * 128, tc * 512:(tc + 1) * 512],
                           ring[:, rs * 512:(rs + 1) * 512], [ev])
                prev_ev[use] = ev
                stores.append(st)
                use += 1
            wcons[mi * 24 + ic] = g
    n_tp = p.add("pe", lambda e: e.transpose(pmx[0:1, :], mc[:, 0:1], idn_s[:, :]),
                 [prev_red, n_idn])
    n_e1 = acopy(p, mrow[0:1, :], pmx[0:1, :], [n_tp])
    f1 = vred(p, mc[0:1, 2:3], mrow[0:1, :], ALU.max, [n_e1, prev_red])
    n_tp2 = p.add("pe", lambda e: e.transpose(pmx[0:1, :], mc[:, 1:2], idn_s[:, :]),
                  [f1, n_tp])
    n_e2 = acopy(p, mrow[0:1, :], pmx[0:1, :], [n_tp2, f1])
    f2 = vred(p, mc[0:1, 3:4], mrow[0:1, :], ALU.max, [n_e2, f1])
    st_m = p.dma("sp", y_m, mc[0:1, 2:4], [f2])
    sink(p, stores + [st_m])
    return p.build()


def build_ffn2():
    p = Prog("ffn2")
    nc = p.nc
    gT = nc.dram_tensor("gT", [FI, TOK], F32, kind="ExternalInput").ap()
    m16 = nc.dram_tensor("m16", [1, 16], F32, kind="ExternalInput").ap()
    onr = nc.dram_tensor("onr", [1, 128], F32, kind="ExternalInput").ap()
    idn = nc.dram_tensor("idn", [128, 128], F32, kind="ExternalInput").ap()
    y_gi = nc.dram_tensor("y_gi", [FI, TOK], I16, kind="ExternalOutput").ap()
    y_sr = nc.dram_tensor("y_sr", [FI, TOK], U16, kind="ExternalOutput").ap()
    y_m = nc.dram_tensor("y_m", [1, 1], F32, kind="ExternalOutput").ap()

    mr = p.sbuf("mr", [1, 16], F32)
    srow = p.sbuf("srow", [1, 4], F32)
    sgb = p.sbuf("sgb", [128, 4], F32)
    gld = p.sbuf("gld", [128, 4 * 512], F32)
    tf = p.sbuf("tf", [128, 2 * 512], F32)
    tf2 = p.sbuf("tf2", [128, 512], F32)
    tf3 = p.sbuf("tf3", [128, 512], F32)
    sg = p.sbuf("sg", [128, 2 * 512], F32)
    gi = p.sbuf("gi", [128, 4 * 512], I16)
    sr = p.sbuf("sr", [128, 4 * 512], U16)
    mc = p.sbuf("mc", [128, 2], F32)
    mrow = p.sbuf("mrow", [1, 128], F32)
    onr_s = p.sbuf("onr_s", [1, 128], F32)
    idn_s = p.sbuf("idn_s", [128, 128], F32)
    pbc = p.psum("pbc", [128, 4])
    pmx = p.psum("pmx", [1, 128])

    n_onr = p.dma("sp", onr_s[:], onr)
    n_idn = p.dma("sp", idn_s[:], idn)
    n_mx = p.dma("sp", mr[:], m16)
    r_m = vred(p, srow[0:1, 0:2], mr[0:1, :].rearrange("a (b two) -> a two b", two=2),
               ALU.max, [n_mx])
    r_s = ts(p, srow[0:1, 0:1], srow[0:1, 1:2], 1.0 / 32767.0, 1e-12, ALU.mult, ALU.max, [r_m])
    r_r = vrecip(p, srow[0:1, 1:2], srow[0:1, 0:1], [r_s])
    r_2 = ts(p, srow[0:1, 2:3], srow[0:1, 0:1], 1.0 / 65536.0, None, ALU.mult, ALU.bypass, [r_r])
    n_bc = mm(p, pbc[:, 0:4], onr_s[0:1, :], srow[0:1, 0:4], True, True, [r_2, n_onr])
    n_sx = acopy(p, sgb[:], pbc[:, 0:4], [n_bc])

    stores = []
    prev = {}
    prev_red = None
    mzero = p.add("dve", lambda e: e.memset(mc[:, 0:1], 0.0), [])
    st1s, st2s = {}, {}
    stage1 = {}

    def emit_s1(i):
        ic, tc = i // 4, i % 4
        bq = i % 4
        pb = i % 2
        ldeps = [prev[i - 4]] if i >= 4 else []
        ld = p.dma("sp", gld[:, bq * 512:(bq + 1) * 512],
                   gT[ic * 128:(ic + 1) * 128, tc * 512:(tc + 1) * 512], ldeps)
        t1deps = [ld, n_sx]
        if i >= 2:
            t1deps.append(stage1[i - 2]["t2"])
        t1 = p.add("dve", (lambda e, a=gld[:, bq * 512:(bq + 1) * 512], pb=pb:
                           e.tensor_scalar(tf[:, pb * 512:(pb + 1) * 512], a,
                                           sgb[:, 1:2], MAGIC, ALU.mult, ALU.add)),
                   t1deps)
        t2 = p.add("dve", (lambda e, o=gi[:, bq * 512:(bq + 1) * 512], pb=pb:
                           e.tensor_scalar(o, tf[:, pb * 512:(pb + 1) * 512], MAGIC, None,
                                           ALU.subtract, ALU.bypass)),
                   [t1] + ([st1s[i - 4]] if i >= 4 else []))
        st1 = p.dma("pool", y_gi[ic * 128:(ic + 1) * 128, tc * 512:(tc + 1) * 512],
                    gi[:, bq * 512:(bq + 1) * 512], [t2])
        sdep = [t2]
        if i >= 2:
            sdep.append(stage1[i - 2]["t3"])
        a_s = act(p, sg[:, pb * 512:(pb + 1) * 512], gi[:, bq * 512:(bq + 1) * 512],
                  AF.Sigmoid, sdep, scale=sgb[:, 0:1])
        st1s[i] = st1
        stage1[i] = {"t2": t2, "a_s": a_s, "st1": st1, "t3": None}

    def emit_s2(i, prev_red):
        ic, tc = i // 4, i % 4
        bq = i % 4
        pb = i % 2
        a_s = stage1[i]["a_s"]
        t3 = ts(p, tf2[:], sg[:, pb * 512:(pb + 1) * 512], 65536.0, MAGIC,
                ALU.mult, ALU.add, [a_s])
        stage1[i]["t3"] = t3
        t4 = p.add("dve", (lambda e, o=sr[:, bq * 512:(bq + 1) * 512]:
                           e.tensor_scalar(o, tf2[:], MAGIC, 65535.0, ALU.subtract, ALU.min)),
                   [t3] + ([st2s[i - 4]] if i >= 4 else []))
        st2 = p.dma("pool", y_sr[ic * 128:(ic + 1) * 128, tc * 512:(tc + 1) * 512],
                    sr[:, bq * 512:(bq + 1) * 512], [t4])
        g2 = p.add("dve", (lambda e, a=sr[:, bq * 512:(bq + 1) * 512],
                           b=gi[:, bq * 512:(bq + 1) * 512]:
                           e.scalar_tensor_tensor(tf3[:], a, sgb[:, 2:3], b, ALU.mult, ALU.mult)),
                   [t4])
        rdeps = [g2]
        if prev_red is not None:
            rdeps.append(prev_red)
        rd = vred(p, mc[:, 1:2], tf3[:], ALU.max, rdeps, absval=True)
        nred = tt(p, mc[:, 0:1], mc[:, 0:1], mc[:, 1:2], ALU.max, [rd, mzero])
        prev[i] = g2
        st2s[i] = st2
        stores.extend([stage1[i]["st1"], st2])
        return nred

    emit_s1(0)
    for i in range(96):
        if i + 1 < 96:
            emit_s1(i + 1)
        prev_red = emit_s2(i, prev_red)
    n_tp = p.add("pe", lambda e: e.transpose(pmx[0:1, :], mc[:, 0:1], idn_s[:, :]),
                 [prev_red, n_idn])
    n_e1 = acopy(p, mrow[0:1, :], pmx[0:1, :], [n_tp])
    f1 = vred(p, mc[0:1, 1:2], mrow[0:1, :], ALU.max, [n_e1, prev_red])
    st_m = p.dma("sp", y_m, mc[0:1, 1:2], [f1])
    sink(p, stores + [st_m])
    return p.build()


def build_ffn3():
    p = Prog("ffn3")
    nc = p.nc
    uT = nc.dram_tensor("uT", [FI, TOK], F32, kind="ExternalInput").ap()
    giT = nc.dram_tensor("giT", [FI, TOK], I16, kind="ExternalInput").ap()
    srT = nc.dram_tensor("srT", [FI, TOK], U16, kind="ExternalInput").ap()
    m24 = nc.dram_tensor("m24", [1, 24], F32, kind="ExternalInput").ap()
    onr = nc.dram_tensor("onr", [1, 128], F32, kind="ExternalInput").ap()
    idn = nc.dram_tensor("idn", [128, 128], F32, kind="ExternalInput").ap()
    y_ui = nc.dram_tensor("y_ui", [FI, TOK], I16, kind="ExternalOutput").ap()
    y_g2i = nc.dram_tensor("y_g2i", [FI, TOK], I16, kind="ExternalOutput").ap()
    y_m = nc.dram_tensor("y_m", [1, 1], F32, kind="ExternalOutput").ap()

    mr = p.sbuf("mr", [1, 24], F32)
    srow = p.sbuf("srow", [1, 8], F32)
    sgb = p.sbuf("sgb", [128, 8], F32)
    uld = p.sbuf("uld", [128, 4 * 512], F32)
    gild = p.sbuf("gild", [128, 4 * 512], I16)
    srld = p.sbuf("srld", [128, 4 * 512], U16)
    tf = p.sbuf("tf", [128, 512], F32)
    tf2 = p.sbuf("tf2", [128, 512], F32)
    tf3 = p.sbuf("tf3", [128, 512], F32)
    ui = p.sbuf("ui", [128, 4 * 512], I16)
    g2i = p.sbuf("g2i", [128, 4 * 512], I16)
    mc = p.sbuf("mc", [128, 2], F32)
    mrow = p.sbuf("mrow", [1, 128], F32)
    onr_s = p.sbuf("onr_s", [1, 128], F32)
    idn_s = p.sbuf("idn_s", [128, 128], F32)
    pbc = p.psum("pbc", [128, 8])
    pmx = p.psum("pmx", [1, 128])

    n_onr = p.dma("sp", onr_s[:], onr)
    n_idn = p.dma("sp", idn_s[:], idn)
    n_mx = p.dma("sp", mr[:], m24)
    r_m = vred(p, srow[0:1, 0:3], mr[0:1, :].rearrange("a (b three) -> a three b", three=3),
               ALU.max, [n_mx])
    r1 = ts(p, srow[0:1, 3:4], srow[0:1, 0:1], 1.0 / 32767.0, 1e-12, ALU.mult, ALU.max, [r_m])
    r2 = ts(p, srow[0:1, 5:6], srow[0:1, 1:2], 1.0 / 32767.0, 1e-12, ALU.mult, ALU.max, [r1])
    r3 = ts(p, srow[0:1, 6:7], srow[0:1, 2:3], 1.0 / 32767.0, 1e-12, ALU.mult, ALU.max, [r2])
    r4 = vrecip(p, srow[0:1, 4:5], srow[0:1, 3:4], [r3])
    r5 = vrecip(p, srow[0:1, 7:8], srow[0:1, 6:7], [r4])
    r6 = ts(p, srow[0:1, 0:1], srow[0:1, 5:6], 1.0 / 65536.0, None, ALU.mult, ALU.bypass, [r5])
    r7 = tt(p, srow[0:1, 1:2], srow[0:1, 6:7], srow[0:1, 3:4], ALU.mult, [r6])
    n_bc = mm(p, pbc[:, 0:8], onr_s[0:1, :], srow[0:1, 0:8], True, True, [r7, n_onr])
    n_sx = acopy(p, sgb[:], pbc[:, 0:8], [n_bc])

    stores = []
    prev = {}
    prev_red = None
    mzero = p.add("dve", lambda e: e.memset(mc[:, 0:1], 0.0), [])
    s1s, s2s = {}, {}
    for i in range(96):
        ic, tc = i // 4, i % 4
        bq = i % 4
        sl = (bq * 512, (bq + 1) * 512)
        ldeps = [prev[i - 4]] if i >= 4 else []
        l1 = p.dma("pool", uld[:, sl[0]:sl[1]],
                   uT[ic * 128:(ic + 1) * 128, tc * 512:(tc + 1) * 512], ldeps)
        l2 = p.dma("pool", gild[:, sl[0]:sl[1]],
                   giT[ic * 128:(ic + 1) * 128, tc * 512:(tc + 1) * 512], ldeps)
        l3 = p.dma("pool", srld[:, sl[0]:sl[1]],
                   srT[ic * 128:(ic + 1) * 128, tc * 512:(tc + 1) * 512], ldeps)
        t1 = p.add("dve", (lambda e, a=uld[:, sl[0]:sl[1]]:
                           e.tensor_scalar(tf[:], a, sgb[:, 4:5], MAGIC, ALU.mult, ALU.add)),
                   [l1, n_sx])
        t2 = p.add("dve", (lambda e, o=ui[:, sl[0]:sl[1]]:
                           e.tensor_scalar(o, tf[:], MAGIC, None, ALU.subtract, ALU.bypass)),
                   [t1] + ([s1s[i - 4]] if i >= 4 else []))
        s1 = p.dma("sp", y_ui[ic * 128:(ic + 1) * 128, tc * 512:(tc + 1) * 512],
                   ui[:, sl[0]:sl[1]], [t2])
        g2 = p.add("dve", (lambda e, a=srld[:, sl[0]:sl[1]], b=gild[:, sl[0]:sl[1]]:
                           e.scalar_tensor_tensor(tf2[:], a, sgb[:, 0:1], b, ALU.mult, ALU.mult)),
                   [l2, l3, n_sx])
        t3 = p.add("dve", (lambda e:
                           e.tensor_scalar(tf[:], tf2[:], sgb[:, 7:8], MAGIC, ALU.mult, ALU.add)),
                   [g2, t2])
        t4 = p.add("dve", (lambda e, o=g2i[:, sl[0]:sl[1]]:
                           e.tensor_scalar(o, tf[:], MAGIC, None, ALU.subtract, ALU.bypass)),
                   [t3] + ([s2s[i - 4]] if i >= 4 else []))
        s2 = p.dma("sp", y_g2i[ic * 128:(ic + 1) * 128, tc * 512:(tc + 1) * 512],
                   g2i[:, sl[0]:sl[1]], [t4])
        oo = p.add("dve", (lambda e, a=g2i[:, sl[0]:sl[1]], b=ui[:, sl[0]:sl[1]]:
                           e.scalar_tensor_tensor(tf3[:], a, sgb[:, 1:2], b, ALU.mult, ALU.mult)),
                   [t4])
        rdeps = [oo]
        if prev_red is not None:
            rdeps.append(prev_red)
        rd = vred(p, mc[:, 1:2], tf3[:], ALU.max, rdeps, absval=True)
        prev_red = tt(p, mc[:, 0:1], mc[:, 0:1], mc[:, 1:2], ALU.max, [rd, mzero])
        prev[i] = oo
        s1s[i], s2s[i] = s1, s2
        stores += [s1, s2]
    n_tp = p.add("pe", lambda e: e.transpose(pmx[0:1, :], mc[:, 0:1], idn_s[:, :]),
                 [prev_red, n_idn])
    n_e1 = acopy(p, mrow[0:1, :], pmx[0:1, :], [n_tp])
    f1 = vred(p, mc[0:1, 1:2], mrow[0:1, :], ALU.max, [n_e1, prev_red])
    st_m = p.dma("sp", y_m, mc[0:1, 1:2], [f1])
    sink(p, stores + [st_m])
    return p.build()


def build_ffn4():
    p = Prog("ffn4")
    nc = p.nc
    giT = nc.dram_tensor("giT", [FI, TOK], I16, kind="ExternalInput").ap()
    uiT = nc.dram_tensor("uiT", [FI, TOK], I16, kind="ExternalInput").ap()
    m32 = nc.dram_tensor("m32", [1, 32], F32, kind="ExternalInput").ap()
    wdT = nc.dram_tensor("wdT", [FI, D], BF16, kind="ExternalInput").ap()
    xT = nc.dram_tensor("xT", [D, TOK], F32, kind="ExternalInput").ap()
    onc = nc.dram_tensor("onc", [128, 1], F32, kind="ExternalInput").ap()
    onr = nc.dram_tensor("onr", [1, 128], F32, kind="ExternalInput").ap()
    idn = nc.dram_tensor("idn", [128, 128], F32, kind="ExternalInput").ap()
    y_x2 = nc.dram_tensor("y_x2", [D, TOK], F32, kind="ExternalOutput").ap()
    y_mx = nc.dram_tensor("y_mx", [1, 1], F32, kind="ExternalOutput").ap()

    mr = p.sbuf("mr", [1, 32], F32)
    srow = p.sbuf("srow", [1, 8], F32)
    sgb = p.sbuf("sgb", [128, 8], F32)
    gild = p.sbuf("gild", [128, 4 * 512], I16)
    uild = p.sbuf("uild", [128, 4 * 512], I16)
    tf = p.sbuf("tf", [128, 512], F32)
    tf2 = p.sbuf("tf2", [128, 512], F32)
    obf = p.sbuf("obf", [128, 2 * 24 * 512], BF16)
    wdb = p.sbuf("wdb", [128, 2 * 24 * 128], BF16)
    xrs = p.sbuf("xrs", [128, 2 * 512], F32)
    rsum = p.sbuf("rsum", [128, 8 * 512], F32)
    x1s = p.sbuf("x1s", [128, 8 * 512], F32)
    sq = p.sbuf("sqln", [128, 8 * 512], F32)
    lnA = p.sbuf("lnA", [65, 512], F32)
    rrstd = p.sbuf("rrstd", [1, 512], F32)
    rmr = p.sbuf("rmr", [1, 512], F32)
    mc = p.sbuf("mc", [128, 2], F32)
    mrow = p.sbuf("mrow", [1, 128], F32)
    onc_s = p.sbuf("onc_s", [128, 1], F32)
    onr_s = p.sbuf("onr_s", [1, 128], F32)
    idn_s = p.sbuf("idn_s", [128, 128], F32)
    pq0 = p.psum("pq0", [128, 512])
    pq1 = p.psum("pq1", [128, 512])
    pst = p.psum("pst", [64, 512])
    pb1 = p.psum("pb1", [128, 512])
    pb2 = p.psum("pb2", [128, 512])
    pbc = p.psum("pbc", [128, 8])
    pmx = p.psum("pmx", [1, 128])

    n_onc = p.dma("sp", onc_s[:], onc)
    n_onr = p.dma("sp", onr_s[:], onr)
    n_idn = p.dma("sp", idn_s[:], idn)
    n_mx = p.dma("sp", mr[:], m32)
    r_m = vred(p, srow[0:1, 0:4], mr[0:1, :].rearrange("a (b four) -> a four b", four=4),
               ALU.max, [n_mx])
    r1 = ts(p, srow[0:1, 4:5], srow[0:1, 0:1], 1.0 / 32767.0, 1e-12, ALU.mult, ALU.max, [r_m])
    r2 = ts(p, srow[0:1, 5:6], srow[0:1, 2:3], 1.0 / 32767.0, 1e-12, ALU.mult, ALU.max, [r1])
    r3 = ts(p, srow[0:1, 6:7], srow[0:1, 3:4], 1.0 / 32767.0, 1e-12, ALU.mult, ALU.max, [r2])
    r4 = tt(p, srow[0:1, 0:1], srow[0:1, 5:6], srow[0:1, 4:5], ALU.mult, [r3])
    r5 = vrecip(p, srow[0:1, 1:2], srow[0:1, 6:7], [r4])
    n_bc = mm(p, pbc[:, 0:8], onr_s[0:1, :], srow[0:1, 0:8], True, True, [r5, n_onr])
    n_sx = acopy(p, sgb[:], pbc[:, 0:8], [n_bc])

    sc = {"ones_col": onc_s[:, 0:1], "ones_row": onr_s[0:1, :], "sq": sq,
          "stat": pst, "bc1": pb1, "bc2": pb2,
          "rmean": lnA[0:1, :], "rvar": lnA[32:33, :], "rtmp": lnA[64:65, :],
          "rrstd": rrstd[0:1, :], "rmr": rmr[0:1, :]}
    stores = []
    prev_o = {}
    prev_red = None
    mzero = p.add("dve", lambda e: e.memset(mc[:, 0:1], 0.0), [])
    ln_prev = None
    wl_prev = {}
    res_prev = {}
    chunks = {0: [], 1: [], 2: [], 3: []}
    OBH = 24 * 512

    def prep_chunk(tc, ic):
        i = tc * 24 + ic
        bq = i % 4
        sl = (bq * 512, (bq + 1) * 512)
        ob = (tc % 2) * OBH
        ldeps = [prev_o[i - 4]] if i >= 4 else []
        l1 = p.dma("pool", gild[:, sl[0]:sl[1]],
                   giT[ic * 128:(ic + 1) * 128, tc * 512:(tc + 1) * 512], ldeps)
        l2 = p.dma("pool", uild[:, sl[0]:sl[1]],
                   uiT[ic * 128:(ic + 1) * 128, tc * 512:(tc + 1) * 512], ldeps)
        oo = p.add("dve", (lambda e, a=gild[:, sl[0]:sl[1]], b=uild[:, sl[0]:sl[1]]:
                           e.scalar_tensor_tensor(tf[:], a, sgb[:, 0:1], b, ALU.mult, ALU.mult)),
                   [l1, l2, n_sx])
        t1 = p.add("dve", (lambda e:
                           e.tensor_scalar(tf2[:], tf[:], sgb[:, 1:2], MAGIC, ALU.mult, ALU.add)),
                   [oo])
        odeps = [t1]
        if tc >= 2:
            odeps.append(res_prev[(tc - 2, 7)])
        t2 = p.add("dve", (lambda e, o=obf[:, ob + ic * 512:ob + (ic + 1) * 512]:
                           e.tensor_scalar(o, tf2[:], MAGIC, sgb[:, 6:7], ALU.subtract, ALU.mult)),
                   odeps)
        prev_o[i] = t2
        chunks[tc].append(t2)

    for ic in range(24):
        prep_chunk(0, ic)
    for tc in range(4):
        o_nodes = chunks[tc]
        obb = (tc % 2) * OBH
        res_nodes = []
        for oc in range(8):
            j = tc * 8 + oc
            wbuf = j % 2
            wdeps = [wl_prev[j - 2]] if j >= 2 else []
            wl = p.dma("pool", wdb[:, wbuf * 3072:(wbuf + 1) * 3072]
                       .rearrange("p (c w) -> p c w", w=128),
                       wdT[:, oc * 128:(oc + 1) * 128].rearrange("(c p) w -> p c w", p=128),
                       wdeps)
            pt = (pq0, pq1)[oc % 2]
            gdeps = o_nodes + [wl]
            if len(res_nodes) >= 2:
                gdeps.append(res_nodes[-2])
            elif tc > 0:
                gdeps.append(res_prev[(tc - 1, 6 + oc)] if (6 + oc) <= 7 else None)
            g = mm_group(p, pt[:, :],
                         [(wdb[:, wbuf * 3072 + kc * 128:wbuf * 3072 + (kc + 1) * 128],
                           obf[:, obb + kc * 512:obb + (kc + 1) * 512])
                          for kc in range(24)], [dd for dd in gdeps if dd])
            xsl = xrs[:, (oc % 2) * 512:(oc % 2 + 1) * 512]
            xwar = res_nodes[-2:-1] if len(res_nodes) >= 2 else (
                [ln_prev] if ln_prev is not None else [])
            xl = p.dma("pool", xsl, xT[oc * 128:(oc + 1) * 128, tc * 512:(tc + 1) * 512], xwar)
            r = tt(p, rsum[:, oc * 512:(oc + 1) * 512], pt[:, :], xsl, ALU.add, [g, xl])
            res_nodes.append(r)
            res_prev[(tc, oc)] = r
            wl_prev[j] = g
            if tc < 3:
                for k in range(3):
                    prep_chunk(tc + 1, oc * 3 + k)
        outs = emit_ln(p, lambda c: rsum[:, c * 512:(c + 1) * 512],
                       lambda c: x1s[:, c * 512:(c + 1) * 512], 512, sc, res_nodes)
        ln_prev = outs[-1]
        for oc in range(8):
            stores.append(p.dma("sp", y_x2[oc * 128:(oc + 1) * 128, tc * 512:(tc + 1) * 512],
                                x1s[:, oc * 512:(oc + 1) * 512], [outs[oc]]))
            rdeps = [outs[oc], mzero]
            if prev_red is not None:
                rdeps.append(prev_red)
            rd = vred(p, mc[:, 1:2], x1s[:, oc * 512:(oc + 1) * 512], ALU.max, rdeps, absval=True)
            prev_red = tt(p, mc[:, 0:1], mc[:, 0:1], mc[:, 1:2], ALU.max, [rd])
    n_tp = p.add("pe", lambda e: e.transpose(pmx[0:1, :], mc[:, 0:1], idn_s[:, :]),
                 [prev_red, n_idn])
    n_e1 = acopy(p, mrow[0:1, :], pmx[0:1, :], [n_tp])
    f1 = vred(p, mc[0:1, 1:2], mrow[0:1, :], ALU.max, [n_e1, prev_red])
    st_m = p.dma("sp", y_mx, mc[0:1, 1:2], [f1])
    sink(p, stores + [st_m])
    return p.build()


def run_ffn(x1, mx_list, wupT_bf, wgT_bf, wdnT_bf):
    def xmaps():
        ms = []
        for c in range(8):
            b, h = c // 2, c % 2
            ms.append(f32c(x1[b, h * 2048:(h + 1) * 2048, :].T))
        return ms

    xTs = xmaps()
    mxs = np.concatenate([np.asarray(m).reshape(1, 1) for m in mx_list], axis=1)
    p1 = run_bass_kernel_spmd(ffn1_prog(), [
        {"xT": xTs[c], "mxs": f32c(mxs), "wupT": wupT_bf, "wgT": wgT_bf,
         "onr": ONES_ROW, "idn": IDENT} for c in range(8)],
        core_ids=list(range(8))).results
    m16 = np.concatenate([p1[c]["y_m"] for c in range(8)], axis=1)
    p2 = run_bass_kernel_spmd(ffn2_prog(), [
        {"gT": p1[c]["y_g"], "m16": f32c(m16), "onr": ONES_ROW, "idn": IDENT}
        for c in range(8)], core_ids=list(range(8))).results
    m24 = np.concatenate([np.concatenate([p1[c]["y_m"], p2[c]["y_m"]], axis=1)
                          for c in range(8)], axis=1)
    p3 = run_bass_kernel_spmd(ffn3_prog(), [
        {"uT": p1[c]["y_up"], "giT": p2[c]["y_gi"], "srT": p2[c]["y_sr"],
         "m24": f32c(m24), "onr": ONES_ROW, "idn": IDENT}
        for c in range(8)], core_ids=list(range(8))).results
    m32 = np.concatenate([np.concatenate([p1[c]["y_m"], p2[c]["y_m"], p3[c]["y_m"]], axis=1)
                          for c in range(8)], axis=1)
    p4 = run_bass_kernel_spmd(ffn4_prog(), [
        {"giT": p3[c]["y_g2i"], "uiT": p3[c]["y_ui"], "m32": f32c(m32),
         "wdT": wdnT_bf, "xT": xTs[c],
         "onc": ONES_COL, "onr": ONES_ROW, "idn": IDENT}
        for c in range(8)], core_ids=list(range(8))).results
    x2 = np.empty((B, S, D), np.float32)
    for c in range(8):
        b, h = c // 2, c % 2
        x2[b, h * 2048:(h + 1) * 2048, :] = p4[c]["y_x2"].T
    return x2, [p4[c]["y_mx"] for c in range(8)]


def ffn1_prog():
    if "ffn1" not in _PROGS:
        _PROGS["ffn1"] = build_ffn1()
    return _PROGS["ffn1"]


def ffn2_prog():
    if "ffn2" not in _PROGS:
        _PROGS["ffn2"] = build_ffn2()
    return _PROGS["ffn2"]


def ffn3_prog():
    if "ffn3" not in _PROGS:
        _PROGS["ffn3"] = build_ffn3()
    return _PROGS["ffn3"]


def ffn4_prog():
    if "ffn4" not in _PROGS:
        _PROGS["ffn4"] = build_ffn4()
    return _PROGS["ffn4"]


XPD = 4224
DI = 1024
DM = 512


def build_mamba(debug=False):
    p = Prog("mamba")
    nc = p.nc
    xdT = nc.dram_tensor("xdT", [DM, XPD], F32, kind="ExternalInput").ap()
    inwT = nc.dram_tensor("inwT", [DM, 2 * DI], F32, kind="ExternalInput").ap()
    xpjT = nc.dram_tensor("xpjT", [DI, 64], F32, kind="ExternalInput").ap()
    dtwT = nc.dram_tensor("dtwT", [32, DI], F32, kind="ExternalInput").ap()
    outwT = nc.dram_tensor("outwT", [DI, DM], F32, kind="ExternalInput").ap()
    convW = nc.dram_tensor("convW", [128, 32], F32, kind="ExternalInput").ap()
    convB = nc.dram_tensor("convB", [128, 8], F32, kind="ExternalInput").ap()
    dtB = nc.dram_tensor("dtB", [128, 8], F32, kind="ExternalInput").ap()
    DPc = nc.dram_tensor("DPc", [128, 8], F32, kind="ExternalInput").ap()
    AL = nc.dram_tensor("AL", [128, 128], F32, kind="ExternalInput").ap()
    esel = nc.dram_tensor("esel", [80, 2048], F32, kind="ExternalInput").ap()
    idnb = nc.dram_tensor("idnb", [128, 128], BF16, kind="ExternalInput").ap()
    y_h = nc.dram_tensor("y_h", [DM, 4096], F32, kind="ExternalOutput").ap()
    kind = dict(kind="ExternalOutput") if debug else {}
    xcbuf = nc.dram_tensor("xcbuf", [DI, 4096], F32, **kind)
    zbuf = nc.dram_tensor("zbuf", [DI, XPD], F32, **kind)
    if debug:
        y_dbc = nc.dram_tensor("y_dbc", [80, 4096], F32, kind="ExternalOutput").ap()
        y_del = nc.dram_tensor("y_del", [128, 8 * 512], F32, kind="ExternalOutput").ap()
        y_B = nc.dram_tensor("y_B", [128, 16 * 512], BF16, kind="ExternalOutput").ap()
        y_yf = nc.dram_tensor("y_yf", [128, 8 * 512], BF16, kind="ExternalOutput").ap()
        y_st = nc.dram_tensor("y_st", [128, 128], F32, kind="ExternalOutput").ap()

    xdb = p.sbuf("xdb", [128, 4 * XPD], BF16)
    inwb = p.sbuf("inwb", [128, 4 * 2048], BF16)
    xpjb = p.sbuf("xpjb", [128, 8 * 64], BF16)
    dtwb = p.sbuf("dtwb", [32, 1024], BF16)
    outwb = p.sbuf("outwb", [128, 8 * 512], BF16)
    cW = p.sbuf("cW", [128, 32], F32)
    cB = p.sbuf("cB", [128, 8], F32)
    dB = p.sbuf("dB", [128, 8], F32)
    dP = p.sbuf("dP", [128, 8], F32)
    Acol = p.sbuf("Acol", [128, 128], F32)
    esl = p.sbuf("esl", [80, 2048], F32)
    idb = p.sbuf("idb", [128, 128], BF16)
    xh = p.sbuf("xh", [128, XPD], F32)
    xcc = p.sbuf("xcc", [128, 4096], F32)
    zev = p.sbuf("zev", [128, 512], F32)
    xcw = p.sbuf("xcw", [128, 8 * 512], F32)
    xcwb = p.sbuf("xcwb", [128, 2 * 512], BF16)
    dtb16 = p.sbuf("dtb16", [32, 512], BF16)
    dbcs = p.sbuf("dbcs", [80, 512], F32)
    dels = p.sbuf("dels", [128, 8 * 512], F32)
    us = p.sbuf("us", [128, 8 * 512], BF16)
    ta = p.sbuf("ta", [128, 512], F32)
    tb = p.sbuf("tb", [128, 512], F32)
    Gt = p.sbuf("Gt", [128, 4 * 512], F32)
    vt = p.sbuf("vt", [128, 4 * 512], BF16)
    ht = p.sbuf("ht", [128, 2 * 512], BF16)
    hct = p.sbuf("hct", [128, 2 * 512], BF16)
    stcols = p.sbuf("stcols", [128, 128], F32)
    sz = p.sbuf("sz", [128, 512], F32)
    yf = p.sbuf("yf", [128, 8 * 512], BF16)
    hor = p.sbuf("hor", [128, 2 * 512], F32)
    Bbc = xh[:].bitcast(BF16)
    Cbc = xcc[:].bitcast(BF16)
    pin0 = p.psum("pin0", [128, 512])
    pin1 = p.psum("pin1", [128, 512])
    pdbc = p.psum("pdbc", [128, 512])
    pdel = p.psum("pdel", [128, 512])
    pBC = p.psum("pBC", [128, 512])
    py0 = p.psum("py0", [128, 512])
    py1 = p.psum("py1", [128, 512])
    pho = p.psum("pho", [128, 512])

    l_cw = p.dma("sp", cW[:], convW)
    l_cb = p.dma("sp", cB[:], convB)
    l_db = p.dma("sp", dB[:], dtB)
    l_dp = p.dma("sp", dP[:], DPc)
    l_al = p.dma("sp", Acol[:], AL)
    l_es = p.dma("sp", esl[:], esel)
    l_id = p.dma("sp", idb[:], idnb)
    n_A1 = act(p, Acol[:], Acol[:], AF.Exp, [l_al])
    n_A = ts(p, Acol[:], Acol[:], -1.0, None, ALU.mult, ALU.bypass, [n_A1])
    casts = []

    def staged(dst, src_ap, width):
        i = len(casts)
        use_xh = (width > 4096) or (i % 2 == 0)
        sbuf_st = xh[:, 0:width] if use_xh else xcc[:, 0:width]
        dep = [c for c in casts[-2:]]
        ld = p.dma("sp" if i % 2 == 0 else "pool", sbuf_st, src_ap, dep)
        casts.append(vcopy(p, dst, sbuf_st, [ld]))
    for c in range(4):
        staged(xdb[:, c * XPD:(c + 1) * XPD], xdT[c * 128:(c + 1) * 128, :], XPD)
    for c in range(4):
        staged(inwb[:, c * 2048:(c + 1) * 2048], inwT[c * 128:(c + 1) * 128, :], 2048)
    xd_done = casts[3]
    inw_done = casts[7]
    phase1_cast_done = casts[-1]
    wcasts = []

    def staged2(dst, src_ap, width, slot):
        dep = [wcasts[-2]] if len(wcasts) >= 2 else [casts[-1]]
        st_ap = xcw[:, slot * 512: slot * 512 + width]
        ld = p.dma("pool", st_ap, src_ap, dep)
        wcasts.append(vcopy(p, dst, st_ap, [ld]))
    for c in range(8):
        staged2(xpjb[:, c * 64:(c + 1) * 64], xpjT[c * 128:(c + 1) * 128, :], 64, c % 2)
    ldw = p.dma("pool", xcw[0:32, 1024:2048], dtwT, [wcasts[-2]])
    wcasts.append(vcopy(p, dtwb[:], xcw[0:32, 1024:2048], [ldw]))
    for c in range(8):
        staged2(outwb[:, c * 512:(c + 1) * 512], outwT[c * 128:(c + 1) * 128, :], 512, 4 + c % 2)
    cast_done = wcasts[-1]

    pins = (pin0, pin1)
    use = 0
    prev_ev = {}
    conv_last = None
    xc_stores = []
    z_stores = []
    for ec in range(16):
        is_xh = ec < 8
        evs = []
        for t9 in range(9):
            w9 = 512 if t9 < 8 else XPD - 8 * 512
            pt = pins[use % 2]
            deps = [inw_done, xd_done]
            if use >= 2:
                deps.append(prev_ev[use - 2])
            g = mm_group(p, pt[:, 0:w9],
                         [(inwb[:, kc * 2048 + ec * 128:kc * 2048 + (ec + 1) * 128],
                           xdb[:, kc * XPD + t9 * 512:kc * XPD + t9 * 512 + w9])
                          for kc in range(4)], deps)
            if is_xh:
                ev = acopy(p, xh[:, t9 * 512:t9 * 512 + w9], pt[:, 0:w9],
                           [g, conv_last, phase1_cast_done])
            else:
                edeps = [g]
                if z_stores:
                    edeps.append(z_stores[-1])
                ev = acopy(p, zev[:, 0:w9], pt[:, 0:w9], edeps)
                z_stores.append(p.dma("pool", zbuf[(ec - 8) * 128:(ec - 7) * 128, t9 * 512:t9 * 512 + w9],
                                      zev[:, 0:w9], [ev]))
            prev_ev[use] = ev
            evs.append(ev)
            use += 1
        if is_xh:
            c0 = p.add("dve", (lambda e, c=ec: e.tensor_scalar(
                xcc[:], xh[:, 0:4096], cW[:, c * 4:c * 4 + 1], None, ALU.mult, ALU.bypass)),
                evs + [phase1_cast_done] + ([conv_last] if conv_last else []))
            c1 = p.add("dve", (lambda e, c=ec: e.scalar_tensor_tensor(
                xcc[:], xh[:, 1:4097], cW[:, c * 4 + 1:c * 4 + 2], xcc[:], ALU.mult, ALU.add)), [c0])
            c2 = p.add("dve", (lambda e, c=ec: e.scalar_tensor_tensor(
                xcc[:], xh[:, 2:4098], cW[:, c * 4 + 2:c * 4 + 3], xcc[:], ALU.mult, ALU.add)), [c1])
            c3 = p.add("dve", (lambda e, c=ec: e.scalar_tensor_tensor(
                xcc[:], xh[:, 3:4099], cW[:, c * 4 + 3:c * 4 + 4], xcc[:], ALU.mult, ALU.add)), [c2])
            c4 = act(p, xcc[:], xcc[:], AF.Silu, [c3, l_cb], bias=cB[:, ec:ec + 1])
            st = p.dma("pool", xcbuf[ec * 128:(ec + 1) * 128, :], xcc[:], [c4])
            xc_stores.append(st)
            conv_last = st

    n_z0 = p.add("dve", lambda e: e.memset(stcols[:], 0.0), [])
    sdel_done = {}
    prev_out = []
    prev_gate = {}
    prev_hos = {}
    prev_xcl = {}
    prev_dtev = None
    prev_scan = {}
    prev_hc = {}
    prev_ymm = {}
    prev_rep = {}
    h_stores = []
    for tw in range(8):
        tcols = (3 + tw * 512, 3 + (tw + 1) * 512)
        xcl = []
        for c in range(8):
            dep = [prev_gate[(tw - 1, c)]] if tw > 0 else [xc_stores[c], cast_done]
            if tw > 0:
                dep.append(xc_stores[c])
            xcl.append(p.dma("sp", xcw[:, c * 512:(c + 1) * 512],
                             xcbuf[c * 128:(c + 1) * 128, tw * 512:(tw + 1) * 512], dep))
        dbc_mms = []
        for c in range(8):
            cb = c % 2
            ccdep = [xcl[c]]
            if len(dbc_mms) >= 2:
                ccdep.append(dbc_mms[-2])
            elif prev_dtev is not None:
                ccdep.append(prev_dtev)
            cc = vcopy(p, xcwb[:, cb * 512:(cb + 1) * 512], xcw[:, c * 512:(c + 1) * 512], ccdep)
            g1 = mm(p, pdbc[0:48, :], xpjb[:, c * 64:c * 64 + 48],
                    xcwb[:, cb * 512:(cb + 1) * 512], c == 0, c == 7, [cc, cast_done])
            g2 = mm(p, pdbc[64:80, :], xpjb[:, c * 64 + 48:c * 64 + 64],
                    xcwb[:, cb * 512:(cb + 1) * 512], c == 0, c == 7, [cc])
            dbc_mms.append(g2)
        dbc_ev = acopy(p, dbcs[0:80, 0:512], pdbc[0:80, :], dbc_mms)
        dtev = vcopy(p, dtb16[:], dbcs[0:32, 0:512], [dbc_ev])
        prev_dtev = dtev
        del_nodes = []
        gds = {}
        t2s = {}

        def emit_gd(c):
            pdl = (pin0, pin1)[c % 2]
            gdeps = [dtev]
            if c - 2 in t2s:
                gdeps.append(t2s[c - 2])
            elif tw == 0 and c < 2:
                gdeps.extend(z_stores[-2:])
            gds[c] = mm(p, pdl[:, :], dtwb[0:32, c * 128:(c + 1) * 128], dtb16[:],
                        True, True, [d for d in gdeps if d])

        emit_gd(0)
        emit_gd(1)
        for c in range(8):
            pdl = (pin0, pin1)[c % 2]
            tah = ta[:, 0:512] if False else ta[:]
            a1 = act(p, tah, pdl[:, :], AF.Abs,
                     [gds[c], l_db] + ([del_nodes[-1]] if del_nodes else []),
                     bias=dB[:, c:c + 1])
            a2 = act(p, tah, tah, AF.Exp, [a1], scale=-1.0)
            t1 = ts(p, tah, tah, 1.0, None, ALU.add, ALU.bypass, [a2])
            a3 = act(p, tah, tah, AF.Ln, [t1])
            t2 = p.add("dve", (lambda e, c=c, pdl=pdl: e.tensor_scalar(
                tb[:], pdl[:, :], dB[:, c:c + 1], 0.0, ALU.add, ALU.max)), [a3])
            t2s[c] = t2
            if c + 2 < 8:
                emit_gd(c + 2)
            dl = tt(p, dels[:, c * 512:(c + 1) * 512], tah, tb[:], ALU.add, [t2, a3])
            uu = tt(p, us[:, c * 512:(c + 1) * 512], dels[:, c * 512:(c + 1) * 512],
                    xcw[:, c * 512:(c + 1) * 512], ALU.mult, [dl])
            del_nodes.append(uu)
        if debug:
            h_stores.append(p.dma("pool", y_dbc[:, tw * 512:(tw + 1) * 512],
                                  dbcs[0:80, tw * 512:(tw + 1) * 512], [dbc_ev]))
            if tw == 0:
                h_stores.append(p.dma("pool", y_del, dels[:], [del_nodes[-1]]))
        rep_nodes = {}
        last_eB = [None]
        last_eC = [None]

        def emit_rep(s, extra):
            dep = [dbc_ev, l_es] + extra
            if (tw - 1, "v", s) in prev_rep:
                dep.append(prev_rep[(tw - 1, "v", s)])
            if (tw - 1, "h", s) in prev_rep:
                dep.append(prev_rep[(tw - 1, "h", s)])
            if last_eB[0] is not None:
                dep.append(last_eB[0])
            gB = mm(p, pBC[:, :], esl[32:48, s * 128:(s + 1) * 128],
                    dbcs[32:48, 0:512], True, True, dep)
            eB = acopy(p, Bbc[:, s * 512:(s + 1) * 512], pBC[:, :], [gB])
            dep2 = [dbc_ev]
            if last_eC[0] is not None:
                dep2.append(last_eC[0])
            gC = mm(p, pdbc[0:128, :], esl[64:80, s * 128:(s + 1) * 128],
                    dbcs[64:80, 0:512], True, True, dep2 + [eB])
            eC = acopy(p, Cbc[:, s * 512:(s + 1) * 512], pdbc[0:128, :], [gC])
            last_eB[0] = eB
            last_eC[0] = eC
            rep_nodes[s] = eC
        for s in range(3):
            emit_rep(s, [])
        if debug and tw == 0:
            h_stores.append(p.dma("pool", y_B, Bbc[:, 0:16 * 512], [rep_nodes[2]]))
        def lin(dic, tw_, c_, s_, d):
            g = (tw_ * 8 + c_) * 16 + s_ - d
            if g < 0:
                return None
            tw2, r_ = divmod(g, 128)
            c2, s2 = divmod(r_, 16)
            return dic.get((tw2, c2, s2))

        for c in range(8):
            pyt = (py0, py1)[c % 2]

            def ring_war(j, c=c, tw=tw):
                return lin(prev_scan, tw, c, j, 4)

            def emit_gx(s, c=c, tw=tw):
                sb = s % 4
                gdep = [del_nodes[c], n_A, ring_war(s)]
                return act(p, Gt[:, sb * 512:(sb + 1) * 512],
                           dels[:, c * 512:(c + 1) * 512], AF.Exp, gdep,
                           scale=Acol[:, c * 16 + s:c * 16 + s + 1])

            def emit_vv(s, c=c, tw=tw):
                sb = s % 4
                vdeps = [rep_nodes[s], del_nodes[c], ring_war(s)]
                return tt(p, vt[:, sb * 512:(sb + 1) * 512], us[:, c * 512:(c + 1) * 512],
                          Bbc[:, s * 512:(s + 1) * 512], ALU.mult, vdeps)

            gxs = {0: emit_gx(0), 1: emit_gx(1)}
            vvs = {0: emit_vv(0), 1: emit_vv(1)}
            for s in range(16):
                sb = s % 4
                hb = s % 2
                col = c * 16 + s
                sdeps = [gxs[s], vvs[s], n_z0, lin(prev_hc, tw, c, s, 2)]
                if (tw - 1, c, s) in prev_scan:
                    sdeps.append(prev_scan[(tw - 1, c, s)])
                sc_ = p.add("dve", (lambda e, sb=sb, hb=hb, col=col:
                                    e.tensor_tensor_scan(ht[:, hb * 512:(hb + 1) * 512],
                                                         Gt[:, sb * 512:(sb + 1) * 512],
                                                         vt[:, sb * 512:(sb + 1) * 512],
                                                         stcols[:, col:col + 1],
                                                         ALU.mult, ALU.add)),
                            sdeps, special=True)
                if c == 0 and s + 3 < 16:
                    emit_rep(s + 3, [])
                if s + 2 < 16:
                    gxs[s + 2] = emit_gx(s + 2)
                    vvs[s + 2] = emit_vv(s + 2)
                st_cp = p.add("dve", (lambda e, hb=hb, col=col:
                                      e.tensor_copy(stcols[:, col:col + 1],
                                                    ht[:, hb * 512 + 511: hb * 512 + 512])),
                              [sc_])
                prev_scan[(tw, c, s)] = st_cp
                hcdep = [sc_, rep_nodes[s], lin(prev_ymm, tw, c, s, 2)]
                hc = tt(p, hct[:, hb * 512:(hb + 1) * 512], ht[:, hb * 512:(hb + 1) * 512],
                        Cbc[:, s * 512:(s + 1) * 512], ALU.mult, hcdep)
                prev_hc[(tw, c, s)] = hc
                if c == 7:
                    prev_rep[(tw, "v", s)] = vvs[s]
                    prev_rep[(tw, "h", s)] = hc
                ydep = [hc, l_id]
                if s == 0 and (tw, c - 2, "gate") in prev_gate:
                    ydep.append(prev_gate[(tw, c - 2, "gate")])
                ym = mm(p, pyt[:, :], idb[:], hct[:, hb * 512:(hb + 1) * 512],
                        s == 0, s == 15, ydep)
                prev_ymm[(tw, c, s)] = ym
            zldep = []
            if prev_gate.get((tw, c - 1)) is not None:
                zldep.append(prev_gate[(tw, c - 1)])
            elif tw > 0:
                zldep.append(prev_gate[(tw - 1, 7)])
            elif c == 0:
                zldep.append(z_stores[-1])
            zl = p.dma("pool", sz[:], zbuf[c * 128:(c + 1) * 128, tcols[0]:tcols[1]], zldep)
            slu = act(p, sz[:], sz[:], AF.Silu, [zl])
            g1 = p.add("dve", (lambda e, c=c: e.scalar_tensor_tensor(
                ta[:], xcw[:, c * 512:(c + 1) * 512], dP[:, c:c + 1],
                (py0, py1)[c % 2][:, :], ALU.mult, ALU.add)),
                [prev_ymm[(tw, c, 15)], l_dp, slu])
            g2 = tt(p, yf[:, c * 512:(c + 1) * 512], ta[:], sz[:], ALU.mult,
                    [g1] + ([prev_out[-1]] if prev_out else []))
            prev_gate[(tw, c)] = g2
            prev_gate[(tw, c, "gate")] = g1
        if debug and tw == 0:
            h_stores.append(p.dma("pool", y_yf, yf[:],
                                  [prev_gate[(tw, cc)] for cc in range(8)]))
        for oc in range(4):
            hdeps = [prev_gate[(tw, cc)] for cc in range(8)] + [cast_done]
            if (tw - 1, oc) in prev_hos:
                hdeps.append(prev_hos[(tw - 1, oc)])
            g = mm_group(p, pho[:, :],
                         [(outwb[:, kc * 512 + oc * 128:kc * 512 + (oc + 1) * 128],
                           yf[:, kc * 512:(kc + 1) * 512]) for kc in range(8)],
                         hdeps + ([prev_hos[(tw, oc - 1)]] if (tw, oc - 1) in prev_hos else []))
            hb = oc % 2
            ev = acopy(p, hor[:, hb * 512:(hb + 1) * 512], pho[:, :],
                       [g] + ([h_stores[-2]] if len(h_stores) >= 2 else []))
            prev_hos[(tw, oc)] = ev
            h_stores.append(p.dma("sp", y_h[oc * 128:(oc + 1) * 128, tw * 512:(tw + 1) * 512],
                                  hor[:, hb * 512:(hb + 1) * 512], [ev]))
        prev_out.append(prev_hos[(tw, 3)])
    if debug:
        h_stores.append(p.dma("pool", y_st, stcols[:],
                              [prev_scan[(7, c, 15)] for c in range(8)]))
    sink(p, h_stores + z_stores + xc_stores)
    return p.build()


def mamba_prog():
    if "mamba" not in _PROGS:
        _PROGS["mamba"] = build_mamba()
    return _PROGS["mamba"]


def build_lnmax():
    p = Prog("lnmax")
    nc = p.nc
    hidT = nc.dram_tensor("hidT", [D, TOK], F32, kind="ExternalInput").ap()
    xT = nc.dram_tensor("xT", [D, TOK], F32, kind="ExternalInput").ap()
    onc = nc.dram_tensor("onc", [128, 1], F32, kind="ExternalInput").ap()
    onr = nc.dram_tensor("onr", [1, 128], F32, kind="ExternalInput").ap()
    idn = nc.dram_tensor("idn", [128, 128], F32, kind="ExternalInput").ap()
    y_x3 = nc.dram_tensor("y_x3", [D, TOK], F32, kind="ExternalOutput").ap()
    y_mx = nc.dram_tensor("y_mx", [1, 1], F32, kind="ExternalOutput").ap()

    ha = p.sbuf("ha", [128, 4 * 512], F32)
    xa = p.sbuf("xa", [128, 4 * 512], F32)
    rsum = p.sbuf("rsum", [128, 8 * 512], F32)
    x1s = p.sbuf("x1s", [128, 8 * 512], F32)
    sq = p.sbuf("sqln", [128, 8 * 512], F32)
    lnA = p.sbuf("lnA", [65, 512], F32)
    rrstd = p.sbuf("rrstd", [1, 512], F32)
    rmr = p.sbuf("rmr", [1, 512], F32)
    mc = p.sbuf("mc", [128, 2], F32)
    mrow = p.sbuf("mrow", [1, 128], F32)
    onc_s = p.sbuf("onc_s", [128, 1], F32)
    onr_s = p.sbuf("onr_s", [1, 128], F32)
    idn_s = p.sbuf("idn_s", [128, 128], F32)
    pst = p.psum("pst", [64, 512])
    pb1 = p.psum("pb1", [128, 512])
    pb2 = p.psum("pb2", [128, 512])
    pmx = p.psum("pmx", [1, 128])

    n_onc = p.dma("sp", onc_s[:], onc)
    n_onr = p.dma("sp", onr_s[:], onr)
    n_idn = p.dma("sp", idn_s[:], idn)
    sc = {"ones_col": onc_s[:, 0:1], "ones_row": onr_s[0:1, :], "sq": sq,
          "stat": pst, "bc1": pb1, "bc2": pb2,
          "rmean": lnA[0:1, :], "rvar": lnA[32:33, :], "rtmp": lnA[64:65, :],
          "rrstd": rrstd[0:1, :], "rmr": rmr[0:1, :]}
    stores = []
    prev_red = None
    mzero = p.add("dve", lambda e: e.memset(mc[:, 0:1], 0.0), [])
    ln_last = None
    res_hist = []
    for tc in range(4):
        res_nodes = []
        for oc in range(8):
            b = oc % 4
            war = [res_hist[-4]] if len(res_hist) >= 4 else []
            l1 = p.dma("pool", ha[:, b * 512:(b + 1) * 512],
                       hidT[oc * 128:(oc + 1) * 128, tc * 512:(tc + 1) * 512], war)
            l2 = p.dma("pool", xa[:, b * 512:(b + 1) * 512],
                       xT[oc * 128:(oc + 1) * 128, tc * 512:(tc + 1) * 512], war)
            r = tt(p, rsum[:, oc * 512:(oc + 1) * 512], ha[:, b * 512:(b + 1) * 512],
                   xa[:, b * 512:(b + 1) * 512], ALU.add,
                   [l1, l2] + ([ln_last] if tc > 0 and oc == 0 else []))
            res_nodes.append(r)
            res_hist.append(r)
        outs = emit_ln(p, lambda c: rsum[:, c * 512:(c + 1) * 512],
                       lambda c: x1s[:, c * 512:(c + 1) * 512], 512, sc, res_nodes)
        ln_last = outs[-1]
        for oc in range(8):
            stores.append(p.dma("sp", y_x3[oc * 128:(oc + 1) * 128, tc * 512:(tc + 1) * 512],
                                x1s[:, oc * 512:(oc + 1) * 512], [outs[oc]]))
            rdeps = [outs[oc], mzero]
            if prev_red is not None:
                rdeps.append(prev_red)
            rd = vred(p, mc[:, 1:2], x1s[:, oc * 512:(oc + 1) * 512], ALU.max, rdeps, absval=True)
            prev_red = tt(p, mc[:, 0:1], mc[:, 0:1], mc[:, 1:2], ALU.max, [rd])
    n_tp = p.add("pe", lambda e: e.transpose(pmx[0:1, :], mc[:, 0:1], idn_s[:, :]),
                 [prev_red, n_idn])
    n_ev = acopy(p, mrow[0:1, :], pmx[0:1, :], [n_tp])
    n_fm = vred(p, mc[0:1, 0:1], mrow[0:1, :], ALU.max, [n_ev])
    stores.append(p.dma("sp", y_mx, mc[0:1, 0:1], [n_fm]))
    sink(p, stores)
    return p.build()


def lnmax_prog():
    if "lnmax" not in _PROGS:
        _PROGS["lnmax"] = build_lnmax()
    return _PROGS["lnmax"]


def _conv_layout(w):
    return f32c(w.reshape(8, 128, 4).transpose(1, 0, 2).reshape(128, 32))


def _col_layout(v):
    return f32c(v.reshape(8, 128).T)


def _al_layout(a):
    return f32c(a.reshape(8, 128, 16).transpose(1, 0, 2).reshape(128, 128))


def make_esel():
    e = np.zeros((80, 2048), np.float32)
    for k in range(16):
        for s in range(16):
            if k == s:
                e[32 + k, s * 128:(s + 1) * 128] = 1.0
                e[64 + k, s * 128:(s + 1) * 128] = 1.0
    return e


IDENT_BF = np.eye(128, dtype=np.float32).astype(bf16)
ESEL = make_esel()


def mamba_inmap(xd, prm, tag):
    xpad = np.zeros((DM, XPD), np.float32)
    xpad[:, 3:3 + 4096] = xd.T
    return {
        "xdT": xpad,
        "inwT": f32c(prm[f"in_w{tag}"].T),
        "xpjT": f32c(prm[f"xproj_w{tag}"].T),
        "dtwT": f32c(prm[f"dt_w{tag}"].T),
        "outwT": f32c(prm[f"out_w{tag}"].T),
        "convW": _conv_layout(prm[f"conv_w{tag}"]),
        "convB": _col_layout(prm[f"conv_b{tag}"]),
        "dtB": _col_layout(prm[f"dt_b{tag}"]),
        "DPc": _col_layout(prm[f"D{tag}"]),
        "AL": _al_layout(prm[f"A_log{tag}"]),
        "esel": ESEL, "idnb": IDENT_BF,
    }


def run_mamba(x2, prm):
    in_maps = []
    for c in range(8):
        b, dirb = c // 2, c % 2
        if dirb == 0:
            xd = x2[b, :, 0:512]
            in_maps.append(mamba_inmap(xd, prm, "1"))
        else:
            xd = x2[b, ::-1, 512:1024]
            in_maps.append(mamba_inmap(xd, prm, "2"))
    return run_bass_kernel_spmd(mamba_prog(), in_maps, core_ids=list(range(8))).results


def run_lnmax(x2, mres):
    in_maps = []
    for c in range(8):
        b, h = c // 2, c % 2
        sl = slice(h * 2048, (h + 1) * 2048)
        hid = np.concatenate([mres[2 * b]["y_h"][:, sl], mres[2 * b + 1]["y_h"][:, sl]], axis=0)
        in_maps.append({"hidT": f32c(hid), "xT": f32c(x2[b, sl, :].T),
                        "onc": ONES_COL, "onr": ONES_ROW, "idn": IDENT})
    res = run_bass_kernel_spmd(lnmax_prog(), in_maps, core_ids=list(range(8))).results
    x3 = np.empty((B, S, D), np.float32)
    for c in range(8):
        b, h = c // 2, c % 2
        x3[b, h * 2048:(h + 1) * 2048, :] = res[c]["y_x3"].T
    return x3, [res[c]["y_mx"] for c in range(8)]



def kernel(L0_attn_x, L0_attn_Wq, L0_attn_Wk, L0_attn_Wv, L0_attn_Wo,
           L0_ffn_w_gate, L0_ffn_w_up, L0_ffn_w_down,
           L1_attn_in_w1, L1_attn_conv_w1, L1_attn_conv_b1, L1_attn_xproj_w1,
           L1_attn_dt_w1, L1_attn_dt_b1, L1_attn_A_log1, L1_attn_D1, L1_attn_out_w1,
           L1_attn_in_w2, L1_attn_conv_w2, L1_attn_conv_b2, L1_attn_xproj_w2,
           L1_attn_dt_w2, L1_attn_dt_b2, L1_attn_A_log2, L1_attn_D2, L1_attn_out_w2,
           L1_ffn_w_gate, L1_ffn_w_up, L1_ffn_w_down):
    x = np.asarray(L0_attn_x, np.float32)
    dq_packs = [dq_pack(c, np.asarray(L0_ffn_w_up), np.asarray(L0_ffn_w_gate),
                        np.asarray(L1_ffn_w_up), np.asarray(L1_ffn_w_gate),
                        np.asarray(L0_ffn_w_down), np.asarray(L1_ffn_w_down))
                for c in range(8)]
    qkv_res = run_qkv(x, L0_attn_Wq, L0_attn_Wk, L0_attn_Wv, dq_packs)
    attn_res = run_attn(x, qkv_res, L0_attn_Wo)
    x1 = gather_interleaved(attn_res, "y_x1")
    mx1 = [attn_res[c]["y_mx"] for c in range(8)]
    up0, g0, up1, g1, dn0, dn1 = dq_unpack([qkv_res[c]["y_dq"] for c in range(8)])
    x2, mx2 = run_ffn(x1, mx1, np.ascontiguousarray(up0.T), np.ascontiguousarray(g0.T),
                      np.ascontiguousarray(dn0.T))
    prm = {
        "in_w1": L1_attn_in_w1, "conv_w1": L1_attn_conv_w1, "conv_b1": L1_attn_conv_b1,
        "xproj_w1": L1_attn_xproj_w1, "dt_w1": L1_attn_dt_w1, "dt_b1": L1_attn_dt_b1,
        "A_log1": L1_attn_A_log1, "D1": L1_attn_D1, "out_w1": L1_attn_out_w1,
        "in_w2": L1_attn_in_w2, "conv_w2": L1_attn_conv_w2, "conv_b2": L1_attn_conv_b2,
        "xproj_w2": L1_attn_xproj_w2, "dt_w2": L1_attn_dt_w2, "dt_b2": L1_attn_dt_b2,
        "A_log2": L1_attn_A_log2, "D2": L1_attn_D2, "out_w2": L1_attn_out_w2,
    }
    prm = {k: np.asarray(v, np.float32) for k, v in prm.items()}
    mres = run_mamba(x2, prm)
    x3, mx3 = run_lnmax(x2, mres)
    x4, _ = run_ffn(x3, mx3, np.ascontiguousarray(up1.T), np.ascontiguousarray(g1.T),
                    np.ascontiguousarray(dn1.T))
    return x4


# version 13 (final)
# speedup vs baseline: 1.0467x; 1.0187x over previous
import contextlib
import numpy as np
import ml_dtypes
import concourse.bass as bass
import concourse.mybir as mybir
from concourse.bass_utils import run_bass_kernel_spmd

F32 = mybir.dt.float32
BF16 = mybir.dt.bfloat16
I16 = mybir.dt.int16
U16 = mybir.dt.uint16
AF = mybir.ActivationFunctionType
ALU = mybir.AluOpType
AX = mybir.AxisListType

MAGIC = 12582912.0
B, S, D = 4, 4096, 1024
TOK = 2048
NH, DK = 16, 64
FI = 3072
EPS = 1e-5

bf16 = ml_dtypes.bfloat16

ENGINES = ("pe", "act", "dve", "pool", "sp")
_ENG_ATTR = {"pe": "tensor", "act": "scalar", "dve": "vector", "pool": "gpsimd", "sp": "sync"}
NDMASEM = 6


class Node:
    __slots__ = ("eng", "fn", "deps", "is_dma", "idx", "flagged", "special")

    def __init__(self, eng, fn, deps, is_dma, special=False):
        self.eng, self.fn, self.deps, self.is_dma = eng, fn, deps, is_dma
        self.idx = -1
        self.flagged = False
        self.special = special


class Prog:

    def __init__(self, name):
        self.nc = bass.Bass("TRN2", target_bir_lowering=False)
        self.name = name
        self.nodes = []
        self._stack = contextlib.ExitStack()

    def sbuf(self, name, shape, dtype):
        return self._stack.enter_context(self.nc.sbuf_tensor(name, shape, dtype))

    def psum(self, name, shape, dtype=F32):
        return self._stack.enter_context(self.nc.psum_tensor(name, shape, dtype))

    def add(self, eng, fn, deps=(), special=False):
        n = Node(eng, fn, [d for d in deps if d is not None], False, special)
        self.nodes.append(n)
        return n

    def dma(self, queue, out, in_, deps=()):
        def fn(e):
            return e.dma_start(out=out, in_=in_)
        n = Node(queue, fn, [d for d in deps if d is not None], True)
        self.nodes.append(n)
        return n

    def build(self):
        nc = self.nc
        for n in self.nodes:
            if n.is_dma:
                n.flagged = True
            for d in n.deps:
                if d.eng != n.eng or d.is_dma or d.special or n.special:
                    d.flagged = True
        comp_count = {e: 0 for e in ENGINES}
        dma_count = {e: 0 for e in ENGINES}
        for n in self.nodes:
            if not n.flagged:
                continue
            if n.is_dma:
                n.idx = dma_count[n.eng]
                dma_count[n.eng] += 1
            else:
                comp_count[n.eng] += 1
                n.idx = comp_count[n.eng]
        self._sems = {}
        for e in ENGINES:
            if comp_count[e]:
                self._sems[("c", e)] = nc.semaphore(f"S_{e}").__enter__()
            if dma_count[e]:
                self._sems[("d", e)] = [nc.semaphore(f"D_{e}{i}").__enter__()
                                        for i in range(NDMASEM)]
        block = nc.Block().__enter__()
        per_eng = {e: [n for n in self.nodes if n.eng == e] for e in ENGINES}

        def dep_req(d):
            if d.is_dma:
                sems = self._sems[("d", d.eng)]
                return (sems[d.idx % NDMASEM], 16 * (d.idx // NDMASEM + 1))
            return (self._sems[("c", d.eng)], d.idx)

        for e in ENGINES:
            nodes = per_eng[e]
            if not nodes:
                continue

            def make_body(e=e, nodes=nodes):
                def body(eng):
                    watermark = {}
                    for n in nodes:
                        for d in n.deps:
                            if (d.eng == e and not d.is_dma and not d.special
                                    and not n.special):
                                continue
                            sem, val = dep_req(d)
                            key = id(sem)
                            if watermark.get(key, -1) >= val:
                                continue
                            eng.wait_ge(sem, val)
                            watermark[key] = val
                        if n.is_dma and n.flagged:
                            sems = self._sems[("d", e)]
                            slot, rnd = n.idx % NDMASEM, n.idx // NDMASEM
                            if rnd > 0:
                                key = id(sems[slot])
                                val = 16 * rnd
                                if watermark.get(key, -1) < val:
                                    eng.wait_ge(sems[slot], val)
                                    watermark[key] = val
                        instr = n.fn(eng)
                        if n.flagged:
                            if n.is_dma:
                                instr.then_inc(self._sems[("d", e)][n.idx % NDMASEM], 16)
                            else:
                                instr.then_inc(self._sems[("c", e)], 1)
                return body
            getattr(block, _ENG_ATTR[e])(make_body())
        block.__exit__(None, None, None)
        return nc


def f32c(x):
    return np.ascontiguousarray(x, dtype=np.float32)


def bf16c(x):
    return np.ascontiguousarray(np.asarray(x).astype(bf16))



def mm(p, out_ap, lhsT, rhs, start, stop, deps=()):
    return p.add("pe", lambda e: e.matmul(out_ap, lhsT, rhs, start=start, stop=stop,
                                          skip_group_check=True), deps)


def mm_group(p, out_ap, pairs, deps):
    def fn(eng):
        last = None
        for i, (lhsT, rhs) in enumerate(pairs):
            last = eng.matmul(out_ap, lhsT, rhs, start=(i == 0),
                              stop=(i == len(pairs) - 1), skip_group_check=True)
        return last
    return p.add("pe", fn, deps)


def act(p, out, in_, func, deps, bias=None, scale=None):
    def fn(eng):
        kw = {}
        if bias is not None:
            kw["bias"] = bias
        if scale is not None:
            kw["scale"] = scale
        return eng.activation(out, in_, func, **kw)
    return p.add("act", fn, deps)


def acopy(p, out, in_, deps):
    return p.add("act", lambda e: e.copy(out, in_), deps)


def ts(p, out, in_, s1, s2, op0, op1, deps):
    return p.add("dve", lambda e: e.tensor_scalar(out, in_, s1, s2, op0, op1), deps)


def tt(p, out, a, b, op, deps):
    return p.add("dve", lambda e: e.tensor_tensor(out, a, b, op), deps)


def stt(p, out, in0, scalar, in1, op0, op1, deps):
    return p.add("dve", lambda e: e.scalar_tensor_tensor(out, in0, scalar, in1, op0, op1), deps)


def vred(p, out, in_, op, deps, absval=None):
    return p.add("dve", lambda e: e.tensor_reduce(out, in_, AX.X, op,
                                                  apply_absolute_value=absval),
                 deps, special=True)


def vcopy(p, out, in_, deps):
    return p.add("dve", lambda e: e.tensor_copy(out, in_), deps)


def vscan(p, out, d0, d1, init, deps):
    return p.add("dve", lambda e: e.tensor_tensor_scan(out, d0, d1, init, ALU.mult, ALU.add),
                 deps, special=True)


def vrecip(p, out, in_, deps):
    return p.add("dve", lambda e: e.reciprocal(out, in_), deps, special=True)


def sink(p, deps):
    return p.add("sp", lambda e: e.nop(), deps)


def chunks2d(handle, n, width):
    def sl(c, a=0, b=None):
        b = width if b is None else b
        return handle[:, c * width + a: c * width + b]
    return sl



def emit_ln(p, in_sl, out_sl, T, sc, dep_in):
    ones_col, ones_row = sc["ones_col"], sc["ones_row"]
    sq = sc["sq"]
    nodes_sq = [act(p, sq[:, c * T:(c + 1) * T], in_sl(c), AF.Square, dep_in)
                for c in range(8)]
    n_sum = mm_group(p, sc["stat"][0:1, 0:T],
                     [(ones_col, in_sl(c)) for c in range(8)], dep_in)
    n_ssq = mm_group(p, sc["stat"][32:33, 0:T],
                     [(ones_col, sq[:, c * T:(c + 1) * T]) for c in range(8)], nodes_sq)
    n_mean = ts(p, sc["rmean"], sc["stat"][0:1, 0:T], 1.0 / D, None,
                ALU.mult, ALU.bypass, [n_sum])
    n_msq = tt(p, sc["rvar"], sc["rmean"], sc["rmean"],
               ALU.mult, [n_mean])
    n_var = stt(p, sc["rvar"], sc["stat"][32:33, 0:T], 1.0 / D,
                sc["rvar"], ALU.mult, ALU.subtract, [n_ssq, n_msq])
    n_vpe = ts(p, sc["rvar"], sc["rvar"], EPS, None, ALU.add, ALU.bypass, [n_var])
    n_ln = act(p, sc["rtmp"], sc["rvar"], AF.Ln, [n_vpe])
    n_rstd = act(p, sc["rrstd"], sc["rtmp"], AF.Exp, [n_ln], scale=-0.5)
    n_mr = tt(p, sc["rmr"], sc["rmean"], sc["rrstd"],
              ALU.mult, [n_rstd])
    n_b1 = mm(p, sc["bc1"][:, 0:T], ones_row, sc["rrstd"], True, True, [n_rstd])
    n_b2 = mm(p, sc["bc2"][:, 0:T], ones_row, sc["rmr"], True, True, [n_mr])
    outs = []
    for c in range(8):
        n_a = tt(p, sq[:, c * T:(c + 1) * T], in_sl(c), sc["bc1"][:, 0:T],
                 ALU.mult, [n_b1, n_ssq])
        n_o = tt(p, out_sl(c), sq[:, c * T:(c + 1) * T], sc["bc2"][:, 0:T],
                 ALU.subtract, [n_b2, n_a])
        outs.append(n_o)
    return outs



NDQ = 18


def build_qkv():
    p = Prog("qkv")
    nc = p.nc
    xT = nc.dram_tensor("xT", [D, TOK], F32, kind="ExternalInput").ap()
    wq = nc.dram_tensor("wq", [D, D], F32, kind="ExternalInput").ap()
    wk = nc.dram_tensor("wk", [D, D], F32, kind="ExternalInput").ap()
    wv = nc.dram_tensor("wv", [D, D], F32, kind="ExternalInput").ap()
    sel = nc.dram_tensor("sel", [128, 128], F32, kind="ExternalInput").ap()
    dq_in = nc.dram_tensor("dq_in", [NDQ * 128, 1024], F32, kind="ExternalInput").ap()

    y_q = nc.dram_tensor("y_q", [D, TOK], BF16, kind="ExternalOutput").ap()
    y_k = nc.dram_tensor("y_k", [D, TOK], BF16, kind="ExternalOutput").ap()
    y_v = nc.dram_tensor("y_v", [TOK, D], BF16, kind="ExternalOutput").ap()
    y_qn2 = nc.dram_tensor("y_qn2", [16, 16], F32, kind="ExternalOutput").ap()
    y_kn2 = nc.dram_tensor("y_kn2", [16, 16], F32, kind="ExternalOutput").ap()
    y_dq = nc.dram_tensor("y_dq", [NDQ * 128, 1024], BF16, kind="ExternalOutput").ap()

    xs = p.sbuf("xs", [128, 2 * TOK], F32)
    xb = p.sbuf("xb", [128, 8 * TOK], BF16)
    ws = p.sbuf("ws", [128, 2 * D], F32)
    wqb = p.sbuf("wqb", [128, 8 * D], BF16)
    wkb = p.sbuf("wkb", [128, 8 * D], BF16)
    wvb = p.sbuf("wvb", [128, 8 * D], BF16)
    ob = p.sbuf("ob", [128, 8 * 512], BF16)
    sels = p.sbuf("sels", [128, 128], F32)
    sq2 = p.sbuf("sq2", [128, 2 * 512], F32)
    n2t = p.sbuf("n2t", [16, 32], F32)
    dqw = p.sbuf("dqw", [128, 2 * 1024], F32)
    dqo = p.sbuf("dqo", [128, 2 * 1024], BF16)
    wi = p.sbuf("wi", [128, 2 * 1024], F32)
    bmax = p.sbuf("bmax", [128, 2 * 32], F32)
    bs = p.sbuf("bs", [128, 2 * 32], F32)
    brec = p.sbuf("brec", [128, 2 * 32], F32)
    ps0 = p.psum("ps0", [128, 512])
    ps1 = p.psum("ps1", [128, 512])
    pn = p.psum("pn", [16, 512])
    if True:
        n_sel = p.dma("sp", sels[:], sel)
        casts = {}
        prev_cast = {}
        for name, src_d, dstb, nch, w in (("x", xT, xb, 8, TOK),):
            for c in range(8):
                buf = c % 2
                deps = []
                if c >= 2:
                    deps.append(casts[("x", c - 2)])
                ld = p.dma("sp", xs[:, buf * TOK:(buf + 1) * TOK],
                           xT[c * 128:(c + 1) * 128, :], deps)
                casts[("x", c)] = vcopy(p, xb[:, c * TOK:(c + 1) * TOK],
                                        xs[:, buf * TOK:(buf + 1) * TOK], [ld])
        for nm, wsrc, dstb in (("wq", wq, wqb), ("wk", wk, wkb), ("wv", wv, wvb)):
            for c in range(8):
                buf = c % 2
                deps = []
                key = (nm, c - 2)
                if key in casts:
                    deps.append(casts[key])
                elif c < 2 and nm != "wq":
                    pk = {"wk": "wq", "wv": "wk"}[nm]
                    deps.append(casts[(pk, 6 + c)])
                ld = p.dma("pool", ws[:, buf * D:(buf + 1) * D],
                           wsrc[c * 128:(c + 1) * 128, :], deps)
                casts[(nm, c)] = vcopy(p, dstb[:, c * D:(c + 1) * D],
                                       ws[:, buf * D:(buf + 1) * D], [ld])
        cx = [casts[("x", c)] for c in range(8)]
        cwq = [casts[("wq", c)] for c in range(8)]
        cwk = [casts[("wk", c)] for c in range(8)]
        cwv = [casts[("wv", c)] for c in range(8)]

        ps = [ps0, ps1]
        use = 0
        prev_evac = {}
        prev_store = {}
        red_nodes = []
        sel_prev = None
        all_selmm = []
        for which, wsb, ydst, cw in ((0, wqb, y_q, cwq), (1, wkb, y_k, cwk)):
            for tc in range(4):
                sq_list = []
                for oc in range(8):
                    pt = ps[use % 2]
                    deps = cx + cw if use == 0 else [cx[-1], cw[-1]]
                    deps = list(deps)
                    if use >= 2:
                        deps.append(prev_evac[use - 2])
                    g = mm_group(p, pt[:, :],
                                 [(wsb[:, kc * D + oc * 128:kc * D + (oc + 1) * 128],
                                   xb[:, kc * TOK + tc * 512:kc * TOK + (tc + 1) * 512])
                                  for kc in range(8)], deps)
                    ring = use % 8
                    edeps = [g]
                    if use >= 8:
                        edeps.append(prev_store[use - 8])
                    ev = acopy(p, ob[:, ring * 512:(ring + 1) * 512], pt[:, :], edeps)
                    st = p.dma("sp", ydst[oc * 128:(oc + 1) * 128, tc * 512:(tc + 1) * 512],
                               ob[:, ring * 512:(ring + 1) * 512], [ev])
                    prev_evac[use] = ev
                    prev_store[use] = st
                    sqb = oc % 2
                    sdeps = [ev]
                    if len(all_selmm) >= 2:
                        sdeps.append(all_selmm[-2])
                    nsq = act(p, sq2[:, sqb * 512:(sqb + 1) * 512],
                              ob[:, ring * 512:(ring + 1) * 512], AF.Square, sdeps)
                    smdeps = [nsq, n_sel]
                    if sel_prev is not None:
                        smdeps.append(sel_prev)
                    sm = mm(p, pn[:, :], sels[:, oc * 16:(oc + 1) * 16],
                            sq2[:, sqb * 512:(sqb + 1) * 512], oc == 0, oc == 7, smdeps)
                    all_selmm.append(sm)
                    sel_prev = sm
                    sq_list.append(sm)
                    use += 1
                rd = vred(p, n2t[0:16, which * 16 + tc * 4: which * 16 + (tc + 1) * 4],
                          pn[0:16, :].rearrange("p (a b) -> p a b", b=128), ALU.max,
                          [sq_list[-1]] + red_nodes[-1:])
                sel_prev = rd
                red_nodes.append(rd)
        for tki in range(16):
            for vc in range(2):
                pt = ps[use % 2]
                deps = [cx[-1], cwv[-1]]
                if use >= 2:
                    deps.append(prev_evac[use - 2])
                g = mm_group(p, pt[:, :],
                             [(xb[:, kc * TOK + tki * 128: kc * TOK + (tki + 1) * 128],
                               wvb[:, kc * D + vc * 512: kc * D + (vc + 1) * 512])
                              for kc in range(8)], deps)
                ring = use % 8
                edeps = [g]
                if use >= 8:
                    edeps.append(prev_store[use - 8])
                ev = acopy(p, ob[:, ring * 512:(ring + 1) * 512], pt[:, :], edeps)
                st = p.dma("sp", y_v[tki * 128:(tki + 1) * 128, vc * 512:(vc + 1) * 512],
                           ob[:, ring * 512:(ring + 1) * 512], [ev])
                prev_evac[use] = ev
                prev_store[use] = st
                use += 1

        dq_stores = {}
        for ch in range(NDQ):
            bq = ch % 2
            o = bq * 1024
            obk = bq * 32
            ldeps = []
            if ch >= 2:
                ldeps.append(dq_stores[ch - 2])
            ld = p.dma("pool", dqw[:, o:o + 1024], dq_in[ch * 128:(ch + 1) * 128, :], ldeps)
            w3 = dqw[:, o:o + 1024].rearrange("p (b l) -> p b l", l=32)
            wi3 = wi[:, o:o + 1024].rearrange("p (b l) -> p b l", l=32)
            dqo3 = dqo[:, o:o + 1024].rearrange("p (b l) -> p b l", l=32)
            r1 = vred(p, bmax[:, obk:obk + 32], w3, ALU.max, [ld] + ldeps, absval=True)
            r2 = ts(p, bs[:, obk:obk + 32], bmax[:, obk:obk + 32], 1.0 / 7.0, 1e-12,
                    ALU.mult, ALU.max, [r1])
            r3 = vrecip(p, brec[:, obk:obk + 32], bs[:, obk:obk + 32], [r2])
            r4 = tt(p, wi3, w3,
                    brec[:, obk:obk + 32].rearrange("p (b o) -> p b o", o=1).broadcast_to([128, 32, 32]),
                    ALU.mult, [r3])
            r5 = ts(p, wi[:, o:o + 1024], wi[:, o:o + 1024], MAGIC, MAGIC, ALU.add, ALU.subtract, [r4])
            r6 = tt(p, dqo3, wi3,
                    bs[:, obk:obk + 32].rearrange("p (b o) -> p b o", o=1).broadcast_to([128, 32, 32]),
                    ALU.mult, [r5])
            dq_stores[ch] = p.dma("pool", y_dq[ch * 128:(ch + 1) * 128, :], dqo[:, o:o + 1024], [r6])

        s4 = p.dma("sp", y_qn2, n2t[0:16, 0:16], [red_nodes[3]])
        s5 = p.dma("sp", y_kn2, n2t[0:16, 16:32], [red_nodes[7]])
        sink(p, [s4, s5] + list(prev_store.values()) + list(dq_stores.values()))
        return p.build()



def build_attn():
    p = Prog("attn")
    nc = p.nc
    qT = nc.dram_tensor("qT", [D, TOK], BF16, kind="ExternalInput").ap()
    kT = nc.dram_tensor("kT", [D, S], BF16, kind="ExternalInput").ap()
    vo = nc.dram_tensor("vo", [S, NH * 65], BF16, kind="ExternalInput").ap()
    xT = nc.dram_tensor("xT", [D, TOK], F32, kind="ExternalInput").ap()
    wo = nc.dram_tensor("wo", [D, D], F32, kind="ExternalInput").ap()
    n2in = nc.dram_tensor("n2in", [16, 48], F32, kind="ExternalInput").ap()
    msk = nc.dram_tensor("msk", [128, 8 * 512], BF16, kind="ExternalInput").ap()
    onc = nc.dram_tensor("onc", [128, 1], F32, kind="ExternalInput").ap()
    onr = nc.dram_tensor("onr", [1, 128], F32, kind="ExternalInput").ap()
    idn = nc.dram_tensor("idn", [128, 128], F32, kind="ExternalInput").ap()
    scr = nc.dram_tensor("scr16", [16, 1], F32)

    y_x1 = nc.dram_tensor("y_x1", [D, TOK], F32, kind="ExternalOutput").ap()
    y_mx = nc.dram_tensor("y_mx", [1, 1], F32, kind="ExternalOutput").ap()

    qb = p.sbuf("qb", [128, 8 * TOK], BF16)
    kb = p.sbuf("kb", [128, 2 * S], BF16)
    vb = p.sbuf("vb", [128, 2 * (32 * 130)], BF16)
    att = p.sbuf("att", [128, 8 * TOK], BF16)
    msks = p.sbuf("msks", [128, 8 * 512], BF16)
    pt = p.sbuf("pt", [128, 4 * 512], BF16)
    n2s = p.sbuf("n2s", [16, 48], F32)
    cq = p.sbuf("cq", [16, 4], F32)
    negc = p.sbuf("negc", [1, 16], F32)
    negcb = p.sbuf("negcb", [128, 16], F32)
    rcp = p.sbuf("rcp", [1, 2 * 512], F32)
    tav = p.sbuf("tav", [64, 2 * 512], F32)
    onc_s = p.sbuf("onc_s", [128, 1], F32)
    onr_s = p.sbuf("onr_s", [1, 128], F32)
    idn_s = p.sbuf("idn_s", [128, 128], F32)
    wos = p.sbuf("wos", [128, 2 * D], F32)
    wob = p.sbuf("wob", [128, 8 * D], BF16)
    xrs = p.sbuf("xrs", [128, 2 * 512], F32)
    rsum = p.sbuf("rsum", [128, 8 * 512], F32)
    x1s = p.sbuf("x1s", [128, 8 * 512], F32)
    sq = p.sbuf("sqln", [128, 8 * 512], F32)
    lnA = p.sbuf("lnA", [65, 512], F32)
    rrstd = p.sbuf("rrstd", [1, 512], F32)
    rmr = p.sbuf("rmr", [1, 512], F32)
    mcol = p.sbuf("mcol", [128, 2], F32)
    mrow = p.sbuf("mrow", [1, 128], F32)

    pp0 = p.psum("pp0", [128, 1024])
    pp1 = p.psum("pp1", [128, 1024])
    avA = p.psum("avA", [128, 512])
    avB = p.psum("avB", [128, 512])
    pmx = p.psum("pmx", [128, 512])
    pbc = p.psum("pbc", [128, 16])

    if True:
        n_q = p.dma("sp", qb[:].rearrange("p (c t) -> p c t", t=TOK),
                    qT.rearrange("(c p) t -> p c t", p=128))
        n_m = p.dma("sp", msks[:], msk)
        n_n2 = p.dma("sp", n2s[:], n2in)
        n_onc = p.dma("sp", onc_s[:], onc)
        n_onr = p.dma("sp", onr_s[:], onr)
        n_idn = p.dma("sp", idn_s[:], idn)

        r_qm = vred(p, cq[0:16, 0:1], n2s[0:16, 0:16], ALU.max, [n_n2])
        r_km = vred(p, cq[0:16, 1:2], n2s[0:16, 16:48], ALU.max, [r_qm])
        a_sq = act(p, cq[0:16, 2:3], cq[0:16, 0:1], AF.Sqrt, [r_qm, r_km])
        a_sk = act(p, cq[0:16, 3:4], cq[0:16, 1:2], AF.Sqrt, [r_km])
        t_c = tt(p, cq[0:16, 0:1], cq[0:16, 2:3], cq[0:16, 3:4], ALU.mult, [a_sq, a_sk])
        t_b = ts(p, cq[0:16, 1:2], cq[0:16, 0:1], -0.1275, -0.00125, ALU.mult, ALU.add, [t_c])
        d_out = p.dma("sp", scr[:, :], cq[0:16, 1:2], [t_b])
        d_in = p.dma("sp", negc[:], scr.rearrange("a b -> b a"), [d_out])
        bc_mm = mm(p, pbc[:, 0:16], onr_s[0:1, :], negc[0:1, :], True, True, [d_in, n_onr])
        n_bias = acopy(p, negcb[:], pbc[:, 0:16], [bc_mm])

        exp_hist = {}
        av_hist = {}
        attn_done = []
        prev_postA = {}
        prev_postB = {}
        last_exp_on = {}
        last_av_on = {}
        gp = [0]
        for hp in range(8):
            dbl = hp % 2
            ldeps = []
            if hp >= 2:
                ldeps.append(exp_hist[(hp - 2, 3, "B")])
                ldeps.append(av_hist[(hp - 2, 3, "B")])
            lk = p.dma("pool", kb[:, dbl * S:(dbl + 1) * S], kT[hp * 128:(hp + 1) * 128, :], ldeps)
            lv = p.dma("pool", vb[:, dbl * 4160:(dbl + 1) * 4160]
                       .rearrange("p (c w) -> p c w", w=130),
                       vo[:, hp * 130:(hp + 1) * 130].rearrange("(c p) w -> p c w", p=128), ldeps)
            for qg in range(4):
                E = 8 * qg + 8
                post_queue = []
                for head, avp in (("A", avA), ("B", avB)):
                    hh = 2 * hp + (0 if head == "A" else 1)
                    base = 0 if head == "A" else 64
                    pmask = {}
                    pt_of = {}

                    def emit_av(kt, head=head, avp=avp, lv=lv, dbl=dbl,
                                hp=hp, qg=qg, E=E):
                        avdeps = [pmask[kt]]
                        if kt == 0:
                            avdeps.append(lv)
                            pp_ = prev_postA if head == "A" else prev_postB
                            if pp_:
                                avdeps.append(pp_["node"])
                        av = mm(p, avp[0:65, :],
                                vb[:, dbl * 4160 + kt * 130 + (0 if head == "A" else 65):
                                   dbl * 4160 + kt * 130 + (65 if head == "A" else 130)],
                                pt_of[kt], kt == 0, kt == E - 1, avdeps)
                        av_hist[(hp, qg, head, kt)] = av
                        return av

                    for kp in range(E // 2):
                        par = gp[0] % 2
                        ring = (pp0, pp1)[par]
                        kt0, kt1 = 2 * kp, 2 * kp + 1
                        sdeps = [lk, n_q]
                        if par in last_exp_on:
                            sdeps.append(last_exp_on[par])
                        smm0 = mm(p, ring[:, 0:512],
                                  kb[base:base + 64, dbl * S + kt0 * 128: dbl * S + (kt0 + 1) * 128],
                                  qb[base:base + 64, hp * TOK + qg * 512: hp * TOK + (qg + 1) * 512],
                                  True, True, sdeps)
                        smm1 = mm(p, ring[:, 512:1024],
                                  kb[base:base + 64, dbl * S + kt1 * 128: dbl * S + (kt1 + 1) * 128],
                                  qb[base:base + 64, hp * TOK + qg * 512: hp * TOK + (qg + 1) * 512],
                                  True, True, sdeps)
                        edeps = [smm0, smm1, n_bias]
                        if par in last_av_on:
                            edeps.append(last_av_on[par])
                        ex = act(p, pt[:, par * 1024:(par + 1) * 1024],
                                 ring[:, 0:1024], AF.Exp, edeps,
                                 bias=negcb[:, hh:hh + 1], scale=0.125)
                        exp_hist[(hp, qg, head)] = ex
                        last_exp_on[par] = ex
                        for kt in (kt0, kt1):
                            half = pt[:, par * 1024 + (kt & 1) * 512: par * 1024 + (kt & 1) * 512 + 512]
                            pt_of[kt] = half
                            lastn = ex
                            u = kt - (E - 8)
                            if u >= 0:
                                lastn = tt(p, half, half,
                                           msks[:, u * 512:(u + 1) * 512], ALU.mult, [ex, n_m])
                            pmask[kt] = lastn
                        if kp >= 1:
                            a0 = emit_av(kt0 - 2)
                            a1 = emit_av(kt1 - 2)
                            last_av_on[(gp[0] - 1) % 2] = a1
                        gp[0] += 1
                    a0 = emit_av(E - 2)
                    a1 = emit_av(E - 1)
                    last_av_on[(gp[0] - 1) % 2] = a1
                    av_hist[(hp, qg, head)] = a1
                    post_queue.append((head, avp))
                for head, avp in post_queue:
                    hh = 2 * hp + (0 if head == "A" else 1)
                    base = 0 if head == "A" else 64
                    fin = av_hist[(hp, qg, head)]
                    half = 0 if head == "A" else 1
                    rc = vrecip(p, rcp[0:1, half * 512:(half + 1) * 512],
                                avp[64:65, :], [fin])
                    bcdep = [rc]
                    if prev_postA:
                        bcdep.append(prev_postA["node"])
                    if prev_postB:
                        bcdep.append(prev_postB["node"])
                    bcm = mm(p, pmx[0 if head == "A" else 64:64 if head == "A" else 128, :],
                             onr_s[0:1, 0:64],
                             rcp[0:1, half * 512:(half + 1) * 512], True, True, bcdep)
                    ev = acopy(p, tav[0:64, half * 512:(half + 1) * 512], avp[0:64, :], [fin, rc])
                    ntt = tt(p, att[base:base + 64, hp * TOK + qg * 512: hp * TOK + (qg + 1) * 512],
                             tav[0:64, half * 512:(half + 1) * 512],
                             pmx[0 if head == "A" else 64:64 if head == "A" else 128, :],
                             ALU.mult, [bcm, ev])
                    if head == "A":
                        prev_postA = {"node": ntt}
                    else:
                        prev_postB = {"node": ntt}
                    attn_done.append(ntt)

        wo_casts = []
        for c in range(8):
            buf = c % 2
            deps = [wo_casts[c - 2]] if c >= 2 else []
            ld = p.dma("pool", wos[:, buf * D:(buf + 1) * D], wo[c * 128:(c + 1) * 128, :], deps)
            wo_casts.append(vcopy(p, wob[:, c * D:(c + 1) * D], wos[:, buf * D:(buf + 1) * D], [ld]))

        sc = {"ones_col": onc_s[:, 0:1], "ones_row": onr_s[0:1, :], "sq": sq,
              "stat": avA, "bc1": pp0[:, 512:1024], "bc2": pp1[:, 512:1024],
              "rmean": lnA[0:1, :], "rvar": lnA[32:33, :], "rtmp": lnA[64:65, :],
              "rrstd": rrstd[0:1, :], "rmr": rmr[0:1, :]}
        n_init = p.add("dve", lambda e: e.memset(mcol[:, 0:1], 0.0), [])
        stores = []
        prev_red = n_init
        ln_last = attn_done[-4:]
        for tc in range(4):
            res_nodes = []
            for oc in range(8):
                xsl = xrs[:, (oc % 2) * 512:(oc % 2 + 1) * 512]
                xwar = res_nodes[-2:-1] if len(res_nodes) >= 2 else (
                    ln_last if tc == 0 else [ln_last[-1]])
                xl = p.dma("pool", xsl,
                           xT[oc * 128:(oc + 1) * 128, tc * 512:(tc + 1) * 512], xwar)
                g = mm_group(p, pp0[:, 0:512] if oc % 2 == 0 else pp1[:, 0:512],
                             [(wob[:, kc * D + oc * 128: kc * D + (oc + 1) * 128],
                               att[:, kc * TOK + tc * 512: kc * TOK + (tc + 1) * 512])
                              for kc in range(8)],
                             attn_done + wo_casts + (res_nodes[-2:-1] if len(res_nodes) >= 2 else []))
                r = stt(p, rsum[:, oc * 512:(oc + 1) * 512], xsl, 2.0,
                        pp0[:, 0:512] if oc % 2 == 0 else pp1[:, 0:512],
                        ALU.mult, ALU.add, [g, xl])
                res_nodes.append(r)
            outs = emit_ln(p, lambda c: rsum[:, c * 512:(c + 1) * 512],
                           lambda c: x1s[:, c * 512:(c + 1) * 512], 512, sc, res_nodes)
            ln_last = outs
            for oc in range(8):
                stores.append(p.dma("sp", y_x1[oc * 128:(oc + 1) * 128, tc * 512:(tc + 1) * 512],
                                    x1s[:, oc * 512:(oc + 1) * 512], [outs[oc]]))
                rd = vred(p, mcol[:, 1:2], x1s[:, oc * 512:(oc + 1) * 512], ALU.max,
                          [outs[oc], prev_red], absval=True)
                prev_red = tt(p, mcol[:, 0:1], mcol[:, 0:1], mcol[:, 1:2], ALU.max, [rd])
        n_tp = p.add("pe", lambda e: e.transpose(pmx[0:1, 0:128], mcol[:, 0:1], idn_s[:, :]),
                     [prev_red, n_idn])
        n_ev = acopy(p, mrow[0:1, :], pmx[0:1, 0:128], [n_tp])
        n_fm = vred(p, mcol[0:1, 0:1], mrow[0:1, :], ALU.max, [n_ev])
        stores.append(p.dma("sp", y_mx, mcol[0:1, 0:1], [n_fm]))
        sink(p, stores)
    return p.build()



def tiles_of(h):
    return [8 * j + h + 2 * m for j in range(4) for m in range(4)]


def make_sel_qkv():
    sel = np.zeros((128, 128), np.float32)
    for oc in range(8):
        for r in range(128):
            head = (oc * 128 + r) // 64
            sel[r, oc * 16 + head] = 1.0
    return sel


def make_masks(h):
    m = np.zeros((128, 8, 4, 128), np.float32)
    tri = (np.arange(128)[:, None] <= np.arange(128)[None, :])
    for u in range(8):
        for sm in range(4):
            d = h + 2 * sm
            if u < d:
                m[:, u, sm, :] = 1.0
            elif u == d:
                m[:, u, sm, :] = tri
    return bf16c(m.reshape(128, 8 * 512))


def dq_pack(c, Wup0, Wg0, Wup1, Wg1, Wdn0, Wdn1):
    rows = []
    for W in (Wup0, Wg0, Wup1, Wg1):
        rows.append(W[c * 384:(c + 1) * 384, :])
    for W in (Wdn0, Wdn1):
        sl = W[c * 128:(c + 1) * 128, :]
        rows.append(sl.reshape(128, 3, 1024).transpose(1, 0, 2).reshape(384, 1024))
    return f32c(np.concatenate(rows, 0))


def dq_unpack(outs):
    ups0, gs0, ups1, gs1, dn0, dn1 = [], [], [], [], [], []
    for c in range(8):
        o = outs[c]
        ups0.append(o[0:384]); gs0.append(o[384:768])
        ups1.append(o[768:1152]); gs1.append(o[1152:1536])
        dn0.append(o[1536:1920].reshape(3, 128, 1024).transpose(1, 0, 2).reshape(128, 3072))
        dn1.append(o[1920:2304].reshape(3, 128, 1024).transpose(1, 0, 2).reshape(128, 3072))
    return (np.concatenate(ups0, 0), np.concatenate(gs0, 0),
            np.concatenate(ups1, 0), np.concatenate(gs1, 0),
            np.concatenate(dn0, 0), np.concatenate(dn1, 0))


ONES_COL = np.ones((128, 1), np.float32)
ONES_ROW = np.ones((1, 128), np.float32)
IDENT = np.eye(128, dtype=np.float32)


def run_qkv(x, Wq, Wk, Wv, dq_packs):
    nc = qkv_prog()
    in_maps = []
    sel = make_sel_qkv()
    for c in range(8):
        b, h = c // 2, c % 2
        in_maps.append({
            "xT": f32c(x[b, h * 2048:(h + 1) * 2048, :].T),
            "wq": f32c(Wq), "wk": f32c(Wk), "wv": f32c(Wv),
            "sel": sel, "dq_in": dq_packs[c],
        })
    return run_bass_kernel_spmd(nc, in_maps, core_ids=list(range(8))).results


def run_attn(x, qkv_res, Wo):
    nc = attn_prog()
    in_maps = []
    for c in range(8):
        b, h = c // 2, c % 2
        ca, cb = 2 * b, 2 * b + 1
        qf = np.concatenate([qkv_res[ca]["y_q"], qkv_res[cb]["y_q"]], axis=1)
        kf = np.concatenate([qkv_res[ca]["y_k"], qkv_res[cb]["y_k"]], axis=1)
        vf = np.concatenate([qkv_res[ca]["y_v"], qkv_res[cb]["y_v"]], axis=0)
        tl = tiles_of(h)
        cols = np.concatenate([np.arange(g * 128, (g + 1) * 128) for g in tl])
        vo = np.ones((S, NH * 65), bf16)
        for hh in range(NH):
            vo[:, hh * 65:hh * 65 + 64] = vf[:, hh * 64:(hh + 1) * 64]
        qn2f = np.concatenate([qkv_res[ca]["y_qn2"], qkv_res[cb]["y_qn2"]], axis=1)
        kn2f = np.concatenate([qkv_res[ca]["y_kn2"], qkv_res[cb]["y_kn2"]], axis=1)
        n2in = np.concatenate([qn2f[:, tl], kn2f], axis=1)
        in_maps.append({
            "qT": np.ascontiguousarray(qf[:, cols]),
            "kT": np.ascontiguousarray(kf),
            "vo": np.ascontiguousarray(vo),
            "xT": f32c(x[b].T[:, cols]),
            "wo": f32c(Wo),
            "n2in": f32c(n2in),
            "msk": make_masks(h),
            "onc": ONES_COL, "onr": ONES_ROW, "idn": IDENT,
        })
    return run_bass_kernel_spmd(nc, in_maps, core_ids=list(range(8))).results


def gather_interleaved(res, key):
    out = np.empty((B, S, D), np.float32)
    for c in range(8):
        b, h = c // 2, c % 2
        tl = tiles_of(h)
        cols = np.concatenate([np.arange(g * 128, (g + 1) * 128) for g in tl])
        out[b, cols, :] = res[c][key].T
    return out


_PROGS = {}


def qkv_prog():
    if "qkv" not in _PROGS:
        _PROGS["qkv"] = build_qkv()
    return _PROGS["qkv"]


def attn_prog():
    if "attn" not in _PROGS:
        _PROGS["attn"] = build_attn()
    return _PROGS["attn"]



def _emit_scales(p, nc, mx_ap, n_mx, srow, sxb, onr_ap, n_onr, pbc, specs):
    raise NotImplementedError


def build_ffn1():
    p = Prog("ffn1")
    nc = p.nc
    xT = nc.dram_tensor("xT", [D, TOK], F32, kind="ExternalInput").ap()
    mxs = nc.dram_tensor("mxs", [1, 8], F32, kind="ExternalInput").ap()
    wupT = nc.dram_tensor("wupT", [D, FI], BF16, kind="ExternalInput").ap()
    wgT = nc.dram_tensor("wgT", [D, FI], BF16, kind="ExternalInput").ap()
    onr = nc.dram_tensor("onr", [1, 128], F32, kind="ExternalInput").ap()
    idn = nc.dram_tensor("idn", [128, 128], F32, kind="ExternalInput").ap()
    y_up = nc.dram_tensor("y_up", [FI, TOK], F32, kind="ExternalOutput").ap()
    y_g = nc.dram_tensor("y_g", [FI, TOK], F32, kind="ExternalOutput").ap()
    y_m = nc.dram_tensor("y_m", [1, 2], F32, kind="ExternalOutput").ap()

    xs = p.sbuf("xs", [128, 2 * TOK], F32)
    xq = p.sbuf("xq", [128, 8 * TOK], BF16)
    tqs = p.sbuf("tqs", [128, TOK], F32)
    wtb = p.sbuf("wtb", [128, 2 * 1024], BF16)
    srow = p.sbuf("srow", [1, 2], F32)
    sxb = p.sbuf("sxb", [128, 2], F32)
    ring = p.sbuf("ring", [128, 4 * 512], F32)
    mc = p.sbuf("mc", [128, 4], F32)
    mrow = p.sbuf("mrow", [1, 128], F32)
    onr_s = p.sbuf("onr_s", [1, 128], F32)
    idn_s = p.sbuf("idn_s", [128, 128], F32)
    pq0 = p.psum("pq0", [128, 512])
    pq1 = p.psum("pq1", [128, 512])
    pq2 = p.psum("pq2", [128, 512])
    pq3 = p.psum("pq3", [128, 512])
    pbc = p.psum("pbc", [128, 2])
    pmx = p.psum("pmx", [1, 128])

    n_onr = p.dma("sp", onr_s[:], onr)
    n_idn = p.dma("sp", idn_s[:], idn)
    n_mx = p.dma("sp", mrow[0:1, 0:8], mxs)
    r_m = vred(p, srow[0:1, 0:1], mrow[0:1, 0:8], ALU.max, [n_mx])
    r_s = ts(p, srow[0:1, 0:1], srow[0:1, 0:1], 1.0 / 32767.0, 1e-12, ALU.mult, ALU.max, [r_m])
    r_r = vrecip(p, srow[0:1, 1:2], srow[0:1, 0:1], [r_s])
    n_bc = mm(p, pbc[:, 0:2], onr_s[0:1, :], srow[0:1, 0:2], True, True, [r_r, n_onr])
    n_sx = acopy(p, sxb[:], pbc[:, 0:2], [n_bc])

    xq_nodes = []
    for c in range(8):
        buf = c % 2
        deps = [xq_nodes[c - 2]] if c >= 2 else []
        ld = p.dma("sp", xs[:, buf * TOK:(buf + 1) * TOK], xT[c * 128:(c + 1) * 128, :], deps)
        t1 = p.add("dve", (lambda e, a=xs[:, buf * TOK:(buf + 1) * TOK], o=tqs[:]:
                           e.tensor_scalar(o, a, sxb[:, 1:2], MAGIC, ALU.mult, ALU.add)),
                   [ld, n_sx] + deps)
        t2 = p.add("dve", (lambda e, o=xq[:, c * TOK:(c + 1) * TOK], a=tqs[:]:
                           e.tensor_scalar(o, a, MAGIC, sxb[:, 0:1], ALU.subtract, ALU.mult)),
                   [t1])
        xq_nodes.append(t2)

    stores = []
    use = 0
    prev_ev = {}
    prev_rd = {}
    prev_red = None
    mzero = p.add("dve", lambda e: e.memset(mc[:, 0:2], 0.0), [])
    wcons = {}
    for mi, (wsrc, ydst) in enumerate(((wupT, y_up), (wgT, y_g))):
        for ic in range(24):
            wbuf = (mi * 24 + ic) % 2
            wdeps = []
            if (mi * 24 + ic) >= 2:
                wdeps.append(wcons[mi * 24 + ic - 2])
            wl = p.dma("pool", wtb[:, wbuf * 1024:(wbuf + 1) * 1024]
                       .rearrange("p (c w) -> p c w", w=128),
                       wsrc[:, ic * 128:(ic + 1) * 128].rearrange("(c p) w -> p c w", p=128),
                       wdeps)
            for tc in range(4):
                pt = (pq0, pq1, pq2, pq3)[use % 4]
                deps = [wl, xq_nodes[-1]]
                if use >= 4:
                    deps.append(prev_ev[use - 4])
                    deps.append(prev_rd[use - 4])
                g = mm_group(p, pt[:, :],
                             [(wtb[:, wbuf * 1024 + kc * 128:wbuf * 1024 + (kc + 1) * 128],
                               xq[:, kc * TOK + tc * 512:kc * TOK + (tc + 1) * 512])
                              for kc in range(8)], deps)
                rdeps = [g]
                if prev_red is not None:
                    rdeps.append(prev_red)
                rd = vred(p, mc[:, 2 + mi:3 + mi], pt[:, :], ALU.max, rdeps, absval=True)
                prev_rd[use] = rd
                mx = tt(p, mc[:, mi:mi + 1], mc[:, mi:mi + 1], mc[:, 2 + mi:3 + mi],
                        ALU.max, [rd, mzero])
                prev_red = mx
                rs = use % 4
                edeps = [g, rd]
                if use >= 4:
                    edeps.append(stores[use - 4])
                ev = acopy(p, ring[:, rs * 512:(rs + 1) * 512], pt[:, :], edeps)
                st = p.dma("sp", ydst[ic * 128:(ic + 1) * 128, tc * 512:(tc + 1) * 512],
                           ring[:, rs * 512:(rs + 1) * 512], [ev])
                prev_ev[use] = ev
                stores.append(st)
                use += 1
            wcons[mi * 24 + ic] = g
    n_tp = p.add("pe", lambda e: e.transpose(pmx[0:1, :], mc[:, 0:1], idn_s[:, :]),
                 [prev_red, n_idn])
    n_e1 = acopy(p, mrow[0:1, :], pmx[0:1, :], [n_tp])
    f1 = vred(p, mc[0:1, 2:3], mrow[0:1, :], ALU.max, [n_e1, prev_red])
    n_tp2 = p.add("pe", lambda e: e.transpose(pmx[0:1, :], mc[:, 1:2], idn_s[:, :]),
                  [f1, n_tp])
    n_e2 = acopy(p, mrow[0:1, :], pmx[0:1, :], [n_tp2, f1])
    f2 = vred(p, mc[0:1, 3:4], mrow[0:1, :], ALU.max, [n_e2, f1])
    st_m = p.dma("sp", y_m, mc[0:1, 2:4], [f2])
    sink(p, stores + [st_m])
    return p.build()


def build_ffn2():
    p = Prog("ffn2")
    nc = p.nc
    gT = nc.dram_tensor("gT", [FI, TOK], F32, kind="ExternalInput").ap()
    m16 = nc.dram_tensor("m16", [1, 16], F32, kind="ExternalInput").ap()
    onr = nc.dram_tensor("onr", [1, 128], F32, kind="ExternalInput").ap()
    idn = nc.dram_tensor("idn", [128, 128], F32, kind="ExternalInput").ap()
    y_gi = nc.dram_tensor("y_gi", [FI, TOK], I16, kind="ExternalOutput").ap()
    y_sr = nc.dram_tensor("y_sr", [FI, TOK], U16, kind="ExternalOutput").ap()
    y_m = nc.dram_tensor("y_m", [1, 1], F32, kind="ExternalOutput").ap()

    mr = p.sbuf("mr", [1, 16], F32)
    srow = p.sbuf("srow", [1, 4], F32)
    sgb = p.sbuf("sgb", [128, 4], F32)
    gld = p.sbuf("gld", [128, 4 * 512], F32)
    tf = p.sbuf("tf", [128, 2 * 512], F32)
    tf2 = p.sbuf("tf2", [128, 512], F32)
    tf3 = p.sbuf("tf3", [128, 512], F32)
    sg = p.sbuf("sg", [128, 2 * 512], F32)
    gi = p.sbuf("gi", [128, 4 * 512], I16)
    sr = p.sbuf("sr", [128, 4 * 512], U16)
    mc = p.sbuf("mc", [128, 2], F32)
    mrow = p.sbuf("mrow", [1, 128], F32)
    onr_s = p.sbuf("onr_s", [1, 128], F32)
    idn_s = p.sbuf("idn_s", [128, 128], F32)
    pbc = p.psum("pbc", [128, 4])
    pmx = p.psum("pmx", [1, 128])

    n_onr = p.dma("sp", onr_s[:], onr)
    n_idn = p.dma("sp", idn_s[:], idn)
    n_mx = p.dma("sp", mr[:], m16)
    r_m = vred(p, srow[0:1, 0:2], mr[0:1, :].rearrange("a (b two) -> a two b", two=2),
               ALU.max, [n_mx])
    r_s = ts(p, srow[0:1, 0:1], srow[0:1, 1:2], 1.0 / 32767.0, 1e-12, ALU.mult, ALU.max, [r_m])
    r_r = vrecip(p, srow[0:1, 1:2], srow[0:1, 0:1], [r_s])
    r_2 = ts(p, srow[0:1, 2:3], srow[0:1, 0:1], 1.0 / 65536.0, None, ALU.mult, ALU.bypass, [r_r])
    n_bc = mm(p, pbc[:, 0:4], onr_s[0:1, :], srow[0:1, 0:4], True, True, [r_2, n_onr])
    n_sx = acopy(p, sgb[:], pbc[:, 0:4], [n_bc])

    stores = []
    prev = {}
    prev_red = None
    mzero = p.add("dve", lambda e: e.memset(mc[:, 0:1], 0.0), [])
    st1s, st2s = {}, {}
    stage1 = {}

    def emit_s1(i):
        ic, tc = i // 4, i % 4
        bq = i % 4
        pb = i % 2
        ldeps = [prev[i - 4]] if i >= 4 else []
        ld = p.dma("sp", gld[:, bq * 512:(bq + 1) * 512],
                   gT[ic * 128:(ic + 1) * 128, tc * 512:(tc + 1) * 512], ldeps)
        t1deps = [ld, n_sx]
        if i >= 2:
            t1deps.append(stage1[i - 2]["t2"])
        t1 = p.add("dve", (lambda e, a=gld[:, bq * 512:(bq + 1) * 512], pb=pb:
                           e.tensor_scalar(tf[:, pb * 512:(pb + 1) * 512], a,
                                           sgb[:, 1:2], MAGIC, ALU.mult, ALU.add)),
                   t1deps)
        t2 = p.add("dve", (lambda e, o=gi[:, bq * 512:(bq + 1) * 512], pb=pb:
                           e.tensor_scalar(o, tf[:, pb * 512:(pb + 1) * 512], MAGIC, None,
                                           ALU.subtract, ALU.bypass)),
                   [t1] + ([st1s[i - 4]] if i >= 4 else []))
        st1 = p.dma("pool", y_gi[ic * 128:(ic + 1) * 128, tc * 512:(tc + 1) * 512],
                    gi[:, bq * 512:(bq + 1) * 512], [t2])
        sdep = [t2]
        if i >= 2:
            sdep.append(stage1[i - 2]["t3"])
        a_s = act(p, sg[:, pb * 512:(pb + 1) * 512], gi[:, bq * 512:(bq + 1) * 512],
                  AF.Sigmoid, sdep, scale=sgb[:, 0:1])
        st1s[i] = st1
        stage1[i] = {"t2": t2, "a_s": a_s, "st1": st1, "t3": None}

    def emit_s2(i, prev_red):
        ic, tc = i // 4, i % 4
        bq = i % 4
        pb = i % 2
        a_s = stage1[i]["a_s"]
        t3 = ts(p, tf2[:], sg[:, pb * 512:(pb + 1) * 512], 65536.0, MAGIC,
                ALU.mult, ALU.add, [a_s])
        stage1[i]["t3"] = t3
        t4 = p.add("dve", (lambda e, o=sr[:, bq * 512:(bq + 1) * 512]:
                           e.tensor_scalar(o, tf2[:], MAGIC, 65535.0, ALU.subtract, ALU.min)),
                   [t3] + ([st2s[i - 4]] if i >= 4 else []))
        st2 = p.dma("pool", y_sr[ic * 128:(ic + 1) * 128, tc * 512:(tc + 1) * 512],
                    sr[:, bq * 512:(bq + 1) * 512], [t4])
        g2 = p.add("dve", (lambda e, a=sr[:, bq * 512:(bq + 1) * 512],
                           b=gi[:, bq * 512:(bq + 1) * 512]:
                           e.scalar_tensor_tensor(tf3[:], a, sgb[:, 2:3], b, ALU.mult, ALU.mult)),
                   [t4])
        rdeps = [g2]
        if prev_red is not None:
            rdeps.append(prev_red)
        rd = vred(p, mc[:, 1:2], tf3[:], ALU.max, rdeps, absval=True)
        nred = tt(p, mc[:, 0:1], mc[:, 0:1], mc[:, 1:2], ALU.max, [rd, mzero])
        prev[i] = g2
        st2s[i] = st2
        stores.extend([stage1[i]["st1"], st2])
        return nred

    emit_s1(0)
    for i in range(96):
        if i + 1 < 96:
            emit_s1(i + 1)
        prev_red = emit_s2(i, prev_red)
    n_tp = p.add("pe", lambda e: e.transpose(pmx[0:1, :], mc[:, 0:1], idn_s[:, :]),
                 [prev_red, n_idn])
    n_e1 = acopy(p, mrow[0:1, :], pmx[0:1, :], [n_tp])
    f1 = vred(p, mc[0:1, 1:2], mrow[0:1, :], ALU.max, [n_e1, prev_red])
    st_m = p.dma("sp", y_m, mc[0:1, 1:2], [f1])
    sink(p, stores + [st_m])
    return p.build()


def build_ffn3():
    p = Prog("ffn3")
    nc = p.nc
    uT = nc.dram_tensor("uT", [FI, TOK], F32, kind="ExternalInput").ap()
    giT = nc.dram_tensor("giT", [FI, TOK], I16, kind="ExternalInput").ap()
    srT = nc.dram_tensor("srT", [FI, TOK], U16, kind="ExternalInput").ap()
    m24 = nc.dram_tensor("m24", [1, 24], F32, kind="ExternalInput").ap()
    onr = nc.dram_tensor("onr", [1, 128], F32, kind="ExternalInput").ap()
    idn = nc.dram_tensor("idn", [128, 128], F32, kind="ExternalInput").ap()
    y_ui = nc.dram_tensor("y_ui", [FI, TOK], I16, kind="ExternalOutput").ap()
    y_g2i = nc.dram_tensor("y_g2i", [FI, TOK], I16, kind="ExternalOutput").ap()
    y_m = nc.dram_tensor("y_m", [1, 1], F32, kind="ExternalOutput").ap()

    mr = p.sbuf("mr", [1, 24], F32)
    srow = p.sbuf("srow", [1, 8], F32)
    sgb = p.sbuf("sgb", [128, 8], F32)
    uld = p.sbuf("uld", [128, 4 * 512], F32)
    gild = p.sbuf("gild", [128, 4 * 512], I16)
    srld = p.sbuf("srld", [128, 4 * 512], U16)
    tf = p.sbuf("tf", [128, 512], F32)
    tf2 = p.sbuf("tf2", [128, 512], F32)
    tf3 = p.sbuf("tf3", [128, 512], F32)
    ui = p.sbuf("ui", [128, 4 * 512], I16)
    g2i = p.sbuf("g2i", [128, 4 * 512], I16)
    mc = p.sbuf("mc", [128, 2], F32)
    mrow = p.sbuf("mrow", [1, 128], F32)
    onr_s = p.sbuf("onr_s", [1, 128], F32)
    idn_s = p.sbuf("idn_s", [128, 128], F32)
    pbc = p.psum("pbc", [128, 8])
    pmx = p.psum("pmx", [1, 128])

    n_onr = p.dma("sp", onr_s[:], onr)
    n_idn = p.dma("sp", idn_s[:], idn)
    n_mx = p.dma("sp", mr[:], m24)
    r_m = vred(p, srow[0:1, 0:3], mr[0:1, :].rearrange("a (b three) -> a three b", three=3),
               ALU.max, [n_mx])
    r1 = ts(p, srow[0:1, 3:4], srow[0:1, 0:1], 1.0 / 32767.0, 1e-12, ALU.mult, ALU.max, [r_m])
    r2 = ts(p, srow[0:1, 5:6], srow[0:1, 1:2], 1.0 / 32767.0, 1e-12, ALU.mult, ALU.max, [r1])
    r3 = ts(p, srow[0:1, 6:7], srow[0:1, 2:3], 1.0 / 32767.0, 1e-12, ALU.mult, ALU.max, [r2])
    r4 = vrecip(p, srow[0:1, 4:5], srow[0:1, 3:4], [r3])
    r5 = vrecip(p, srow[0:1, 7:8], srow[0:1, 6:7], [r4])
    r6 = ts(p, srow[0:1, 0:1], srow[0:1, 5:6], 1.0 / 65536.0, None, ALU.mult, ALU.bypass, [r5])
    r7 = tt(p, srow[0:1, 1:2], srow[0:1, 6:7], srow[0:1, 3:4], ALU.mult, [r6])
    n_bc = mm(p, pbc[:, 0:8], onr_s[0:1, :], srow[0:1, 0:8], True, True, [r7, n_onr])
    n_sx = acopy(p, sgb[:], pbc[:, 0:8], [n_bc])

    stores = []
    prev = {}
    prev_red = None
    mzero = p.add("dve", lambda e: e.memset(mc[:, 0:1], 0.0), [])
    s1s, s2s = {}, {}
    for i in range(96):
        ic, tc = i // 4, i % 4
        bq = i % 4
        sl = (bq * 512, (bq + 1) * 512)
        ldeps = [prev[i - 4]] if i >= 4 else []
        l1 = p.dma("pool", uld[:, sl[0]:sl[1]],
                   uT[ic * 128:(ic + 1) * 128, tc * 512:(tc + 1) * 512], ldeps)
        l2 = p.dma("pool", gild[:, sl[0]:sl[1]],
                   giT[ic * 128:(ic + 1) * 128, tc * 512:(tc + 1) * 512], ldeps)
        l3 = p.dma("pool", srld[:, sl[0]:sl[1]],
                   srT[ic * 128:(ic + 1) * 128, tc * 512:(tc + 1) * 512], ldeps)
        t1 = p.add("dve", (lambda e, a=uld[:, sl[0]:sl[1]]:
                           e.tensor_scalar(tf[:], a, sgb[:, 4:5], MAGIC, ALU.mult, ALU.add)),
                   [l1, n_sx])
        t2 = p.add("dve", (lambda e, o=ui[:, sl[0]:sl[1]]:
                           e.tensor_scalar(o, tf[:], MAGIC, None, ALU.subtract, ALU.bypass)),
                   [t1] + ([s1s[i - 4]] if i >= 4 else []))
        s1 = p.dma("sp", y_ui[ic * 128:(ic + 1) * 128, tc * 512:(tc + 1) * 512],
                   ui[:, sl[0]:sl[1]], [t2])
        g2 = p.add("dve", (lambda e, a=srld[:, sl[0]:sl[1]], b=gild[:, sl[0]:sl[1]]:
                           e.scalar_tensor_tensor(tf2[:], a, sgb[:, 0:1], b, ALU.mult, ALU.mult)),
                   [l2, l3, n_sx])
        t3 = p.add("dve", (lambda e:
                           e.tensor_scalar(tf[:], tf2[:], sgb[:, 7:8], MAGIC, ALU.mult, ALU.add)),
                   [g2, t2])
        t4 = p.add("dve", (lambda e, o=g2i[:, sl[0]:sl[1]]:
                           e.tensor_scalar(o, tf[:], MAGIC, None, ALU.subtract, ALU.bypass)),
                   [t3] + ([s2s[i - 4]] if i >= 4 else []))
        s2 = p.dma("sp", y_g2i[ic * 128:(ic + 1) * 128, tc * 512:(tc + 1) * 512],
                   g2i[:, sl[0]:sl[1]], [t4])
        oo = p.add("dve", (lambda e, a=g2i[:, sl[0]:sl[1]], b=ui[:, sl[0]:sl[1]]:
                           e.scalar_tensor_tensor(tf3[:], a, sgb[:, 1:2], b, ALU.mult, ALU.mult)),
                   [t4])
        rdeps = [oo]
        if prev_red is not None:
            rdeps.append(prev_red)
        rd = vred(p, mc[:, 1:2], tf3[:], ALU.max, rdeps, absval=True)
        prev_red = tt(p, mc[:, 0:1], mc[:, 0:1], mc[:, 1:2], ALU.max, [rd, mzero])
        prev[i] = oo
        s1s[i], s2s[i] = s1, s2
        stores += [s1, s2]
    n_tp = p.add("pe", lambda e: e.transpose(pmx[0:1, :], mc[:, 0:1], idn_s[:, :]),
                 [prev_red, n_idn])
    n_e1 = acopy(p, mrow[0:1, :], pmx[0:1, :], [n_tp])
    f1 = vred(p, mc[0:1, 1:2], mrow[0:1, :], ALU.max, [n_e1, prev_red])
    st_m = p.dma("sp", y_m, mc[0:1, 1:2], [f1])
    sink(p, stores + [st_m])
    return p.build()


def build_ffn4():
    p = Prog("ffn4")
    nc = p.nc
    giT = nc.dram_tensor("giT", [FI, TOK], I16, kind="ExternalInput").ap()
    uiT = nc.dram_tensor("uiT", [FI, TOK], I16, kind="ExternalInput").ap()
    m32 = nc.dram_tensor("m32", [1, 32], F32, kind="ExternalInput").ap()
    wdT = nc.dram_tensor("wdT", [FI, D], BF16, kind="ExternalInput").ap()
    xT = nc.dram_tensor("xT", [D, TOK], F32, kind="ExternalInput").ap()
    onc = nc.dram_tensor("onc", [128, 1], F32, kind="ExternalInput").ap()
    onr = nc.dram_tensor("onr", [1, 128], F32, kind="ExternalInput").ap()
    idn = nc.dram_tensor("idn", [128, 128], F32, kind="ExternalInput").ap()
    y_x2 = nc.dram_tensor("y_x2", [D, TOK], F32, kind="ExternalOutput").ap()
    y_mx = nc.dram_tensor("y_mx", [1, 1], F32, kind="ExternalOutput").ap()

    mr = p.sbuf("mr", [1, 32], F32)
    srow = p.sbuf("srow", [1, 8], F32)
    sgb = p.sbuf("sgb", [128, 8], F32)
    gild = p.sbuf("gild", [128, 4 * 512], I16)
    uild = p.sbuf("uild", [128, 4 * 512], I16)
    tf = p.sbuf("tf", [128, 512], F32)
    tf2 = p.sbuf("tf2", [128, 512], F32)
    obf = p.sbuf("obf", [128, 2 * 24 * 512], BF16)
    wdb = p.sbuf("wdb", [128, 8 * 24 * 128], BF16)
    xrs = p.sbuf("xrs", [128, 2 * 512], F32)
    rsum = p.sbuf("rsum", [128, 8 * 512], F32)
    x1s = p.sbuf("x1s", [128, 8 * 512], F32)
    sq = p.sbuf("sqln", [128, 8 * 512], F32)
    lnA = p.sbuf("lnA", [65, 512], F32)
    rrstd = p.sbuf("rrstd", [1, 512], F32)
    rmr = p.sbuf("rmr", [1, 512], F32)
    mc = p.sbuf("mc", [128, 2], F32)
    mrow = p.sbuf("mrow", [1, 128], F32)
    onc_s = p.sbuf("onc_s", [128, 1], F32)
    onr_s = p.sbuf("onr_s", [1, 128], F32)
    idn_s = p.sbuf("idn_s", [128, 128], F32)
    pq0 = p.psum("pq0", [128, 512])
    pq1 = p.psum("pq1", [128, 512])
    pst = p.psum("pst", [64, 512])
    pb1 = p.psum("pb1", [128, 512])
    pb2 = p.psum("pb2", [128, 512])
    pbc = p.psum("pbc", [128, 8])
    pmx = p.psum("pmx", [1, 128])

    n_onc = p.dma("sp", onc_s[:], onc)
    n_onr = p.dma("sp", onr_s[:], onr)
    n_idn = p.dma("sp", idn_s[:], idn)
    n_mx = p.dma("sp", mr[:], m32)
    r_m = vred(p, srow[0:1, 0:4], mr[0:1, :].rearrange("a (b four) -> a four b", four=4),
               ALU.max, [n_mx])
    r1 = ts(p, srow[0:1, 4:5], srow[0:1, 0:1], 1.0 / 32767.0, 1e-12, ALU.mult, ALU.max, [r_m])
    r2 = ts(p, srow[0:1, 5:6], srow[0:1, 2:3], 1.0 / 32767.0, 1e-12, ALU.mult, ALU.max, [r1])
    r3 = ts(p, srow[0:1, 6:7], srow[0:1, 3:4], 1.0 / 32767.0, 1e-12, ALU.mult, ALU.max, [r2])
    r4 = tt(p, srow[0:1, 0:1], srow[0:1, 5:6], srow[0:1, 4:5], ALU.mult, [r3])
    r5 = vrecip(p, srow[0:1, 1:2], srow[0:1, 6:7], [r4])
    n_bc = mm(p, pbc[:, 0:8], onr_s[0:1, :], srow[0:1, 0:8], True, True, [r5, n_onr])
    n_sx = acopy(p, sgb[:], pbc[:, 0:8], [n_bc])

    sc = {"ones_col": onc_s[:, 0:1], "ones_row": onr_s[0:1, :], "sq": sq,
          "stat": pst, "bc1": pb1, "bc2": pb2,
          "rmean": lnA[0:1, :], "rvar": lnA[32:33, :], "rtmp": lnA[64:65, :],
          "rrstd": rrstd[0:1, :], "rmr": rmr[0:1, :]}
    stores = []
    prev_o = {}
    prev_red = None
    mzero = p.add("dve", lambda e: e.memset(mc[:, 0:1], 0.0), [])
    ln_prev = None
    wl_prev = {}
    res_prev = {}
    chunks = {0: [], 1: [], 2: [], 3: []}
    OBH = 24 * 512

    def prep_chunk(tc, ic):
        i = tc * 24 + ic
        bq = i % 4
        sl = (bq * 512, (bq + 1) * 512)
        ob = (tc % 2) * OBH
        ldeps = [prev_o[i - 4]] if i >= 4 else []
        l1 = p.dma("pool", gild[:, sl[0]:sl[1]],
                   giT[ic * 128:(ic + 1) * 128, tc * 512:(tc + 1) * 512], ldeps)
        l2 = p.dma("pool", uild[:, sl[0]:sl[1]],
                   uiT[ic * 128:(ic + 1) * 128, tc * 512:(tc + 1) * 512], ldeps)
        oo = p.add("dve", (lambda e, a=gild[:, sl[0]:sl[1]], b=uild[:, sl[0]:sl[1]]:
                           e.scalar_tensor_tensor(tf[:], a, sgb[:, 0:1], b, ALU.mult, ALU.mult)),
                   [l1, l2, n_sx])
        t1 = p.add("dve", (lambda e:
                           e.tensor_scalar(tf2[:], tf[:], sgb[:, 1:2], MAGIC, ALU.mult, ALU.add)),
                   [oo])
        odeps = [t1]
        if tc >= 2:
            odeps.append(res_prev[(tc - 2, 7)])
        t2 = p.add("dve", (lambda e, o=obf[:, ob + ic * 512:ob + (ic + 1) * 512]:
                           e.tensor_scalar(o, tf2[:], MAGIC, sgb[:, 6:7], ALU.subtract, ALU.mult)),
                   odeps)
        prev_o[i] = t2
        chunks[tc].append(t2)

    for ic in range(24):
        prep_chunk(0, ic)
    for tc in range(4):
        o_nodes = chunks[tc]
        obb = (tc % 2) * OBH
        res_nodes = []
        for oc in range(8):
            if tc == 0:
                wl_prev[oc] = p.dma("pool", wdb[:, oc * 3072:(oc + 1) * 3072]
                                    .rearrange("p (c w) -> p c w", w=128),
                                    wdT[:, oc * 128:(oc + 1) * 128]
                                    .rearrange("(c p) w -> p c w", p=128), [])
            wl = wl_prev[oc]
            pt = (pq0, pq1)[oc % 2]
            gdeps = o_nodes + [wl]
            if len(res_nodes) >= 2:
                gdeps.append(res_nodes[-2])
            elif tc > 0:
                gdeps.append(res_prev[(tc - 1, 6 + oc)] if (6 + oc) <= 7 else None)
            g = mm_group(p, pt[:, :],
                         [(wdb[:, oc * 3072 + kc * 128:oc * 3072 + (kc + 1) * 128],
                           obf[:, obb + kc * 512:obb + (kc + 1) * 512])
                          for kc in range(24)], [dd for dd in gdeps if dd])
            xsl = xrs[:, (oc % 2) * 512:(oc % 2 + 1) * 512]
            xwar = res_nodes[-2:-1] if len(res_nodes) >= 2 else (
                [ln_prev] if ln_prev is not None else [])
            xl = p.dma("pool", xsl, xT[oc * 128:(oc + 1) * 128, tc * 512:(tc + 1) * 512], xwar)
            r = tt(p, rsum[:, oc * 512:(oc + 1) * 512], pt[:, :], xsl, ALU.add, [g, xl])
            res_nodes.append(r)
            res_prev[(tc, oc)] = r
            if tc < 3:
                for k in range(3):
                    prep_chunk(tc + 1, oc * 3 + k)
        outs = emit_ln(p, lambda c: rsum[:, c * 512:(c + 1) * 512],
                       lambda c: x1s[:, c * 512:(c + 1) * 512], 512, sc, res_nodes)
        ln_prev = outs[-1]
        for oc in range(8):
            stores.append(p.dma("sp", y_x2[oc * 128:(oc + 1) * 128, tc * 512:(tc + 1) * 512],
                                x1s[:, oc * 512:(oc + 1) * 512], [outs[oc]]))
            rdeps = [outs[oc], mzero]
            if prev_red is not None:
                rdeps.append(prev_red)
            rd = vred(p, mc[:, 1:2], x1s[:, oc * 512:(oc + 1) * 512], ALU.max, rdeps, absval=True)
            prev_red = tt(p, mc[:, 0:1], mc[:, 0:1], mc[:, 1:2], ALU.max, [rd])
    n_tp = p.add("pe", lambda e: e.transpose(pmx[0:1, :], mc[:, 0:1], idn_s[:, :]),
                 [prev_red, n_idn])
    n_e1 = acopy(p, mrow[0:1, :], pmx[0:1, :], [n_tp])
    f1 = vred(p, mc[0:1, 1:2], mrow[0:1, :], ALU.max, [n_e1, prev_red])
    st_m = p.dma("sp", y_mx, mc[0:1, 1:2], [f1])
    sink(p, stores + [st_m])
    return p.build()


def run_ffn(x1, mx_list, wupT_bf, wgT_bf, wdnT_bf):
    def xmaps():
        ms = []
        for c in range(8):
            b, h = c // 2, c % 2
            ms.append(f32c(x1[b, h * 2048:(h + 1) * 2048, :].T))
        return ms

    xTs = xmaps()
    mxs = np.concatenate([np.asarray(m).reshape(1, 1) for m in mx_list], axis=1)
    p1 = run_bass_kernel_spmd(ffn1_prog(), [
        {"xT": xTs[c], "mxs": f32c(mxs), "wupT": wupT_bf, "wgT": wgT_bf,
         "onr": ONES_ROW, "idn": IDENT} for c in range(8)],
        core_ids=list(range(8))).results
    m16 = np.concatenate([p1[c]["y_m"] for c in range(8)], axis=1)
    p2 = run_bass_kernel_spmd(ffn2_prog(), [
        {"gT": p1[c]["y_g"], "m16": f32c(m16), "onr": ONES_ROW, "idn": IDENT}
        for c in range(8)], core_ids=list(range(8))).results
    m24 = np.concatenate([np.concatenate([p1[c]["y_m"], p2[c]["y_m"]], axis=1)
                          for c in range(8)], axis=1)
    p3 = run_bass_kernel_spmd(ffn3_prog(), [
        {"uT": p1[c]["y_up"], "giT": p2[c]["y_gi"], "srT": p2[c]["y_sr"],
         "m24": f32c(m24), "onr": ONES_ROW, "idn": IDENT}
        for c in range(8)], core_ids=list(range(8))).results
    m32 = np.concatenate([np.concatenate([p1[c]["y_m"], p2[c]["y_m"], p3[c]["y_m"]], axis=1)
                          for c in range(8)], axis=1)
    p4 = run_bass_kernel_spmd(ffn4_prog(), [
        {"giT": p3[c]["y_g2i"], "uiT": p3[c]["y_ui"], "m32": f32c(m32),
         "wdT": wdnT_bf, "xT": xTs[c],
         "onc": ONES_COL, "onr": ONES_ROW, "idn": IDENT}
        for c in range(8)], core_ids=list(range(8))).results
    x2 = np.empty((B, S, D), np.float32)
    for c in range(8):
        b, h = c // 2, c % 2
        x2[b, h * 2048:(h + 1) * 2048, :] = p4[c]["y_x2"].T
    return x2, [p4[c]["y_mx"] for c in range(8)]


def ffn1_prog():
    if "ffn1" not in _PROGS:
        _PROGS["ffn1"] = build_ffn1()
    return _PROGS["ffn1"]


def ffn2_prog():
    if "ffn2" not in _PROGS:
        _PROGS["ffn2"] = build_ffn2()
    return _PROGS["ffn2"]


def ffn3_prog():
    if "ffn3" not in _PROGS:
        _PROGS["ffn3"] = build_ffn3()
    return _PROGS["ffn3"]


def ffn4_prog():
    if "ffn4" not in _PROGS:
        _PROGS["ffn4"] = build_ffn4()
    return _PROGS["ffn4"]


XPD = 4224
DI = 1024
DM = 512


def build_mamba(debug=False):
    p = Prog("mamba")
    nc = p.nc
    xdT = nc.dram_tensor("xdT", [DM, XPD], F32, kind="ExternalInput").ap()
    inwT = nc.dram_tensor("inwT", [DM, 2 * DI], F32, kind="ExternalInput").ap()
    xpjT = nc.dram_tensor("xpjT", [DI, 64], F32, kind="ExternalInput").ap()
    dtwT = nc.dram_tensor("dtwT", [32, DI], F32, kind="ExternalInput").ap()
    outwT = nc.dram_tensor("outwT", [DI, DM], F32, kind="ExternalInput").ap()
    convW = nc.dram_tensor("convW", [128, 32], F32, kind="ExternalInput").ap()
    convB = nc.dram_tensor("convB", [128, 8], F32, kind="ExternalInput").ap()
    dtB = nc.dram_tensor("dtB", [128, 8], F32, kind="ExternalInput").ap()
    DPc = nc.dram_tensor("DPc", [128, 8], F32, kind="ExternalInput").ap()
    AL = nc.dram_tensor("AL", [128, 128], F32, kind="ExternalInput").ap()
    esel = nc.dram_tensor("esel", [80, 2048], F32, kind="ExternalInput").ap()
    idnb = nc.dram_tensor("idnb", [128, 128], BF16, kind="ExternalInput").ap()
    y_h = nc.dram_tensor("y_h", [DM, 4096], F32, kind="ExternalOutput").ap()
    kind = dict(kind="ExternalOutput") if debug else {}
    xcbuf = nc.dram_tensor("xcbuf", [DI, 4096], F32, **kind)
    zbuf = nc.dram_tensor("zbuf", [DI, XPD], F32, **kind)
    if debug:
        y_dbc = nc.dram_tensor("y_dbc", [80, 4096], F32, kind="ExternalOutput").ap()
        y_del = nc.dram_tensor("y_del", [128, 8 * 512], F32, kind="ExternalOutput").ap()
        y_B = nc.dram_tensor("y_B", [128, 16 * 512], BF16, kind="ExternalOutput").ap()
        y_yf = nc.dram_tensor("y_yf", [128, 8 * 512], BF16, kind="ExternalOutput").ap()
        y_st = nc.dram_tensor("y_st", [128, 128], F32, kind="ExternalOutput").ap()

    xdb = p.sbuf("xdb", [128, 4 * XPD], BF16)
    inwb = p.sbuf("inwb", [128, 4 * 2048], BF16)
    xpjb = p.sbuf("xpjb", [128, 8 * 64], BF16)
    dtwb = p.sbuf("dtwb", [32, 1024], BF16)
    outwb = p.sbuf("outwb", [128, 8 * 512], BF16)
    cW = p.sbuf("cW", [128, 32], F32)
    cB = p.sbuf("cB", [128, 8], F32)
    dB = p.sbuf("dB", [128, 8], F32)
    dP = p.sbuf("dP", [128, 8], F32)
    Acol = p.sbuf("Acol", [128, 128], F32)
    esl = p.sbuf("esl", [80, 2048], F32)
    idb = p.sbuf("idb", [128, 128], BF16)
    xh = p.sbuf("xh", [128, XPD], F32)
    xcc = p.sbuf("xcc", [128, 4096], F32)
    zev = p.sbuf("zev", [128, 512], F32)
    xcw = p.sbuf("xcw", [128, 8 * 512], F32)
    xcwb = p.sbuf("xcwb", [128, 2 * 512], BF16)
    dtb16 = p.sbuf("dtb16", [32, 512], BF16)
    dbcs = p.sbuf("dbcs", [80, 512], F32)
    dels = p.sbuf("dels", [128, 8 * 512], F32)
    us = p.sbuf("us", [128, 8 * 512], BF16)
    ta = p.sbuf("ta", [128, 512], F32)
    tb = p.sbuf("tb", [128, 512], F32)
    Gt = p.sbuf("Gt", [128, 4 * 512], F32)
    vt = p.sbuf("vt", [128, 4 * 512], BF16)
    ht = p.sbuf("ht", [128, 2 * 512], BF16)
    hct = p.sbuf("hct", [128, 2 * 512], BF16)
    stcols = p.sbuf("stcols", [128, 128], F32)
    sz = p.sbuf("sz", [128, 512], F32)
    yf = p.sbuf("yf", [128, 8 * 512], BF16)
    hor = p.sbuf("hor", [128, 2 * 512], F32)
    Bbc = xh[:].bitcast(BF16)
    Cbc = xcc[:].bitcast(BF16)
    pin0 = p.psum("pin0", [128, 512])
    pin1 = p.psum("pin1", [128, 512])
    pdbc = p.psum("pdbc", [128, 512])
    pdel = p.psum("pdel", [128, 512])
    pBC = p.psum("pBC", [128, 512])
    py0 = p.psum("py0", [128, 512])
    py1 = p.psum("py1", [128, 512])
    pho = p.psum("pho", [128, 512])

    l_cw = p.dma("sp", cW[:], convW)
    l_cb = p.dma("sp", cB[:], convB)
    l_db = p.dma("sp", dB[:], dtB)
    l_dp = p.dma("sp", dP[:], DPc)
    l_al = p.dma("sp", Acol[:], AL)
    l_es = p.dma("sp", esl[:], esel)
    l_id = p.dma("sp", idb[:], idnb)
    n_A1 = act(p, Acol[:], Acol[:], AF.Exp, [l_al])
    n_A = ts(p, Acol[:], Acol[:], -1.0, None, ALU.mult, ALU.bypass, [n_A1])
    casts = []

    def staged(dst, src_ap, width):
        i = len(casts)
        use_xh = (width > 4096) or (i % 2 == 0)
        sbuf_st = xh[:, 0:width] if use_xh else xcc[:, 0:width]
        dep = [c for c in casts[-2:]]
        ld = p.dma("sp" if i % 2 == 0 else "pool", sbuf_st, src_ap, dep)
        casts.append(vcopy(p, dst, sbuf_st, [ld]))
    for c in range(4):
        staged(xdb[:, c * XPD:(c + 1) * XPD], xdT[c * 128:(c + 1) * 128, :], XPD)
    for c in range(4):
        staged(inwb[:, c * 2048:(c + 1) * 2048], inwT[c * 128:(c + 1) * 128, :], 2048)
    xd_done = casts[3]
    inw_done = casts[7]
    phase1_cast_done = casts[-1]
    wcasts = []

    def staged2(dst, src_ap, width, slot):
        dep = [wcasts[-2]] if len(wcasts) >= 2 else [casts[-1]]
        st_ap = xcw[:, slot * 512: slot * 512 + width]
        ld = p.dma("pool", st_ap, src_ap, dep)
        wcasts.append(vcopy(p, dst, st_ap, [ld]))
    for c in range(8):
        staged2(xpjb[:, c * 64:(c + 1) * 64], xpjT[c * 128:(c + 1) * 128, :], 64, c % 2)
    ldw = p.dma("pool", xcw[0:32, 1024:2048], dtwT, [wcasts[-2]])
    wcasts.append(vcopy(p, dtwb[:], xcw[0:32, 1024:2048], [ldw]))
    for c in range(8):
        staged2(outwb[:, c * 512:(c + 1) * 512], outwT[c * 128:(c + 1) * 128, :], 512, 4 + c % 2)
    cast_done = wcasts[-1]

    pins = (pin0, pin1)
    use = 0
    prev_ev = {}
    conv_last = None
    xc_stores = []
    z_stores = []
    for ec in range(16):
        is_xh = ec < 8
        evs = []
        for t9 in range(9):
            w9 = 512 if t9 < 8 else XPD - 8 * 512
            pt = pins[use % 2]
            deps = [inw_done, xd_done]
            if use >= 2:
                deps.append(prev_ev[use - 2])
            g = mm_group(p, pt[:, 0:w9],
                         [(inwb[:, kc * 2048 + ec * 128:kc * 2048 + (ec + 1) * 128],
                           xdb[:, kc * XPD + t9 * 512:kc * XPD + t9 * 512 + w9])
                          for kc in range(4)], deps)
            if is_xh:
                ev = acopy(p, xh[:, t9 * 512:t9 * 512 + w9], pt[:, 0:w9],
                           [g, conv_last, phase1_cast_done])
            else:
                edeps = [g]
                if z_stores:
                    edeps.append(z_stores[-1])
                ev = acopy(p, zev[:, 0:w9], pt[:, 0:w9], edeps)
                z_stores.append(p.dma("pool", zbuf[(ec - 8) * 128:(ec - 7) * 128, t9 * 512:t9 * 512 + w9],
                                      zev[:, 0:w9], [ev]))
            prev_ev[use] = ev
            evs.append(ev)
            use += 1
        if is_xh:
            c0 = p.add("dve", (lambda e, c=ec: e.tensor_scalar(
                xcc[:], xh[:, 0:4096], cW[:, c * 4:c * 4 + 1], None, ALU.mult, ALU.bypass)),
                evs + [phase1_cast_done] + ([conv_last] if conv_last else []))
            c1 = p.add("dve", (lambda e, c=ec: e.scalar_tensor_tensor(
                xcc[:], xh[:, 1:4097], cW[:, c * 4 + 1:c * 4 + 2], xcc[:], ALU.mult, ALU.add)), [c0])
            c2 = p.add("dve", (lambda e, c=ec: e.scalar_tensor_tensor(
                xcc[:], xh[:, 2:4098], cW[:, c * 4 + 2:c * 4 + 3], xcc[:], ALU.mult, ALU.add)), [c1])
            c3 = p.add("dve", (lambda e, c=ec: e.scalar_tensor_tensor(
                xcc[:], xh[:, 3:4099], cW[:, c * 4 + 3:c * 4 + 4], xcc[:], ALU.mult, ALU.add)), [c2])
            c4 = act(p, xcc[:], xcc[:], AF.Silu, [c3, l_cb], bias=cB[:, ec:ec + 1])
            st = p.dma("pool", xcbuf[ec * 128:(ec + 1) * 128, :], xcc[:], [c4])
            xc_stores.append(st)
            conv_last = st

    n_z0 = p.add("dve", lambda e: e.memset(stcols[:], 0.0), [])
    sdel_done = {}
    prev_out = []
    prev_gate = {}
    prev_hos = {}
    prev_xcl = {}
    prev_dtev = None
    prev_scan = {}
    prev_hc = {}
    prev_ymm = {}
    prev_rep = {}
    h_stores = []
    for tw in range(8):
        tcols = (3 + tw * 512, 3 + (tw + 1) * 512)
        xcl = []
        for c in range(8):
            dep = [prev_gate[(tw - 1, c)]] if tw > 0 else [xc_stores[c], cast_done]
            if tw > 0:
                dep.append(xc_stores[c])
            xcl.append(p.dma("sp", xcw[:, c * 512:(c + 1) * 512],
                             xcbuf[c * 128:(c + 1) * 128, tw * 512:(tw + 1) * 512], dep))
        dbc_mms = []
        for c in range(8):
            cb = c % 2
            ccdep = [xcl[c]]
            if len(dbc_mms) >= 2:
                ccdep.append(dbc_mms[-2])
            elif prev_dtev is not None:
                ccdep.append(prev_dtev)
            cc = vcopy(p, xcwb[:, cb * 512:(cb + 1) * 512], xcw[:, c * 512:(c + 1) * 512], ccdep)
            g1 = mm(p, pdbc[0:48, :], xpjb[:, c * 64:c * 64 + 48],
                    xcwb[:, cb * 512:(cb + 1) * 512], c == 0, c == 7, [cc, cast_done])
            g2 = mm(p, pdbc[64:80, :], xpjb[:, c * 64 + 48:c * 64 + 64],
                    xcwb[:, cb * 512:(cb + 1) * 512], c == 0, c == 7, [cc])
            dbc_mms.append(g2)
        dbc_ev = acopy(p, dbcs[0:80, 0:512], pdbc[0:80, :], dbc_mms)
        dtev = vcopy(p, dtb16[:], dbcs[0:32, 0:512], [dbc_ev])
        prev_dtev = dtev
        del_nodes = []
        gds = {}
        t2s = {}

        def emit_gd(c):
            pdl = (pin0, pin1)[c % 2]
            gdeps = [dtev]
            if c - 2 in t2s:
                gdeps.append(t2s[c - 2])
            elif tw == 0 and c < 2:
                gdeps.extend(z_stores[-2:])
            gds[c] = mm(p, pdl[:, :], dtwb[0:32, c * 128:(c + 1) * 128], dtb16[:],
                        True, True, [d for d in gdeps if d])

        emit_gd(0)
        emit_gd(1)
        for c in range(8):
            pdl = (pin0, pin1)[c % 2]
            tah = ta[:, 0:512] if False else ta[:]
            a1 = act(p, tah, pdl[:, :], AF.Abs,
                     [gds[c], l_db] + ([del_nodes[-1]] if del_nodes else []),
                     bias=dB[:, c:c + 1])
            a2 = act(p, tah, tah, AF.Exp, [a1], scale=-1.0)
            t1 = ts(p, tah, tah, 1.0, None, ALU.add, ALU.bypass, [a2])
            a3 = act(p, tah, tah, AF.Ln, [t1])
            t2 = p.add("dve", (lambda e, c=c, pdl=pdl: e.tensor_scalar(
                tb[:], pdl[:, :], dB[:, c:c + 1], 0.0, ALU.add, ALU.max)), [a3])
            t2s[c] = t2
            if c + 2 < 8:
                emit_gd(c + 2)
            dl = tt(p, dels[:, c * 512:(c + 1) * 512], tah, tb[:], ALU.add, [t2, a3])
            uu = tt(p, us[:, c * 512:(c + 1) * 512], dels[:, c * 512:(c + 1) * 512],
                    xcw[:, c * 512:(c + 1) * 512], ALU.mult, [dl])
            del_nodes.append(uu)
        if debug:
            h_stores.append(p.dma("pool", y_dbc[:, tw * 512:(tw + 1) * 512],
                                  dbcs[0:80, tw * 512:(tw + 1) * 512], [dbc_ev]))
            if tw == 0:
                h_stores.append(p.dma("pool", y_del, dels[:], [del_nodes[-1]]))
        rep_nodes = {}
        last_eB = [None]
        last_eC = [None]

        def emit_rep(s, extra):
            dep = [dbc_ev, l_es] + extra
            if (tw - 1, "v", s) in prev_rep:
                dep.append(prev_rep[(tw - 1, "v", s)])
            if (tw - 1, "h", s) in prev_rep:
                dep.append(prev_rep[(tw - 1, "h", s)])
            if last_eB[0] is not None:
                dep.append(last_eB[0])
            gB = mm(p, pBC[:, :], esl[32:48, s * 128:(s + 1) * 128],
                    dbcs[32:48, 0:512], True, True, dep)
            eB = acopy(p, Bbc[:, s * 512:(s + 1) * 512], pBC[:, :], [gB])
            dep2 = [dbc_ev]
            if last_eC[0] is not None:
                dep2.append(last_eC[0])
            gC = mm(p, pdbc[0:128, :], esl[64:80, s * 128:(s + 1) * 128],
                    dbcs[64:80, 0:512], True, True, dep2 + [eB])
            eC = acopy(p, Cbc[:, s * 512:(s + 1) * 512], pdbc[0:128, :], [gC])
            last_eB[0] = eB
            last_eC[0] = eC
            rep_nodes[s] = eC
        for s in range(3):
            emit_rep(s, [])
        if debug and tw == 0:
            h_stores.append(p.dma("pool", y_B, Bbc[:, 0:16 * 512], [rep_nodes[2]]))
        def lin(dic, tw_, c_, s_, d):
            g = (tw_ * 8 + c_) * 16 + s_ - d
            if g < 0:
                return None
            tw2, r_ = divmod(g, 128)
            c2, s2 = divmod(r_, 16)
            return dic.get((tw2, c2, s2))

        for c in range(8):
            pyt = (py0, py1)[c % 2]

            def ring_war(j, c=c, tw=tw):
                return lin(prev_scan, tw, c, j, 4)

            def emit_gx(s, c=c, tw=tw):
                sb = s % 4
                gdep = [del_nodes[c], n_A, ring_war(s)]
                return act(p, Gt[:, sb * 512:(sb + 1) * 512],
                           dels[:, c * 512:(c + 1) * 512], AF.Exp, gdep,
                           scale=Acol[:, c * 16 + s:c * 16 + s + 1])

            def emit_vv(s, c=c, tw=tw):
                sb = s % 4
                vdeps = [rep_nodes[s], del_nodes[c], ring_war(s)]
                return tt(p, vt[:, sb * 512:(sb + 1) * 512], us[:, c * 512:(c + 1) * 512],
                          Bbc[:, s * 512:(s + 1) * 512], ALU.mult, vdeps)

            gxs = {0: emit_gx(0), 1: emit_gx(1)}
            vvs = {0: emit_vv(0), 1: emit_vv(1)}
            for s in range(16):
                sb = s % 4
                hb = s % 2
                col = c * 16 + s
                sdeps = [gxs[s], vvs[s], n_z0, lin(prev_hc, tw, c, s, 2)]
                if (tw - 1, c, s) in prev_scan:
                    sdeps.append(prev_scan[(tw - 1, c, s)])
                sc_ = p.add("dve", (lambda e, sb=sb, hb=hb, col=col:
                                    e.tensor_tensor_scan(ht[:, hb * 512:(hb + 1) * 512],
                                                         Gt[:, sb * 512:(sb + 1) * 512],
                                                         vt[:, sb * 512:(sb + 1) * 512],
                                                         stcols[:, col:col + 1],
                                                         ALU.mult, ALU.add)),
                            sdeps, special=True)
                if c == 0 and s + 3 < 16:
                    emit_rep(s + 3, [])
                if s + 2 < 16:
                    gxs[s + 2] = emit_gx(s + 2)
                    vvs[s + 2] = emit_vv(s + 2)
                st_cp = p.add("dve", (lambda e, hb=hb, col=col:
                                      e.tensor_copy(stcols[:, col:col + 1],
                                                    ht[:, hb * 512 + 511: hb * 512 + 512])),
                              [sc_])
                prev_scan[(tw, c, s)] = st_cp
                hcdep = [sc_, rep_nodes[s], lin(prev_ymm, tw, c, s, 2)]
                hc = tt(p, hct[:, hb * 512:(hb + 1) * 512], ht[:, hb * 512:(hb + 1) * 512],
                        Cbc[:, s * 512:(s + 1) * 512], ALU.mult, hcdep)
                prev_hc[(tw, c, s)] = hc
                if c == 7:
                    prev_rep[(tw, "v", s)] = vvs[s]
                    prev_rep[(tw, "h", s)] = hc
                ydep = [hc, l_id]
                if s == 0 and (tw, c - 2, "gate") in prev_gate:
                    ydep.append(prev_gate[(tw, c - 2, "gate")])
                ym = mm(p, pyt[:, :], idb[:], hct[:, hb * 512:(hb + 1) * 512],
                        s == 0, s == 15, ydep)
                prev_ymm[(tw, c, s)] = ym
            zldep = []
            if prev_gate.get((tw, c - 1)) is not None:
                zldep.append(prev_gate[(tw, c - 1)])
            elif tw > 0:
                zldep.append(prev_gate[(tw - 1, 7)])
            elif c == 0:
                zldep.append(z_stores[-1])
            zl = p.dma("pool", sz[:], zbuf[c * 128:(c + 1) * 128, tcols[0]:tcols[1]], zldep)
            slu = act(p, sz[:], sz[:], AF.Silu, [zl])
            g1 = p.add("dve", (lambda e, c=c: e.scalar_tensor_tensor(
                ta[:], xcw[:, c * 512:(c + 1) * 512], dP[:, c:c + 1],
                (py0, py1)[c % 2][:, :], ALU.mult, ALU.add)),
                [prev_ymm[(tw, c, 15)], l_dp, slu])
            g2 = tt(p, yf[:, c * 512:(c + 1) * 512], ta[:], sz[:], ALU.mult,
                    [g1] + ([prev_out[-1]] if prev_out else []))
            prev_gate[(tw, c)] = g2
            prev_gate[(tw, c, "gate")] = g1
        if debug and tw == 0:
            h_stores.append(p.dma("pool", y_yf, yf[:],
                                  [prev_gate[(tw, cc)] for cc in range(8)]))
        for oc in range(4):
            hdeps = [prev_gate[(tw, cc)] for cc in range(8)] + [cast_done]
            if (tw - 1, oc) in prev_hos:
                hdeps.append(prev_hos[(tw - 1, oc)])
            g = mm_group(p, pho[:, :],
                         [(outwb[:, kc * 512 + oc * 128:kc * 512 + (oc + 1) * 128],
                           yf[:, kc * 512:(kc + 1) * 512]) for kc in range(8)],
                         hdeps + ([prev_hos[(tw, oc - 1)]] if (tw, oc - 1) in prev_hos else []))
            hb = oc % 2
            ev = acopy(p, hor[:, hb * 512:(hb + 1) * 512], pho[:, :],
                       [g] + ([h_stores[-2]] if len(h_stores) >= 2 else []))
            prev_hos[(tw, oc)] = ev
            h_stores.append(p.dma("sp", y_h[oc * 128:(oc + 1) * 128, tw * 512:(tw + 1) * 512],
                                  hor[:, hb * 512:(hb + 1) * 512], [ev]))
        prev_out.append(prev_hos[(tw, 3)])
    if debug:
        h_stores.append(p.dma("pool", y_st, stcols[:],
                              [prev_scan[(7, c, 15)] for c in range(8)]))
    sink(p, h_stores + z_stores + xc_stores)
    return p.build()


def mamba_prog():
    if "mamba" not in _PROGS:
        _PROGS["mamba"] = build_mamba()
    return _PROGS["mamba"]


def build_lnmax():
    p = Prog("lnmax")
    nc = p.nc
    hidT = nc.dram_tensor("hidT", [D, TOK], F32, kind="ExternalInput").ap()
    xT = nc.dram_tensor("xT", [D, TOK], F32, kind="ExternalInput").ap()
    onc = nc.dram_tensor("onc", [128, 1], F32, kind="ExternalInput").ap()
    onr = nc.dram_tensor("onr", [1, 128], F32, kind="ExternalInput").ap()
    idn = nc.dram_tensor("idn", [128, 128], F32, kind="ExternalInput").ap()
    y_x3 = nc.dram_tensor("y_x3", [D, TOK], F32, kind="ExternalOutput").ap()
    y_mx = nc.dram_tensor("y_mx", [1, 1], F32, kind="ExternalOutput").ap()

    ha = p.sbuf("ha", [128, 4 * 512], F32)
    xa = p.sbuf("xa", [128, 4 * 512], F32)
    rsum = p.sbuf("rsum", [128, 8 * 512], F32)
    x1s = p.sbuf("x1s", [128, 8 * 512], F32)
    sq = p.sbuf("sqln", [128, 8 * 512], F32)
    lnA = p.sbuf("lnA", [65, 512], F32)
    rrstd = p.sbuf("rrstd", [1, 512], F32)
    rmr = p.sbuf("rmr", [1, 512], F32)
    mc = p.sbuf("mc", [128, 2], F32)
    mrow = p.sbuf("mrow", [1, 128], F32)
    onc_s = p.sbuf("onc_s", [128, 1], F32)
    onr_s = p.sbuf("onr_s", [1, 128], F32)
    idn_s = p.sbuf("idn_s", [128, 128], F32)
    pst = p.psum("pst", [64, 512])
    pb1 = p.psum("pb1", [128, 512])
    pb2 = p.psum("pb2", [128, 512])
    pmx = p.psum("pmx", [1, 128])

    n_onc = p.dma("sp", onc_s[:], onc)
    n_onr = p.dma("sp", onr_s[:], onr)
    n_idn = p.dma("sp", idn_s[:], idn)
    sc = {"ones_col": onc_s[:, 0:1], "ones_row": onr_s[0:1, :], "sq": sq,
          "stat": pst, "bc1": pb1, "bc2": pb2,
          "rmean": lnA[0:1, :], "rvar": lnA[32:33, :], "rtmp": lnA[64:65, :],
          "rrstd": rrstd[0:1, :], "rmr": rmr[0:1, :]}
    stores = []
    prev_red = None
    mzero = p.add("dve", lambda e: e.memset(mc[:, 0:1], 0.0), [])
    ln_last = None
    res_hist = []
    for tc in range(4):
        res_nodes = []
        for oc in range(8):
            b = oc % 4
            war = [res_hist[-4]] if len(res_hist) >= 4 else []
            l1 = p.dma("pool", ha[:, b * 512:(b + 1) * 512],
                       hidT[oc * 128:(oc + 1) * 128, tc * 512:(tc + 1) * 512], war)
            l2 = p.dma("pool", xa[:, b * 512:(b + 1) * 512],
                       xT[oc * 128:(oc + 1) * 128, tc * 512:(tc + 1) * 512], war)
            r = tt(p, rsum[:, oc * 512:(oc + 1) * 512], ha[:, b * 512:(b + 1) * 512],
                   xa[:, b * 512:(b + 1) * 512], ALU.add,
                   [l1, l2] + ([ln_last] if tc > 0 and oc == 0 else []))
            res_nodes.append(r)
            res_hist.append(r)
        outs = emit_ln(p, lambda c: rsum[:, c * 512:(c + 1) * 512],
                       lambda c: x1s[:, c * 512:(c + 1) * 512], 512, sc, res_nodes)
        ln_last = outs[-1]
        for oc in range(8):
            stores.append(p.dma("sp", y_x3[oc * 128:(oc + 1) * 128, tc * 512:(tc + 1) * 512],
                                x1s[:, oc * 512:(oc + 1) * 512], [outs[oc]]))
            rdeps = [outs[oc], mzero]
            if prev_red is not None:
                rdeps.append(prev_red)
            rd = vred(p, mc[:, 1:2], x1s[:, oc * 512:(oc + 1) * 512], ALU.max, rdeps, absval=True)
            prev_red = tt(p, mc[:, 0:1], mc[:, 0:1], mc[:, 1:2], ALU.max, [rd])
    n_tp = p.add("pe", lambda e: e.transpose(pmx[0:1, :], mc[:, 0:1], idn_s[:, :]),
                 [prev_red, n_idn])
    n_ev = acopy(p, mrow[0:1, :], pmx[0:1, :], [n_tp])
    n_fm = vred(p, mc[0:1, 0:1], mrow[0:1, :], ALU.max, [n_ev])
    stores.append(p.dma("sp", y_mx, mc[0:1, 0:1], [n_fm]))
    sink(p, stores)
    return p.build()


def lnmax_prog():
    if "lnmax" not in _PROGS:
        _PROGS["lnmax"] = build_lnmax()
    return _PROGS["lnmax"]


def _conv_layout(w):
    return f32c(w.reshape(8, 128, 4).transpose(1, 0, 2).reshape(128, 32))


def _col_layout(v):
    return f32c(v.reshape(8, 128).T)


def _al_layout(a):
    return f32c(a.reshape(8, 128, 16).transpose(1, 0, 2).reshape(128, 128))


def make_esel():
    e = np.zeros((80, 2048), np.float32)
    for k in range(16):
        for s in range(16):
            if k == s:
                e[32 + k, s * 128:(s + 1) * 128] = 1.0
                e[64 + k, s * 128:(s + 1) * 128] = 1.0
    return e


IDENT_BF = np.eye(128, dtype=np.float32).astype(bf16)
ESEL = make_esel()


def mamba_inmap(xd, prm, tag):
    xpad = np.zeros((DM, XPD), np.float32)
    xpad[:, 3:3 + 4096] = xd.T
    return {
        "xdT": xpad,
        "inwT": f32c(prm[f"in_w{tag}"].T),
        "xpjT": f32c(prm[f"xproj_w{tag}"].T),
        "dtwT": f32c(prm[f"dt_w{tag}"].T),
        "outwT": f32c(prm[f"out_w{tag}"].T),
        "convW": _conv_layout(prm[f"conv_w{tag}"]),
        "convB": _col_layout(prm[f"conv_b{tag}"]),
        "dtB": _col_layout(prm[f"dt_b{tag}"]),
        "DPc": _col_layout(prm[f"D{tag}"]),
        "AL": _al_layout(prm[f"A_log{tag}"]),
        "esel": ESEL, "idnb": IDENT_BF,
    }


def run_mamba(x2, prm):
    in_maps = []
    for c in range(8):
        b, dirb = c // 2, c % 2
        if dirb == 0:
            xd = x2[b, :, 0:512]
            in_maps.append(mamba_inmap(xd, prm, "1"))
        else:
            xd = x2[b, ::-1, 512:1024]
            in_maps.append(mamba_inmap(xd, prm, "2"))
    return run_bass_kernel_spmd(mamba_prog(), in_maps, core_ids=list(range(8))).results


def run_lnmax(x2, mres):
    in_maps = []
    for c in range(8):
        b, h = c // 2, c % 2
        sl = slice(h * 2048, (h + 1) * 2048)
        hid = np.concatenate([mres[2 * b]["y_h"][:, sl], mres[2 * b + 1]["y_h"][:, sl]], axis=0)
        in_maps.append({"hidT": f32c(hid), "xT": f32c(x2[b, sl, :].T),
                        "onc": ONES_COL, "onr": ONES_ROW, "idn": IDENT})
    res = run_bass_kernel_spmd(lnmax_prog(), in_maps, core_ids=list(range(8))).results
    x3 = np.empty((B, S, D), np.float32)
    for c in range(8):
        b, h = c // 2, c % 2
        x3[b, h * 2048:(h + 1) * 2048, :] = res[c]["y_x3"].T
    return x3, [res[c]["y_mx"] for c in range(8)]



def kernel(L0_attn_x, L0_attn_Wq, L0_attn_Wk, L0_attn_Wv, L0_attn_Wo,
           L0_ffn_w_gate, L0_ffn_w_up, L0_ffn_w_down,
           L1_attn_in_w1, L1_attn_conv_w1, L1_attn_conv_b1, L1_attn_xproj_w1,
           L1_attn_dt_w1, L1_attn_dt_b1, L1_attn_A_log1, L1_attn_D1, L1_attn_out_w1,
           L1_attn_in_w2, L1_attn_conv_w2, L1_attn_conv_b2, L1_attn_xproj_w2,
           L1_attn_dt_w2, L1_attn_dt_b2, L1_attn_A_log2, L1_attn_D2, L1_attn_out_w2,
           L1_ffn_w_gate, L1_ffn_w_up, L1_ffn_w_down):
    x = np.asarray(L0_attn_x, np.float32)
    dq_packs = [dq_pack(c, np.asarray(L0_ffn_w_up), np.asarray(L0_ffn_w_gate),
                        np.asarray(L1_ffn_w_up), np.asarray(L1_ffn_w_gate),
                        np.asarray(L0_ffn_w_down), np.asarray(L1_ffn_w_down))
                for c in range(8)]
    qkv_res = run_qkv(x, L0_attn_Wq, L0_attn_Wk, L0_attn_Wv, dq_packs)
    attn_res = run_attn(x, qkv_res, L0_attn_Wo)
    x1 = gather_interleaved(attn_res, "y_x1")
    mx1 = [attn_res[c]["y_mx"] for c in range(8)]
    up0, g0, up1, g1, dn0, dn1 = dq_unpack([qkv_res[c]["y_dq"] for c in range(8)])
    x2, mx2 = run_ffn(x1, mx1, np.ascontiguousarray(up0.T), np.ascontiguousarray(g0.T),
                      np.ascontiguousarray(dn0.T))
    prm = {
        "in_w1": L1_attn_in_w1, "conv_w1": L1_attn_conv_w1, "conv_b1": L1_attn_conv_b1,
        "xproj_w1": L1_attn_xproj_w1, "dt_w1": L1_attn_dt_w1, "dt_b1": L1_attn_dt_b1,
        "A_log1": L1_attn_A_log1, "D1": L1_attn_D1, "out_w1": L1_attn_out_w1,
        "in_w2": L1_attn_in_w2, "conv_w2": L1_attn_conv_w2, "conv_b2": L1_attn_conv_b2,
        "xproj_w2": L1_attn_xproj_w2, "dt_w2": L1_attn_dt_w2, "dt_b2": L1_attn_dt_b2,
        "A_log2": L1_attn_A_log2, "D2": L1_attn_D2, "out_w2": L1_attn_out_w2,
    }
    prm = {k: np.asarray(v, np.float32) for k, v in prm.items()}
    mres = run_mamba(x2, prm)
    x3, mx3 = run_lnmax(x2, mres)
    x4, _ = run_ffn(x3, mx3, np.ascontiguousarray(up1.T), np.ascontiguousarray(g1.T),
                    np.ascontiguousarray(dn1.T))
    return x4
